# version 04
# baseline (speedup 1.0000x reference)
import math
import numpy as np

import concourse.bass as bass
import concourse.mybir as mybir
from concourse.bass_utils import run_bass_kernel_spmd

F32 = mybir.dt.float32
BF16 = mybir.dt.bfloat16
AF = mybir.ActivationFunctionType
ALU = mybir.AluOpType
AX = mybir.AxisListType

D = 1024
DC = 8
S = 2048
NSEQ = 2
T = 4096
NCH = 32
CH_SEQ = 16
HID = 2048
G = 256
NG = 8
E = 8
H = 4096
NQ = 4
QH = 1024
NHEAD = 16
DK = 64
EPS_LN = 1e-5
EPS_FFN = 1e-6
NEGBIG = -1.0e30

DMA_Q = {"sync": "sS", "gpsimd": "sG", "scalar": "sC"}

PHASES = {"attn_a": True, "ffn1": True, "attn_b": True, "ffn2": True}
DEBUG = False
DEBUG_POINT = "attn_a"
NSEQ_RUN = NSEQ

WM_QK = 0
WM_QKACT = 1024
WM_LINKV = 3072
WM_LKT = 5120
WM_QQ = 5376
WM_QKD = 5632
WM_LQ = 5888
WM_ATT = 6144
WM_WOUT = 6656
WM_DQ = 0
WM_DKV = 2048
WM_CKV = 4096
WM_UK = 0
WM_UV = 2048
WM_UQ = 0
WM_PSB = 4096
WM_PT = 6144
WM_QT = 8192
WM_AO = 9216

AF_XA = 0
AF_V = 2048
AF_GATE = 6144
AF_GATED = 10240

WB_W1Q = (0, 8192)
WB_W2Q = (16384, 24576)

C_LNG = 0
C_LNB = 8
C_GAM = 16
C_BET = 20
C_BQK = 24
C_RLS = 32
C_ELS = 64


class Prog:
    def __init__(self, nc, sems):
        self.nc = nc
        self.sems = sems
        self.cur = None
        self.eng = None
        self.sim = {}
        self.final = {}
        self.ns = {}

    def begin(self, name, eng):
        self.cur = name
        self.eng = eng
        self.sim = {k: 0 for k in self.sems}
        self.ns = {}

    def end(self):
        if self.final:
            assert self.final == self.sim, (
                f"sim divergence on {self.cur}: {self.final} vs {self.sim}")
        else:
            self.final = dict(self.sim)

    def _emit(self, engine, make, inc):
        if inc is not None:
            self.sim[inc[0]] += inc[1]
        if engine == self.cur:
            instr = make(self.eng)
            if inc is not None:
                instr.then_inc(self.sems[inc[0]], inc[1])
        return self.sim[inc[0]] if inc is not None else None

    def wait(self, engine, sem, v):
        if v is None:
            return
        if engine == self.cur:
            self.eng.wait_ge(self.sems[sem], v)

    def tag(self, sem):
        return self.sim[sem]

    def dma(self, q, out, in_, **kw):
        return self._emit(q, lambda e: e.dma_start(out=out, in_=in_, **kw),
                          (DMA_Q[q], 16))

    def mm(self, out, lhsT, rhs, start, stop, inc=False, **kw):
        return self._emit(
            "tensor",
            lambda e: e.matmul(out, lhsT, rhs, start=start, stop=stop, **kw),
            ("sPE", 1) if inc else None)

    def tp(self, out, in_, ident, inc=False):
        return self._emit("tensor", lambda e: e.transpose(out, in_, ident),
                          ("sPE", 1) if inc else None)

    def vtt(self, out, in0, in1, op, inc=False):
        return self._emit("vector", lambda e: e.tensor_tensor(out, in0, in1, op),
                          ("sV", 1) if inc else None)

    def vts(self, out, in0, s1, s2, op0, op1, inc=False):
        return self._emit(
            "vector", lambda e: e.tensor_scalar(out, in0, s1, s2, op0, op1),
            ("sV", 1) if inc else None)

    def vts1(self, out, in0, s1, op0, inc=False):
        return self._emit(
            "vector", lambda e: e.tensor_scalar(out, in0, s1, None, op0),
            ("sV", 1) if inc else None)

    def vstt(self, out, in0, scalar, in1, op0, op1, inc=False):
        return self._emit(
            "vector",
            lambda e: e.scalar_tensor_tensor(out, in0, scalar, in1, op0, op1),
            ("sV", 1) if inc else None)

    def vred(self, out, in_, op, inc=False, **kw):
        return self._emit(
            "vector", lambda e: e.tensor_reduce(out, in_, AX.X, op, **kw),
            ("sV", 1) if inc else None)

    def vcopy(self, out, in_, inc=False):
        return self._emit("vector", lambda e: e.tensor_copy(out, in_),
                          ("sV", 1) if inc else None)

    def vrecip(self, out, in_, inc=False):
        return self._emit("vector", lambda e: e.reciprocal(out, in_),
                          ("sV", 1) if inc else None)

    def vmemset(self, ap, c, inc=False):
        return self._emit("vector", lambda e: e.memset(ap, c),
                          ("sV", 1) if inc else None)

    def act(self, out, in_, func, bias=None, scale=None, inc=False):
        kw = {}
        if bias is not None:
            kw["bias"] = bias
        if scale is not None:
            kw["scale"] = scale
        return self._emit("scalar",
                          lambda e: e.activation(out, in_, func, **kw),
                          ("sA", 1) if inc else None)

    def gmemset(self, ap, c, inc=False):
        return self._emit("gpsimd", lambda e: e.memset(ap, c),
                          ("sP", 1) if inc else None)

    def gaffine(self, out, in_, pattern, compare_op, fill, base, chmul,
                inc=False):
        return self._emit(
            "gpsimd",
            lambda e: e.affine_select(out=out, in_=in_, pattern=pattern,
                                      compare_op=compare_op, fill=fill,
                                      base=base, channel_multiplier=chmul),
            ("sP", 1) if inc else None)

    def barrier(self, sb):
        stats = sb["stats"]
        pe = self.ns.get("pe_last")
        self.vmemset(stats[:, 252:253], 0.0, inc=True)
        self.act(stats[:, 253:254], stats[:, 253:254], AF.Copy, scale=1.0,
                 inc=True)
        self.gmemset(stats[:, 254:255], 0.0, inc=True)
        targets = [("sV", self.tag("sV")), ("sA", self.tag("sA")),
                   ("sP", self.tag("sP")), ("sS", self.tag("sS")),
                   ("sG", self.tag("sG")), ("sC", self.tag("sC"))]
        if pe is not None:
            targets.append(("sPE", pe))
        for engname in ("sync", "tensor", "vector", "scalar", "gpsimd"):
            for sem, v in targets:
                self.wait(engname, sem, v)



def build_nc():
    nc = bass.Bass("TRN2", target_bir_lowering=False)

    dram = {}

    def din(name, shape, dtype=F32):
        dram[name] = nc.dram_tensor(name, list(shape), dtype,
                                    kind="ExternalInput").ap()

    din("x", (T, D))
    din("ln_g", (D,))
    din("ln_b", (D,))
    din("W_h", (D, 2 * HID))
    din("b_h", (2 * HID,))
    din("W_qk", (D, 128))
    din("b_qk", (128,))
    din("os_gamma", (4, 128))
    din("os_beta", (4, 128))
    din("rel_emb", (32, 1))
    din("W_out", (HID, D))
    din("b_out", (D,))
    din("W_dq", (D, 256))
    din("W_uq", (256, D))
    din("W_dkv", (D, 256))
    din("W_uk", (256, D))
    din("W_uv", (256, D))
    din("W_o", (D, D))
    din("r_ls", (D,))
    din("r_lb", (D,))
    din("r_w", (E, D))
    din("r_b", (E,))
    din("e_ls", (E, D))
    din("e_lb", (E, D))
    din("w1", (E, H, D))
    din("b1", (E, H))
    din("w2", (E, D, H))
    din("b2", (E, D))
    din("OH", (32, 511))

    y_out = nc.dram_tensor("y", [T, D], F32, kind="ExternalOutput").ap()
    dbg = {}
    if DEBUG:
        for nm, shape, dt in (("d_stats", [128, 256], F32),
                              ("d_wmed", [128, 10240], BF16),
                              ("d_actfm", [128, 16384], BF16),
                              ("d_wbig", [128, 4096], BF16),
                              ("d_rows", [1, 3200], BF16),
                              ("d_cols", [128, 192], F32),
                              ("d_cols2", [128, 64], F32),
                              ("d_b1col", [128, 256], F32),
                              ("d_biasT", [128, 512], BF16),
                              ("d_gates", [128, 128], F32),
                              ("d_hbuf", [128, 4096], BF16),
                              ("d_rows32", [1, 640], F32),
                              ("d_bvec", [1, 511], F32),
                              ("d_rwT", [128, 64], BF16),
                              ("d_w1T", [128, 4096], BF16)):
            dbg[nm] = nc.dram_tensor(nm, shape, dt, kind="ExternalOutput").ap()

    scr = {}

    def dscr(name, shape, dtype=BF16):
        scr[name] = nc.dram_tensor(name, list(shape), dtype,
                                   kind="Internal").ap()

    dscr("w1c", (E, H, D))
    dscr("w2c", (E, D, H))
    dscr("w1T", (E, D, H))
    dscr("Whc", (D, 2 * HID))
    dscr("Wqkc", (D, 128))
    dscr("Woutc", (HID, D))
    dscr("Wdqc", (D, 256))
    dscr("Wdkvc", (D, 256))
    dscr("Wuqc", (256, D))
    dscr("Wukc", (256, D))
    dscr("Wuvc", (256, D))
    dscr("Woc", (D, D))
    dscr("rowscr", (1, 4096), F32)
    dscr("bvec", (1, 511), F32)

    import contextlib
    with contextlib.ExitStack() as est:
        EE = est.enter_context
        sb = {}

        def sbuf(name, shape, dtype):
            sb[name] = EE(nc.sbuf_tensor(name, list(shape), dtype))

        sbuf("xres", (128, NCH * D), BF16)
        sbuf("wbig", (128, 32768), BF16)
        sbuf("actfm", (128, 16384), BF16)
        sbuf("wmed", (128, 10240), BF16)
        sbuf("hbuf", (128, 4096), BF16)
        sbuf("scr32", (128, 1024), F32)
        sbuf("stats", (128, 256), F32)
        sbuf("gates", (128, 128), F32)
        sbuf("biasT", (128, 512), BF16)
        sbuf("cmask", (128, 128), F32)
        sbuf("ident", (128, 128), BF16)
        sbuf("b1col", (128, 256), F32)
        sbuf("cols", (128, 192), F32)
        sbuf("cols2", (128, 64), F32)
        sbuf("colsb", (128, 96), BF16)
        sbuf("relsb", (32, 1), BF16)
        sbuf("ohsb", (32, 512), BF16)
        sbuf("rows", (1, 3200), BF16)
        sbuf("rows32", (1, 640), F32)
        sbuf("onesb", (1, 128), BF16)
        sbuf("rwT", (128, 64), BF16)

        ps = {}
        for nm, shape, dt in (("pA", [128, 512], F32), ("pB", [128, 512], F32),
                              ("pC", [128, 512], F32),
                              ("pE2", [128, 1024], F32),
                              ("pF2", [128, 1024], F32),
                              ("pTP", [128, 1024], BF16)):
            ps[nm] = EE(nc.psum_tensor(nm, shape, dt))

        sems = {}
        for sname in ("sS", "sG", "sC", "sPE", "sV", "sA", "sP"):
            sems[sname] = EE(nc.semaphore(sname))

        block = EE(nc.Block())
        prog = Prog(nc, sems)

        def emit_all(engine, eng):
            prog.begin(engine, eng)
            program(prog, dram, scr, sb, ps, y_out, dbg if DEBUG else None)
            prog.end()

        @block.sync
        def _(e):
            emit_all("sync", e)

        @block.tensor
        def _(e):
            emit_all("tensor", e)

        @block.vector
        def _(e):
            emit_all("vector", e)

        @block.scalar
        def _(e):
            emit_all("scalar", e)

        @block.gpsimd
        def _(e):
            emit_all("gpsimd", e)

    return nc


def program(p, dram, scr, sb, ps, y_out, dbg=None):
    phase_load_and_prep(p, dram, scr, sb, ps)
    p.barrier(sb)
    if PHASES["attn_a"]:
        for seq in range(NSEQ_RUN):
            attn_a_seq(p, dram, scr, sb, ps, seq)
            p.barrier(sb)
    if dbg is not None and DEBUG_POINT == "attn_a":
        dump_debug(p, sb, dbg)
    prep_ffn_weights(p, dram, scr, sb, ps)
    p.barrier(sb)
    if PHASES["ffn1"]:
        for seq in range(NSEQ_RUN):
            ffn_seq(p, dram, scr, sb, ps, seq)
            p.barrier(sb)
    if dbg is not None and DEBUG_POINT == "ffn1":
        dump_debug(p, sb, dbg)
    if PHASES["attn_b"]:
        for seq in range(NSEQ_RUN):
            attn_b_seq(p, dram, scr, sb, ps, seq)
            p.barrier(sb)
    if PHASES["ffn2"]:
        for seq in range(NSEQ_RUN):
            ffn_seq(p, dram, scr, sb, ps, seq)
            p.barrier(sb)

    phase_output(p, sb, y_out)


def dump_debug(p, sb, dbg):
    pairs = [("d_stats", sb["stats"][:]), ("d_wmed", sb["wmed"][:]),
             ("d_actfm", sb["actfm"][:]), ("d_wbig", sb["wbig"][:, 0:4096]),
             ("d_rows", sb["rows"][:]), ("d_cols", sb["cols"][:]),
             ("d_cols2", sb["cols2"][:]), ("d_b1col", sb["b1col"][:]),
             ("d_biasT", sb["biasT"][:]), ("d_gates", sb["gates"][:]),
             ("d_hbuf", sb["hbuf"][:]), ("d_rows32", sb["rows32"][:]),
             ("d_rwT", sb["rwT"][:])]
    for nm, ap in pairs:
        p.dma("sync", dbg[nm], ap)
    p.dma("sync", dbg["d_bvec"], p.ns["bvec_scr"])
    p.dma("sync", dbg["d_w1T"], p.ns["w1T_scr"][0][0:128, :])
    p.wait("sync", "sS", p.tag("sS"))
    p.barrier(sb)



def ln_stats_seq(p, sb, seq, double, eps1, eps2):
    xres, stats, scr32, hbuf = sb["xres"], sb["stats"], sb["scr32"], sb["hbuf"]
    n = float(D)
    ssq_rd = {0: None, 1: None}
    for ci in range(CH_SEQ):
        sl = ci % 2
        xc = xres[:, (seq * CH_SEQ + ci) * D:(seq * CH_SEQ + ci + 1) * D]
        sq = hbuf[:, sl * D:(sl + 1) * D]
        p.wait("scalar", "sV", ssq_rd[sl])
        p.act(sq, xc, AF.Square, inc=True)
        sq_tag = p.tag("sA")
        p.vred(stats[:, 64 + ci:65 + ci], xc, ALU.add)
        p.wait("vector", "sA", sq_tag)
        p.vred(stats[:, 80 + ci:81 + ci], sq, ALU.add, inc=True)
        ssq_rd[sl] = p.tag("sV")
    p.wait("vector", "sV", p.tag("sV"))
    p.vts1(stats[:, 0:16], stats[:, 64:80], 1.0 / n, ALU.mult, inc=True)
    p.wait("vector", "sV", p.tag("sV"))
    p.vtt(stats[:, 128:144], stats[:, 0:16], stats[:, 0:16], ALU.mult,
          inc=True)
    p.wait("vector", "sV", p.tag("sV"))
    p.vstt(stats[:, 144:160], stats[:, 80:96], 1.0 / n, stats[:, 128:144],
           ALU.mult, ALU.subtract, inc=True)
    p.wait("scalar", "sV", p.tag("sV"))
    p.act(stats[:, 160:176], stats[:, 144:160], AF.Sqrt,
          bias=stats[:, 248:249] if eps1 == EPS_LN else stats[:, 249:250],
          scale=1.0, inc=True)
    p.wait("vector", "sA", p.tag("sA"))
    p.vrecip(stats[:, 16:32], stats[:, 160:176], inc=True)
    p.wait("vector", "sV", p.tag("sV"))
    if not double:
        return
    p.vtt(stats[:, 128:144], stats[:, 16:32], stats[:, 16:32], ALU.mult,
          inc=True)
    p.wait("vector", "sV", p.tag("sV"))
    p.vtt(stats[:, 144:160], stats[:, 144:160], stats[:, 128:144], ALU.mult,
          inc=True)
    p.wait("scalar", "sV", p.tag("sV"))
    p.wait("vector", "sV", p.tag("sV"))
    p.act(stats[:, 160:176], stats[:, 144:160], AF.Sqrt,
          bias=stats[:, 248:249] if eps2 == EPS_LN else stats[:, 249:250],
          scale=1.0, inc=True)
    p.wait("vector", "sA", p.tag("sA"))
    p.vrecip(stats[:, 32:48], stats[:, 160:176], inc=True)
    p.wait("vector", "sV", p.tag("sV"))
    p.vtt(stats[:, 48:64], stats[:, 16:32], stats[:, 32:48], ALU.mult,
          inc=True)
    p.wait("vector", "sV", p.tag("sV"))


def build_fm_chunk(p, sb, ps, seq, ci, dst_view, double, slot):
    xres, stats, scr32, hbuf = sb["xres"], sb["stats"], sb["scr32"], sb["hbuf"]
    ident, pTP = sb["ident"], ps["pTP"]
    st = p.ns.setdefault("fm", {"evac": None, "slot": {}})
    xc = xres[:, (seq * CH_SEQ + ci) * D:(seq * CH_SEQ + ci + 1) * D]
    stage = hbuf[:, slot * D:(slot + 1) * D]
    p.wait("vector", "sPE", st["slot"].get(slot))
    rcol = stats[:, 48 + ci:49 + ci] if double else stats[:, 16 + ci:17 + ci]
    p.vts(stage, xc, stats[:, ci:ci + 1], rcol, ALU.subtract, ALU.mult,
          inc=True)
    aff = p.tag("sV")
    p.wait("tensor", "sV", aff)
    p.wait("tensor", "sV", st["evac"])
    for dc in range(DC):
        p.tp(pTP[:, dc * 128:(dc + 1) * 128], stage[:, dc * 128:(dc + 1) * 128],
             ident[:], inc=(dc == DC - 1))
    tptag = p.tag("sPE")
    st["slot"][slot] = tptag
    p.wait("vector", "sPE", tptag)
    ptv = pTP[:].rearrange("q (dc t) -> q dc t", t=128)
    p.vcopy(dst_view, ptv, inc=True)
    st["evac"] = p.tag("sV")
    return p.tag("sV")



def phase_load_and_prep(p, dram, scr, sb, ps):
    ns = p.ns
    xres, wbig, wmed = sb["xres"], sb["wbig"], sb["wmed"]
    cols, cols2, colsb = sb["cols"], sb["cols2"], sb["colsb"]

    xv = dram["x"].rearrange("(c p) d -> p c d", p=128)
    for i in range(4):
        p.dma("gpsimd", xres[:, i * 8 * D:(i + 1) * 8 * D],
              xv[:, i * 8:(i + 1) * 8, :])

    p.gmemset(sb["ident"][:], 0.0)
    p.gaffine(sb["ident"][:], sb["ident"][:], [[-1, 128]], ALU.not_equal, 1.0,
              0, 1)
    p.gmemset(sb["cmask"][:], 0.0)
    p.gaffine(sb["cmask"][:], sb["cmask"][:], [[-1, 128]], ALU.is_ge, NEGBIG,
              0, 1, inc=True)
    p.vmemset(sb["onesb"][:], 1.0)
    p.vmemset(sb["stats"][:, 248:249], EPS_LN)
    p.vmemset(sb["stats"][:, 249:250], EPS_FFN, inc=True)

    for nm_in, nm_out in (("W_h", "Whc"), ("W_qk", "Wqkc"), ("W_out", "Woutc"),
                          ("W_dq", "Wdqc"), ("W_dkv", "Wdkvc"),
                          ("W_uq", "Wuqc"), ("W_uk", "Wukc"),
                          ("W_uv", "Wuvc"), ("W_o", "Woc")):
        p.dma("gpsimd", scr[nm_out], dram[nm_in])
    ns["small_cast"] = p.tag("sG")
    for e in range(E):
        p.dma("gpsimd", scr["w1c"][e], dram["w1"][e])
        p.dma("gpsimd", scr["w2c"][e], dram["w2"][e])
    ns["big_cast"] = p.tag("sG")

    for dc in range(DC):
        p.dma("sync", cols[:, C_LNG + dc:C_LNG + dc + 1],
              dram["ln_g"][dc * 128:(dc + 1) * 128])
        p.dma("sync", cols[:, C_LNB + dc:C_LNB + dc + 1],
              dram["ln_b"][dc * 128:(dc + 1) * 128])
        p.dma("sync", cols[:, C_RLS + dc:C_RLS + dc + 1],
              dram["r_ls"][dc * 128:(dc + 1) * 128])
        p.dma("gpsimd", colsb[:, dc:dc + 1],
              dram["ln_b"][dc * 128:(dc + 1) * 128])
        p.dma("gpsimd", colsb[:, 8 + dc:9 + dc],
              dram["r_lb"][dc * 128:(dc + 1) * 128])
    for h in range(4):
        p.dma("sync", cols[:, C_GAM + h:C_GAM + h + 1], dram["os_gamma"][h, :])
        p.dma("sync", cols[:, C_BET + h:C_BET + h + 1], dram["os_beta"][h, :])
    for e in range(E):
        for dc in range(DC):
            p.dma("sync", cols[:, C_ELS + e * 8 + dc:C_ELS + e * 8 + dc + 1],
                  dram["e_ls"][e, dc * 128:(dc + 1) * 128])
            p.dma("gpsimd", colsb[:, 16 + e * 8 + dc:17 + e * 8 + dc],
                  dram["e_lb"][e, dc * 128:(dc + 1) * 128])
    for e in range(E):
        for hc in range(32):
            p.dma("sync", sb["b1col"][:, e * 32 + hc:e * 32 + hc + 1],
                  dram["b1"][e, hc * 128:(hc + 1) * 128])
    p.dma("gpsimd", sb["relsb"][:], dram["rel_emb"])
    p.dma("gpsimd", sb["ohsb"][:, 0:511], dram["OH"])
    p.dma("gpsimd", sb["rows"][0:1, 2048:3072], dram["b_out"][None, :])
    ns["cols_f32"] = p.tag("sS")
    ns["cols_bf"] = p.tag("sG")

    p.wait("vector", "sS", ns["cols_f32"])
    p.vts1(cols[:, C_GAM + 1:C_GAM + 2], cols[:, C_GAM + 1:C_GAM + 2],
           1.0 / S, ALU.mult)
    p.vts1(cols[:, C_BET + 1:C_BET + 2], cols[:, C_BET + 1:C_BET + 2],
           1.0 / S, ALU.mult, inc=True)

    p.wait("tensor", "sG", ns["cols_bf"])
    p.mm(ps["pA"][0:1, 0:511], sb["relsb"][:], sb["ohsb"][:, 0:511],
         start=True, stop=True, inc=True)
    t = p.tag("sPE")
    p.wait("vector", "sPE", t)
    p.vcopy(sb["rows32"][0:1, 0:511], ps["pA"][0:1, 0:511], inc=True)
    bvec_evac = p.tag("sV")
    t = p.tag("sV")
    p.wait("sync", "sV", t)
    p.dma("sync", scr["bvec"], sb["rows32"][0:1, 0:511])
    p.ns["bvec_scr"] = scr["bvec"]
    t = p.tag("sS")
    p.wait("gpsimd", "sS", t)
    for jc in range(2):
        for row in range(128):
            j = jc * 128 + row
            p.dma("gpsimd", sb["biasT"][row:row + 1, jc * 256:jc * 256 + 256],
                  scr["bvec"][0:1, 255 - j:255 - j + 256])

    p.wait("sync", "sG", ns["small_cast"])
    whv = scr["Whc"].rearrange("(dcb q) h -> q dcb h", q=128)
    p.dma("sync", wbig[:].rearrange("q (dcb h) -> q dcb h", h=4096), whv)
    wqv = scr["Wqkc"].rearrange("(dcb q) h -> q dcb h", q=128)
    p.dma("sync", wmed[:, 0:1024].rearrange("q (dcb h) -> q dcb h", h=128),
          wqv)
    ns["wh_raw"] = p.tag("sS")

    p.wait("tensor", "sS", ns["wh_raw"])
    p.wait("tensor", "sG", ns["cols_bf"])
    prev_use = {0: bvec_evac, 1: None}
    for nt in range(8):
        pt = ps["pA"] if nt % 2 == 0 else ps["pB"]
        sl = pt[0:1, 0:512]
        p.wait("tensor", "sV", prev_use[nt % 2])
        for dc in range(DC):
            p.mm(sl, colsb[:, dc:dc + 1],
                 wbig[:, dc * 4096 + nt * 512:dc * 4096 + nt * 512 + 512],
                 start=(dc == 0), stop=(dc == DC - 1), inc=(dc == DC - 1))
        mmtag = p.tag("sPE")
        p.wait("sync", "sV", p.ns.get("bh_scr_free"))
        p.dma("sync", sb["scr32"][0:1, 0:512],
              dram["b_h"][None, nt * 512:(nt + 1) * 512])
        ld = p.tag("sS")
        p.wait("vector", "sPE", mmtag)
        p.wait("vector", "sS", ld)
        p.vtt(sb["scr32"][0:1, 0:512], sl, sb["scr32"][0:1, 0:512], ALU.add,
              inc=True)
        prev_use[nt % 2] = p.tag("sV")
        p.ns["bh_scr_free"] = p.tag("sV")
        p.wait("vector", "sV", p.tag("sV"))
        if nt < 4:
            p.vcopy(sb["rows"][0:1, nt * 512:(nt + 1) * 512],
                    sb["scr32"][0:1, 0:512], inc=True)
            prev_use[nt % 2] = p.tag("sV")
            p.ns["bh_scr_free"] = p.tag("sV")
        else:
            t = p.tag("sV")
            p.wait("sync", "sV", t)
            p.dma("sync", scr["rowscr"][0:1, nt * 512:(nt + 1) * 512],
                  sb["scr32"][0:1, 0:512])
    t = p.tag("sS")
    p.wait("sync", "sS", t)
    for gc in range(16):
        a = 2048 + gc * 128
        p.dma("sync", cols2[:, gc:gc + 1], scr["rowscr"][0:1, a:a + 128])

    for dc in range(DC):
        p.mm(ps["pC"][:, 0:1], wmed[:, dc * 128:(dc + 1) * 128],
             colsb[:, dc:dc + 1], start=(dc == 0), stop=(dc == DC - 1),
             inc=(dc == DC - 1))
    t = p.tag("sPE")
    p.dma("sync", cols[:, C_BQK + 1:C_BQK + 2], dram["b_qk"][:])
    ld = p.tag("sS")
    p.wait("vector", "sPE", t)
    p.wait("vector", "sS", ld)
    p.vtt(cols[:, C_BQK:C_BQK + 1], ps["pC"][:, 0:1],
          cols[:, C_BQK + 1:C_BQK + 2], ALU.add, inc=True)
    bqk_read = p.tag("sV")

    p.wait("vector", "sS", ns["cols_f32"])
    p.wait("vector", "sPE", t)
    for dc in range(DC):
        p.vts1(wbig[:, dc * 4096:(dc + 1) * 4096],
               wbig[:, dc * 4096:(dc + 1) * 4096],
               cols[:, C_LNG + dc:C_LNG + dc + 1], ALU.mult)
        p.vts1(wmed[:, dc * 128:(dc + 1) * 128],
               wmed[:, dc * 128:(dc + 1) * 128],
               cols[:, C_LNG + dc:C_LNG + dc + 1], ALU.mult,
               inc=(dc == DC - 1))

    for dc in range(DC):
        for e in range(E):
            p.dma("gpsimd", sb["rwT"][:, dc * 8 + e:dc * 8 + e + 1],
                  dram["r_w"][e, dc * 128:(dc + 1) * 128])
    t = p.tag("sG")
    p.wait("tensor", "sG", t)
    p.wait("tensor", "sV", bqk_read)
    for dc in range(DC):
        p.mm(ps["pC"][0:8, 8:9], sb["rwT"][:, dc * 8:(dc + 1) * 8],
             colsb[:, 8 + dc:9 + dc], start=(dc == 0), stop=(dc == DC - 1),
             inc=(dc == DC - 1))
    t = p.tag("sPE")
    p.dma("sync", sb["rows32"][0:1, 520:528], dram["r_b"][None, :])
    ld = p.tag("sS")
    p.wait("vector", "sPE", t)
    p.wait("vector", "sS", ld)
    p.vcopy(sb["scr32"][0:8, 512:513], ps["pC"][0:8, 8:9], inc=True)
    t = p.tag("sV")
    p.wait("sync", "sV", t)
    p.dma("sync", scr["rowscr"][0:1, 3584:3592], sb["scr32"][0:8, 512:513])
    t = p.tag("sS")
    p.wait("sync", "sS", t)
    p.dma("sync", sb["rows32"][0:1, 512:520], scr["rowscr"][0:1, 3584:3592])
    t = p.tag("sS")
    p.wait("vector", "sS", t)
    p.vtt(sb["rows32"][0:1, 512:520], sb["rows32"][0:1, 512:520],
          sb["rows32"][0:1, 520:528], ALU.add, inc=True)
    p.wait("vector", "sV", p.tag("sV"))
    p.vcopy(sb["rows"][0:1, 3072:3080], sb["rows32"][0:1, 512:520], inc=True)
    for dc in range(DC):
        p.vts1(sb["rwT"][:, dc * 8:(dc + 1) * 8],
               sb["rwT"][:, dc * 8:(dc + 1) * 8],
               cols[:, C_RLS + dc:C_RLS + dc + 1], ALU.mult,
               inc=(dc == DC - 1))
    p.ns["pe_last"] = p.tag("sPE")

def prep_ffn_weights(p, dram, scr, sb, ps):
    p.ns["w1T_scr"] = scr["w1T"]
    """w1T scratch: transpose-DMA strips of w1c, fold e_ls rows, store.
    b1col += lb_e @ W1^T via [128,1]-column matmuls into pC[:,128:160].
    PSUM note: single start=True on the very first MM of the pC session."""
    ns = p.ns
    hbuf, b1col, cols, colsb = sb["hbuf"], sb["b1col"], sb["cols"], sb["colsb"]
    p.wait("sync", "sG", ns["big_cast"])
    store_tag = None
    scale_tag = None
    b1_use = None
    for e in range(E):
        mm_tag = None
        for dc in range(DC):
            p.wait("sync", "sS", store_tag)
            p.wait("sync", "sV", scale_tag)
            p.dma("sync", hbuf[:], scr["w1c"][e][:, dc * 128:(dc + 1) * 128],
                  transpose=True)
            ld = p.tag("sS")
            p.wait("tensor", "sS", ld)
            if dc == 0:
                p.wait("tensor", "sV", b1_use)
            for hc in range(32):
                p.mm(ps["pC"][:, 128 + hc:129 + hc],
                     hbuf[:, hc * 128:(hc + 1) * 128],
                     colsb[:, 16 + e * 8 + dc:17 + e * 8 + dc],
                     start=(dc == 0 and hc == 0),
                     stop=(dc == DC - 1 and hc == 31),
                     inc=(hc == 31))
            mm_tag = p.tag("sPE")
            p.wait("vector", "sPE", mm_tag)
            p.wait("vector", "sS", ld)
            p.vts1(hbuf[:], hbuf[:],
                   cols[:, C_ELS + e * 8 + dc:C_ELS + e * 8 + dc + 1],
                   ALU.mult, inc=True)
            scale_tag = p.tag("sV")
            p.wait("sync", "sV", scale_tag)
            p.dma("sync", scr["w1T"][e][dc * 128:(dc + 1) * 128, :], hbuf[:])
            store_tag = p.tag("sS")
        p.wait("vector", "sPE", mm_tag)
        p.vtt(b1col[:, e * 32:(e + 1) * 32], b1col[:, e * 32:(e + 1) * 32],
              ps["pC"][:, 128:160], ALU.add, inc=True)
        b1_use = p.tag("sV")
    p.ns["pe_last"] = p.tag("sPE")



def attn_a_seq(p, dram, scr, sb, ps, seq):
    ns = p.ns
    xres, wbig, wmed, actfm = sb["xres"], sb["wbig"], sb["wmed"], sb["actfm"]
    cols, cols2, stats, scr32 = (sb["cols"], sb["cols2"], sb["stats"],
                                 sb["scr32"])
    onesb, rows = sb["onesb"], sb["rows"]
    pA, pB, pC, pE2, pF2 = ps["pA"], ps["pB"], ps["pC"], ps["pE2"], ps["pF2"]

    ln_stats_seq(p, sb, seq, True, EPS_LN, EPS_LN)

    xa_view = actfm[:, 0:2048].rearrange("q (dc t) -> q dc t", t=256)

    def build_xa(g):
        for tc in range(2):
            t = build_fm_chunk(p, sb, ps, seq, g * 2 + tc,
                               xa_view[:, :, tc * 128:(tc + 1) * 128], True, tc)
        return t

    def v_tm_group(xa_tag, bankfree):
        for tc in range(2):
            for nt in range(4):
                bk = (tc * 4 + nt) % 2
                pt = pA if bk == 0 else pB
                p.wait("tensor", "sV", xa_tag)
                p.wait("tensor", "sA", bankfree[bk])
                for dc in range(DC):
                    p.mm(pt[:], xa_view[:, dc, tc * 128:(tc + 1) * 128],
                         wbig[:, dc * 4096 + nt * 512:
                              dc * 4096 + nt * 512 + 512],
                         start=(dc == 0), stop=False)
                p.mm(pt[:], onesb[:], rows[0:1, nt * 512:(nt + 1) * 512],
                     start=False, stop=True, inc=True)
                mmt = p.tag("sPE")
                p.wait("scalar", "sPE", mmt)
                p.act(actfm[:, AF_V + tc * 2048 + nt * 512:
                            AF_V + tc * 2048 + nt * 512 + 512],
                      pt[:], AF.Silu, inc=True)
                bankfree[bk] = p.tag("sA")
        return p.tag("sA")

    st = {"vbank": {0: None, 1: None}, "qkfree": None}
    for g in range(NG):
        xa_tag = build_xa(g)
        p.wait("tensor", "sV", xa_tag)
        p.wait("tensor", "sA", st["qkfree"])
        for dc in range(DC):
            p.mm(pC[:, 0:256], wmed[:, dc * 128:(dc + 1) * 128],
                 xa_view[:, dc, :], start=(dc == 0), stop=(dc == DC - 1),
                 inc=(dc == DC - 1))
        qmm = p.tag("sPE")
        p.wait("scalar", "sPE", qmm)
        p.act(wmed[:, WM_QKACT + g * 256:WM_QKACT + (g + 1) * 256],
              pC[:, 0:256], AF.Silu, bias=cols[:, C_BQK:C_BQK + 1], scale=1.0,
              inc=True)
        st["qkfree"] = p.tag("sA")
        qk_tag = st["qkfree"]
        vev = v_tm_group(xa_tag, st["vbank"])
        p.wait("vector", "sA", qk_tag)
        p.wait("vector", "sPE", ns.get("lkt_pe"))
        p.vts(sb["hbuf"][:, 2048:2304],
              wmed[:, WM_QKACT + g * 256:WM_QKACT + (g + 1) * 256],
              cols[:, C_GAM + 3:C_GAM + 4], cols[:, C_BET + 3:C_BET + 4],
              ALU.mult, ALU.add, inc=True)
        lk = p.tag("sV")
        fmst = p.ns["fm"]
        p.wait("tensor", "sV", lk)
        p.wait("tensor", "sV", fmst["evac"])
        for tc in range(2):
            p.tp(ps["pTP"][:, tc * 128:(tc + 1) * 128],
                 sb["hbuf"][:, 2048 + tc * 128:2048 + (tc + 1) * 128],
                 sb["ident"][:], inc=(tc == 1))
        ns["lkt_pe"] = p.tag("sPE")
        p.wait("vector", "sPE", ns["lkt_pe"])
        p.vcopy(wmed[:, WM_LKT:WM_LKT + 256], ps["pTP"][:, 0:256], inc=True)
        fmst["evac"] = p.tag("sV")
        lkt_tag = p.tag("sV")
        p.wait("tensor", "sV", lkt_tag)
        p.wait("tensor", "sA", vev)
        for tc in range(2):
            for nt in range(4):
                pt = pE2 if nt < 2 else pF2
                sl = pt[:, (nt % 2) * 512:(nt % 2) * 512 + 512]
                p.mm(sl, wmed[:, WM_LKT + tc * 128:WM_LKT + (tc + 1) * 128],
                     actfm[:, AF_V + tc * 2048 + nt * 512:
                           AF_V + tc * 2048 + nt * 512 + 512],
                     start=(g == 0 and tc == 0),
                     stop=(g == NG - 1 and tc == 1),
                     inc=(g == NG - 1 and tc == 1 and nt == 3))
    linkv_mm = p.tag("sPE")
    p.wait("vector", "sPE", linkv_mm)
    p.vcopy(wmed[:, WM_LINKV:WM_LINKV + 1024], pE2[:], inc=True)
    p.vcopy(wmed[:, WM_LINKV + 1024:WM_LINKV + 2048], pF2[:], inc=True)
    linkv_tag = p.tag("sV")

    st2 = {"vbank": {0: None, 1: None}, "gslot": {0: None, 1: None},
           "qlslot": {0: None, 1: None, 2: None},
           "wout": {0: None, 1: None},
           "resfree": {0: linkv_tag, 1: linkv_tag}}
    woutv = scr["Woutc"].rearrange("(ecb q) d2 -> q ecb d2", q=128)
    for g in range(NG):
        xa_tag = build_xa(g)
        vev = v_tm_group(xa_tag, st2["vbank"])
        for gc in range(16):
            bk = gc % 2
            pt = pC if bk == 0 else pA
            sl = pt[:, 0:256]
            p.wait("tensor", "sA", vev)
            p.wait("tensor", "sA", st2["gslot"][bk])
            for dc in range(DC):
                p.mm(sl, wbig[:, dc * 4096 + 2048 + gc * 128:
                              dc * 4096 + 2048 + (gc + 1) * 128],
                     xa_view[:, dc, :], start=(dc == 0), stop=(dc == DC - 1),
                     inc=(dc == DC - 1))
            gmm = p.tag("sPE")
            p.wait("scalar", "sPE", gmm)
            p.act(actfm[:, AF_GATE + gc * 256:AF_GATE + (gc + 1) * 256], sl,
                  AF.Silu, bias=cols2[:, gc:gc + 1], scale=1.0, inc=True)
            st2["gslot"][bk] = p.tag("sA")
        gate_done = p.tag("sA")
        qk_g = wmed[:, WM_QKACT + g * 256:WM_QKACT + (g + 1) * 256]
        p.vts(wmed[:, WM_QQ:WM_QQ + 256], qk_g, cols[:, C_GAM:C_GAM + 1],
              cols[:, C_BET:C_BET + 1], ALU.mult, ALU.add)
        p.vts(wmed[:, WM_QKD:WM_QKD + 256], qk_g,
              cols[:, C_GAM + 2:C_GAM + 3], cols[:, C_BET + 2:C_BET + 3],
              ALU.mult, ALU.add)
        p.vts(wmed[:, WM_LQ:WM_LQ + 256], qk_g, cols[:, C_GAM + 1:C_GAM + 2],
              cols[:, C_BET + 1:C_BET + 2], ALU.mult, ALU.add, inc=True)
        heads_tag = p.tag("sV")
        p.wait("tensor", "sV", heads_tag)
        p.wait("tensor", "sA", gate_done)
        for jc in range(2):
            p.mm(pC[:, jc * 256:(jc + 1) * 256],
                 wmed[:, WM_QKD + jc * 128:WM_QKD + (jc + 1) * 128],
                 wmed[:, WM_QQ:WM_QQ + 256], start=True, stop=True,
                 inc=(jc == 1))
        sim_tag = p.tag("sPE")
        p.wait("vector", "sPE", sim_tag)
        for jc in range(2):
            p.vstt(scr32[:, jc * 256:(jc + 1) * 256],
                   pC[:, jc * 256:(jc + 1) * 256], 1.0 / G,
                   sb["biasT"][:, jc * 256:(jc + 1) * 256], ALU.mult, ALU.add)
            p.vstt(wmed[:, WM_ATT + jc * 256:WM_ATT + (jc + 1) * 256],
                   scr32[:, jc * 256:(jc + 1) * 256], 0.0,
                   scr32[:, jc * 256:(jc + 1) * 256], ALU.max, ALU.mult,
                   inc=(jc == 1))
        att_tag = p.tag("sV")
        for ec in range(16):
            bk = ec % 3
            pt = (pA, pB, pC)[bk]
            sl = pt[:, 0:256]
            p.wait("tensor", "sV", st2["qlslot"][bk])
            p.wait("tensor", "sV", att_tag)
            for tc in range(2):
                p.mm(sl, actfm[:, AF_V + tc * 2048 + ec * 128:
                               AF_V + tc * 2048 + (ec + 1) * 128],
                     wmed[:, WM_ATT + tc * 256:WM_ATT + (tc + 1) * 256],
                     start=(tc == 0), stop=False)
            p.mm(sl, wmed[:, WM_LINKV + ec * 128:WM_LINKV + (ec + 1) * 128],
                 wmed[:, WM_LQ:WM_LQ + 256], start=False, stop=True, inc=True)
            qlmm = p.tag("sPE")
            p.wait("vector", "sPE", qlmm)
            p.vtt(actfm[:, AF_GATED + ec * 256:AF_GATED + (ec + 1) * 256], sl,
                  actfm[:, AF_GATE + ec * 256:AF_GATE + (ec + 1) * 256],
                  ALU.mult, inc=True)
            st2["qlslot"][bk] = p.tag("sV")
        gated_tag = p.tag("sV")
        for ec in range(16):
            slot = ec % 2
            p.wait("sync", "sPE", st2["wout"][slot])
            p.dma("sync", wmed[:, WM_WOUT + slot * 1024:
                               WM_WOUT + (slot + 1) * 1024], woutv[:, ec, :])
            dtag = p.tag("sS")
            wsl = wmed[:, WM_WOUT + slot * 1024:WM_WOUT + (slot + 1) * 1024]
            p.wait("tensor", "sS", dtag)
            if ec == 0:
                p.wait("tensor", "sV", gated_tag)
                p.wait("tensor", "sV", st2["resfree"][0])
                p.wait("tensor", "sV", st2["resfree"][1])
            for tc in range(2):
                pt = pE2 if tc == 0 else pF2
                gsl = actfm[:, AF_GATED + ec * 256 + tc * 128:
                            AF_GATED + ec * 256 + (tc + 1) * 128]
                for half in range(2):
                    p.mm(pt[:, half * 512:(half + 1) * 512], gsl,
                         wsl[:, half * 512:(half + 1) * 512],
                         start=(ec == 0), stop=False,
                         inc=(tc == 1 and half == 1 and ec < 15))
            if ec == 15:
                for tc in range(2):
                    pt = pE2 if tc == 0 else pF2
                    for half in range(2):
                        p.mm(pt[:, half * 512:(half + 1) * 512], onesb[:],
                             rows[0:1, 2048 + half * 512:
                                  2048 + (half + 1) * 512],
                             start=False, stop=True,
                             inc=(tc == 1 and half == 1))
            st2["wout"][slot] = p.tag("sPE")
        wout_mm = p.tag("sPE")
        for tc in range(2):
            ci = g * 2 + tc
            c = seq * CH_SEQ + ci
            pt = pE2 if tc == 0 else pF2
            xc = xres[:, c * D:(c + 1) * D]
            p.wait("vector", "sPE", wout_mm)
            p.vts(scr32[:, 0:D], xc, stats[:, ci:ci + 1],
                  stats[:, 16 + ci:17 + ci], ALU.subtract, ALU.mult)
            p.vtt(scr32[:, 0:D], scr32[:, 0:D], pt[:], ALU.add)
            p.vtt(xc, scr32[:, 0:D], xc, ALU.add, inc=True)
            st2["resfree"][tc] = p.tag("sV")
    p.ns["pe_last"] = p.tag("sPE")



def ffn_seq(p, dram, scr, sb, ps, seq):
    ns = p.ns
    xres, wbig, wmed, actfm = sb["xres"], sb["wbig"], sb["wmed"], sb["actfm"]
    hbuf, stats, scr32, gates = (sb["hbuf"], sb["stats"], sb["scr32"],
                                 sb["gates"])
    gatesT = sb["wmed"][0:8, 0:2048]
    b2sb = sb["wmed"][0:8, 2048:3072]
    onesb, rows = sb["onesb"], sb["rows"]
    p.dma("gpsimd", b2sb, dram["b2"])
    b2_tag = p.tag("sG")
    pA, pB, pC, pE2, pF2, pTP = (ps["pA"], ps["pB"], ps["pC"], ps["pE2"],
                                 ps["pF2"], ps["pTP"])

    ln_stats_seq(p, sb, seq, True, EPS_LN, EPS_FFN)

    xh_view = actfm[:].rearrange("q (dc t) -> q dc t", t=2048)
    xhat_tag = None
    for ci in range(CH_SEQ):
        xhat_tag = build_fm_chunk(p, sb, ps, seq, ci,
                                  xh_view[:, :, ci * 128:(ci + 1) * 128],
                                  True, ci % 2)

    lfree = {0: None, 1: None}
    for ci in range(CH_SEQ):
        bk = ci % 2
        pt = pC if bk == 0 else pA
        sl = pt[:, 0:8]
        p.wait("tensor", "sV", xhat_tag)
        p.wait("tensor", "sV", lfree[bk])
        for dc in range(DC):
            p.mm(sl, xh_view[:, dc, ci * 128:(ci + 1) * 128],
                 sb["rwT"][:, dc * 8:(dc + 1) * 8], start=(dc == 0),
                 stop=False)
        p.mm(sl, onesb[:], rows[0:1, 3072:3080], start=False, stop=True,
             inc=True)
        lmm = p.tag("sPE")
        p.wait("vector", "sPE", lmm)
        p.vcopy(gates[:, ci * 8:(ci + 1) * 8], sl, inc=True)
        lfree[bk] = p.tag("sV")
    logits_tag = p.tag("sV")
    p.wait("scalar", "sV", logits_tag)
    p.act(gates[:], gates[:], AF.Exp, inc=True)
    t = p.tag("sA")
    p.wait("vector", "sA", t)
    gv = gates[:].rearrange("q (c e) -> q c e", e=8)
    m1 = stats[:, 176:192]
    m2 = stats[:, 192:208]
    dinv = stats[:, 208:224]
    w1t = stats[:, 224:240]
    p.vred(m1, gv, ALU.max, inc=True)
    p.wait("vector", "sV", p.tag("sV"))
    m1b = m1.unsqueeze(2).broadcast_to([128, 16, 8])
    is1 = scr32[:, 0:128].rearrange("q (c e) -> q c e", e=8)
    p.vtt(is1, gv, m1b, ALU.is_equal, inc=True)
    p.wait("vector", "sV", p.tag("sV"))
    masked = scr32[:, 128:256].rearrange("q (c e) -> q c e", e=8)
    p.vstt(masked, is1, NEGBIG, gv, ALU.mult, ALU.add, inc=True)
    p.wait("vector", "sV", p.tag("sV"))
    p.vred(m2, masked, ALU.max, inc=True)
    p.wait("vector", "sV", p.tag("sV"))
    m2b = m2.unsqueeze(2).broadcast_to([128, 16, 8])
    is2 = scr32[:, 256:384].rearrange("q (c e) -> q c e", e=8)
    p.vtt(is2, masked, m2b, ALU.is_equal, inc=True)
    p.vtt(dinv, m1, m2, ALU.add, inc=True)
    p.wait("vector", "sV", p.tag("sV"))
    p.vrecip(dinv, dinv, inc=True)
    p.wait("vector", "sV", p.tag("sV"))
    p.vtt(w1t, m1, dinv, ALU.mult)
    p.vtt(m2, m2, dinv, ALU.mult, inc=True)
    p.wait("vector", "sV", p.tag("sV"))
    w1b = w1t.unsqueeze(2).broadcast_to([128, 16, 8])
    w2b = m2.unsqueeze(2).broadcast_to([128, 16, 8])
    p.vtt(gv, is1, w1b, ALU.mult)
    p.vtt(is2, is2, w2b, ALU.mult, inc=True)
    p.wait("vector", "sV", p.tag("sV"))
    p.vtt(gv, gv, is2, ALU.add, inc=True)
    gates_tag = p.tag("sV")
    p.wait("vector", "sV", gates_tag)

    fmst = p.ns["fm"]
    p.wait("vector", "sPE", fmst["slot"].get(0))
    p.vcopy(hbuf[:, 0:128], gates[:], inc=True)
    gt0 = p.tag("sV")
    for rnd in range(2):
        p.wait("tensor", "sV", gt0)
        p.wait("tensor", "sV", fmst["evac"])
        for k in range(8):
            ci = rnd * 8 + k
            p.tp(pTP[0:8, k * 128:(k + 1) * 128],
                 hbuf[:, ci * 8:(ci + 1) * 8], sb["ident"][:], inc=(k == 7))
        tpt = p.tag("sPE")
        p.wait("vector", "sPE", tpt)
        p.vcopy(gatesT[0:8, rnd * 1024:(rnd + 1) * 1024], pTP[0:8, 0:1024],
                inc=True)
        fmst["evac"] = p.tag("sV")
    gatesT_tag = p.tag("sV")
    fmst["slot"][0] = p.tag("sPE")

    wload = {}
    slot_free = {0: None, 1: None}
    yfree = {0: gatesT_tag, 1: gatesT_tag}
    hfree = {i: None for i in range(4)}
    psfree = {i: None for i in range(3)}

    def issue_wload(e, q):
        slot = (e * NQ + q) % 2
        p.wait("sync", "sPE", slot_free[slot])
        w1v = scr["w1T"][e].rearrange("(dcb q2) h2 -> q2 dcb h2", q2=128)
        p.dma("sync",
              wbig[:, WB_W1Q[0] + slot * 8192:WB_W1Q[0] + slot * 8192 + 8192]
              .rearrange("q2 (dcb h2) -> q2 dcb h2", h2=1024),
              w1v[:, :, q * 1024:(q + 1) * 1024])
        for hc in range(8):
            p.dma("sync",
                  wbig[:, WB_W2Q[0] + slot * 8192 + hc * 1024:
                        WB_W2Q[0] + slot * 8192 + (hc + 1) * 1024],
                  scr["w2c"][e][:, q * 1024 + hc * 128:
                                q * 1024 + (hc + 1) * 128],
                  transpose=True)
        wload[(e, q)] = p.tag("sS")

    issue_wload(0, 0)
    for e in range(E):
        for q in range(NQ):
            slot = (e * NQ + q) % 2
            nxt = (e, q + 1) if q + 1 < NQ else ((e + 1, 0) if e + 1 < E
                                                 else None)
            if nxt is not None:
                issue_wload(*nxt)
            w1sl = wbig[:, WB_W1Q[0] + slot * 8192:
                        WB_W1Q[0] + slot * 8192 + 8192]
            w2sl = wbig[:, WB_W2Q[0] + slot * 8192:
                        WB_W2Q[0] + slot * 8192 + 8192]
            p.wait("tensor", "sS", wload[(e, q)])
            for tb in range(8):
                hact = {}

                def emit_h(hc):
                    hs = hc % 4
                    pb = hc % 3
                    pt = (pA, pB, pC)[pb]
                    sl = pt[:, 0:256]
                    p.wait("tensor", "sA", psfree[pb])
                    for dc in range(DC):
                        p.mm(sl, w1sl[:, dc * 1024 + hc * 128:
                                      dc * 1024 + (hc + 1) * 128],
                             xh_view[:, dc, tb * 256:(tb + 1) * 256],
                             start=(dc == 0), stop=(dc == DC - 1),
                             inc=(dc == DC - 1))
                    hmm = p.tag("sPE")
                    p.wait("scalar", "sPE", hmm)
                    p.wait("scalar", "sPE", hfree[hs])
                    p.act(hbuf[:, hs * 256:(hs + 1) * 256], sl, AF.Gelu,
                          bias=sb["b1col"][:, e * 32 + q * 8 + hc:
                                           e * 32 + q * 8 + hc + 1],
                          scale=1.0, inc=True)
                    psfree[pb] = p.tag("sA")
                    hact[hc] = p.tag("sA")

                def emit_y2(hc):
                    hs = hc % 4
                    p.wait("tensor", "sA", hact[hc])
                    if hc == 0:
                        p.wait("tensor", "sV", yfree[0])
                        p.wait("tensor", "sV", yfree[1])
                    for t128 in range(2):
                        pt2 = pE2 if t128 == 0 else pF2
                        for half in range(2):
                            p.mm(pt2[:, half * 512:(half + 1) * 512],
                                 hbuf[:, hs * 256 + t128 * 128:
                                      hs * 256 + (t128 + 1) * 128],
                                 w2sl[:, hc * 1024 + half * 512:
                                      hc * 1024 + (half + 1) * 512],
                                 start=(hc == 0), stop=(hc == 7),
                                 inc=(t128 == 1 and half == 1))
                    hfree[hs] = p.tag("sPE")

                emit_h(0)
                for hc in range(1, 8):
                    emit_h(hc)
                    emit_y2(hc - 1)
                emit_y2(7)
                y2mm = p.tag("sPE")
                for t128 in range(2):
                    ci = tb * 2 + t128
                    c = seq * CH_SEQ + ci
                    pt2 = pE2 if t128 == 0 else pF2
                    xc = xres[:, c * D:(c + 1) * D]
                    p.wait("vector", "sPE", y2mm)
                    p.vstt(xc, pt2[:], gates[:, ci * 8 + e:ci * 8 + e + 1],
                           xc, ALU.mult, ALU.add, inc=True)
                    yfree[t128] = p.tag("sV")
            slot_free[slot] = p.tag("sPE")
    bfree = {0: yfree[0], 1: yfree[1]}
    for ci in range(CH_SEQ):
        c = seq * CH_SEQ + ci
        bk = ci % 2
        pt2 = pE2 if bk == 0 else pF2
        p.wait("tensor", "sV", bfree[bk])
        for half in range(2):
            p.wait("tensor", "sG", b2_tag)
            p.mm(pt2[:, half * 512:(half + 1) * 512],
                 gatesT[0:8, ci * 128:(ci + 1) * 128],
                 b2sb[:, half * 512:(half + 1) * 512],
                 start=True, stop=True, inc=(half == 1))
        bmm = p.tag("sPE")
        xc = xres[:, c * D:(c + 1) * D]
        p.wait("vector", "sPE", bmm)
        p.vtt(xc, pt2[:], xc, ALU.add, inc=True)
        bfree[bk] = p.tag("sV")
    p.ns["pe_last"] = p.tag("sPE")



def attn_b_seq(p, dram, scr, sb, ps, seq):
    ns = p.ns
    xres, wbig, wmed, actfm = sb["xres"], sb["wbig"], sb["wmed"], sb["actfm"]
    hbuf, stats, scr32 = sb["hbuf"], sb["stats"], sb["scr32"]
    pA, pB, pC, pE2, pF2, pTP = (ps["pA"], ps["pB"], ps["pC"], ps["pE2"],
                                 ps["pF2"], ps["pTP"])

    ln_stats_seq(p, sb, seq, False, EPS_LN, EPS_LN)
    yv = actfm[:].rearrange("q (dc t) -> q dc t", t=2048)
    y_tag = None
    for ci in range(CH_SEQ):
        y_tag = build_fm_chunk(p, sb, ps, seq, ci,
                               yv[:, :, ci * 128:(ci + 1) * 128], False,
                               ci % 2)

    for nm, off in (("Wdqc", WM_DQ), ("Wdkvc", WM_DKV)):
        wv = scr[nm].rearrange("(dcb q2) c2 -> q2 dcb c2", q2=128)
        p.dma("sync", wmed[:, off:off + 2048]
              .rearrange("q2 (dcb c2) -> q2 dcb c2", c2=256), wv)
    dq_tag = p.tag("sS")

    bank = {0: ns["fm"]["evac"], 1: ns["fm"]["evac"]}
    for which, off_w, dst, dsto in ((0, WM_DQ, hbuf, 0),
                                    (1, WM_DKV, wmed, WM_CKV)):
        for cc in range(2):
            for nt in range(4):
                bk = (cc * 4 + nt) % 2
                pt = pA if bk == 0 else pB
                p.wait("tensor", "sS", dq_tag)
                p.wait("tensor", "sV", y_tag)
                p.wait("tensor", "sV", bank[bk])
                for dc in range(DC):
                    p.mm(pt[:], wmed[:, off_w + dc * 256 + cc * 128:
                                     off_w + dc * 256 + (cc + 1) * 128],
                         yv[:, dc, nt * 512:(nt + 1) * 512],
                         start=(dc == 0), stop=(dc == DC - 1),
                         inc=(dc == DC - 1))
                cmm = p.tag("sPE")
                p.wait("vector", "sPE", cmm)
                p.vcopy(dst[:, dsto + cc * 2048 + nt * 512:
                            dsto + cc * 2048 + (nt + 1) * 512], pt[:],
                        inc=True)
                bank[bk] = p.tag("sV")
    c_done_pe = p.tag("sPE")
    c_done = p.tag("sV")

    p.wait("sync", "sPE", c_done_pe)
    for nm, off in (("Wukc", WM_UK), ("Wuvc", WM_UV)):
        wv = scr[nm].rearrange("(ccb q2) d2 -> q2 ccb d2", q2=128)
        p.dma("sync", wmed[:, off:off + 2048]
              .rearrange("q2 (ccb d2) -> q2 ccb d2", d2=1024), wv)
    uk_tag = p.tag("sS")

    for qc in range(8):
        for nt in range(4):
            bk = (qc * 4 + nt) % 2
            pt = pA if bk == 0 else pB
            p.wait("tensor", "sS", uk_tag)
            p.wait("tensor", "sV", c_done)
            p.wait("tensor", "sV", bank[bk])
            for cc in range(2):
                p.mm(pt[:], wmed[:, WM_UK + cc * 1024 + qc * 128:
                                 WM_UK + cc * 1024 + (qc + 1) * 128],
                     wmed[:, WM_CKV + cc * 2048 + nt * 512:
                          WM_CKV + cc * 2048 + (nt + 1) * 512],
                     start=(cc == 0), stop=(cc == 1), inc=(cc == 1))
            kmm = p.tag("sPE")
            p.wait("vector", "sPE", kmm)
            p.vcopy(wbig[:, qc * 2048 + nt * 512:qc * 2048 + (nt + 1) * 512],
                    pt[:], inc=True)
            bank[bk] = p.tag("sV")
    vfree = {0: None, 1: None}
    for tc in range(16):
        bk = tc % 2
        pt2 = pE2 if bk == 0 else pF2
        p.wait("tensor", "sV", vfree[bk])
        for half in range(2):
            for cc in range(2):
                p.mm(pt2[:, half * 512:(half + 1) * 512],
                     wmed[:, WM_CKV + cc * 2048 + tc * 128:
                          WM_CKV + cc * 2048 + (tc + 1) * 128],
                     wmed[:, WM_UV + cc * 1024 + half * 512:
                          WM_UV + cc * 1024 + (half + 1) * 512],
                     start=(cc == 0), stop=(cc == 1),
                     inc=(cc == 1 and half == 1))
        vmm = p.tag("sPE")
        p.wait("vector", "sPE", vmm)
        p.vcopy(wbig[:, 16384 + tc * 1024:16384 + (tc + 1) * 1024], pt2[:],
                inc=True)
        vfree[bk] = p.tag("sV")
    kv_done_pe = p.tag("sPE")
    kv_done = p.tag("sV")

    p.wait("sync", "sPE", kv_done_pe)
    wv = scr["Wuqc"].rearrange("(ccb q2) d2 -> q2 ccb d2", q2=128)
    p.dma("sync", wmed[:, WM_UQ:WM_UQ + 2048]
          .rearrange("q2 (ccb d2) -> q2 ccb d2", d2=1024), wv)
    wv = scr["Woc"].rearrange("(vcb q2) d2 -> q2 vcb d2", q2=128)
    p.dma("sync", actfm[:, 0:8192]
          .rearrange("q2 (vcb d2) -> q2 vcb d2", d2=1024), wv)
    uq_tag = p.tag("sS")

    st = {"qt": {0: None, 1: None}, "exp": None, "avfree": None,
          "res": kv_done, "tpdone": None, "pn": {}, "J": {}}
    fmst = p.ns["fm"]

    def emit_head_scores(i, h):
        J = (i + 1) * 128
        njt = (J + 511) // 512
        st["J"][h] = (i, J, njt)
        qc, base = h // 2, (h % 2) * 64
        p.wait("tensor", "sV", st["qt"][0])
        p.wait("tensor", "sV", st["qt"][1])
        p.wait("tensor", "sA", st["exp"])
        if h == 0:
            p.wait("tensor", "sV", st["res"])
        for jt in range(njt):
            N = min(512, J - jt * 512)
            pt2 = pE2 if jt < 2 else pF2
            sl = pt2[:, (jt % 2) * 512:(jt % 2) * 512 + N]
            p.mm(sl, wmed[base:base + 64, WM_QT + qc * 128:
                          WM_QT + (qc + 1) * 128],
                 wbig[base:base + 64, qc * 2048 + jt * 512:
                      qc * 2048 + jt * 512 + N],
                 start=True, stop=True, inc=(jt == njt - 1))
        smm = p.tag("sPE")
        dj = i * 128
        jt = dj // 512
        pt2 = pE2 if jt < 2 else pF2
        dsl = pt2[:, (jt % 2) * 512 + dj % 512:
                  (jt % 2) * 512 + dj % 512 + 128]
        p.wait("vector", "sPE", smm)
        p.vtt(dsl, dsl, sb["cmask"][:], ALU.add, inc=True)
        p.wait("vector", "sV", p.tag("sV"))
        for jt in range(njt):
            N = min(512, J - jt * 512)
            pt2 = pE2 if jt < 2 else pF2
            p.vred(stats[:, 176 + jt:177 + jt],
                   pt2[:, (jt % 2) * 512:(jt % 2) * 512 + N], ALU.max,
                   inc=(jt == njt - 1))
        p.wait("vector", "sV", p.tag("sV"))
        if njt > 1:
            p.vred(stats[:, 180:181], stats[:, 176:176 + njt], ALU.max,
                   inc=True)
            p.wait("vector", "sV", p.tag("sV"))
            p.vts1(stats[:, 181:182], stats[:, 180:181], -0.125, ALU.mult,
                   inc=True)
        else:
            p.vts1(stats[:, 181:182], stats[:, 176:177], -0.125, ALU.mult,
                   inc=True)
        mtag = p.tag("sV")
        p.wait("scalar", "sV", mtag)
        p.wait("scalar", "sPE", st["tpdone"])
        for jt in range(njt):
            N = min(512, J - jt * 512)
            pt2 = pE2 if jt < 2 else pF2
            p.act(wmed[:, WM_PSB + jt * 512:WM_PSB + jt * 512 + N],
                  pt2[:, (jt % 2) * 512:(jt % 2) * 512 + N], AF.Exp,
                  bias=stats[:, 181:182], scale=0.125,
                  inc=(jt == njt - 1))
        st["exp"] = p.tag("sA")
        p.wait("vector", "sA", st["exp"])
        p.vred(stats[:, 182:183], wmed[:, WM_PSB:WM_PSB + J], ALU.add,
               inc=True)
        p.wait("vector", "sV", p.tag("sV"))
        p.vrecip(stats[:, 183:184], stats[:, 182:183], inc=True)
        rtag = p.tag("sV")
        p.wait("scalar", "sV", rtag)
        p.act(wmed[:, WM_PSB:WM_PSB + J], wmed[:, WM_PSB:WM_PSB + J],
              AF.Identity, scale=stats[:, 183:184], inc=True)
        st["pn"][h] = p.tag("sA")

    def emit_head_tpav(h):
        i, J, njt = st["J"][h]
        qc, base = h // 2, (h % 2) * 64
        pn_tag = st["pn"][h]
        for jc in range(i + 1):
            ksl = jc % 8
            p.wait("tensor", "sA", pn_tag)
            if ksl == 0:
                p.wait("tensor", "sV", fmst["evac"])
            p.tp(pTP[:, ksl * 128:(ksl + 1) * 128],
                 wmed[:, WM_PSB + jc * 128:WM_PSB + (jc + 1) * 128],
                 sb["ident"][:], inc=(ksl == 7 or jc == i))
            if ksl == 7 or jc == i:
                tpt = p.tag("sPE")
                p.wait("vector", "sPE", tpt)
                lo = (jc // 8) * 8
                p.vcopy(wmed[:, WM_PT + lo * 128:WM_PT + (jc + 1) * 128],
                        pTP[:, 0:(jc + 1 - lo) * 128], inc=True)
                fmst["evac"] = p.tag("sV")
        st["tpdone"] = p.tag("sPE")
        pt_tag = p.tag("sV")
        p.wait("tensor", "sV", pt_tag)
        if h % 2 == 0:
            p.wait("tensor", "sV", st["avfree"])
        for jc in range(i + 1):
            kw = {}
            if base:
                kw["tile_position"] = (0, base)
            p.mm(pB[base:base + 64, 256:384],
                 wbig[:, 16384 + jc * 1024 + h * 64:
                      16384 + jc * 1024 + (h + 1) * 64],
                 wmed[:, WM_PT + jc * 128:WM_PT + (jc + 1) * 128],
                 start=(jc == 0), stop=(jc == i), inc=(jc == i), **kw)
        avmm = p.tag("sPE")
        if h % 2 == 1:
            p.wait("vector", "sPE", avmm)
            p.vcopy(wmed[:, WM_AO + qc * 128:WM_AO + (qc + 1) * 128],
                    pB[:, 256:384], inc=True)
            st["avfree"] = p.tag("sV")

    for i in range(CH_SEQ):
        for qc in range(8):
            bk = qc % 2
            pt = pA if bk == 0 else pC
            sl = pt[:, 0:128]
            p.wait("tensor", "sS", uq_tag)
            p.wait("tensor", "sV", st["qt"][bk])
            for cc in range(2):
                p.mm(sl, wmed[:, WM_UQ + cc * 1024 + qc * 128:
                              WM_UQ + cc * 1024 + (qc + 1) * 128],
                     hbuf[:, cc * 2048 + i * 128:cc * 2048 + (i + 1) * 128],
                     start=(cc == 0), stop=(cc == 1), inc=(cc == 1))
            qmm = p.tag("sPE")
            p.wait("vector", "sPE", qmm)
            p.vcopy(wmed[:, WM_QT + qc * 128:WM_QT + (qc + 1) * 128], sl,
                    inc=True)
            st["qt"][bk] = p.tag("sV")
        for h in range(NHEAD):
            if h > 0:
                emit_head_tpav(h - 1)
            emit_head_scores(i, h)
        emit_head_tpav(NHEAD - 1)
        ao_tag = st["avfree"]
        p.wait("tensor", "sV", ao_tag)
        p.wait("tensor", "sV", st["res"])
        for vc in range(8):
            for half in range(2):
                p.mm(pE2[:, half * 512:(half + 1) * 512],
                     wmed[:, WM_AO + vc * 128:WM_AO + (vc + 1) * 128],
                     actfm[:, vc * 1024 + half * 512:
                           vc * 1024 + (half + 1) * 512],
                     start=(vc == 0), stop=(vc == 7),
                     inc=(vc == 7 and half == 1))
        wmm = p.tag("sPE")
        c = seq * CH_SEQ + i
        xc = xres[:, c * D:(c + 1) * D]
        p.wait("vector", "sPE", wmm)
        p.vtt(xc, pE2[:], xc, ALU.add, inc=True)
        st["res"] = p.tag("sV")
    p.ns["pe_last"] = p.tag("sPE")

def phase_output(p, sb, y_out):
    xres, s32 = sb["xres"], sb["scr32"]
    yv = y_out.rearrange("(c p) d -> p c d", p=128)
    store = {}
    for c in range(NCH):
        if c >= 1:
            p.wait("scalar", "sS", store[c - 1])
        p.act(s32[:, :], xres[:, c * D:(c + 1) * D], AF.Copy, scale=1.0,
              inc=True)
        t = p.tag("sA")
        p.wait("sync", "sA", t)
        store[c] = p.dma("sync", yv[:, c, :], s32[:, :])
    p.wait("sync", "sS", p.tag("sS"))



def _bucket1d():
    NUM_BUCKETS = 32
    MAX_DIST = 128
    nb = NUM_BUCKETS // 2
    max_exact = nb // 2
    out = np.zeros(511, dtype=np.int64)
    for idx, dd in enumerate(range(-255, 256)):
        n = dd
        ret = nb if n < 0 else 0
        n = abs(n)
        if n < max_exact:
            val = n
        else:
            val = max_exact + int(
                math.log(max(n, 1) / max_exact)
                / math.log(MAX_DIST / max_exact) * (nb - max_exact))
            val = min(val, nb - 1)
        out[idx] = ret + val
    return out


def _oh_const():
    b = _bucket1d()
    oh = np.zeros((32, 511), dtype=np.float32)
    oh[b, np.arange(511)] = 1.0
    return oh * math.sqrt(128.0)


_NC_CACHE = None


def kernel(L0_attn_a_x, L0_attn_a_ln_g, L0_attn_a_ln_b, L0_attn_a_W_h,
           L0_attn_a_b_h, L0_attn_a_W_qk, L0_attn_a_b_qk, L0_attn_a_os_gamma,
           L0_attn_a_os_beta, L0_attn_a_rel_emb, L0_attn_a_W_out,
           L0_attn_a_b_out, L0_attn_b_W_dq, L0_attn_b_W_uq, L0_attn_b_W_dkv,
           L0_attn_b_W_uk, L0_attn_b_W_uv, L0_attn_b_W_o,
           L0_ffn_router_ln_scale, L0_ffn_router_ln_bias, L0_ffn_router_w,
           L0_ffn_router_b, L0_ffn_expert_ln_scale, L0_ffn_expert_ln_bias,
           L0_ffn_w1, L0_ffn_b1, L0_ffn_w2, L0_ffn_b2):
    global _NC_CACHE
    x = np.ascontiguousarray(np.asarray(L0_attn_a_x)).astype(np.float32)

    common = {
        "ln_g": L0_attn_a_ln_g, "ln_b": L0_attn_a_ln_b,
        "W_h": L0_attn_a_W_h, "b_h": L0_attn_a_b_h,
        "W_qk": L0_attn_a_W_qk, "b_qk": L0_attn_a_b_qk,
        "os_gamma": L0_attn_a_os_gamma, "os_beta": L0_attn_a_os_beta,
        "rel_emb": L0_attn_a_rel_emb, "W_out": L0_attn_a_W_out,
        "b_out": L0_attn_a_b_out,
        "W_dq": L0_attn_b_W_dq, "W_uq": L0_attn_b_W_uq,
        "W_dkv": L0_attn_b_W_dkv, "W_uk": L0_attn_b_W_uk,
        "W_uv": L0_attn_b_W_uv, "W_o": L0_attn_b_W_o,
        "r_ls": L0_ffn_router_ln_scale, "r_lb": L0_ffn_router_ln_bias,
        "r_w": L0_ffn_router_w, "r_b": L0_ffn_router_b,
        "e_ls": L0_ffn_expert_ln_scale, "e_lb": L0_ffn_expert_ln_bias,
        "w1": L0_ffn_w1, "b1": L0_ffn_b1, "w2": L0_ffn_w2, "b2": L0_ffn_b2,
        "OH": _oh_const(),
    }
    common = {k: np.ascontiguousarray(np.asarray(v)).astype(np.float32)
              for k, v in common.items()}

    in_maps = []
    for c in range(8):
        m = dict(common)
        m["x"] = np.ascontiguousarray(x[2 * c:2 * c + 2].reshape(T, D))
        in_maps.append(m)

    if _NC_CACHE is None:
        _NC_CACHE = build_nc()
    res = run_bass_kernel_spmd(_NC_CACHE, in_maps, core_ids=list(range(8)))
    out = np.concatenate(
        [r["y"].reshape(NSEQ, S, D) for r in res.results], axis=0)
    return out.astype(np.float32)


# version 05
# speedup vs baseline: 1.0089x; 1.0089x over previous
import math
import numpy as np

import concourse.bass as bass
import concourse.mybir as mybir
from concourse.bass_utils import run_bass_kernel_spmd

F32 = mybir.dt.float32
BF16 = mybir.dt.bfloat16
AF = mybir.ActivationFunctionType
ALU = mybir.AluOpType
AX = mybir.AxisListType

D = 1024
DC = 8
S = 2048
NSEQ = 2
T = 4096
NCH = 32
CH_SEQ = 16
HID = 2048
G = 256
NG = 8
E = 8
H = 4096
NQ = 4
QH = 1024
NHEAD = 16
DK = 64
EPS_LN = 1e-5
EPS_FFN = 1e-6
NEGBIG = -1.0e30

DMA_Q = {"sync": "sS", "gpsimd": "sG", "scalar": "sC"}

PHASES = {"attn_a": True, "ffn1": True, "attn_b": True, "ffn2": True}
DEBUG = False
DEBUG_POINT = "attn_a"
NSEQ_RUN = NSEQ

WM_QK = 0
WM_QKACT = 1024
WM_LINKV = 3072
WM_LKT = 5120
WM_QQ = 5376
WM_QKD = 5632
WM_LQ = 5888
WM_ATT = 6144
WM_WOUT = 6656
WM_DQ = 0
WM_DKV = 2048
WM_CKV = 4096
WM_UK = 0
WM_UV = 2048
WM_UQ = 0
WM_PSB = 4096
WM_PT = 6144
WM_QT = 8192
WM_AO = 9216

AF_XA = 0
AF_V = 2048
AF_GATE = 6144
AF_GATED = 10240

WB_W1Q = (0, 8192)
WB_W2Q = (16384, 24576)

C_LNG = 0
C_LNB = 8
C_GAM = 16
C_BET = 20
C_BQK = 24
C_RLS = 32
C_ELS = 64


class Prog:
    def __init__(self, nc, sems):
        self.nc = nc
        self.sems = sems
        self.cur = None
        self.eng = None
        self.sim = {}
        self.final = {}
        self.ns = {}

    def begin(self, name, eng):
        self.cur = name
        self.eng = eng
        self.sim = {k: 0 for k in self.sems}
        self.ns = {}

    def end(self):
        if self.final:
            assert self.final == self.sim, (
                f"sim divergence on {self.cur}: {self.final} vs {self.sim}")
        else:
            self.final = dict(self.sim)

    def _emit(self, engine, make, inc):
        if inc is not None:
            self.sim[inc[0]] += inc[1]
        if engine == self.cur:
            instr = make(self.eng)
            if inc is not None:
                instr.then_inc(self.sems[inc[0]], inc[1])
        return self.sim[inc[0]] if inc is not None else None

    def wait(self, engine, sem, v):
        if v is None:
            return
        if engine == self.cur:
            self.eng.wait_ge(self.sems[sem], v)

    def tag(self, sem):
        return self.sim[sem]

    def dma(self, q, out, in_, **kw):
        return self._emit(q, lambda e: e.dma_start(out=out, in_=in_, **kw),
                          (DMA_Q[q], 16))

    def mm(self, out, lhsT, rhs, start, stop, inc=False, **kw):
        return self._emit(
            "tensor",
            lambda e: e.matmul(out, lhsT, rhs, start=start, stop=stop, **kw),
            ("sPE", 1) if inc else None)

    def tp(self, out, in_, ident, inc=False):
        return self._emit("tensor", lambda e: e.transpose(out, in_, ident),
                          ("sPE", 1) if inc else None)

    def vtt(self, out, in0, in1, op, inc=False):
        return self._emit("vector", lambda e: e.tensor_tensor(out, in0, in1, op),
                          ("sV", 1) if inc else None)

    def vts(self, out, in0, s1, s2, op0, op1, inc=False):
        return self._emit(
            "vector", lambda e: e.tensor_scalar(out, in0, s1, s2, op0, op1),
            ("sV", 1) if inc else None)

    def vts1(self, out, in0, s1, op0, inc=False):
        return self._emit(
            "vector", lambda e: e.tensor_scalar(out, in0, s1, None, op0),
            ("sV", 1) if inc else None)

    def vstt(self, out, in0, scalar, in1, op0, op1, inc=False):
        return self._emit(
            "vector",
            lambda e: e.scalar_tensor_tensor(out, in0, scalar, in1, op0, op1),
            ("sV", 1) if inc else None)

    def vred(self, out, in_, op, inc=False, **kw):
        return self._emit(
            "vector", lambda e: e.tensor_reduce(out, in_, AX.X, op, **kw),
            ("sV", 1) if inc else None)

    def vcopy(self, out, in_, inc=False):
        return self._emit("vector", lambda e: e.tensor_copy(out, in_),
                          ("sV", 1) if inc else None)

    def vrecip(self, out, in_, inc=False):
        return self._emit("vector", lambda e: e.reciprocal(out, in_),
                          ("sV", 1) if inc else None)

    def vmemset(self, ap, c, inc=False):
        return self._emit("vector", lambda e: e.memset(ap, c),
                          ("sV", 1) if inc else None)

    def act(self, out, in_, func, bias=None, scale=None, inc=False):
        kw = {}
        if bias is not None:
            kw["bias"] = bias
        if scale is not None:
            kw["scale"] = scale
        return self._emit("scalar",
                          lambda e: e.activation(out, in_, func, **kw),
                          ("sA", 1) if inc else None)

    def gmemset(self, ap, c, inc=False):
        return self._emit("gpsimd", lambda e: e.memset(ap, c),
                          ("sP", 1) if inc else None)

    def gaffine(self, out, in_, pattern, compare_op, fill, base, chmul,
                inc=False):
        return self._emit(
            "gpsimd",
            lambda e: e.affine_select(out=out, in_=in_, pattern=pattern,
                                      compare_op=compare_op, fill=fill,
                                      base=base, channel_multiplier=chmul),
            ("sP", 1) if inc else None)

    def barrier(self, sb):
        stats = sb["stats"]
        pe = self.ns.get("pe_last")
        self.vmemset(stats[:, 252:253], 0.0, inc=True)
        self.act(stats[:, 253:254], stats[:, 253:254], AF.Copy, scale=1.0,
                 inc=True)
        self.gmemset(stats[:, 254:255], 0.0, inc=True)
        targets = [("sV", self.tag("sV")), ("sA", self.tag("sA")),
                   ("sP", self.tag("sP")), ("sS", self.tag("sS")),
                   ("sG", self.tag("sG")), ("sC", self.tag("sC"))]
        if pe is not None:
            targets.append(("sPE", pe))
        for engname in ("sync", "tensor", "vector", "scalar", "gpsimd"):
            for sem, v in targets:
                self.wait(engname, sem, v)



def build_nc():
    nc = bass.Bass("TRN2", target_bir_lowering=False)

    dram = {}

    def din(name, shape, dtype=F32):
        dram[name] = nc.dram_tensor(name, list(shape), dtype,
                                    kind="ExternalInput").ap()

    din("x", (T, D))
    din("ln_g", (D,))
    din("ln_b", (D,))
    din("W_h", (D, 2 * HID))
    din("b_h", (2 * HID,))
    din("W_qk", (D, 128))
    din("b_qk", (128,))
    din("os_gamma", (4, 128))
    din("os_beta", (4, 128))
    din("rel_emb", (32, 1))
    din("W_out", (HID, D))
    din("b_out", (D,))
    din("W_dq", (D, 256))
    din("W_uq", (256, D))
    din("W_dkv", (D, 256))
    din("W_uk", (256, D))
    din("W_uv", (256, D))
    din("W_o", (D, D))
    din("r_ls", (D,))
    din("r_lb", (D,))
    din("r_w", (E, D))
    din("r_b", (E,))
    din("e_ls", (E, D))
    din("e_lb", (E, D))
    din("w1", (E, H, D))
    din("b1", (E, H))
    din("w2", (E, D, H))
    din("b2", (E, D))
    din("OH", (32, 511))

    y_out = nc.dram_tensor("y", [T, D], F32, kind="ExternalOutput").ap()
    dbg = {}
    if DEBUG:
        for nm, shape, dt in (("d_stats", [128, 256], F32),
                              ("d_wmed", [128, 10240], BF16),
                              ("d_actfm", [128, 16384], BF16),
                              ("d_wbig", [128, 4096], BF16),
                              ("d_rows", [1, 3200], BF16),
                              ("d_cols", [128, 192], F32),
                              ("d_cols2", [128, 64], F32),
                              ("d_b1col", [128, 256], F32),
                              ("d_biasT", [128, 512], BF16),
                              ("d_gates", [128, 128], F32),
                              ("d_hbuf", [128, 4096], BF16),
                              ("d_rows32", [1, 640], F32),
                              ("d_bvec", [1, 511], F32),
                              ("d_rwT", [128, 64], BF16),
                              ("d_w1T", [128, 4096], BF16)):
            dbg[nm] = nc.dram_tensor(nm, shape, dt, kind="ExternalOutput").ap()

    scr = {}

    def dscr(name, shape, dtype=BF16):
        scr[name] = nc.dram_tensor(name, list(shape), dtype,
                                   kind="Internal").ap()

    dscr("w1c", (E, H, D))
    dscr("w2c", (E, D, H))
    dscr("w1T", (E, D, H))
    dscr("Whc", (D, 2 * HID))
    dscr("Wqkc", (D, 128))
    dscr("Woutc", (HID, D))
    dscr("Wdqc", (D, 256))
    dscr("Wdkvc", (D, 256))
    dscr("Wuqc", (256, D))
    dscr("Wukc", (256, D))
    dscr("Wuvc", (256, D))
    dscr("Woc", (D, D))
    dscr("rowscr", (1, 4096), F32)
    dscr("bvec", (1, 511), F32)

    import contextlib
    with contextlib.ExitStack() as est:
        EE = est.enter_context
        sb = {}

        def sbuf(name, shape, dtype):
            sb[name] = EE(nc.sbuf_tensor(name, list(shape), dtype))

        sbuf("xres", (128, NCH * D), BF16)
        sbuf("wbig", (128, 32768), BF16)
        sbuf("actfm", (128, 16384), BF16)
        sbuf("wmed", (128, 10240), BF16)
        sbuf("hbuf", (128, 4096), BF16)
        sbuf("scr32", (128, 1024), F32)
        sbuf("stats", (128, 256), F32)
        sbuf("gates", (128, 128), F32)
        sbuf("biasT", (128, 512), BF16)
        sbuf("cmask", (128, 128), F32)
        sbuf("ident", (128, 128), BF16)
        sbuf("b1col", (128, 256), F32)
        sbuf("cols", (128, 192), F32)
        sbuf("cols2", (128, 64), F32)
        sbuf("colsb", (128, 96), BF16)
        sbuf("relsb", (32, 1), BF16)
        sbuf("ohsb", (32, 512), BF16)
        sbuf("rows", (1, 3200), BF16)
        sbuf("rows32", (1, 640), F32)
        sbuf("onesb", (1, 128), BF16)
        sbuf("rwT", (128, 64), BF16)

        ps = {}
        for nm, shape, dt in (("pA", [128, 512], F32), ("pB", [128, 512], F32),
                              ("pC", [128, 512], F32),
                              ("pE2", [128, 1024], F32),
                              ("pF2", [128, 1024], F32),
                              ("pTP", [128, 1024], BF16)):
            ps[nm] = EE(nc.psum_tensor(nm, shape, dt))

        sems = {}
        for sname in ("sS", "sG", "sC", "sPE", "sV", "sA", "sP"):
            sems[sname] = EE(nc.semaphore(sname))

        block = EE(nc.Block())
        prog = Prog(nc, sems)

        def emit_all(engine, eng):
            prog.begin(engine, eng)
            program(prog, dram, scr, sb, ps, y_out, dbg if DEBUG else None)
            prog.end()

        @block.sync
        def _(e):
            emit_all("sync", e)

        @block.tensor
        def _(e):
            emit_all("tensor", e)

        @block.vector
        def _(e):
            emit_all("vector", e)

        @block.scalar
        def _(e):
            emit_all("scalar", e)

        @block.gpsimd
        def _(e):
            emit_all("gpsimd", e)

    return nc


def program(p, dram, scr, sb, ps, y_out, dbg=None):
    phase_load_and_prep(p, dram, scr, sb, ps)
    p.barrier(sb)
    if PHASES["attn_a"]:
        for seq in range(NSEQ_RUN):
            attn_a_seq(p, dram, scr, sb, ps, seq)
            p.barrier(sb)
    if dbg is not None and DEBUG_POINT == "attn_a":
        dump_debug(p, sb, dbg)
    prep_ffn_weights(p, dram, scr, sb, ps)
    p.barrier(sb)
    if PHASES["ffn1"]:
        for seq in range(NSEQ_RUN):
            ffn_seq(p, dram, scr, sb, ps, seq)
            p.barrier(sb)
    if dbg is not None and DEBUG_POINT == "ffn1":
        dump_debug(p, sb, dbg)
    if PHASES["attn_b"]:
        for seq in range(NSEQ_RUN):
            attn_b_seq(p, dram, scr, sb, ps, seq)
            p.barrier(sb)
    if PHASES["ffn2"]:
        for seq in range(NSEQ_RUN):
            ffn_seq(p, dram, scr, sb, ps, seq)
            p.barrier(sb)

    phase_output(p, sb, y_out)


def dump_debug(p, sb, dbg):
    pairs = [("d_stats", sb["stats"][:]), ("d_wmed", sb["wmed"][:]),
             ("d_actfm", sb["actfm"][:]), ("d_wbig", sb["wbig"][:, 0:4096]),
             ("d_rows", sb["rows"][:]), ("d_cols", sb["cols"][:]),
             ("d_cols2", sb["cols2"][:]), ("d_b1col", sb["b1col"][:]),
             ("d_biasT", sb["biasT"][:]), ("d_gates", sb["gates"][:]),
             ("d_hbuf", sb["hbuf"][:]), ("d_rows32", sb["rows32"][:]),
             ("d_rwT", sb["rwT"][:])]
    for nm, ap in pairs:
        p.dma("sync", dbg[nm], ap)
    p.dma("sync", dbg["d_bvec"], p.ns["bvec_scr"])
    p.dma("sync", dbg["d_w1T"], p.ns["w1T_scr"][0][0:128, :])
    p.wait("sync", "sS", p.tag("sS"))
    p.barrier(sb)



def ln_stats_seq(p, sb, seq, double, eps1, eps2):
    xres, stats, scr32, hbuf = sb["xres"], sb["stats"], sb["scr32"], sb["hbuf"]
    n = float(D)
    ssq_rd = {0: None, 1: None}
    for ci in range(CH_SEQ):
        sl = ci % 2
        xc = xres[:, (seq * CH_SEQ + ci) * D:(seq * CH_SEQ + ci + 1) * D]
        sq = hbuf[:, sl * D:(sl + 1) * D]
        p.wait("scalar", "sV", ssq_rd[sl])
        p.act(sq, xc, AF.Square, inc=True)
        sq_tag = p.tag("sA")
        p.vred(stats[:, 64 + ci:65 + ci], xc, ALU.add)
        p.wait("vector", "sA", sq_tag)
        p.vred(stats[:, 80 + ci:81 + ci], sq, ALU.add, inc=True)
        ssq_rd[sl] = p.tag("sV")
    p.wait("vector", "sV", p.tag("sV"))
    p.vts1(stats[:, 0:16], stats[:, 64:80], 1.0 / n, ALU.mult, inc=True)
    p.wait("vector", "sV", p.tag("sV"))
    p.vtt(stats[:, 128:144], stats[:, 0:16], stats[:, 0:16], ALU.mult,
          inc=True)
    p.wait("vector", "sV", p.tag("sV"))
    p.vstt(stats[:, 144:160], stats[:, 80:96], 1.0 / n, stats[:, 128:144],
           ALU.mult, ALU.subtract, inc=True)
    p.wait("scalar", "sV", p.tag("sV"))
    p.act(stats[:, 160:176], stats[:, 144:160], AF.Sqrt,
          bias=stats[:, 248:249] if eps1 == EPS_LN else stats[:, 249:250],
          scale=1.0, inc=True)
    p.wait("vector", "sA", p.tag("sA"))
    p.vrecip(stats[:, 16:32], stats[:, 160:176], inc=True)
    p.wait("vector", "sV", p.tag("sV"))
    if not double:
        return
    p.vtt(stats[:, 128:144], stats[:, 16:32], stats[:, 16:32], ALU.mult,
          inc=True)
    p.wait("vector", "sV", p.tag("sV"))
    p.vtt(stats[:, 144:160], stats[:, 144:160], stats[:, 128:144], ALU.mult,
          inc=True)
    p.wait("scalar", "sV", p.tag("sV"))
    p.wait("vector", "sV", p.tag("sV"))
    p.act(stats[:, 160:176], stats[:, 144:160], AF.Sqrt,
          bias=stats[:, 248:249] if eps2 == EPS_LN else stats[:, 249:250],
          scale=1.0, inc=True)
    p.wait("vector", "sA", p.tag("sA"))
    p.vrecip(stats[:, 32:48], stats[:, 160:176], inc=True)
    p.wait("vector", "sV", p.tag("sV"))
    p.vtt(stats[:, 48:64], stats[:, 16:32], stats[:, 32:48], ALU.mult,
          inc=True)
    p.wait("vector", "sV", p.tag("sV"))


def build_fm_chunk(p, sb, ps, seq, ci, dst_view, double, slot):
    xres, stats, scr32, hbuf = sb["xres"], sb["stats"], sb["scr32"], sb["hbuf"]
    ident, pTP = sb["ident"], ps["pTP"]
    st = p.ns.setdefault("fm", {"evac": None, "slot": {}})
    xc = xres[:, (seq * CH_SEQ + ci) * D:(seq * CH_SEQ + ci + 1) * D]
    stage = hbuf[:, slot * D:(slot + 1) * D]
    p.wait("vector", "sPE", st["slot"].get(slot))
    rcol = stats[:, 48 + ci:49 + ci] if double else stats[:, 16 + ci:17 + ci]
    p.vts(stage, xc, stats[:, ci:ci + 1], rcol, ALU.subtract, ALU.mult,
          inc=True)
    aff = p.tag("sV")
    p.wait("tensor", "sV", aff)
    p.wait("tensor", "sV", st["evac"])
    for dc in range(DC):
        p.tp(pTP[:, dc * 128:(dc + 1) * 128], stage[:, dc * 128:(dc + 1) * 128],
             ident[:], inc=(dc == DC - 1))
    tptag = p.tag("sPE")
    st["slot"][slot] = tptag
    p.wait("vector", "sPE", tptag)
    ptv = pTP[:].rearrange("q (dc t) -> q dc t", t=128)
    p.vcopy(dst_view, ptv, inc=True)
    st["evac"] = p.tag("sV")
    return p.tag("sV")



def phase_load_and_prep(p, dram, scr, sb, ps):
    ns = p.ns
    xres, wbig, wmed = sb["xres"], sb["wbig"], sb["wmed"]
    cols, cols2, colsb = sb["cols"], sb["cols2"], sb["colsb"]

    xv = dram["x"].rearrange("(c p) d -> p c d", p=128)
    for i in range(4):
        p.dma("gpsimd", xres[:, i * 8 * D:(i + 1) * 8 * D],
              xv[:, i * 8:(i + 1) * 8, :])

    p.gmemset(sb["ident"][:], 0.0)
    p.gaffine(sb["ident"][:], sb["ident"][:], [[-1, 128]], ALU.not_equal, 1.0,
              0, 1)
    p.gmemset(sb["cmask"][:], 0.0)
    p.gaffine(sb["cmask"][:], sb["cmask"][:], [[-1, 128]], ALU.is_ge, NEGBIG,
              0, 1, inc=True)
    p.vmemset(sb["onesb"][:], 1.0)
    p.vmemset(sb["colsb"][:, 80:81], 1.0)
    p.vmemset(sb["stats"][:, 248:249], EPS_LN)
    p.vmemset(sb["stats"][:, 249:250], EPS_FFN, inc=True)

    for nm_in, nm_out in (("W_h", "Whc"), ("W_qk", "Wqkc"), ("W_out", "Woutc"),
                          ("W_dq", "Wdqc"), ("W_dkv", "Wdkvc"),
                          ("W_uq", "Wuqc"), ("W_uk", "Wukc"),
                          ("W_uv", "Wuvc"), ("W_o", "Woc")):
        p.dma("gpsimd", scr[nm_out], dram[nm_in])
    ns["small_cast"] = p.tag("sG")
    for e in range(E):
        p.dma("gpsimd", scr["w1c"][e], dram["w1"][e])
        p.dma("gpsimd", scr["w2c"][e], dram["w2"][e])
    ns["big_cast"] = p.tag("sG")

    for dc in range(DC):
        p.dma("sync", cols[:, C_LNG + dc:C_LNG + dc + 1],
              dram["ln_g"][dc * 128:(dc + 1) * 128])
        p.dma("sync", cols[:, C_LNB + dc:C_LNB + dc + 1],
              dram["ln_b"][dc * 128:(dc + 1) * 128])
        p.dma("sync", cols[:, C_RLS + dc:C_RLS + dc + 1],
              dram["r_ls"][dc * 128:(dc + 1) * 128])
        p.dma("gpsimd", colsb[:, dc:dc + 1],
              dram["ln_b"][dc * 128:(dc + 1) * 128])
        p.dma("gpsimd", colsb[:, 8 + dc:9 + dc],
              dram["r_lb"][dc * 128:(dc + 1) * 128])
    for h in range(4):
        p.dma("sync", cols[:, C_GAM + h:C_GAM + h + 1], dram["os_gamma"][h, :])
        p.dma("sync", cols[:, C_BET + h:C_BET + h + 1], dram["os_beta"][h, :])
    for e in range(E):
        for dc in range(DC):
            p.dma("sync", cols[:, C_ELS + e * 8 + dc:C_ELS + e * 8 + dc + 1],
                  dram["e_ls"][e, dc * 128:(dc + 1) * 128])
            p.dma("gpsimd", colsb[:, 16 + e * 8 + dc:17 + e * 8 + dc],
                  dram["e_lb"][e, dc * 128:(dc + 1) * 128])
    for e in range(E):
        for hc in range(32):
            p.dma("sync", sb["b1col"][:, e * 32 + hc:e * 32 + hc + 1],
                  dram["b1"][e, hc * 128:(hc + 1) * 128])
    p.dma("gpsimd", sb["relsb"][:], dram["rel_emb"])
    p.dma("gpsimd", sb["ohsb"][:, 0:511], dram["OH"])
    p.dma("gpsimd", sb["rows"][0:1, 2048:3072], dram["b_out"][None, :])
    ns["cols_f32"] = p.tag("sS")
    ns["cols_bf"] = p.tag("sG")

    p.wait("vector", "sS", ns["cols_f32"])
    p.vts1(cols[:, C_GAM + 1:C_GAM + 2], cols[:, C_GAM + 1:C_GAM + 2],
           1.0 / S, ALU.mult)
    p.vts1(cols[:, C_BET + 1:C_BET + 2], cols[:, C_BET + 1:C_BET + 2],
           1.0 / S, ALU.mult, inc=True)

    p.wait("tensor", "sG", ns["cols_bf"])
    p.mm(ps["pA"][0:1, 0:511], sb["relsb"][:], sb["ohsb"][:, 0:511],
         start=True, stop=True, inc=True)
    t = p.tag("sPE")
    p.wait("vector", "sPE", t)
    p.vcopy(sb["rows32"][0:1, 0:511], ps["pA"][0:1, 0:511], inc=True)
    bvec_evac = p.tag("sV")
    t = p.tag("sV")
    p.wait("sync", "sV", t)
    p.dma("sync", scr["bvec"], sb["rows32"][0:1, 0:511])
    p.ns["bvec_scr"] = scr["bvec"]
    t = p.tag("sS")
    p.wait("gpsimd", "sS", t)
    for jc in range(2):
        for row in range(128):
            j = jc * 128 + row
            p.dma("gpsimd", sb["biasT"][row:row + 1, jc * 256:jc * 256 + 256],
                  scr["bvec"][0:1, 255 - j:255 - j + 256])

    p.wait("sync", "sG", ns["small_cast"])
    whv = scr["Whc"].rearrange("(dcb q) h -> q dcb h", q=128)
    p.dma("sync", wbig[:].rearrange("q (dcb h) -> q dcb h", h=4096), whv)
    wqv = scr["Wqkc"].rearrange("(dcb q) h -> q dcb h", q=128)
    p.dma("sync", wmed[:, 0:1024].rearrange("q (dcb h) -> q dcb h", h=128),
          wqv)
    ns["wh_raw"] = p.tag("sS")

    p.wait("tensor", "sS", ns["wh_raw"])
    p.wait("tensor", "sG", ns["cols_bf"])
    prev_use = {0: bvec_evac, 1: None}
    for nt in range(8):
        pt = ps["pA"] if nt % 2 == 0 else ps["pB"]
        sl = pt[0:1, 0:512]
        p.wait("tensor", "sV", prev_use[nt % 2])
        for dc in range(DC):
            p.mm(sl, colsb[:, dc:dc + 1],
                 wbig[:, dc * 4096 + nt * 512:dc * 4096 + nt * 512 + 512],
                 start=(dc == 0), stop=(dc == DC - 1), inc=(dc == DC - 1))
        mmtag = p.tag("sPE")
        p.wait("sync", "sV", p.ns.get("bh_scr_free"))
        p.dma("sync", sb["scr32"][0:1, 0:512],
              dram["b_h"][None, nt * 512:(nt + 1) * 512])
        ld = p.tag("sS")
        p.wait("vector", "sPE", mmtag)
        p.wait("vector", "sS", ld)
        p.vtt(sb["scr32"][0:1, 0:512], sl, sb["scr32"][0:1, 0:512], ALU.add,
              inc=True)
        prev_use[nt % 2] = p.tag("sV")
        p.ns["bh_scr_free"] = p.tag("sV")
        p.wait("vector", "sV", p.tag("sV"))
        if nt < 4:
            p.vcopy(sb["rows"][0:1, nt * 512:(nt + 1) * 512],
                    sb["scr32"][0:1, 0:512], inc=True)
            prev_use[nt % 2] = p.tag("sV")
            p.ns["bh_scr_free"] = p.tag("sV")
        else:
            t = p.tag("sV")
            p.wait("sync", "sV", t)
            p.dma("sync", scr["rowscr"][0:1, nt * 512:(nt + 1) * 512],
                  sb["scr32"][0:1, 0:512])
    t = p.tag("sS")
    p.wait("sync", "sS", t)
    for gc in range(16):
        a = 2048 + gc * 128
        p.dma("sync", cols2[:, gc:gc + 1], scr["rowscr"][0:1, a:a + 128])

    for dc in range(DC):
        p.mm(ps["pC"][:, 0:1], wmed[:, dc * 128:(dc + 1) * 128],
             colsb[:, dc:dc + 1], start=(dc == 0), stop=(dc == DC - 1),
             inc=(dc == DC - 1))
    t = p.tag("sPE")
    p.dma("sync", cols[:, C_BQK + 1:C_BQK + 2], dram["b_qk"][:])
    ld = p.tag("sS")
    p.wait("vector", "sPE", t)
    p.wait("vector", "sS", ld)
    p.vtt(cols[:, C_BQK:C_BQK + 1], ps["pC"][:, 0:1],
          cols[:, C_BQK + 1:C_BQK + 2], ALU.add, inc=True)
    bqk_read = p.tag("sV")

    p.wait("vector", "sS", ns["cols_f32"])
    p.wait("vector", "sPE", t)
    for dc in range(DC):
        p.vts1(wbig[:, dc * 4096:(dc + 1) * 4096],
               wbig[:, dc * 4096:(dc + 1) * 4096],
               cols[:, C_LNG + dc:C_LNG + dc + 1], ALU.mult)
        p.vts1(wmed[:, dc * 128:(dc + 1) * 128],
               wmed[:, dc * 128:(dc + 1) * 128],
               cols[:, C_LNG + dc:C_LNG + dc + 1], ALU.mult,
               inc=(dc == DC - 1))

    for dc in range(DC):
        for e in range(E):
            p.dma("gpsimd", sb["rwT"][:, dc * 8 + e:dc * 8 + e + 1],
                  dram["r_w"][e, dc * 128:(dc + 1) * 128])
    t = p.tag("sG")
    p.wait("tensor", "sG", t)
    p.wait("tensor", "sV", bqk_read)
    for dc in range(DC):
        p.mm(ps["pC"][0:8, 8:9], sb["rwT"][:, dc * 8:(dc + 1) * 8],
             colsb[:, 8 + dc:9 + dc], start=(dc == 0), stop=(dc == DC - 1),
             inc=(dc == DC - 1))
    t = p.tag("sPE")
    p.dma("sync", sb["rows32"][0:1, 520:528], dram["r_b"][None, :])
    ld = p.tag("sS")
    p.wait("vector", "sPE", t)
    p.wait("vector", "sS", ld)
    p.vcopy(sb["scr32"][0:8, 512:513], ps["pC"][0:8, 8:9], inc=True)
    t = p.tag("sV")
    p.wait("sync", "sV", t)
    p.dma("sync", scr["rowscr"][0:1, 3584:3592], sb["scr32"][0:8, 512:513])
    t = p.tag("sS")
    p.wait("sync", "sS", t)
    p.dma("sync", sb["rows32"][0:1, 512:520], scr["rowscr"][0:1, 3584:3592])
    t = p.tag("sS")
    p.wait("vector", "sS", t)
    p.vtt(sb["rows32"][0:1, 512:520], sb["rows32"][0:1, 512:520],
          sb["rows32"][0:1, 520:528], ALU.add, inc=True)
    p.wait("vector", "sV", p.tag("sV"))
    p.vcopy(sb["rows"][0:1, 3072:3080], sb["rows32"][0:1, 512:520], inc=True)
    for dc in range(DC):
        p.vts1(sb["rwT"][:, dc * 8:(dc + 1) * 8],
               sb["rwT"][:, dc * 8:(dc + 1) * 8],
               cols[:, C_RLS + dc:C_RLS + dc + 1], ALU.mult,
               inc=(dc == DC - 1))
    p.ns["pe_last"] = p.tag("sPE")

def prep_ffn_weights(p, dram, scr, sb, ps):
    p.ns["w1T_scr"] = scr["w1T"]
    """w1T scratch: transpose-DMA strips of w1c, fold e_ls rows, store.
    b1col += lb_e @ W1^T via [128,1]-column matmuls into pC[:,128:160].
    PSUM note: single start=True on the very first MM of the pC session."""
    ns = p.ns
    hbuf, b1col, cols, colsb = sb["hbuf"], sb["b1col"], sb["cols"], sb["colsb"]
    p.wait("sync", "sG", ns["big_cast"])
    store_tag = None
    scale_tag = None
    b1_use = None
    for e in range(E):
        mm_tag = None
        for dc in range(DC):
            p.wait("sync", "sS", store_tag)
            p.wait("sync", "sV", scale_tag)
            p.dma("sync", hbuf[:], scr["w1c"][e][:, dc * 128:(dc + 1) * 128],
                  transpose=True)
            ld = p.tag("sS")
            p.wait("tensor", "sS", ld)
            if dc == 0:
                p.wait("tensor", "sV", b1_use)
            for hc in range(32):
                p.mm(ps["pC"][:, 128 + hc:129 + hc],
                     hbuf[:, hc * 128:(hc + 1) * 128],
                     colsb[:, 16 + e * 8 + dc:17 + e * 8 + dc],
                     start=(dc == 0 and hc == 0),
                     stop=(dc == DC - 1 and hc == 31),
                     inc=(hc == 31))
            mm_tag = p.tag("sPE")
            p.wait("vector", "sPE", mm_tag)
            p.wait("vector", "sS", ld)
            p.vts1(hbuf[:], hbuf[:],
                   cols[:, C_ELS + e * 8 + dc:C_ELS + e * 8 + dc + 1],
                   ALU.mult, inc=True)
            scale_tag = p.tag("sV")
            p.wait("sync", "sV", scale_tag)
            p.dma("sync", scr["w1T"][e][dc * 128:(dc + 1) * 128, :], hbuf[:])
            store_tag = p.tag("sS")
        p.wait("vector", "sPE", mm_tag)
        p.vtt(b1col[:, e * 32:(e + 1) * 32], b1col[:, e * 32:(e + 1) * 32],
              ps["pC"][:, 128:160], ALU.add, inc=True)
        b1_use = p.tag("sV")
    p.ns["pe_last"] = p.tag("sPE")



def attn_a_seq(p, dram, scr, sb, ps, seq):
    ns = p.ns
    xres, wbig, wmed, actfm = sb["xres"], sb["wbig"], sb["wmed"], sb["actfm"]
    cols, cols2, stats, scr32 = (sb["cols"], sb["cols2"], sb["stats"],
                                 sb["scr32"])
    onesb, rows = sb["onesb"], sb["rows"]
    pA, pB, pC, pE2, pF2 = ps["pA"], ps["pB"], ps["pC"], ps["pE2"], ps["pF2"]

    ln_stats_seq(p, sb, seq, True, EPS_LN, EPS_LN)

    xa_view = actfm[:, 0:2048].rearrange("q (dc t) -> q dc t", t=256)

    def build_xa(g):
        for tc in range(2):
            t = build_fm_chunk(p, sb, ps, seq, g * 2 + tc,
                               xa_view[:, :, tc * 128:(tc + 1) * 128], True, tc)
        return t

    def v_tm_group(xa_tag, bankfree):
        for tc in range(2):
            for nt in range(4):
                bk = (tc * 4 + nt) % 2
                pt = pA if bk == 0 else pB
                p.wait("tensor", "sV", xa_tag)
                p.wait("tensor", "sA", bankfree[bk])
                for dc in range(DC):
                    p.mm(pt[:], xa_view[:, dc, tc * 128:(tc + 1) * 128],
                         wbig[:, dc * 4096 + nt * 512:
                              dc * 4096 + nt * 512 + 512],
                         start=(dc == 0), stop=False)
                p.mm(pt[:], onesb[:], rows[0:1, nt * 512:(nt + 1) * 512],
                     start=False, stop=True, inc=True)
                mmt = p.tag("sPE")
                p.wait("scalar", "sPE", mmt)
                p.act(actfm[:, AF_V + tc * 2048 + nt * 512:
                            AF_V + tc * 2048 + nt * 512 + 512],
                      pt[:], AF.Silu, inc=True)
                bankfree[bk] = p.tag("sA")
        return p.tag("sA")

    st = {"vbank": {0: None, 1: None}, "qkfree": None}
    for g in range(NG):
        xa_tag = build_xa(g)
        p.wait("tensor", "sV", xa_tag)
        p.wait("tensor", "sA", st["qkfree"])
        for dc in range(DC):
            p.mm(pC[:, 0:256], wmed[:, dc * 128:(dc + 1) * 128],
                 xa_view[:, dc, :], start=(dc == 0), stop=(dc == DC - 1),
                 inc=(dc == DC - 1))
        qmm = p.tag("sPE")
        p.wait("scalar", "sPE", qmm)
        p.act(wmed[:, WM_QKACT + g * 256:WM_QKACT + (g + 1) * 256],
              pC[:, 0:256], AF.Silu, bias=cols[:, C_BQK:C_BQK + 1], scale=1.0,
              inc=True)
        st["qkfree"] = p.tag("sA")
        qk_tag = st["qkfree"]
        vev = v_tm_group(xa_tag, st["vbank"])
        p.wait("vector", "sA", qk_tag)
        p.wait("vector", "sPE", ns.get("lkt_pe"))
        p.vts(sb["hbuf"][:, 2048:2304],
              wmed[:, WM_QKACT + g * 256:WM_QKACT + (g + 1) * 256],
              cols[:, C_GAM + 3:C_GAM + 4], cols[:, C_BET + 3:C_BET + 4],
              ALU.mult, ALU.add, inc=True)
        lk = p.tag("sV")
        fmst = p.ns["fm"]
        p.wait("tensor", "sV", lk)
        p.wait("tensor", "sV", fmst["evac"])
        for tc in range(2):
            p.tp(ps["pTP"][:, tc * 128:(tc + 1) * 128],
                 sb["hbuf"][:, 2048 + tc * 128:2048 + (tc + 1) * 128],
                 sb["ident"][:], inc=(tc == 1))
        ns["lkt_pe"] = p.tag("sPE")
        p.wait("vector", "sPE", ns["lkt_pe"])
        p.vcopy(wmed[:, WM_LKT:WM_LKT + 256], ps["pTP"][:, 0:256], inc=True)
        fmst["evac"] = p.tag("sV")
        lkt_tag = p.tag("sV")
        p.wait("tensor", "sV", lkt_tag)
        p.wait("tensor", "sA", vev)
        for tc in range(2):
            for nt in range(4):
                pt = pE2 if nt < 2 else pF2
                sl = pt[:, (nt % 2) * 512:(nt % 2) * 512 + 512]
                p.mm(sl, wmed[:, WM_LKT + tc * 128:WM_LKT + (tc + 1) * 128],
                     actfm[:, AF_V + tc * 2048 + nt * 512:
                           AF_V + tc * 2048 + nt * 512 + 512],
                     start=(g == 0 and tc == 0),
                     stop=(g == NG - 1 and tc == 1),
                     inc=(g == NG - 1 and tc == 1 and nt == 3))
    linkv_mm = p.tag("sPE")
    p.wait("vector", "sPE", linkv_mm)
    p.vcopy(wmed[:, WM_LINKV:WM_LINKV + 1024], pE2[:], inc=True)
    p.vcopy(wmed[:, WM_LINKV + 1024:WM_LINKV + 2048], pF2[:], inc=True)
    linkv_tag = p.tag("sV")

    st2 = {"vbank": {0: None, 1: None}, "gslot": {0: None, 1: None},
           "qlslot": {0: None, 1: None, 2: None},
           "wout": {0: None, 1: None},
           "resfree": {0: linkv_tag, 1: linkv_tag}}
    woutv = scr["Woutc"].rearrange("(ecb q) d2 -> q ecb d2", q=128)
    for g in range(NG):
        xa_tag = build_xa(g)
        vev = v_tm_group(xa_tag, st2["vbank"])
        for gc in range(16):
            bk = gc % 2
            pt = pC if bk == 0 else pA
            sl = pt[:, 0:256]
            p.wait("tensor", "sA", vev)
            p.wait("tensor", "sA", st2["gslot"][bk])
            for dc in range(DC):
                p.mm(sl, wbig[:, dc * 4096 + 2048 + gc * 128:
                              dc * 4096 + 2048 + (gc + 1) * 128],
                     xa_view[:, dc, :], start=(dc == 0), stop=(dc == DC - 1),
                     inc=(dc == DC - 1))
            gmm = p.tag("sPE")
            p.wait("scalar", "sPE", gmm)
            p.act(actfm[:, AF_GATE + gc * 256:AF_GATE + (gc + 1) * 256], sl,
                  AF.Silu, bias=cols2[:, gc:gc + 1], scale=1.0, inc=True)
            st2["gslot"][bk] = p.tag("sA")
        gate_done = p.tag("sA")
        qk_g = wmed[:, WM_QKACT + g * 256:WM_QKACT + (g + 1) * 256]
        p.vts(wmed[:, WM_QQ:WM_QQ + 256], qk_g, cols[:, C_GAM:C_GAM + 1],
              cols[:, C_BET:C_BET + 1], ALU.mult, ALU.add)
        p.vts(wmed[:, WM_QKD:WM_QKD + 256], qk_g,
              cols[:, C_GAM + 2:C_GAM + 3], cols[:, C_BET + 2:C_BET + 3],
              ALU.mult, ALU.add)
        p.vts(wmed[:, WM_LQ:WM_LQ + 256], qk_g, cols[:, C_GAM + 1:C_GAM + 2],
              cols[:, C_BET + 1:C_BET + 2], ALU.mult, ALU.add, inc=True)
        heads_tag = p.tag("sV")
        p.wait("tensor", "sV", heads_tag)
        p.wait("tensor", "sA", gate_done)
        for jc in range(2):
            p.mm(pC[:, jc * 256:(jc + 1) * 256],
                 wmed[:, WM_QKD + jc * 128:WM_QKD + (jc + 1) * 128],
                 wmed[:, WM_QQ:WM_QQ + 256], start=True, stop=True,
                 inc=(jc == 1))
        sim_tag = p.tag("sPE")
        p.wait("vector", "sPE", sim_tag)
        for jc in range(2):
            p.vstt(scr32[:, jc * 256:(jc + 1) * 256],
                   pC[:, jc * 256:(jc + 1) * 256], 1.0 / G,
                   sb["biasT"][:, jc * 256:(jc + 1) * 256], ALU.mult, ALU.add)
            p.vstt(wmed[:, WM_ATT + jc * 256:WM_ATT + (jc + 1) * 256],
                   scr32[:, jc * 256:(jc + 1) * 256], 0.0,
                   scr32[:, jc * 256:(jc + 1) * 256], ALU.max, ALU.mult,
                   inc=(jc == 1))
        att_tag = p.tag("sV")
        for ec in range(16):
            bk = ec % 3
            pt = (pA, pB, pC)[bk]
            sl = pt[:, 0:256]
            p.wait("tensor", "sV", st2["qlslot"][bk])
            p.wait("tensor", "sV", att_tag)
            for tc in range(2):
                p.mm(sl, actfm[:, AF_V + tc * 2048 + ec * 128:
                               AF_V + tc * 2048 + (ec + 1) * 128],
                     wmed[:, WM_ATT + tc * 256:WM_ATT + (tc + 1) * 256],
                     start=(tc == 0), stop=False)
            p.mm(sl, wmed[:, WM_LINKV + ec * 128:WM_LINKV + (ec + 1) * 128],
                 wmed[:, WM_LQ:WM_LQ + 256], start=False, stop=True, inc=True)
            qlmm = p.tag("sPE")
            p.wait("vector", "sPE", qlmm)
            p.vtt(actfm[:, AF_GATED + ec * 256:AF_GATED + (ec + 1) * 256], sl,
                  actfm[:, AF_GATE + ec * 256:AF_GATE + (ec + 1) * 256],
                  ALU.mult, inc=True)
            st2["qlslot"][bk] = p.tag("sV")
        gated_tag = p.tag("sV")
        for ec in range(16):
            slot = ec % 2
            p.wait("sync", "sPE", st2["wout"][slot])
            p.dma("sync", wmed[:, WM_WOUT + slot * 1024:
                               WM_WOUT + (slot + 1) * 1024], woutv[:, ec, :])
            dtag = p.tag("sS")
            wsl = wmed[:, WM_WOUT + slot * 1024:WM_WOUT + (slot + 1) * 1024]
            p.wait("tensor", "sS", dtag)
            if ec == 0:
                p.wait("tensor", "sV", gated_tag)
                p.wait("tensor", "sV", st2["resfree"][0])
                p.wait("tensor", "sV", st2["resfree"][1])
            for tc in range(2):
                pt = pE2 if tc == 0 else pF2
                gsl = actfm[:, AF_GATED + ec * 256 + tc * 128:
                            AF_GATED + ec * 256 + (tc + 1) * 128]
                for half in range(2):
                    p.mm(pt[:, half * 512:(half + 1) * 512], gsl,
                         wsl[:, half * 512:(half + 1) * 512],
                         start=(ec == 0), stop=False,
                         inc=(tc == 1 and half == 1 and ec < 15))
            if ec == 15:
                for tc in range(2):
                    pt = pE2 if tc == 0 else pF2
                    for half in range(2):
                        p.mm(pt[:, half * 512:(half + 1) * 512], onesb[:],
                             rows[0:1, 2048 + half * 512:
                                  2048 + (half + 1) * 512],
                             start=False, stop=True,
                             inc=(tc == 1 and half == 1))
            st2["wout"][slot] = p.tag("sPE")
        wout_mm = p.tag("sPE")
        for tc in range(2):
            ci = g * 2 + tc
            c = seq * CH_SEQ + ci
            pt = pE2 if tc == 0 else pF2
            xc = xres[:, c * D:(c + 1) * D]
            p.wait("vector", "sPE", wout_mm)
            p.vts(scr32[:, 0:D], xc, stats[:, ci:ci + 1],
                  stats[:, 16 + ci:17 + ci], ALU.subtract, ALU.mult)
            p.vtt(scr32[:, 0:D], scr32[:, 0:D], pt[:], ALU.add)
            p.vtt(xc, scr32[:, 0:D], xc, ALU.add, inc=True)
            st2["resfree"][tc] = p.tag("sV")
    p.ns["pe_last"] = p.tag("sPE")



def ffn_seq(p, dram, scr, sb, ps, seq):
    ns = p.ns
    xres, wbig, wmed, actfm = sb["xres"], sb["wbig"], sb["wmed"], sb["actfm"]
    hbuf, stats, scr32, gates = (sb["hbuf"], sb["stats"], sb["scr32"],
                                 sb["gates"])
    gatesT = sb["wmed"][0:8, 0:2048]
    b2sb = sb["wmed"][0:8, 2048:3072]
    onesb, rows = sb["onesb"], sb["rows"]
    p.dma("gpsimd", b2sb, dram["b2"])
    b2_tag = p.tag("sG")
    pA, pB, pC, pE2, pF2, pTP = (ps["pA"], ps["pB"], ps["pC"], ps["pE2"],
                                 ps["pF2"], ps["pTP"])

    ln_stats_seq(p, sb, seq, True, EPS_LN, EPS_FFN)

    xh_view = actfm[:].rearrange("q (dc t) -> q dc t", t=2048)
    xhat_tag = None
    for ci in range(CH_SEQ):
        xhat_tag = build_fm_chunk(p, sb, ps, seq, ci,
                                  xh_view[:, :, ci * 128:(ci + 1) * 128],
                                  True, ci % 2)

    lfree = {0: None, 1: None}
    for ci in range(CH_SEQ):
        bk = ci % 2
        pt = pC if bk == 0 else pA
        sl = pt[:, 0:8]
        p.wait("tensor", "sV", xhat_tag)
        p.wait("tensor", "sV", lfree[bk])
        for dc in range(DC):
            p.mm(sl, xh_view[:, dc, ci * 128:(ci + 1) * 128],
                 sb["rwT"][:, dc * 8:(dc + 1) * 8], start=(dc == 0),
                 stop=False)
        p.mm(sl, onesb[:], rows[0:1, 3072:3080], start=False, stop=True,
             inc=True)
        lmm = p.tag("sPE")
        p.wait("vector", "sPE", lmm)
        p.vcopy(gates[:, ci * 8:(ci + 1) * 8], sl, inc=True)
        lfree[bk] = p.tag("sV")
    logits_tag = p.tag("sV")
    p.wait("scalar", "sV", logits_tag)
    p.act(gates[:], gates[:], AF.Exp, inc=True)
    t = p.tag("sA")
    p.wait("vector", "sA", t)
    gv = gates[:].rearrange("q (c e) -> q c e", e=8)
    m1 = stats[:, 176:192]
    m2 = stats[:, 192:208]
    dinv = stats[:, 208:224]
    w1t = stats[:, 224:240]
    p.vred(m1, gv, ALU.max, inc=True)
    p.wait("vector", "sV", p.tag("sV"))
    m1b = m1.unsqueeze(2).broadcast_to([128, 16, 8])
    is1 = scr32[:, 0:128].rearrange("q (c e) -> q c e", e=8)
    p.vtt(is1, gv, m1b, ALU.is_equal, inc=True)
    p.wait("vector", "sV", p.tag("sV"))
    masked = scr32[:, 128:256].rearrange("q (c e) -> q c e", e=8)
    p.vstt(masked, is1, NEGBIG, gv, ALU.mult, ALU.add, inc=True)
    p.wait("vector", "sV", p.tag("sV"))
    p.vred(m2, masked, ALU.max, inc=True)
    p.wait("vector", "sV", p.tag("sV"))
    m2b = m2.unsqueeze(2).broadcast_to([128, 16, 8])
    is2 = scr32[:, 256:384].rearrange("q (c e) -> q c e", e=8)
    p.vtt(is2, masked, m2b, ALU.is_equal, inc=True)
    p.vtt(dinv, m1, m2, ALU.add, inc=True)
    p.wait("vector", "sV", p.tag("sV"))
    p.vrecip(dinv, dinv, inc=True)
    p.wait("vector", "sV", p.tag("sV"))
    p.vtt(w1t, m1, dinv, ALU.mult)
    p.vtt(m2, m2, dinv, ALU.mult, inc=True)
    p.wait("vector", "sV", p.tag("sV"))
    w1b = w1t.unsqueeze(2).broadcast_to([128, 16, 8])
    w2b = m2.unsqueeze(2).broadcast_to([128, 16, 8])
    p.vtt(gv, is1, w1b, ALU.mult)
    p.vtt(is2, is2, w2b, ALU.mult, inc=True)
    p.wait("vector", "sV", p.tag("sV"))
    p.vtt(gv, gv, is2, ALU.add, inc=True)
    gates_tag = p.tag("sV")
    p.wait("vector", "sV", gates_tag)

    fmst = p.ns["fm"]
    p.wait("vector", "sPE", fmst["slot"].get(0))
    p.vcopy(hbuf[:, 0:128], gates[:], inc=True)
    gt0 = p.tag("sV")
    for rnd in range(2):
        p.wait("tensor", "sV", gt0)
        p.wait("tensor", "sV", fmst["evac"])
        for k in range(8):
            ci = rnd * 8 + k
            p.tp(pTP[0:8, k * 128:(k + 1) * 128],
                 hbuf[:, ci * 8:(ci + 1) * 8], sb["ident"][:], inc=(k == 7))
        tpt = p.tag("sPE")
        p.wait("vector", "sPE", tpt)
        p.vcopy(gatesT[0:8, rnd * 1024:(rnd + 1) * 1024], pTP[0:8, 0:1024],
                inc=True)
        fmst["evac"] = p.tag("sV")
    gatesT_tag = p.tag("sV")
    fmst["slot"][0] = p.tag("sPE")

    wload = {}
    slot_free = {0: None, 1: None}
    yfree = {0: gatesT_tag, 1: gatesT_tag}
    hfree = {i: None for i in range(4)}
    psfree = {i: None for i in range(3)}

    def issue_wload(e, q):
        slot = (e * NQ + q) % 2
        p.wait("sync", "sPE", slot_free[slot])
        w1v = scr["w1T"][e].rearrange("(dcb q2) h2 -> q2 dcb h2", q2=128)
        p.dma("sync",
              wbig[:, WB_W1Q[0] + slot * 8192:WB_W1Q[0] + slot * 8192 + 8192]
              .rearrange("q2 (dcb h2) -> q2 dcb h2", h2=1024),
              w1v[:, :, q * 1024:(q + 1) * 1024])
        for hc in range(8):
            p.dma("sync",
                  wbig[:, WB_W2Q[0] + slot * 8192 + hc * 1024:
                        WB_W2Q[0] + slot * 8192 + (hc + 1) * 1024],
                  scr["w2c"][e][:, q * 1024 + hc * 128:
                                q * 1024 + (hc + 1) * 128],
                  transpose=True)
        wload[(e, q)] = p.tag("sS")

    issue_wload(0, 0)
    for e in range(E):
        for q in range(NQ):
            slot = (e * NQ + q) % 2
            nxt = (e, q + 1) if q + 1 < NQ else ((e + 1, 0) if e + 1 < E
                                                 else None)
            if nxt is not None:
                issue_wload(*nxt)
            w1sl = wbig[:, WB_W1Q[0] + slot * 8192:
                        WB_W1Q[0] + slot * 8192 + 8192]
            w2sl = wbig[:, WB_W2Q[0] + slot * 8192:
                        WB_W2Q[0] + slot * 8192 + 8192]
            p.wait("tensor", "sS", wload[(e, q)])
            for tb in range(8):
                hact = {}

                def emit_h(hc):
                    hs = hc % 4
                    pb = hc % 3
                    pt = (pA, pB, pC)[pb]
                    sl = pt[:, 0:256]
                    p.wait("tensor", "sA", psfree[pb])
                    for dc in range(DC):
                        p.mm(sl, w1sl[:, dc * 1024 + hc * 128:
                                      dc * 1024 + (hc + 1) * 128],
                             xh_view[:, dc, tb * 256:(tb + 1) * 256],
                             start=(dc == 0), stop=(dc == DC - 1),
                             inc=(dc == DC - 1))
                    hmm = p.tag("sPE")
                    p.wait("scalar", "sPE", hmm)
                    p.wait("scalar", "sPE", hfree[hs])
                    p.act(hbuf[:, hs * 256:(hs + 1) * 256], sl, AF.Gelu,
                          bias=sb["b1col"][:, e * 32 + q * 8 + hc:
                                           e * 32 + q * 8 + hc + 1],
                          scale=1.0, inc=True)
                    psfree[pb] = p.tag("sA")
                    hact[hc] = p.tag("sA")

                def emit_y2(hc):
                    hs = hc % 4
                    p.wait("tensor", "sA", hact[hc])
                    if hc == 0:
                        p.wait("tensor", "sV", yfree[0])
                        p.wait("tensor", "sV", yfree[1])
                    for t128 in range(2):
                        pt2 = pE2 if t128 == 0 else pF2
                        for half in range(2):
                            p.mm(pt2[:, half * 512:(half + 1) * 512],
                                 hbuf[:, hs * 256 + t128 * 128:
                                      hs * 256 + (t128 + 1) * 128],
                                 w2sl[:, hc * 1024 + half * 512:
                                      hc * 1024 + (half + 1) * 512],
                                 start=(hc == 0), stop=(hc == 7),
                                 inc=(t128 == 1 and half == 1))
                    hfree[hs] = p.tag("sPE")

                emit_h(0)
                for hc in range(1, 8):
                    emit_h(hc)
                    emit_y2(hc - 1)
                emit_y2(7)
                y2mm = p.tag("sPE")
                for t128 in range(2):
                    ci = tb * 2 + t128
                    c = seq * CH_SEQ + ci
                    pt2 = pE2 if t128 == 0 else pF2
                    xc = xres[:, c * D:(c + 1) * D]
                    p.wait("vector", "sPE", y2mm)
                    p.vstt(xc, pt2[:], gates[:, ci * 8 + e:ci * 8 + e + 1],
                           xc, ALU.mult, ALU.add, inc=True)
                    yfree[t128] = p.tag("sV")
            slot_free[slot] = p.tag("sPE")
    bfree = {0: yfree[0], 1: yfree[1]}
    for ci in range(CH_SEQ):
        c = seq * CH_SEQ + ci
        bk = ci % 2
        pt2 = pE2 if bk == 0 else pF2
        p.wait("tensor", "sV", bfree[bk])
        for half in range(2):
            p.wait("tensor", "sG", b2_tag)
            p.mm(pt2[:, half * 512:(half + 1) * 512],
                 gatesT[0:8, ci * 128:(ci + 1) * 128],
                 b2sb[:, half * 512:(half + 1) * 512],
                 start=True, stop=True, inc=(half == 1))
        bmm = p.tag("sPE")
        xc = xres[:, c * D:(c + 1) * D]
        p.wait("vector", "sPE", bmm)
        p.vtt(xc, pt2[:], xc, ALU.add, inc=True)
        bfree[bk] = p.tag("sV")
    p.ns["pe_last"] = p.tag("sPE")



def attn_b_seq(p, dram, scr, sb, ps, seq):
    ns = p.ns
    xres, wbig, wmed, actfm = sb["xres"], sb["wbig"], sb["wmed"], sb["actfm"]
    hbuf, stats, scr32 = sb["hbuf"], sb["stats"], sb["scr32"]
    pA, pB, pC, pE2, pF2, pTP = (ps["pA"], ps["pB"], ps["pC"], ps["pE2"],
                                 ps["pF2"], ps["pTP"])

    ln_stats_seq(p, sb, seq, False, EPS_LN, EPS_LN)
    yv = actfm[:].rearrange("q (dc t) -> q dc t", t=2048)
    y_tag = None
    for ci in range(CH_SEQ):
        y_tag = build_fm_chunk(p, sb, ps, seq, ci,
                               yv[:, :, ci * 128:(ci + 1) * 128], False,
                               ci % 2)

    for nm, off in (("Wdqc", WM_DQ), ("Wdkvc", WM_DKV)):
        wv = scr[nm].rearrange("(dcb q2) c2 -> q2 dcb c2", q2=128)
        p.dma("sync", wmed[:, off:off + 2048]
              .rearrange("q2 (dcb c2) -> q2 dcb c2", c2=256), wv)
    dq_tag = p.tag("sS")

    bank = {0: ns["fm"]["evac"], 1: ns["fm"]["evac"]}
    for which, off_w, dst, dsto in ((0, WM_DQ, hbuf, 0),
                                    (1, WM_DKV, wmed, WM_CKV)):
        for cc in range(2):
            for nt in range(4):
                bk = (cc * 4 + nt) % 2
                pt = pA if bk == 0 else pB
                p.wait("tensor", "sS", dq_tag)
                p.wait("tensor", "sV", y_tag)
                p.wait("tensor", "sV", bank[bk])
                for dc in range(DC):
                    p.mm(pt[:], wmed[:, off_w + dc * 256 + cc * 128:
                                     off_w + dc * 256 + (cc + 1) * 128],
                         yv[:, dc, nt * 512:(nt + 1) * 512],
                         start=(dc == 0), stop=(dc == DC - 1),
                         inc=(dc == DC - 1))
                cmm = p.tag("sPE")
                p.wait("vector", "sPE", cmm)
                p.vcopy(dst[:, dsto + cc * 2048 + nt * 512:
                            dsto + cc * 2048 + (nt + 1) * 512], pt[:],
                        inc=True)
                bank[bk] = p.tag("sV")
    c_done_pe = p.tag("sPE")
    c_done = p.tag("sV")

    p.wait("sync", "sPE", c_done_pe)
    for nm, off in (("Wukc", WM_UK), ("Wuvc", WM_UV)):
        wv = scr[nm].rearrange("(ccb q2) d2 -> q2 ccb d2", q2=128)
        p.dma("sync", wmed[:, off:off + 2048]
              .rearrange("q2 (ccb d2) -> q2 ccb d2", d2=1024), wv)
    uk_tag = p.tag("sS")

    for qc in range(8):
        for nt in range(4):
            bk = (qc * 4 + nt) % 2
            pt = pA if bk == 0 else pB
            p.wait("tensor", "sS", uk_tag)
            p.wait("tensor", "sV", c_done)
            p.wait("tensor", "sV", bank[bk])
            for cc in range(2):
                p.mm(pt[:], wmed[:, WM_UK + cc * 1024 + qc * 128:
                                 WM_UK + cc * 1024 + (qc + 1) * 128],
                     wmed[:, WM_CKV + cc * 2048 + nt * 512:
                          WM_CKV + cc * 2048 + (nt + 1) * 512],
                     start=(cc == 0), stop=(cc == 1), inc=(cc == 1))
            kmm = p.tag("sPE")
            p.wait("vector", "sPE", kmm)
            p.vcopy(wbig[:, qc * 2048 + nt * 512:qc * 2048 + (nt + 1) * 512],
                    pt[:], inc=True)
            bank[bk] = p.tag("sV")
    vfree = {0: None, 1: None}
    for tc in range(16):
        bk = tc % 2
        pt2 = pE2 if bk == 0 else pF2
        p.wait("tensor", "sV", vfree[bk])
        for half in range(2):
            for cc in range(2):
                p.mm(pt2[:, half * 512:(half + 1) * 512],
                     wmed[:, WM_CKV + cc * 2048 + tc * 128:
                          WM_CKV + cc * 2048 + (tc + 1) * 128],
                     wmed[:, WM_UV + cc * 1024 + half * 512:
                          WM_UV + cc * 1024 + (half + 1) * 512],
                     start=(cc == 0), stop=(cc == 1),
                     inc=(cc == 1 and half == 1))
        vmm = p.tag("sPE")
        p.wait("vector", "sPE", vmm)
        p.vcopy(wbig[:, 16384 + tc * 1024:16384 + (tc + 1) * 1024], pt2[:],
                inc=True)
        vfree[bk] = p.tag("sV")
    kv_done_pe = p.tag("sPE")
    kv_done = p.tag("sV")

    p.wait("sync", "sPE", kv_done_pe)
    wv = scr["Wuqc"].rearrange("(ccb q2) d2 -> q2 ccb d2", q2=128)
    p.dma("sync", wmed[:, WM_UQ:WM_UQ + 2048]
          .rearrange("q2 (ccb d2) -> q2 ccb d2", d2=1024), wv)
    wv = scr["Woc"].rearrange("(vcb q2) d2 -> q2 vcb d2", q2=128)
    p.dma("sync", actfm[:, 0:8192]
          .rearrange("q2 (vcb d2) -> q2 vcb d2", d2=1024), wv)
    uq_tag = p.tag("sS")

    st = {"qt": {0: None, 1: None}, "exp": None, "avfree": None,
          "res": kv_done, "tpdone": None, "pn": {}, "J": {}}
    fmst = p.ns["fm"]

    def emit_head_scores(i, h):
        J = (i + 1) * 128
        njt = (J + 511) // 512
        st["J"][h] = (i, J, njt)
        qc, base = h // 2, (h % 2) * 64
        p.wait("tensor", "sV", st["qt"][0])
        p.wait("tensor", "sV", st["qt"][1])
        p.wait("tensor", "sA", st["exp"])
        if h == 0:
            p.wait("tensor", "sV", st["res"])
        for jt in range(njt):
            N = min(512, J - jt * 512)
            pt2 = pE2 if jt < 2 else pF2
            sl = pt2[:, (jt % 2) * 512:(jt % 2) * 512 + N]
            p.mm(sl, wmed[base:base + 64, WM_QT + qc * 128:
                          WM_QT + (qc + 1) * 128],
                 wbig[base:base + 64, qc * 2048 + jt * 512:
                      qc * 2048 + jt * 512 + N],
                 start=True, stop=True, inc=(jt == njt - 1))
        smm = p.tag("sPE")
        dj = i * 128
        jt = dj // 512
        pt2 = pE2 if jt < 2 else pF2
        dsl = pt2[:, (jt % 2) * 512 + dj % 512:
                  (jt % 2) * 512 + dj % 512 + 128]
        p.wait("vector", "sPE", smm)
        p.vtt(dsl, dsl, sb["cmask"][:], ALU.add, inc=True)
        p.wait("vector", "sV", p.tag("sV"))
        for jt in range(njt):
            N = min(512, J - jt * 512)
            pt2 = pE2 if jt < 2 else pF2
            p.vred(stats[:, 176 + jt:177 + jt],
                   pt2[:, (jt % 2) * 512:(jt % 2) * 512 + N], ALU.max,
                   inc=(jt == njt - 1))
        p.wait("vector", "sV", p.tag("sV"))
        if njt > 1:
            p.vred(stats[:, 180:181], stats[:, 176:176 + njt], ALU.max,
                   inc=True)
            p.wait("vector", "sV", p.tag("sV"))
            p.vts1(stats[:, 181:182], stats[:, 180:181], -0.125, ALU.mult,
                   inc=True)
        else:
            p.vts1(stats[:, 181:182], stats[:, 176:177], -0.125, ALU.mult,
                   inc=True)
        mtag = p.tag("sV")
        p.wait("scalar", "sV", mtag)
        p.wait("scalar", "sPE", st["tpdone"])
        for jt in range(njt):
            N = min(512, J - jt * 512)
            pt2 = pE2 if jt < 2 else pF2
            p.act(wmed[:, WM_PSB + jt * 512:WM_PSB + jt * 512 + N],
                  pt2[:, (jt % 2) * 512:(jt % 2) * 512 + N], AF.Exp,
                  bias=stats[:, 181:182], scale=0.125,
                  inc=(jt == njt - 1))
        st["exp"] = p.tag("sA")
        st["pn"][h] = st["exp"]

    def emit_head_tpav(h):
        i, J, njt = st["J"][h]
        qc, base = h // 2, (h % 2) * 64
        pn_tag = st["pn"][h]
        for jc in range(i + 1):
            ksl = jc % 8
            p.wait("tensor", "sA", pn_tag)
            if ksl == 0:
                p.wait("tensor", "sV", fmst["evac"])
            p.tp(pTP[:, ksl * 128:(ksl + 1) * 128],
                 wmed[:, WM_PSB + jc * 128:WM_PSB + (jc + 1) * 128],
                 sb["ident"][:], inc=(ksl == 7 or jc == i))
            if ksl == 7 or jc == i:
                tpt = p.tag("sPE")
                p.wait("vector", "sPE", tpt)
                lo = (jc // 8) * 8
                p.vcopy(wmed[:, WM_PT + lo * 128:WM_PT + (jc + 1) * 128],
                        pTP[:, 0:(jc + 1 - lo) * 128], inc=True)
                fmst["evac"] = p.tag("sV")
        st["tpdone"] = p.tag("sPE")
        pt_tag = p.tag("sV")
        p.wait("tensor", "sV", pt_tag)
        p.wait("tensor", "sV", st.get("lrow_free"))
        lsl = pC[0:1, 256 + (h % 2) * 128:256 + (h % 2) * 128 + 128]
        for jc in range(i + 1):
            p.mm(lsl, sb["colsb"][:, 80:81],
                 wmed[:, WM_PT + jc * 128:WM_PT + (jc + 1) * 128],
                 start=(jc == 0), stop=(jc == i))
        if h % 2 == 0:
            p.wait("tensor", "sV", st["avfree"])
        for jc in range(i + 1):
            kw = {}
            if base:
                kw["tile_position"] = (0, base)
            p.mm(pB[base:base + 64, 256:384],
                 wbig[:, 16384 + jc * 1024 + h * 64:
                      16384 + jc * 1024 + (h + 1) * 64],
                 wmed[:, WM_PT + jc * 128:WM_PT + (jc + 1) * 128],
                 start=(jc == 0), stop=(jc == i), inc=(jc == i), **kw)
        avmm = p.tag("sPE")
        if h % 2 == 1:
            p.wait("vector", "sPE", avmm)
            p.vrecip(scr32[0:1, 512:640], pC[0:1, 256:384])
            p.vrecip(scr32[0:1, 640:768], pC[0:1, 384:512], inc=True)
            p.wait("vector", "sV", p.tag("sV"))
            p.vcopy(wmed[0:1, 2048:2304], scr32[0:1, 512:768], inc=True)
            p.wait("vector", "sV", p.tag("sV"))
            st["lrow_free"] = p.tag("sV")
            p.wait("tensor", "sV", st["lrow_free"])
            p.mm(pA[0:64, 256:384], sb["onesb"][0:1, 0:64],
                 wmed[0:1, 2048:2176], start=True, stop=True)
            p.mm(pA[64:128, 256:384], sb["onesb"][0:1, 0:64],
                 wmed[0:1, 2176:2304], start=True, stop=True,
                 tile_position=(0, 64), inc=True)
            bmm = p.tag("sPE")
            p.wait("vector", "sPE", bmm)
            p.vcopy(scr32[:, 0:128], pA[:, 256:384], inc=True)
            p.wait("vector", "sV", p.tag("sV"))
            p.vtt(wmed[:, WM_AO + qc * 128:WM_AO + (qc + 1) * 128],
                  pB[:, 256:384], scr32[:, 0:128], ALU.mult, inc=True)
            st["avfree"] = p.tag("sV")
            st["bc_free"] = p.tag("sV")

    for i in range(CH_SEQ):
        for qc in range(8):
            bk = qc % 2
            pt = pA if bk == 0 else pC
            sl = pt[:, 0:128]
            p.wait("tensor", "sS", uq_tag)
            p.wait("tensor", "sV", st["qt"][bk])
            p.wait("tensor", "sV", st.get("bc_free"))
            p.wait("tensor", "sV", st.get("lrow_free"))
            for cc in range(2):
                p.mm(sl, wmed[:, WM_UQ + cc * 1024 + qc * 128:
                              WM_UQ + cc * 1024 + (qc + 1) * 128],
                     hbuf[:, cc * 2048 + i * 128:cc * 2048 + (i + 1) * 128],
                     start=(cc == 0), stop=(cc == 1), inc=(cc == 1))
            qmm = p.tag("sPE")
            p.wait("vector", "sPE", qmm)
            p.vcopy(wmed[:, WM_QT + qc * 128:WM_QT + (qc + 1) * 128], sl,
                    inc=True)
            st["qt"][bk] = p.tag("sV")
        for h in range(NHEAD):
            if h > 0:
                emit_head_tpav(h - 1)
            emit_head_scores(i, h)
        emit_head_tpav(NHEAD - 1)
        ao_tag = st["avfree"]
        p.wait("tensor", "sV", ao_tag)
        p.wait("tensor", "sV", st["res"])
        for vc in range(8):
            for half in range(2):
                p.mm(pE2[:, half * 512:(half + 1) * 512],
                     wmed[:, WM_AO + vc * 128:WM_AO + (vc + 1) * 128],
                     actfm[:, vc * 1024 + half * 512:
                           vc * 1024 + (half + 1) * 512],
                     start=(vc == 0), stop=(vc == 7),
                     inc=(vc == 7 and half == 1))
        wmm = p.tag("sPE")
        c = seq * CH_SEQ + i
        xc = xres[:, c * D:(c + 1) * D]
        p.wait("vector", "sPE", wmm)
        p.vtt(xc, pE2[:], xc, ALU.add, inc=True)
        st["res"] = p.tag("sV")
    p.ns["pe_last"] = p.tag("sPE")

def phase_output(p, sb, y_out):
    xres, s32 = sb["xres"], sb["scr32"]
    yv = y_out.rearrange("(c p) d -> p c d", p=128)
    store = {}
    for c in range(NCH):
        if c >= 1:
            p.wait("scalar", "sS", store[c - 1])
        p.act(s32[:, :], xres[:, c * D:(c + 1) * D], AF.Copy, scale=1.0,
              inc=True)
        t = p.tag("sA")
        p.wait("sync", "sA", t)
        store[c] = p.dma("sync", yv[:, c, :], s32[:, :])
    p.wait("sync", "sS", p.tag("sS"))



def _bucket1d():
    NUM_BUCKETS = 32
    MAX_DIST = 128
    nb = NUM_BUCKETS // 2
    max_exact = nb // 2
    out = np.zeros(511, dtype=np.int64)
    for idx, dd in enumerate(range(-255, 256)):
        n = dd
        ret = nb if n < 0 else 0
        n = abs(n)
        if n < max_exact:
            val = n
        else:
            val = max_exact + int(
                math.log(max(n, 1) / max_exact)
                / math.log(MAX_DIST / max_exact) * (nb - max_exact))
            val = min(val, nb - 1)
        out[idx] = ret + val
    return out


def _oh_const():
    b = _bucket1d()
    oh = np.zeros((32, 511), dtype=np.float32)
    oh[b, np.arange(511)] = 1.0
    return oh * math.sqrt(128.0)


_NC_CACHE = None


def kernel(L0_attn_a_x, L0_attn_a_ln_g, L0_attn_a_ln_b, L0_attn_a_W_h,
           L0_attn_a_b_h, L0_attn_a_W_qk, L0_attn_a_b_qk, L0_attn_a_os_gamma,
           L0_attn_a_os_beta, L0_attn_a_rel_emb, L0_attn_a_W_out,
           L0_attn_a_b_out, L0_attn_b_W_dq, L0_attn_b_W_uq, L0_attn_b_W_dkv,
           L0_attn_b_W_uk, L0_attn_b_W_uv, L0_attn_b_W_o,
           L0_ffn_router_ln_scale, L0_ffn_router_ln_bias, L0_ffn_router_w,
           L0_ffn_router_b, L0_ffn_expert_ln_scale, L0_ffn_expert_ln_bias,
           L0_ffn_w1, L0_ffn_b1, L0_ffn_w2, L0_ffn_b2):
    global _NC_CACHE
    x = np.ascontiguousarray(np.asarray(L0_attn_a_x)).astype(np.float32)

    common = {
        "ln_g": L0_attn_a_ln_g, "ln_b": L0_attn_a_ln_b,
        "W_h": L0_attn_a_W_h, "b_h": L0_attn_a_b_h,
        "W_qk": L0_attn_a_W_qk, "b_qk": L0_attn_a_b_qk,
        "os_gamma": L0_attn_a_os_gamma, "os_beta": L0_attn_a_os_beta,
        "rel_emb": L0_attn_a_rel_emb, "W_out": L0_attn_a_W_out,
        "b_out": L0_attn_a_b_out,
        "W_dq": L0_attn_b_W_dq, "W_uq": L0_attn_b_W_uq,
        "W_dkv": L0_attn_b_W_dkv, "W_uk": L0_attn_b_W_uk,
        "W_uv": L0_attn_b_W_uv, "W_o": L0_attn_b_W_o,
        "r_ls": L0_ffn_router_ln_scale, "r_lb": L0_ffn_router_ln_bias,
        "r_w": L0_ffn_router_w, "r_b": L0_ffn_router_b,
        "e_ls": L0_ffn_expert_ln_scale, "e_lb": L0_ffn_expert_ln_bias,
        "w1": L0_ffn_w1, "b1": L0_ffn_b1, "w2": L0_ffn_w2, "b2": L0_ffn_b2,
        "OH": _oh_const(),
    }
    common = {k: np.ascontiguousarray(np.asarray(v)).astype(np.float32)
              for k, v in common.items()}

    in_maps = []
    for c in range(8):
        m = dict(common)
        m["x"] = np.ascontiguousarray(x[2 * c:2 * c + 2].reshape(T, D))
        in_maps.append(m)

    if _NC_CACHE is None:
        _NC_CACHE = build_nc()
    res = run_bass_kernel_spmd(_NC_CACHE, in_maps, core_ids=list(range(8)))
    out = np.concatenate(
        [r["y"].reshape(NSEQ, S, D) for r in res.results], axis=0)
    return out.astype(np.float32)


# version 06
# speedup vs baseline: 1.0388x; 1.0296x over previous
import math
import numpy as np

import concourse.bass as bass
import concourse.mybir as mybir
from concourse.bass_utils import run_bass_kernel_spmd

F32 = mybir.dt.float32
BF16 = mybir.dt.bfloat16
AF = mybir.ActivationFunctionType
ALU = mybir.AluOpType
AX = mybir.AxisListType

D = 1024
DC = 8
S = 2048
NSEQ = 2
T = 4096
NCH = 32
CH_SEQ = 16
HID = 2048
G = 256
NG = 8
E = 8
H = 4096
NQ = 4
QH = 1024
NHEAD = 16
DK = 64
EPS_LN = 1e-5
EPS_FFN = 1e-6
NEGBIG = -1.0e30

DMA_Q = {"sync": "sS", "gpsimd": "sG", "scalar": "sC"}

PHASES = {"attn_a": True, "ffn1": True, "attn_b": True, "ffn2": True}
DEBUG = False
DEBUG_POINT = "attn_a"
NSEQ_RUN = NSEQ

WM_QK = 0
WM_QKACT = 1024
WM_LINKV = 3072
WM_LKT = 5120
WM_QQ = 5376
WM_QKD = 5632
WM_LQ = 5888
WM_ATT = 6144
WM_WOUT = 6656
WM_DQ = 0
WM_DKV = 2048
WM_CKV = 4096
WM_UK = 0
WM_UV = 2048
WM_UQ = 0
WM_PSB = 4096
WM_PT = 6144
WM_QT = 8192
WM_AO = 9216

AF_XA = 0
AF_V = 2048
AF_GATE = 6144
AF_GATED = 10240

WB_W1Q = (0, 8192)
WB_W2Q = (16384, 24576)

C_LNG = 0
C_LNB = 8
C_GAM = 16
C_BET = 20
C_BQK = 24
C_RLS = 32
C_ELS = 64


class Prog:
    def __init__(self, nc, sems):
        self.nc = nc
        self.sems = sems
        self.cur = None
        self.eng = None
        self.sim = {}
        self.final = {}
        self.ns = {}

    def begin(self, name, eng):
        self.cur = name
        self.eng = eng
        self.sim = {k: 0 for k in self.sems}
        self.ns = {}

    def end(self):
        if self.final:
            assert self.final == self.sim, (
                f"sim divergence on {self.cur}: {self.final} vs {self.sim}")
        else:
            self.final = dict(self.sim)

    def _emit(self, engine, make, inc):
        if inc is not None:
            self.sim[inc[0]] += inc[1]
        if engine == self.cur:
            instr = make(self.eng)
            if inc is not None:
                instr.then_inc(self.sems[inc[0]], inc[1])
        return self.sim[inc[0]] if inc is not None else None

    def wait(self, engine, sem, v):
        if v is None:
            return
        if engine == self.cur:
            self.eng.wait_ge(self.sems[sem], v)

    def tag(self, sem):
        return self.sim[sem]

    def dma(self, q, out, in_, **kw):
        return self._emit(q, lambda e: e.dma_start(out=out, in_=in_, **kw),
                          (DMA_Q[q], 16))

    def mm(self, out, lhsT, rhs, start, stop, inc=False, **kw):
        return self._emit(
            "tensor",
            lambda e: e.matmul(out, lhsT, rhs, start=start, stop=stop, **kw),
            ("sPE", 1) if inc else None)

    def tp(self, out, in_, ident, inc=False):
        return self._emit("tensor", lambda e: e.transpose(out, in_, ident),
                          ("sPE", 1) if inc else None)

    def vtt(self, out, in0, in1, op, inc=False):
        return self._emit("vector", lambda e: e.tensor_tensor(out, in0, in1, op),
                          ("sV", 1) if inc else None)

    def vts(self, out, in0, s1, s2, op0, op1, inc=False):
        return self._emit(
            "vector", lambda e: e.tensor_scalar(out, in0, s1, s2, op0, op1),
            ("sV", 1) if inc else None)

    def vts1(self, out, in0, s1, op0, inc=False):
        return self._emit(
            "vector", lambda e: e.tensor_scalar(out, in0, s1, None, op0),
            ("sV", 1) if inc else None)

    def vstt(self, out, in0, scalar, in1, op0, op1, inc=False):
        return self._emit(
            "vector",
            lambda e: e.scalar_tensor_tensor(out, in0, scalar, in1, op0, op1),
            ("sV", 1) if inc else None)

    def vred(self, out, in_, op, inc=False, **kw):
        return self._emit(
            "vector", lambda e: e.tensor_reduce(out, in_, AX.X, op, **kw),
            ("sV", 1) if inc else None)

    def vcopy(self, out, in_, inc=False):
        return self._emit("vector", lambda e: e.tensor_copy(out, in_),
                          ("sV", 1) if inc else None)

    def vrecip(self, out, in_, inc=False):
        return self._emit("vector", lambda e: e.reciprocal(out, in_),
                          ("sV", 1) if inc else None)

    def vmemset(self, ap, c, inc=False):
        return self._emit("vector", lambda e: e.memset(ap, c),
                          ("sV", 1) if inc else None)

    def act(self, out, in_, func, bias=None, scale=None, inc=False):
        kw = {}
        if bias is not None:
            kw["bias"] = bias
        if scale is not None:
            kw["scale"] = scale
        return self._emit("scalar",
                          lambda e: e.activation(out, in_, func, **kw),
                          ("sA", 1) if inc else None)

    def gmemset(self, ap, c, inc=False):
        return self._emit("gpsimd", lambda e: e.memset(ap, c),
                          ("sP", 1) if inc else None)

    def gaffine(self, out, in_, pattern, compare_op, fill, base, chmul,
                inc=False):
        return self._emit(
            "gpsimd",
            lambda e: e.affine_select(out=out, in_=in_, pattern=pattern,
                                      compare_op=compare_op, fill=fill,
                                      base=base, channel_multiplier=chmul),
            ("sP", 1) if inc else None)

    def barrier(self, sb):
        stats = sb["stats"]
        pe = self.ns.get("pe_last")
        self.vmemset(stats[:, 252:253], 0.0, inc=True)
        self.act(stats[:, 253:254], stats[:, 253:254], AF.Copy, scale=1.0,
                 inc=True)
        self.gmemset(stats[:, 254:255], 0.0, inc=True)
        targets = [("sV", self.tag("sV")), ("sA", self.tag("sA")),
                   ("sP", self.tag("sP")), ("sS", self.tag("sS")),
                   ("sG", self.tag("sG")), ("sC", self.tag("sC"))]
        if pe is not None:
            targets.append(("sPE", pe))
        for engname in ("sync", "tensor", "vector", "scalar", "gpsimd"):
            for sem, v in targets:
                self.wait(engname, sem, v)



def build_nc():
    nc = bass.Bass("TRN2", target_bir_lowering=False)

    dram = {}

    def din(name, shape, dtype=F32):
        dram[name] = nc.dram_tensor(name, list(shape), dtype,
                                    kind="ExternalInput").ap()

    din("x", (T, D))
    din("ln_g", (D,))
    din("ln_b", (D,))
    din("W_h", (D, 2 * HID))
    din("b_h", (2 * HID,))
    din("W_qk", (D, 128))
    din("b_qk", (128,))
    din("os_gamma", (4, 128))
    din("os_beta", (4, 128))
    din("rel_emb", (32, 1))
    din("W_out", (HID, D))
    din("b_out", (D,))
    din("W_dq", (D, 256))
    din("W_uq", (256, D))
    din("W_dkv", (D, 256))
    din("W_uk", (256, D))
    din("W_uv", (256, D))
    din("W_o", (D, D))
    din("r_ls", (D,))
    din("r_lb", (D,))
    din("r_w", (E, D))
    din("r_b", (E,))
    din("e_ls", (E, D))
    din("e_lb", (E, D))
    din("w1", (E, H, D))
    din("b1", (E, H))
    din("w2", (E, D, H))
    din("b2", (E, D))
    din("OH", (32, 511))

    y_out = nc.dram_tensor("y", [T, D], F32, kind="ExternalOutput").ap()
    dbg = {}
    if DEBUG:
        for nm, shape, dt in (("d_stats", [128, 256], F32),
                              ("d_wmed", [128, 10240], BF16),
                              ("d_actfm", [128, 16384], BF16),
                              ("d_wbig", [128, 4096], BF16),
                              ("d_rows", [1, 3200], BF16),
                              ("d_cols", [128, 192], F32),
                              ("d_cols2", [128, 64], F32),
                              ("d_b1col", [128, 256], F32),
                              ("d_biasT", [128, 512], BF16),
                              ("d_gates", [128, 128], F32),
                              ("d_hbuf", [128, 4096], BF16),
                              ("d_rows32", [1, 640], F32),
                              ("d_bvec", [1, 511], F32),
                              ("d_rwT", [128, 64], BF16),
                              ("d_w1T", [128, 4096], BF16)):
            dbg[nm] = nc.dram_tensor(nm, shape, dt, kind="ExternalOutput").ap()

    scr = {}

    def dscr(name, shape, dtype=BF16):
        scr[name] = nc.dram_tensor(name, list(shape), dtype,
                                   kind="Internal").ap()

    dscr("w1c", (E, H, D))
    dscr("w2c", (E, D, H))
    dscr("w1T", (E, D, H))
    dscr("Whc", (D, 2 * HID))
    dscr("Wqkc", (D, 128))
    dscr("Woutc", (HID, D))
    dscr("Wdqc", (D, 256))
    dscr("Wdkvc", (D, 256))
    dscr("Wuqc", (256, D))
    dscr("Wukc", (256, D))
    dscr("Wuvc", (256, D))
    dscr("Woc", (D, D))
    dscr("rowscr", (1, 4096), F32)
    dscr("bvec", (1, 511), F32)

    import contextlib
    with contextlib.ExitStack() as est:
        EE = est.enter_context
        sb = {}

        def sbuf(name, shape, dtype):
            sb[name] = EE(nc.sbuf_tensor(name, list(shape), dtype))

        sbuf("xres", (128, NCH * D), BF16)
        sbuf("wbig", (128, 32768), BF16)
        sbuf("actfm", (128, 16384), BF16)
        sbuf("wmed", (128, 10240), BF16)
        sbuf("hbuf", (128, 4096), BF16)
        sbuf("scr32", (128, 1024), F32)
        sbuf("stats", (128, 256), F32)
        sbuf("gates", (128, 128), F32)
        sbuf("biasT", (128, 512), BF16)
        sbuf("cmask", (128, 128), F32)
        sbuf("ident", (128, 128), BF16)
        sbuf("b1col", (128, 256), F32)
        sbuf("cols", (128, 192), F32)
        sbuf("cols2", (128, 64), F32)
        sbuf("colsb", (128, 96), BF16)
        sbuf("relsb", (32, 1), BF16)
        sbuf("ohsb", (32, 512), BF16)
        sbuf("rows", (1, 3200), BF16)
        sbuf("rows32", (1, 640), F32)
        sbuf("onesb", (1, 128), BF16)
        sbuf("rwT", (128, 64), BF16)

        ps = {}
        for nm, shape, dt in (("pA", [128, 512], F32), ("pB", [128, 512], F32),
                              ("pC", [128, 512], F32),
                              ("pE2", [128, 1024], F32),
                              ("pF2", [128, 1024], F32),
                              ("pTP", [128, 1024], BF16)):
            ps[nm] = EE(nc.psum_tensor(nm, shape, dt))

        sems = {}
        for sname in ("sS", "sG", "sC", "sPE", "sV", "sA", "sP"):
            sems[sname] = EE(nc.semaphore(sname))

        block = EE(nc.Block())
        prog = Prog(nc, sems)

        def emit_all(engine, eng):
            prog.begin(engine, eng)
            program(prog, dram, scr, sb, ps, y_out, dbg if DEBUG else None)
            prog.end()

        @block.sync
        def _(e):
            emit_all("sync", e)

        @block.tensor
        def _(e):
            emit_all("tensor", e)

        @block.vector
        def _(e):
            emit_all("vector", e)

        @block.scalar
        def _(e):
            emit_all("scalar", e)

        @block.gpsimd
        def _(e):
            emit_all("gpsimd", e)

    return nc


def program(p, dram, scr, sb, ps, y_out, dbg=None):
    phase_load_and_prep(p, dram, scr, sb, ps)
    p.barrier(sb)
    if PHASES["attn_a"]:
        for seq in range(NSEQ_RUN):
            attn_a_seq(p, dram, scr, sb, ps, seq)
            p.barrier(sb)
    if dbg is not None and DEBUG_POINT == "attn_a":
        dump_debug(p, sb, dbg)
    prep_ffn_weights(p, dram, scr, sb, ps)
    p.barrier(sb)
    if PHASES["ffn1"]:
        for seq in range(NSEQ_RUN):
            ffn_seq(p, dram, scr, sb, ps, seq)
            p.barrier(sb)
    if dbg is not None and DEBUG_POINT == "ffn1":
        dump_debug(p, sb, dbg)
    if PHASES["attn_b"]:
        for seq in range(NSEQ_RUN):
            attn_b_seq(p, dram, scr, sb, ps, seq)
            p.barrier(sb)
    if PHASES["ffn2"]:
        for seq in range(NSEQ_RUN):
            ffn_seq(p, dram, scr, sb, ps, seq)
            p.barrier(sb)

    phase_output(p, sb, y_out)


def dump_debug(p, sb, dbg):
    pairs = [("d_stats", sb["stats"][:]), ("d_wmed", sb["wmed"][:]),
             ("d_actfm", sb["actfm"][:]), ("d_wbig", sb["wbig"][:, 0:4096]),
             ("d_rows", sb["rows"][:]), ("d_cols", sb["cols"][:]),
             ("d_cols2", sb["cols2"][:]), ("d_b1col", sb["b1col"][:]),
             ("d_biasT", sb["biasT"][:]), ("d_gates", sb["gates"][:]),
             ("d_hbuf", sb["hbuf"][:]), ("d_rows32", sb["rows32"][:]),
             ("d_rwT", sb["rwT"][:])]
    for nm, ap in pairs:
        p.dma("sync", dbg[nm], ap)
    p.dma("sync", dbg["d_bvec"], p.ns["bvec_scr"])
    p.dma("sync", dbg["d_w1T"], p.ns["w1T_scr"][0][0:128, :])
    p.wait("sync", "sS", p.tag("sS"))
    p.barrier(sb)



def ln_stats_seq(p, sb, seq, double, eps1, eps2):
    xres, stats, scr32, hbuf = sb["xres"], sb["stats"], sb["scr32"], sb["hbuf"]
    n = float(D)
    ssq_rd = {0: None, 1: None}
    for ci in range(CH_SEQ):
        sl = ci % 2
        xc = xres[:, (seq * CH_SEQ + ci) * D:(seq * CH_SEQ + ci + 1) * D]
        sq = hbuf[:, sl * D:(sl + 1) * D]
        p.wait("scalar", "sV", ssq_rd[sl])
        p.act(sq, xc, AF.Square, inc=True)
        sq_tag = p.tag("sA")
        p.vred(stats[:, 64 + ci:65 + ci], xc, ALU.add)
        p.wait("vector", "sA", sq_tag)
        p.vred(stats[:, 80 + ci:81 + ci], sq, ALU.add, inc=True)
        ssq_rd[sl] = p.tag("sV")
    p.wait("vector", "sV", p.tag("sV"))
    p.vts1(stats[:, 0:16], stats[:, 64:80], 1.0 / n, ALU.mult, inc=True)
    p.wait("vector", "sV", p.tag("sV"))
    p.vtt(stats[:, 128:144], stats[:, 0:16], stats[:, 0:16], ALU.mult,
          inc=True)
    p.wait("vector", "sV", p.tag("sV"))
    p.vstt(stats[:, 144:160], stats[:, 80:96], 1.0 / n, stats[:, 128:144],
           ALU.mult, ALU.subtract, inc=True)
    p.wait("scalar", "sV", p.tag("sV"))
    p.act(stats[:, 160:176], stats[:, 144:160], AF.Sqrt,
          bias=stats[:, 248:249] if eps1 == EPS_LN else stats[:, 249:250],
          scale=1.0, inc=True)
    p.wait("vector", "sA", p.tag("sA"))
    p.vrecip(stats[:, 16:32], stats[:, 160:176], inc=True)
    p.wait("vector", "sV", p.tag("sV"))
    if not double:
        return
    p.vtt(stats[:, 128:144], stats[:, 16:32], stats[:, 16:32], ALU.mult,
          inc=True)
    p.wait("vector", "sV", p.tag("sV"))
    p.vtt(stats[:, 144:160], stats[:, 144:160], stats[:, 128:144], ALU.mult,
          inc=True)
    p.wait("scalar", "sV", p.tag("sV"))
    p.wait("vector", "sV", p.tag("sV"))
    p.act(stats[:, 160:176], stats[:, 144:160], AF.Sqrt,
          bias=stats[:, 248:249] if eps2 == EPS_LN else stats[:, 249:250],
          scale=1.0, inc=True)
    p.wait("vector", "sA", p.tag("sA"))
    p.vrecip(stats[:, 32:48], stats[:, 160:176], inc=True)
    p.wait("vector", "sV", p.tag("sV"))
    p.vtt(stats[:, 48:64], stats[:, 16:32], stats[:, 32:48], ALU.mult,
          inc=True)
    p.wait("vector", "sV", p.tag("sV"))


def build_fm_chunk(p, sb, ps, seq, ci, dst_view, double, slot):
    xres, stats, scr32, hbuf = sb["xres"], sb["stats"], sb["scr32"], sb["hbuf"]
    ident, pTP = sb["ident"], ps["pTP"]
    st = p.ns.setdefault("fm", {"evac": None, "slot": {}})
    xc = xres[:, (seq * CH_SEQ + ci) * D:(seq * CH_SEQ + ci + 1) * D]
    stage = hbuf[:, slot * D:(slot + 1) * D]
    p.wait("vector", "sPE", st["slot"].get(slot))
    rcol = stats[:, 48 + ci:49 + ci] if double else stats[:, 16 + ci:17 + ci]
    p.vts(stage, xc, stats[:, ci:ci + 1], rcol, ALU.subtract, ALU.mult,
          inc=True)
    aff = p.tag("sV")
    p.wait("tensor", "sV", aff)
    p.wait("tensor", "sV", st["evac"])
    for dc in range(DC):
        p.tp(pTP[:, dc * 128:(dc + 1) * 128], stage[:, dc * 128:(dc + 1) * 128],
             ident[:], inc=(dc == DC - 1))
    tptag = p.tag("sPE")
    st["slot"][slot] = tptag
    p.wait("vector", "sPE", tptag)
    ptv = pTP[:].rearrange("q (dc t) -> q dc t", t=128)
    p.vcopy(dst_view, ptv, inc=True)
    st["evac"] = p.tag("sV")
    return p.tag("sV")



def phase_load_and_prep(p, dram, scr, sb, ps):
    ns = p.ns
    xres, wbig, wmed = sb["xres"], sb["wbig"], sb["wmed"]
    cols, cols2, colsb = sb["cols"], sb["cols2"], sb["colsb"]

    xv = dram["x"].rearrange("(c p) d -> p c d", p=128)
    for i in range(4):
        p.dma("gpsimd", xres[:, i * 8 * D:(i + 1) * 8 * D],
              xv[:, i * 8:(i + 1) * 8, :])

    p.gmemset(sb["ident"][:], 0.0)
    p.gaffine(sb["ident"][:], sb["ident"][:], [[-1, 128]], ALU.not_equal, 1.0,
              0, 1)
    p.gmemset(sb["cmask"][:], 0.0)
    p.gaffine(sb["cmask"][:], sb["cmask"][:], [[-1, 128]], ALU.is_ge, NEGBIG,
              0, 1, inc=True)
    p.vmemset(sb["onesb"][:], 1.0)
    p.vmemset(sb["colsb"][:, 80:81], 1.0)
    p.vmemset(sb["stats"][:, 248:249], EPS_LN)
    p.vmemset(sb["stats"][:, 249:250], EPS_FFN, inc=True)

    for nm_in, nm_out in (("W_h", "Whc"), ("W_qk", "Wqkc"), ("W_out", "Woutc"),
                          ("W_dq", "Wdqc"), ("W_dkv", "Wdkvc"),
                          ("W_uq", "Wuqc"), ("W_uk", "Wukc"),
                          ("W_uv", "Wuvc"), ("W_o", "Woc")):
        p.dma("gpsimd", scr[nm_out], dram[nm_in])
    ns["small_cast"] = p.tag("sG")
    for e in range(E):
        p.dma("gpsimd", scr["w1c"][e], dram["w1"][e])
        p.dma("gpsimd", scr["w2c"][e], dram["w2"][e])
    ns["big_cast"] = p.tag("sG")

    for dc in range(DC):
        p.dma("sync", cols[:, C_LNG + dc:C_LNG + dc + 1],
              dram["ln_g"][dc * 128:(dc + 1) * 128])
        p.dma("sync", cols[:, C_LNB + dc:C_LNB + dc + 1],
              dram["ln_b"][dc * 128:(dc + 1) * 128])
        p.dma("sync", cols[:, C_RLS + dc:C_RLS + dc + 1],
              dram["r_ls"][dc * 128:(dc + 1) * 128])
        p.dma("gpsimd", colsb[:, dc:dc + 1],
              dram["ln_b"][dc * 128:(dc + 1) * 128])
        p.dma("gpsimd", colsb[:, 8 + dc:9 + dc],
              dram["r_lb"][dc * 128:(dc + 1) * 128])
    for h in range(4):
        p.dma("sync", cols[:, C_GAM + h:C_GAM + h + 1], dram["os_gamma"][h, :])
        p.dma("sync", cols[:, C_BET + h:C_BET + h + 1], dram["os_beta"][h, :])
    for e in range(E):
        for dc in range(DC):
            p.dma("sync", cols[:, C_ELS + e * 8 + dc:C_ELS + e * 8 + dc + 1],
                  dram["e_ls"][e, dc * 128:(dc + 1) * 128])
            p.dma("gpsimd", colsb[:, 16 + e * 8 + dc:17 + e * 8 + dc],
                  dram["e_lb"][e, dc * 128:(dc + 1) * 128])
    for e in range(E):
        for hc in range(32):
            p.dma("sync", sb["b1col"][:, e * 32 + hc:e * 32 + hc + 1],
                  dram["b1"][e, hc * 128:(hc + 1) * 128])
    p.dma("gpsimd", sb["relsb"][:], dram["rel_emb"])
    p.dma("gpsimd", sb["ohsb"][:, 0:511], dram["OH"])
    p.dma("gpsimd", sb["rows"][0:1, 2048:3072], dram["b_out"][None, :])
    ns["cols_f32"] = p.tag("sS")
    ns["cols_bf"] = p.tag("sG")

    p.wait("vector", "sS", ns["cols_f32"])
    p.vts1(cols[:, C_GAM + 1:C_GAM + 2], cols[:, C_GAM + 1:C_GAM + 2],
           1.0 / S, ALU.mult)
    p.vts1(cols[:, C_BET + 1:C_BET + 2], cols[:, C_BET + 1:C_BET + 2],
           1.0 / S, ALU.mult, inc=True)

    p.wait("tensor", "sG", ns["cols_bf"])
    p.mm(ps["pA"][0:1, 0:511], sb["relsb"][:], sb["ohsb"][:, 0:511],
         start=True, stop=True, inc=True)
    t = p.tag("sPE")
    p.wait("vector", "sPE", t)
    p.vcopy(sb["rows32"][0:1, 0:511], ps["pA"][0:1, 0:511], inc=True)
    bvec_evac = p.tag("sV")
    t = p.tag("sV")
    p.wait("sync", "sV", t)
    p.dma("sync", scr["bvec"], sb["rows32"][0:1, 0:511])
    p.ns["bvec_scr"] = scr["bvec"]
    t = p.tag("sS")
    p.wait("gpsimd", "sS", t)
    for jc in range(2):
        for row in range(128):
            j = jc * 128 + row
            p.dma("gpsimd", sb["biasT"][row:row + 1, jc * 256:jc * 256 + 256],
                  scr["bvec"][0:1, 255 - j:255 - j + 256])

    p.wait("sync", "sG", ns["small_cast"])
    whv = scr["Whc"].rearrange("(dcb q) h -> q dcb h", q=128)
    p.dma("sync", wbig[:].rearrange("q (dcb h) -> q dcb h", h=4096), whv)
    wqv = scr["Wqkc"].rearrange("(dcb q) h -> q dcb h", q=128)
    p.dma("sync", wmed[:, 0:1024].rearrange("q (dcb h) -> q dcb h", h=128),
          wqv)
    ns["wh_raw"] = p.tag("sS")

    p.wait("tensor", "sS", ns["wh_raw"])
    p.wait("tensor", "sG", ns["cols_bf"])
    prev_use = {0: bvec_evac, 1: None}
    for nt in range(8):
        pt = ps["pA"] if nt % 2 == 0 else ps["pB"]
        sl = pt[0:1, 0:512]
        p.wait("tensor", "sV", prev_use[nt % 2])
        for dc in range(DC):
            p.mm(sl, colsb[:, dc:dc + 1],
                 wbig[:, dc * 4096 + nt * 512:dc * 4096 + nt * 512 + 512],
                 start=(dc == 0), stop=(dc == DC - 1), inc=(dc == DC - 1))
        mmtag = p.tag("sPE")
        p.wait("sync", "sV", p.ns.get("bh_scr_free"))
        p.dma("sync", sb["scr32"][0:1, 0:512],
              dram["b_h"][None, nt * 512:(nt + 1) * 512])
        ld = p.tag("sS")
        p.wait("vector", "sPE", mmtag)
        p.wait("vector", "sS", ld)
        p.vtt(sb["scr32"][0:1, 0:512], sl, sb["scr32"][0:1, 0:512], ALU.add,
              inc=True)
        prev_use[nt % 2] = p.tag("sV")
        p.ns["bh_scr_free"] = p.tag("sV")
        p.wait("vector", "sV", p.tag("sV"))
        if nt < 4:
            p.vcopy(sb["rows"][0:1, nt * 512:(nt + 1) * 512],
                    sb["scr32"][0:1, 0:512], inc=True)
            prev_use[nt % 2] = p.tag("sV")
            p.ns["bh_scr_free"] = p.tag("sV")
        else:
            t = p.tag("sV")
            p.wait("sync", "sV", t)
            p.dma("sync", scr["rowscr"][0:1, nt * 512:(nt + 1) * 512],
                  sb["scr32"][0:1, 0:512])
    t = p.tag("sS")
    p.wait("sync", "sS", t)
    for gc in range(16):
        a = 2048 + gc * 128
        p.dma("sync", cols2[:, gc:gc + 1], scr["rowscr"][0:1, a:a + 128])

    for dc in range(DC):
        p.mm(ps["pC"][:, 0:1], wmed[:, dc * 128:(dc + 1) * 128],
             colsb[:, dc:dc + 1], start=(dc == 0), stop=(dc == DC - 1),
             inc=(dc == DC - 1))
    t = p.tag("sPE")
    p.dma("sync", cols[:, C_BQK + 1:C_BQK + 2], dram["b_qk"][:])
    ld = p.tag("sS")
    p.wait("vector", "sPE", t)
    p.wait("vector", "sS", ld)
    p.vtt(cols[:, C_BQK:C_BQK + 1], ps["pC"][:, 0:1],
          cols[:, C_BQK + 1:C_BQK + 2], ALU.add, inc=True)
    bqk_read = p.tag("sV")

    p.wait("vector", "sS", ns["cols_f32"])
    p.wait("vector", "sPE", t)
    for dc in range(DC):
        p.vts1(wbig[:, dc * 4096:(dc + 1) * 4096],
               wbig[:, dc * 4096:(dc + 1) * 4096],
               cols[:, C_LNG + dc:C_LNG + dc + 1], ALU.mult)
        p.vts1(wmed[:, dc * 128:(dc + 1) * 128],
               wmed[:, dc * 128:(dc + 1) * 128],
               cols[:, C_LNG + dc:C_LNG + dc + 1], ALU.mult,
               inc=(dc == DC - 1))

    for dc in range(DC):
        for e in range(E):
            p.dma("gpsimd", sb["rwT"][:, dc * 8 + e:dc * 8 + e + 1],
                  dram["r_w"][e, dc * 128:(dc + 1) * 128])
    t = p.tag("sG")
    p.wait("tensor", "sG", t)
    p.wait("tensor", "sV", bqk_read)
    for dc in range(DC):
        p.mm(ps["pC"][0:8, 8:9], sb["rwT"][:, dc * 8:(dc + 1) * 8],
             colsb[:, 8 + dc:9 + dc], start=(dc == 0), stop=(dc == DC - 1),
             inc=(dc == DC - 1))
    t = p.tag("sPE")
    p.dma("sync", sb["rows32"][0:1, 520:528], dram["r_b"][None, :])
    ld = p.tag("sS")
    p.wait("vector", "sPE", t)
    p.wait("vector", "sS", ld)
    p.vcopy(sb["scr32"][0:8, 512:513], ps["pC"][0:8, 8:9], inc=True)
    t = p.tag("sV")
    p.wait("sync", "sV", t)
    p.dma("sync", scr["rowscr"][0:1, 3584:3592], sb["scr32"][0:8, 512:513])
    t = p.tag("sS")
    p.wait("sync", "sS", t)
    p.dma("sync", sb["rows32"][0:1, 512:520], scr["rowscr"][0:1, 3584:3592])
    t = p.tag("sS")
    p.wait("vector", "sS", t)
    p.vtt(sb["rows32"][0:1, 512:520], sb["rows32"][0:1, 512:520],
          sb["rows32"][0:1, 520:528], ALU.add, inc=True)
    p.wait("vector", "sV", p.tag("sV"))
    p.vcopy(sb["rows"][0:1, 3072:3080], sb["rows32"][0:1, 512:520], inc=True)
    for dc in range(DC):
        p.vts1(sb["rwT"][:, dc * 8:(dc + 1) * 8],
               sb["rwT"][:, dc * 8:(dc + 1) * 8],
               cols[:, C_RLS + dc:C_RLS + dc + 1], ALU.mult,
               inc=(dc == DC - 1))
    p.ns["pe_last"] = p.tag("sPE")

def prep_ffn_weights(p, dram, scr, sb, ps):
    p.ns["w1T_scr"] = scr["w1T"]
    """w1T scratch: transpose-DMA strips of w1c, fold e_ls rows, store.
    b1col += lb_e @ W1^T via [128,1]-column matmuls into pC[:,128:160].
    PSUM note: single start=True on the very first MM of the pC session."""
    ns = p.ns
    hbuf, b1col, cols, colsb = sb["hbuf"], sb["b1col"], sb["cols"], sb["colsb"]
    p.wait("sync", "sG", ns["big_cast"])
    store_tag = None
    scale_tag = None
    b1_use = None
    for e in range(E):
        mm_tag = None
        for dc in range(DC):
            p.wait("sync", "sS", store_tag)
            p.wait("sync", "sV", scale_tag)
            p.dma("sync", hbuf[:], scr["w1c"][e][:, dc * 128:(dc + 1) * 128],
                  transpose=True)
            ld = p.tag("sS")
            p.wait("tensor", "sS", ld)
            if dc == 0:
                p.wait("tensor", "sV", b1_use)
            for hc in range(32):
                p.mm(ps["pC"][:, 128 + hc:129 + hc],
                     hbuf[:, hc * 128:(hc + 1) * 128],
                     colsb[:, 16 + e * 8 + dc:17 + e * 8 + dc],
                     start=(dc == 0 and hc == 0),
                     stop=(dc == DC - 1 and hc == 31),
                     inc=(hc == 31))
            mm_tag = p.tag("sPE")
            p.wait("vector", "sPE", mm_tag)
            p.wait("vector", "sS", ld)
            p.vts1(hbuf[:], hbuf[:],
                   cols[:, C_ELS + e * 8 + dc:C_ELS + e * 8 + dc + 1],
                   ALU.mult, inc=True)
            scale_tag = p.tag("sV")
            p.wait("sync", "sV", scale_tag)
            p.dma("sync", scr["w1T"][e][dc * 128:(dc + 1) * 128, :], hbuf[:])
            store_tag = p.tag("sS")
        p.wait("vector", "sPE", mm_tag)
        p.vtt(b1col[:, e * 32:(e + 1) * 32], b1col[:, e * 32:(e + 1) * 32],
              ps["pC"][:, 128:160], ALU.add, inc=True)
        b1_use = p.tag("sV")
    p.ns["pe_last"] = p.tag("sPE")



def attn_a_seq(p, dram, scr, sb, ps, seq):
    ns = p.ns
    xres, wbig, wmed, actfm = sb["xres"], sb["wbig"], sb["wmed"], sb["actfm"]
    cols, cols2, stats, scr32 = (sb["cols"], sb["cols2"], sb["stats"],
                                 sb["scr32"])
    onesb, rows = sb["onesb"], sb["rows"]
    pA, pB, pC, pE2, pF2 = ps["pA"], ps["pB"], ps["pC"], ps["pE2"], ps["pF2"]

    ln_stats_seq(p, sb, seq, True, EPS_LN, EPS_LN)

    xa_view = actfm[:, 0:2048].rearrange("q (dc t) -> q dc t", t=256)

    def build_xa(g):
        for tc in range(2):
            t = build_fm_chunk(p, sb, ps, seq, g * 2 + tc,
                               xa_view[:, :, tc * 128:(tc + 1) * 128], True, tc)
        return t

    def v_tm_group(xa_tag, bankfree):
        for tc in range(2):
            for nt in range(4):
                bk = (tc * 4 + nt) % 2
                pt = pA if bk == 0 else pB
                p.wait("tensor", "sV", xa_tag)
                p.wait("tensor", "sA", bankfree[bk])
                for dc in range(DC):
                    p.mm(pt[:], xa_view[:, dc, tc * 128:(tc + 1) * 128],
                         wbig[:, dc * 4096 + nt * 512:
                              dc * 4096 + nt * 512 + 512],
                         start=(dc == 0), stop=False)
                p.mm(pt[:], onesb[:], rows[0:1, nt * 512:(nt + 1) * 512],
                     start=False, stop=True, inc=True)
                mmt = p.tag("sPE")
                p.wait("scalar", "sPE", mmt)
                p.act(actfm[:, AF_V + tc * 2048 + nt * 512:
                            AF_V + tc * 2048 + nt * 512 + 512],
                      pt[:], AF.Silu, inc=True)
                bankfree[bk] = p.tag("sA")
        return p.tag("sA")

    st = {"vbank": {0: None, 1: None}, "qkfree": None}
    for g in range(NG):
        xa_tag = build_xa(g)
        p.wait("tensor", "sV", xa_tag)
        p.wait("tensor", "sA", st["qkfree"])
        for dc in range(DC):
            p.mm(pC[:, 0:256], wmed[:, dc * 128:(dc + 1) * 128],
                 xa_view[:, dc, :], start=(dc == 0), stop=(dc == DC - 1),
                 inc=(dc == DC - 1))
        qmm = p.tag("sPE")
        p.wait("scalar", "sPE", qmm)
        p.act(wmed[:, WM_QKACT + g * 256:WM_QKACT + (g + 1) * 256],
              pC[:, 0:256], AF.Silu, bias=cols[:, C_BQK:C_BQK + 1], scale=1.0,
              inc=True)
        st["qkfree"] = p.tag("sA")
        qk_tag = st["qkfree"]
        vev = v_tm_group(xa_tag, st["vbank"])
        p.wait("vector", "sA", qk_tag)
        p.wait("vector", "sPE", ns.get("lkt_pe"))
        p.vts(sb["hbuf"][:, 2048:2304],
              wmed[:, WM_QKACT + g * 256:WM_QKACT + (g + 1) * 256],
              cols[:, C_GAM + 3:C_GAM + 4], cols[:, C_BET + 3:C_BET + 4],
              ALU.mult, ALU.add, inc=True)
        lk = p.tag("sV")
        fmst = p.ns["fm"]
        p.wait("tensor", "sV", lk)
        p.wait("tensor", "sV", fmst["evac"])
        for tc in range(2):
            p.tp(ps["pTP"][:, tc * 128:(tc + 1) * 128],
                 sb["hbuf"][:, 2048 + tc * 128:2048 + (tc + 1) * 128],
                 sb["ident"][:], inc=(tc == 1))
        ns["lkt_pe"] = p.tag("sPE")
        p.wait("vector", "sPE", ns["lkt_pe"])
        p.vcopy(wmed[:, WM_LKT:WM_LKT + 256], ps["pTP"][:, 0:256], inc=True)
        fmst["evac"] = p.tag("sV")
        lkt_tag = p.tag("sV")
        p.wait("tensor", "sV", lkt_tag)
        p.wait("tensor", "sA", vev)
        for tc in range(2):
            for nt in range(4):
                pt = pE2 if nt < 2 else pF2
                sl = pt[:, (nt % 2) * 512:(nt % 2) * 512 + 512]
                p.mm(sl, wmed[:, WM_LKT + tc * 128:WM_LKT + (tc + 1) * 128],
                     actfm[:, AF_V + tc * 2048 + nt * 512:
                           AF_V + tc * 2048 + nt * 512 + 512],
                     start=(g == 0 and tc == 0),
                     stop=(g == NG - 1 and tc == 1),
                     inc=(g == NG - 1 and tc == 1 and nt == 3))
    linkv_mm = p.tag("sPE")
    p.wait("vector", "sPE", linkv_mm)
    p.vcopy(wmed[:, WM_LINKV:WM_LINKV + 1024], pE2[:], inc=True)
    p.vcopy(wmed[:, WM_LINKV + 1024:WM_LINKV + 2048], pF2[:], inc=True)
    linkv_tag = p.tag("sV")

    st2 = {"vbank": {0: None, 1: None}, "gslot": {0: None, 1: None},
           "qlslot": {0: None, 1: None, 2: None},
           "wout": {0: None, 1: None},
           "resfree": {0: linkv_tag, 1: linkv_tag}}
    woutv = scr["Woutc"].rearrange("(ecb q) d2 -> q ecb d2", q=128)
    for g in range(NG):
        xa_tag = build_xa(g)
        vev = v_tm_group(xa_tag, st2["vbank"])
        for gc in range(16):
            bk = gc % 2
            pt = pC if bk == 0 else pA
            sl = pt[:, 0:256]
            p.wait("tensor", "sA", vev)
            p.wait("tensor", "sA", st2["gslot"][bk])
            for dc in range(DC):
                p.mm(sl, wbig[:, dc * 4096 + 2048 + gc * 128:
                              dc * 4096 + 2048 + (gc + 1) * 128],
                     xa_view[:, dc, :], start=(dc == 0), stop=(dc == DC - 1),
                     inc=(dc == DC - 1))
            gmm = p.tag("sPE")
            p.wait("scalar", "sPE", gmm)
            p.act(actfm[:, AF_GATE + gc * 256:AF_GATE + (gc + 1) * 256], sl,
                  AF.Silu, bias=cols2[:, gc:gc + 1], scale=1.0, inc=True)
            st2["gslot"][bk] = p.tag("sA")
        gate_done = p.tag("sA")
        qk_g = wmed[:, WM_QKACT + g * 256:WM_QKACT + (g + 1) * 256]
        p.vts(wmed[:, WM_QQ:WM_QQ + 256], qk_g, cols[:, C_GAM:C_GAM + 1],
              cols[:, C_BET:C_BET + 1], ALU.mult, ALU.add)
        p.vts(wmed[:, WM_QKD:WM_QKD + 256], qk_g,
              cols[:, C_GAM + 2:C_GAM + 3], cols[:, C_BET + 2:C_BET + 3],
              ALU.mult, ALU.add)
        p.vts(wmed[:, WM_LQ:WM_LQ + 256], qk_g, cols[:, C_GAM + 1:C_GAM + 2],
              cols[:, C_BET + 1:C_BET + 2], ALU.mult, ALU.add, inc=True)
        heads_tag = p.tag("sV")
        p.wait("tensor", "sV", heads_tag)
        p.wait("tensor", "sA", gate_done)
        for jc in range(2):
            p.mm(pC[:, jc * 256:(jc + 1) * 256],
                 wmed[:, WM_QKD + jc * 128:WM_QKD + (jc + 1) * 128],
                 wmed[:, WM_QQ:WM_QQ + 256], start=True, stop=True,
                 inc=(jc == 1))
        sim_tag = p.tag("sPE")
        p.wait("vector", "sPE", sim_tag)
        for jc in range(2):
            p.vstt(scr32[:, jc * 256:(jc + 1) * 256],
                   pC[:, jc * 256:(jc + 1) * 256], 1.0 / G,
                   sb["biasT"][:, jc * 256:(jc + 1) * 256], ALU.mult, ALU.add)
            p.vstt(wmed[:, WM_ATT + jc * 256:WM_ATT + (jc + 1) * 256],
                   scr32[:, jc * 256:(jc + 1) * 256], 0.0,
                   scr32[:, jc * 256:(jc + 1) * 256], ALU.max, ALU.mult,
                   inc=(jc == 1))
        att_tag = p.tag("sV")
        for ec in range(16):
            bk = ec % 3
            pt = (pA, pB, pC)[bk]
            sl = pt[:, 0:256]
            p.wait("tensor", "sV", st2["qlslot"][bk])
            p.wait("tensor", "sV", att_tag)
            for tc in range(2):
                p.mm(sl, actfm[:, AF_V + tc * 2048 + ec * 128:
                               AF_V + tc * 2048 + (ec + 1) * 128],
                     wmed[:, WM_ATT + tc * 256:WM_ATT + (tc + 1) * 256],
                     start=(tc == 0), stop=False)
            p.mm(sl, wmed[:, WM_LINKV + ec * 128:WM_LINKV + (ec + 1) * 128],
                 wmed[:, WM_LQ:WM_LQ + 256], start=False, stop=True, inc=True)
            qlmm = p.tag("sPE")
            p.wait("vector", "sPE", qlmm)
            p.vtt(actfm[:, AF_GATED + ec * 256:AF_GATED + (ec + 1) * 256], sl,
                  actfm[:, AF_GATE + ec * 256:AF_GATE + (ec + 1) * 256],
                  ALU.mult, inc=True)
            st2["qlslot"][bk] = p.tag("sV")
        gated_tag = p.tag("sV")
        for ec in range(16):
            slot = ec % 2
            p.wait("sync", "sPE", st2["wout"][slot])
            p.dma("sync", wmed[:, WM_WOUT + slot * 1024:
                               WM_WOUT + (slot + 1) * 1024], woutv[:, ec, :])
            dtag = p.tag("sS")
            wsl = wmed[:, WM_WOUT + slot * 1024:WM_WOUT + (slot + 1) * 1024]
            p.wait("tensor", "sS", dtag)
            if ec == 0:
                p.wait("tensor", "sV", gated_tag)
                p.wait("tensor", "sV", st2["resfree"][0])
                p.wait("tensor", "sV", st2["resfree"][1])
            for tc in range(2):
                pt = pE2 if tc == 0 else pF2
                gsl = actfm[:, AF_GATED + ec * 256 + tc * 128:
                            AF_GATED + ec * 256 + (tc + 1) * 128]
                for half in range(2):
                    p.mm(pt[:, half * 512:(half + 1) * 512], gsl,
                         wsl[:, half * 512:(half + 1) * 512],
                         start=(ec == 0), stop=False,
                         inc=(tc == 1 and half == 1 and ec < 15))
            if ec == 15:
                for tc in range(2):
                    pt = pE2 if tc == 0 else pF2
                    for half in range(2):
                        p.mm(pt[:, half * 512:(half + 1) * 512], onesb[:],
                             rows[0:1, 2048 + half * 512:
                                  2048 + (half + 1) * 512],
                             start=False, stop=True,
                             inc=(tc == 1 and half == 1))
            st2["wout"][slot] = p.tag("sPE")
        wout_mm = p.tag("sPE")
        for tc in range(2):
            ci = g * 2 + tc
            c = seq * CH_SEQ + ci
            pt = pE2 if tc == 0 else pF2
            xc = xres[:, c * D:(c + 1) * D]
            p.wait("vector", "sPE", wout_mm)
            p.vts(scr32[:, 0:D], xc, stats[:, ci:ci + 1],
                  stats[:, 16 + ci:17 + ci], ALU.subtract, ALU.mult)
            p.vtt(scr32[:, 0:D], scr32[:, 0:D], pt[:], ALU.add)
            p.vtt(xc, scr32[:, 0:D], xc, ALU.add, inc=True)
            st2["resfree"][tc] = p.tag("sV")
    p.ns["pe_last"] = p.tag("sPE")



def ffn_seq(p, dram, scr, sb, ps, seq):
    ns = p.ns
    xres, wbig, wmed, actfm = sb["xres"], sb["wbig"], sb["wmed"], sb["actfm"]
    hbuf, stats, scr32, gates = (sb["hbuf"], sb["stats"], sb["scr32"],
                                 sb["gates"])
    gatesT = sb["wmed"][0:8, 0:2048]
    b2sb = sb["wmed"][0:8, 2048:3072]
    onesb, rows = sb["onesb"], sb["rows"]
    p.dma("gpsimd", b2sb, dram["b2"])
    b2_tag = p.tag("sG")
    pA, pB, pC, pE2, pF2, pTP = (ps["pA"], ps["pB"], ps["pC"], ps["pE2"],
                                 ps["pF2"], ps["pTP"])

    ln_stats_seq(p, sb, seq, True, EPS_LN, EPS_FFN)

    xh_view = actfm[:].rearrange("q (dc t) -> q dc t", t=2048)
    xhat_tag = None
    for ci in range(CH_SEQ):
        xhat_tag = build_fm_chunk(p, sb, ps, seq, ci,
                                  xh_view[:, :, ci * 128:(ci + 1) * 128],
                                  True, ci % 2)

    lfree = {0: None, 1: None}
    for ci in range(CH_SEQ):
        bk = ci % 2
        pt = pC if bk == 0 else pA
        sl = pt[:, 0:8]
        p.wait("tensor", "sV", xhat_tag)
        p.wait("tensor", "sV", lfree[bk])
        for dc in range(DC):
            p.mm(sl, xh_view[:, dc, ci * 128:(ci + 1) * 128],
                 sb["rwT"][:, dc * 8:(dc + 1) * 8], start=(dc == 0),
                 stop=False)
        p.mm(sl, onesb[:], rows[0:1, 3072:3080], start=False, stop=True,
             inc=True)
        lmm = p.tag("sPE")
        p.wait("vector", "sPE", lmm)
        p.vcopy(gates[:, ci * 8:(ci + 1) * 8], sl, inc=True)
        lfree[bk] = p.tag("sV")
    logits_tag = p.tag("sV")
    p.wait("scalar", "sV", logits_tag)
    p.act(gates[:], gates[:], AF.Exp, inc=True)
    t = p.tag("sA")
    p.wait("vector", "sA", t)
    gv = gates[:].rearrange("q (c e) -> q c e", e=8)
    m1 = stats[:, 176:192]
    m2 = stats[:, 192:208]
    dinv = stats[:, 208:224]
    w1t = stats[:, 224:240]
    p.vred(m1, gv, ALU.max, inc=True)
    p.wait("vector", "sV", p.tag("sV"))
    m1b = m1.unsqueeze(2).broadcast_to([128, 16, 8])
    is1 = scr32[:, 0:128].rearrange("q (c e) -> q c e", e=8)
    p.vtt(is1, gv, m1b, ALU.is_equal, inc=True)
    p.wait("vector", "sV", p.tag("sV"))
    masked = scr32[:, 128:256].rearrange("q (c e) -> q c e", e=8)
    p.vstt(masked, is1, NEGBIG, gv, ALU.mult, ALU.add, inc=True)
    p.wait("vector", "sV", p.tag("sV"))
    p.vred(m2, masked, ALU.max, inc=True)
    p.wait("vector", "sV", p.tag("sV"))
    m2b = m2.unsqueeze(2).broadcast_to([128, 16, 8])
    is2 = scr32[:, 256:384].rearrange("q (c e) -> q c e", e=8)
    p.vtt(is2, masked, m2b, ALU.is_equal, inc=True)
    p.vtt(dinv, m1, m2, ALU.add, inc=True)
    p.wait("vector", "sV", p.tag("sV"))
    p.vrecip(dinv, dinv, inc=True)
    p.wait("vector", "sV", p.tag("sV"))
    p.vtt(w1t, m1, dinv, ALU.mult)
    p.vtt(m2, m2, dinv, ALU.mult, inc=True)
    p.wait("vector", "sV", p.tag("sV"))
    w1b = w1t.unsqueeze(2).broadcast_to([128, 16, 8])
    w2b = m2.unsqueeze(2).broadcast_to([128, 16, 8])
    p.vtt(gv, is1, w1b, ALU.mult)
    p.vtt(is2, is2, w2b, ALU.mult, inc=True)
    p.wait("vector", "sV", p.tag("sV"))
    p.vtt(gv, gv, is2, ALU.add, inc=True)
    gates_tag = p.tag("sV")
    p.wait("vector", "sV", gates_tag)

    fmst = p.ns["fm"]
    p.wait("vector", "sPE", fmst["slot"].get(0))
    p.vcopy(hbuf[:, 0:128], gates[:], inc=True)
    gt0 = p.tag("sV")
    for rnd in range(2):
        p.wait("tensor", "sV", gt0)
        p.wait("tensor", "sV", fmst["evac"])
        for k in range(8):
            ci = rnd * 8 + k
            p.tp(pTP[0:8, k * 128:(k + 1) * 128],
                 hbuf[:, ci * 8:(ci + 1) * 8], sb["ident"][:], inc=(k == 7))
        tpt = p.tag("sPE")
        p.wait("vector", "sPE", tpt)
        p.vcopy(gatesT[0:8, rnd * 1024:(rnd + 1) * 1024], pTP[0:8, 0:1024],
                inc=True)
        fmst["evac"] = p.tag("sV")
    gatesT_tag = p.tag("sV")
    fmst["slot"][0] = p.tag("sPE")

    wload = {}
    slot_free = {0: None, 1: None}
    yfree = {0: gatesT_tag, 1: gatesT_tag}
    hfree = {i: None for i in range(4)}
    psfree = {i: None for i in range(3)}

    def issue_wload(e, q):
        slot = (e * NQ + q) % 2
        p.wait("sync", "sPE", slot_free[slot])
        w1v = scr["w1T"][e].rearrange("(dcb q2) h2 -> q2 dcb h2", q2=128)
        p.dma("sync",
              wbig[:, WB_W1Q[0] + slot * 8192:WB_W1Q[0] + slot * 8192 + 8192]
              .rearrange("q2 (dcb h2) -> q2 dcb h2", h2=1024),
              w1v[:, :, q * 1024:(q + 1) * 1024])
        for hc in range(8):
            p.dma("sync",
                  wbig[:, WB_W2Q[0] + slot * 8192 + hc * 1024:
                        WB_W2Q[0] + slot * 8192 + (hc + 1) * 1024],
                  scr["w2c"][e][:, q * 1024 + hc * 128:
                                q * 1024 + (hc + 1) * 128],
                  transpose=True)
        wload[(e, q)] = p.tag("sS")

    issue_wload(0, 0)
    for e in range(E):
        for q in range(NQ):
            slot = (e * NQ + q) % 2
            nxt = (e, q + 1) if q + 1 < NQ else ((e + 1, 0) if e + 1 < E
                                                 else None)
            if nxt is not None:
                issue_wload(*nxt)
            w1sl = wbig[:, WB_W1Q[0] + slot * 8192:
                        WB_W1Q[0] + slot * 8192 + 8192]
            w2sl = wbig[:, WB_W2Q[0] + slot * 8192:
                        WB_W2Q[0] + slot * 8192 + 8192]
            p.wait("tensor", "sS", wload[(e, q)])
            for tb in range(8):
                hact = {}

                def emit_h(hc):
                    hs = hc % 4
                    pb = hc % 3
                    pt = (pA, pB, pC)[pb]
                    sl = pt[:, 0:256]
                    p.wait("tensor", "sA", psfree[pb])
                    for dc in range(DC):
                        p.mm(sl, w1sl[:, dc * 1024 + hc * 128:
                                      dc * 1024 + (hc + 1) * 128],
                             xh_view[:, dc, tb * 256:(tb + 1) * 256],
                             start=(dc == 0), stop=(dc == DC - 1),
                             inc=(dc == DC - 1))
                    hmm = p.tag("sPE")
                    p.wait("scalar", "sPE", hmm)
                    p.wait("scalar", "sPE", hfree[hs])
                    p.act(hbuf[:, hs * 256:(hs + 1) * 256], sl, AF.Gelu,
                          bias=sb["b1col"][:, e * 32 + q * 8 + hc:
                                           e * 32 + q * 8 + hc + 1],
                          scale=1.0, inc=True)
                    psfree[pb] = p.tag("sA")
                    hact[hc] = p.tag("sA")

                def emit_y2(hc):
                    hs = hc % 4
                    p.wait("tensor", "sA", hact[hc])
                    if hc == 0:
                        p.wait("tensor", "sV", yfree[0])
                        p.wait("tensor", "sV", yfree[1])
                    for t128 in range(2):
                        pt2 = pE2 if t128 == 0 else pF2
                        for half in range(2):
                            p.mm(pt2[:, half * 512:(half + 1) * 512],
                                 hbuf[:, hs * 256 + t128 * 128:
                                      hs * 256 + (t128 + 1) * 128],
                                 w2sl[:, hc * 1024 + half * 512:
                                      hc * 1024 + (half + 1) * 512],
                                 start=(hc == 0), stop=(hc == 7),
                                 inc=(t128 == 1 and half == 1))
                    hfree[hs] = p.tag("sPE")

                emit_h(0)
                for hc in range(1, 8):
                    emit_h(hc)
                    emit_y2(hc - 1)
                emit_y2(7)
                y2mm = p.tag("sPE")
                for t128 in range(2):
                    ci = tb * 2 + t128
                    c = seq * CH_SEQ + ci
                    pt2 = pE2 if t128 == 0 else pF2
                    xc = xres[:, c * D:(c + 1) * D]
                    p.wait("vector", "sPE", y2mm)
                    p.vstt(xc, pt2[:], gates[:, ci * 8 + e:ci * 8 + e + 1],
                           xc, ALU.mult, ALU.add, inc=True)
                    yfree[t128] = p.tag("sV")
            slot_free[slot] = p.tag("sPE")
    bfree = {0: yfree[0], 1: yfree[1]}
    for ci in range(CH_SEQ):
        c = seq * CH_SEQ + ci
        bk = ci % 2
        pt2 = pE2 if bk == 0 else pF2
        p.wait("tensor", "sV", bfree[bk])
        for half in range(2):
            p.wait("tensor", "sG", b2_tag)
            p.mm(pt2[:, half * 512:(half + 1) * 512],
                 gatesT[0:8, ci * 128:(ci + 1) * 128],
                 b2sb[:, half * 512:(half + 1) * 512],
                 start=True, stop=True, inc=(half == 1))
        bmm = p.tag("sPE")
        xc = xres[:, c * D:(c + 1) * D]
        p.wait("vector", "sPE", bmm)
        p.vtt(xc, pt2[:], xc, ALU.add, inc=True)
        bfree[bk] = p.tag("sV")
    p.ns["pe_last"] = p.tag("sPE")



def attn_b_seq(p, dram, scr, sb, ps, seq):
    ns = p.ns
    xres, wbig, wmed, actfm = sb["xres"], sb["wbig"], sb["wmed"], sb["actfm"]
    hbuf, stats, scr32 = sb["hbuf"], sb["stats"], sb["scr32"]
    pA, pB, pC, pE2, pF2, pTP = (ps["pA"], ps["pB"], ps["pC"], ps["pE2"],
                                 ps["pF2"], ps["pTP"])

    ln_stats_seq(p, sb, seq, False, EPS_LN, EPS_LN)
    yv = actfm[:].rearrange("q (dc t) -> q dc t", t=2048)
    y_tag = None
    for ci in range(CH_SEQ):
        y_tag = build_fm_chunk(p, sb, ps, seq, ci,
                               yv[:, :, ci * 128:(ci + 1) * 128], False,
                               ci % 2)

    for nm, off in (("Wdqc", WM_DQ), ("Wdkvc", WM_DKV)):
        wv = scr[nm].rearrange("(dcb q2) c2 -> q2 dcb c2", q2=128)
        p.dma("sync", wmed[:, off:off + 2048]
              .rearrange("q2 (dcb c2) -> q2 dcb c2", c2=256), wv)
    dq_tag = p.tag("sS")

    bank = {0: ns["fm"]["evac"], 1: ns["fm"]["evac"]}
    for which, off_w, dst, dsto in ((0, WM_DQ, hbuf, 0),
                                    (1, WM_DKV, wmed, WM_CKV)):
        for cc in range(2):
            for nt in range(4):
                bk = (cc * 4 + nt) % 2
                pt = pA if bk == 0 else pB
                p.wait("tensor", "sS", dq_tag)
                p.wait("tensor", "sV", y_tag)
                p.wait("tensor", "sV", bank[bk])
                for dc in range(DC):
                    p.mm(pt[:], wmed[:, off_w + dc * 256 + cc * 128:
                                     off_w + dc * 256 + (cc + 1) * 128],
                         yv[:, dc, nt * 512:(nt + 1) * 512],
                         start=(dc == 0), stop=(dc == DC - 1),
                         inc=(dc == DC - 1))
                cmm = p.tag("sPE")
                p.wait("vector", "sPE", cmm)
                p.vcopy(dst[:, dsto + cc * 2048 + nt * 512:
                            dsto + cc * 2048 + (nt + 1) * 512], pt[:],
                        inc=True)
                bank[bk] = p.tag("sV")
    c_done_pe = p.tag("sPE")
    c_done = p.tag("sV")

    p.wait("sync", "sPE", c_done_pe)
    for nm, off in (("Wukc", WM_UK), ("Wuvc", WM_UV)):
        wv = scr[nm].rearrange("(ccb q2) d2 -> q2 ccb d2", q2=128)
        p.dma("sync", wmed[:, off:off + 2048]
              .rearrange("q2 (ccb d2) -> q2 ccb d2", d2=1024), wv)
    uk_tag = p.tag("sS")

    for qc in range(8):
        for nt in range(4):
            bk = (qc * 4 + nt) % 2
            pt = pA if bk == 0 else pB
            p.wait("tensor", "sS", uk_tag)
            p.wait("tensor", "sV", c_done)
            p.wait("tensor", "sV", bank[bk])
            for cc in range(2):
                p.mm(pt[:], wmed[:, WM_UK + cc * 1024 + qc * 128:
                                 WM_UK + cc * 1024 + (qc + 1) * 128],
                     wmed[:, WM_CKV + cc * 2048 + nt * 512:
                          WM_CKV + cc * 2048 + (nt + 1) * 512],
                     start=(cc == 0), stop=(cc == 1), inc=(cc == 1))
            kmm = p.tag("sPE")
            p.wait("vector", "sPE", kmm)
            p.vcopy(wbig[:, qc * 2048 + nt * 512:qc * 2048 + (nt + 1) * 512],
                    pt[:], inc=True)
            bank[bk] = p.tag("sV")
    vfree = {0: None, 1: None}
    for tc in range(16):
        bk = tc % 2
        pt2 = pE2 if bk == 0 else pF2
        p.wait("tensor", "sV", vfree[bk])
        for half in range(2):
            for cc in range(2):
                p.mm(pt2[:, half * 512:(half + 1) * 512],
                     wmed[:, WM_CKV + cc * 2048 + tc * 128:
                          WM_CKV + cc * 2048 + (tc + 1) * 128],
                     wmed[:, WM_UV + cc * 1024 + half * 512:
                          WM_UV + cc * 1024 + (half + 1) * 512],
                     start=(cc == 0), stop=(cc == 1),
                     inc=(cc == 1 and half == 1))
        vmm = p.tag("sPE")
        p.wait("vector", "sPE", vmm)
        p.vcopy(wbig[:, 16384 + tc * 1024:16384 + (tc + 1) * 1024], pt2[:],
                inc=True)
        vfree[bk] = p.tag("sV")
    kv_done_pe = p.tag("sPE")
    kv_done = p.tag("sV")

    p.wait("sync", "sPE", kv_done_pe)
    wv = scr["Wuqc"].rearrange("(ccb q2) d2 -> q2 ccb d2", q2=128)
    p.dma("sync", wmed[:, WM_UQ:WM_UQ + 2048]
          .rearrange("q2 (ccb d2) -> q2 ccb d2", d2=1024), wv)
    wv = scr["Woc"].rearrange("(vcb q2) d2 -> q2 vcb d2", q2=128)
    p.dma("sync", actfm[:, 0:8192]
          .rearrange("q2 (vcb d2) -> q2 vcb d2", d2=1024), wv)
    uq_tag = p.tag("sS")

    st = {"qt": {0: None, 1: None}, "exp": None,
          "avfree": {0: None, 1: None}, "linv": None,
          "res": kv_done, "tpdone": None, "pn": {}, "J": {}}
    fmst = p.ns["fm"]

    def emit_head_scores(i, h):
        J = (i + 1) * 128
        njt = (J + 511) // 512
        st["J"][h] = (i, J, njt)
        qc, base = h // 2, (h % 2) * 64
        p.wait("tensor", "sV", st["qt"][0])
        p.wait("tensor", "sV", st["qt"][1])
        p.wait("tensor", "sA", st["exp"])
        if h == 0:
            p.wait("tensor", "sV", st["res"])
        for jt in range(njt):
            N = min(512, J - jt * 512)
            pt2 = pE2 if jt < 2 else pF2
            sl = pt2[:, (jt % 2) * 512:(jt % 2) * 512 + N]
            p.mm(sl, wmed[base:base + 64, WM_QT + qc * 128:
                          WM_QT + (qc + 1) * 128],
                 wbig[base:base + 64, qc * 2048 + jt * 512:
                      qc * 2048 + jt * 512 + N],
                 start=True, stop=True, inc=(jt == njt - 1))
        smm = p.tag("sPE")
        dj = i * 128
        jt = dj // 512
        pt2 = pE2 if jt < 2 else pF2
        dsl = pt2[:, (jt % 2) * 512 + dj % 512:
                  (jt % 2) * 512 + dj % 512 + 128]
        p.wait("vector", "sPE", smm)
        p.vtt(dsl, dsl, sb["cmask"][:], ALU.add, inc=True)
        p.wait("vector", "sV", p.tag("sV"))
        for jt in range(njt):
            N = min(512, J - jt * 512)
            pt2 = pE2 if jt < 2 else pF2
            p.vred(stats[:, 176 + jt:177 + jt],
                   pt2[:, (jt % 2) * 512:(jt % 2) * 512 + N], ALU.max,
                   inc=(jt == njt - 1))
        p.wait("vector", "sV", p.tag("sV"))
        if njt > 1:
            p.vred(stats[:, 180:181], stats[:, 176:176 + njt], ALU.max,
                   inc=True)
            p.wait("vector", "sV", p.tag("sV"))
            p.vts1(stats[:, 181:182], stats[:, 180:181], -0.125, ALU.mult,
                   inc=True)
        else:
            p.vts1(stats[:, 181:182], stats[:, 176:177], -0.125, ALU.mult,
                   inc=True)
        mtag = p.tag("sV")
        p.wait("scalar", "sV", mtag)
        p.wait("scalar", "sPE", st["tpdone"])
        for jt in range(njt):
            N = min(512, J - jt * 512)
            pt2 = pE2 if jt < 2 else pF2
            p.act(wmed[:, WM_PSB + jt * 512:WM_PSB + jt * 512 + N],
                  pt2[:, (jt % 2) * 512:(jt % 2) * 512 + N], AF.Exp,
                  bias=stats[:, 181:182], scale=0.125,
                  inc=(jt == njt - 1))
        st["exp"] = p.tag("sA")
        st["pn"][h] = st["exp"]
        p.wait("vector", "sA", st["exp"])
        p.vred(stats[:, 182:183], wmed[:, WM_PSB:WM_PSB + J], ALU.add,
               inc=True)
        p.wait("vector", "sV", p.tag("sV"))
        p.vrecip(stats[:, 183:184], stats[:, 182:183], inc=True)
        st["linv"] = p.tag("sV")

    def emit_head_tpav(h):
        i, J, njt = st["J"][h]
        qc, base = h // 2, (h % 2) * 64
        pn_tag = st["pn"][h]
        for jc in range(i + 1):
            ksl = jc % 8
            p.wait("tensor", "sA", pn_tag)
            if ksl == 0:
                p.wait("tensor", "sV", fmst["evac"])
            p.tp(pTP[:, ksl * 128:(ksl + 1) * 128],
                 wmed[:, WM_PSB + jc * 128:WM_PSB + (jc + 1) * 128],
                 sb["ident"][:], inc=(ksl == 7 or jc == i))
            if ksl == 7 or jc == i:
                tpt = p.tag("sPE")
                p.wait("vector", "sPE", tpt)
                lo = (jc // 8) * 8
                p.vcopy(wmed[:, WM_PT + lo * 128:WM_PT + (jc + 1) * 128],
                        pTP[:, 0:(jc + 1 - lo) * 128], inc=True)
                fmst["evac"] = p.tag("sV")
        st["tpdone"] = p.tag("sPE")
        pt_tag = p.tag("sV")
        p.wait("tensor", "sV", pt_tag)
        avp = pB if h % 2 == 0 else pA
        p.wait("tensor", "sA", st["avfree"].get(h % 2))
        for jc in range(i + 1):
            p.mm(avp[:, 256:320],
                 wmed[:, WM_PT + jc * 128:WM_PT + (jc + 1) * 128],
                 wbig[:, 16384 + jc * 1024 + h * 64:
                      16384 + jc * 1024 + (h + 1) * 64],
                 start=(jc == 0), stop=(jc == i), inc=(jc == i))
        avmm = p.tag("sPE")
        p.wait("scalar", "sPE", avmm)
        p.wait("scalar", "sV", st["linv"])
        p.act(wmed[:, WM_AO + h * 64:WM_AO + (h + 1) * 64], avp[:, 256:320],
              AF.Identity, scale=stats[:, 183:184], inc=True)
        st["avfree"][h % 2] = p.tag("sA")

    for i in range(CH_SEQ):
        for qc in range(8):
            bk = qc % 2
            pt = pA if bk == 0 else pC
            sl = pt[:, 0:128]
            p.wait("tensor", "sS", uq_tag)
            p.wait("tensor", "sV", st["qt"][bk])
            p.wait("tensor", "sA", st["avfree"][1])
            for cc in range(2):
                p.mm(sl, wmed[:, WM_UQ + cc * 1024 + qc * 128:
                              WM_UQ + cc * 1024 + (qc + 1) * 128],
                     hbuf[:, cc * 2048 + i * 128:cc * 2048 + (i + 1) * 128],
                     start=(cc == 0), stop=(cc == 1), inc=(cc == 1))
            qmm = p.tag("sPE")
            p.wait("vector", "sPE", qmm)
            p.vcopy(wmed[:, WM_QT + qc * 128:WM_QT + (qc + 1) * 128], sl,
                    inc=True)
            st["qt"][bk] = p.tag("sV")
        for h in range(NHEAD):
            if h > 0:
                emit_head_tpav(h - 1)
            emit_head_scores(i, h)
        emit_head_tpav(NHEAD - 1)
        p.wait("tensor", "sA", st["avfree"][0])
        p.wait("tensor", "sA", st["avfree"][1])
        p.wait("tensor", "sV", fmst["evac"])
        for vc in range(8):
            p.tp(pTP[:, vc * 128:(vc + 1) * 128],
                 wmed[:, WM_AO + vc * 128:WM_AO + (vc + 1) * 128],
                 sb["ident"][:], inc=(vc == 7))
        aot = p.tag("sPE")
        st["tpdone"] = aot
        p.wait("vector", "sPE", aot)
        p.vcopy(wmed[:, 4096:5120], pTP[:], inc=True)
        fmst["evac"] = p.tag("sV")
        ao_tag = p.tag("sV")
        p.wait("tensor", "sV", ao_tag)
        p.wait("tensor", "sV", st["res"])
        for vc in range(8):
            for half in range(2):
                p.mm(pE2[:, half * 512:(half + 1) * 512],
                     wmed[:, 4096 + vc * 128:4096 + (vc + 1) * 128],
                     actfm[:, vc * 1024 + half * 512:
                           vc * 1024 + (half + 1) * 512],
                     start=(vc == 0), stop=(vc == 7),
                     inc=(vc == 7 and half == 1))
        wmm = p.tag("sPE")
        st["tpdone"] = wmm
        c = seq * CH_SEQ + i
        xc = xres[:, c * D:(c + 1) * D]
        p.wait("vector", "sPE", wmm)
        p.vtt(xc, pE2[:], xc, ALU.add, inc=True)
        st["res"] = p.tag("sV")
    p.ns["pe_last"] = p.tag("sPE")

def phase_output(p, sb, y_out):
    xres, s32 = sb["xres"], sb["scr32"]
    yv = y_out.rearrange("(c p) d -> p c d", p=128)
    store = {}
    for c in range(NCH):
        if c >= 1:
            p.wait("scalar", "sS", store[c - 1])
        p.act(s32[:, :], xres[:, c * D:(c + 1) * D], AF.Copy, scale=1.0,
              inc=True)
        t = p.tag("sA")
        p.wait("sync", "sA", t)
        store[c] = p.dma("sync", yv[:, c, :], s32[:, :])
    p.wait("sync", "sS", p.tag("sS"))



def _bucket1d():
    NUM_BUCKETS = 32
    MAX_DIST = 128
    nb = NUM_BUCKETS // 2
    max_exact = nb // 2
    out = np.zeros(511, dtype=np.int64)
    for idx, dd in enumerate(range(-255, 256)):
        n = dd
        ret = nb if n < 0 else 0
        n = abs(n)
        if n < max_exact:
            val = n
        else:
            val = max_exact + int(
                math.log(max(n, 1) / max_exact)
                / math.log(MAX_DIST / max_exact) * (nb - max_exact))
            val = min(val, nb - 1)
        out[idx] = ret + val
    return out


def _oh_const():
    b = _bucket1d()
    oh = np.zeros((32, 511), dtype=np.float32)
    oh[b, np.arange(511)] = 1.0
    return oh * math.sqrt(128.0)


_NC_CACHE = None


def kernel(L0_attn_a_x, L0_attn_a_ln_g, L0_attn_a_ln_b, L0_attn_a_W_h,
           L0_attn_a_b_h, L0_attn_a_W_qk, L0_attn_a_b_qk, L0_attn_a_os_gamma,
           L0_attn_a_os_beta, L0_attn_a_rel_emb, L0_attn_a_W_out,
           L0_attn_a_b_out, L0_attn_b_W_dq, L0_attn_b_W_uq, L0_attn_b_W_dkv,
           L0_attn_b_W_uk, L0_attn_b_W_uv, L0_attn_b_W_o,
           L0_ffn_router_ln_scale, L0_ffn_router_ln_bias, L0_ffn_router_w,
           L0_ffn_router_b, L0_ffn_expert_ln_scale, L0_ffn_expert_ln_bias,
           L0_ffn_w1, L0_ffn_b1, L0_ffn_w2, L0_ffn_b2):
    global _NC_CACHE
    x = np.ascontiguousarray(np.asarray(L0_attn_a_x)).astype(np.float32)

    common = {
        "ln_g": L0_attn_a_ln_g, "ln_b": L0_attn_a_ln_b,
        "W_h": L0_attn_a_W_h, "b_h": L0_attn_a_b_h,
        "W_qk": L0_attn_a_W_qk, "b_qk": L0_attn_a_b_qk,
        "os_gamma": L0_attn_a_os_gamma, "os_beta": L0_attn_a_os_beta,
        "rel_emb": L0_attn_a_rel_emb, "W_out": L0_attn_a_W_out,
        "b_out": L0_attn_a_b_out,
        "W_dq": L0_attn_b_W_dq, "W_uq": L0_attn_b_W_uq,
        "W_dkv": L0_attn_b_W_dkv, "W_uk": L0_attn_b_W_uk,
        "W_uv": L0_attn_b_W_uv, "W_o": L0_attn_b_W_o,
        "r_ls": L0_ffn_router_ln_scale, "r_lb": L0_ffn_router_ln_bias,
        "r_w": L0_ffn_router_w, "r_b": L0_ffn_router_b,
        "e_ls": L0_ffn_expert_ln_scale, "e_lb": L0_ffn_expert_ln_bias,
        "w1": L0_ffn_w1, "b1": L0_ffn_b1, "w2": L0_ffn_w2, "b2": L0_ffn_b2,
        "OH": _oh_const(),
    }
    common = {k: np.ascontiguousarray(np.asarray(v)).astype(np.float32)
              for k, v in common.items()}

    in_maps = []
    for c in range(8):
        m = dict(common)
        m["x"] = np.ascontiguousarray(x[2 * c:2 * c + 2].reshape(T, D))
        in_maps.append(m)

    if _NC_CACHE is None:
        _NC_CACHE = build_nc()
    res = run_bass_kernel_spmd(_NC_CACHE, in_maps, core_ids=list(range(8)))
    out = np.concatenate(
        [r["y"].reshape(NSEQ, S, D) for r in res.results], axis=0)
    return out.astype(np.float32)


# version 07
# speedup vs baseline: 1.0446x; 1.0057x over previous
import math
import numpy as np

import concourse.bass as bass
import concourse.mybir as mybir
from concourse.bass_utils import run_bass_kernel_spmd

F32 = mybir.dt.float32
BF16 = mybir.dt.bfloat16
AF = mybir.ActivationFunctionType
ALU = mybir.AluOpType
AX = mybir.AxisListType

D = 1024
DC = 8
S = 2048
NSEQ = 2
T = 4096
NCH = 32
CH_SEQ = 16
HID = 2048
G = 256
NG = 8
E = 8
H = 4096
NQ = 4
QH = 1024
NHEAD = 16
DK = 64
EPS_LN = 1e-5
EPS_FFN = 1e-6
NEGBIG = -1.0e30

DMA_Q = {"sync": "sS", "gpsimd": "sG", "scalar": "sC"}

PHASES = {"attn_a": True, "ffn1": True, "attn_b": True, "ffn2": True}
DEBUG = False
DEBUG_POINT = "attn_a"
NSEQ_RUN = NSEQ

WM_QK = 0
WM_QKACT = 1024
WM_LINKV = 3072
WM_LKT = 5120
WM_QQ = 5376
WM_QKD = 5632
WM_LQ = 5888
WM_ATT = 6144
WM_WOUT = 6656
WM_DQ = 0
WM_DKV = 2048
WM_CKV = 4096
WM_UK = 0
WM_UV = 2048
WM_UQ = 0
WM_PSB = 4096
WM_PT = 6144
WM_QT = 8192
WM_AO = 9216

AF_XA = 0
AF_V = 2048
AF_GATE = 6144
AF_GATED = 10240

WB_W1Q = (0, 8192)
WB_W2Q = (16384, 24576)

C_LNG = 0
C_LNB = 8
C_GAM = 16
C_BET = 20
C_BQK = 24
C_RLS = 32
C_ELS = 64


class Prog:
    def __init__(self, nc, sems):
        self.nc = nc
        self.sems = sems
        self.cur = None
        self.eng = None
        self.sim = {}
        self.final = {}
        self.ns = {}

    def begin(self, name, eng):
        self.cur = name
        self.eng = eng
        self.sim = {k: 0 for k in self.sems}
        self.ns = {}

    def end(self):
        if self.final:
            assert self.final == self.sim, (
                f"sim divergence on {self.cur}: {self.final} vs {self.sim}")
        else:
            self.final = dict(self.sim)

    def _emit(self, engine, make, inc):
        if inc is not None:
            self.sim[inc[0]] += inc[1]
        if engine == self.cur:
            instr = make(self.eng)
            if inc is not None:
                instr.then_inc(self.sems[inc[0]], inc[1])
        return self.sim[inc[0]] if inc is not None else None

    def wait(self, engine, sem, v):
        if v is None:
            return
        if engine == self.cur:
            self.eng.wait_ge(self.sems[sem], v)

    def tag(self, sem):
        return self.sim[sem]

    def dma(self, q, out, in_, **kw):
        return self._emit(q, lambda e: e.dma_start(out=out, in_=in_, **kw),
                          (DMA_Q[q], 16))

    def mm(self, out, lhsT, rhs, start, stop, inc=False, **kw):
        return self._emit(
            "tensor",
            lambda e: e.matmul(out, lhsT, rhs, start=start, stop=stop, **kw),
            ("sPE", 1) if inc else None)

    def tp(self, out, in_, ident, inc=False):
        return self._emit("tensor", lambda e: e.transpose(out, in_, ident),
                          ("sPE", 1) if inc else None)

    def vtt(self, out, in0, in1, op, inc=False):
        return self._emit("vector", lambda e: e.tensor_tensor(out, in0, in1, op),
                          ("sV", 1) if inc else None)

    def vts(self, out, in0, s1, s2, op0, op1, inc=False):
        return self._emit(
            "vector", lambda e: e.tensor_scalar(out, in0, s1, s2, op0, op1),
            ("sV", 1) if inc else None)

    def vts1(self, out, in0, s1, op0, inc=False):
        return self._emit(
            "vector", lambda e: e.tensor_scalar(out, in0, s1, None, op0),
            ("sV", 1) if inc else None)

    def vstt(self, out, in0, scalar, in1, op0, op1, inc=False):
        return self._emit(
            "vector",
            lambda e: e.scalar_tensor_tensor(out, in0, scalar, in1, op0, op1),
            ("sV", 1) if inc else None)

    def vred(self, out, in_, op, inc=False, **kw):
        return self._emit(
            "vector", lambda e: e.tensor_reduce(out, in_, AX.X, op, **kw),
            ("sV", 1) if inc else None)

    def vcopy(self, out, in_, inc=False):
        return self._emit("vector", lambda e: e.tensor_copy(out, in_),
                          ("sV", 1) if inc else None)

    def vrecip(self, out, in_, inc=False):
        return self._emit("vector", lambda e: e.reciprocal(out, in_),
                          ("sV", 1) if inc else None)

    def vmemset(self, ap, c, inc=False):
        return self._emit("vector", lambda e: e.memset(ap, c),
                          ("sV", 1) if inc else None)

    def act(self, out, in_, func, bias=None, scale=None, inc=False):
        kw = {}
        if bias is not None:
            kw["bias"] = bias
        if scale is not None:
            kw["scale"] = scale
        return self._emit("scalar",
                          lambda e: e.activation(out, in_, func, **kw),
                          ("sA", 1) if inc else None)

    def acopy(self, out, in_, inc=False):
        return self._emit("scalar", lambda e: e.copy(out, in_),
                          ("sA", 1) if inc else None)

    def gmemset(self, ap, c, inc=False):
        return self._emit("gpsimd", lambda e: e.memset(ap, c),
                          ("sP", 1) if inc else None)

    def gaffine(self, out, in_, pattern, compare_op, fill, base, chmul,
                inc=False):
        return self._emit(
            "gpsimd",
            lambda e: e.affine_select(out=out, in_=in_, pattern=pattern,
                                      compare_op=compare_op, fill=fill,
                                      base=base, channel_multiplier=chmul),
            ("sP", 1) if inc else None)

    def barrier(self, sb):
        stats = sb["stats"]
        pe = self.ns.get("pe_last")
        self.vmemset(stats[:, 252:253], 0.0, inc=True)
        self.act(stats[:, 253:254], stats[:, 253:254], AF.Copy, scale=1.0,
                 inc=True)
        self.gmemset(stats[:, 254:255], 0.0, inc=True)
        targets = [("sV", self.tag("sV")), ("sA", self.tag("sA")),
                   ("sP", self.tag("sP")), ("sS", self.tag("sS")),
                   ("sG", self.tag("sG")), ("sC", self.tag("sC"))]
        if pe is not None:
            targets.append(("sPE", pe))
        for engname in ("sync", "tensor", "vector", "scalar", "gpsimd"):
            for sem, v in targets:
                self.wait(engname, sem, v)



def build_nc():
    nc = bass.Bass("TRN2", target_bir_lowering=False)

    dram = {}

    def din(name, shape, dtype=F32):
        dram[name] = nc.dram_tensor(name, list(shape), dtype,
                                    kind="ExternalInput").ap()

    din("x", (T, D))
    din("ln_g", (D,))
    din("ln_b", (D,))
    din("W_h", (D, 2 * HID))
    din("b_h", (2 * HID,))
    din("W_qk", (D, 128))
    din("b_qk", (128,))
    din("os_gamma", (4, 128))
    din("os_beta", (4, 128))
    din("rel_emb", (32, 1))
    din("W_out", (HID, D))
    din("b_out", (D,))
    din("W_dq", (D, 256))
    din("W_uq", (256, D))
    din("W_dkv", (D, 256))
    din("W_uk", (256, D))
    din("W_uv", (256, D))
    din("W_o", (D, D))
    din("r_ls", (D,))
    din("r_lb", (D,))
    din("r_w", (E, D))
    din("r_b", (E,))
    din("e_ls", (E, D))
    din("e_lb", (E, D))
    din("w1", (E, H, D))
    din("b1", (E, H))
    din("w2", (E, D, H))
    din("b2", (E, D))
    din("OH", (32, 511))

    y_out = nc.dram_tensor("y", [T, D], F32, kind="ExternalOutput").ap()
    dbg = {}
    if DEBUG:
        for nm, shape, dt in (("d_stats", [128, 256], F32),
                              ("d_wmed", [128, 10240], BF16),
                              ("d_actfm", [128, 16384], BF16),
                              ("d_wbig", [128, 4096], BF16),
                              ("d_rows", [1, 3200], BF16),
                              ("d_cols", [128, 192], F32),
                              ("d_cols2", [128, 64], F32),
                              ("d_b1col", [128, 256], F32),
                              ("d_biasT", [128, 512], BF16),
                              ("d_gates", [128, 128], F32),
                              ("d_hbuf", [128, 4096], BF16),
                              ("d_rows32", [1, 640], F32),
                              ("d_bvec", [1, 511], F32),
                              ("d_rwT", [128, 64], BF16),
                              ("d_w1T", [128, 4096], BF16)):
            dbg[nm] = nc.dram_tensor(nm, shape, dt, kind="ExternalOutput").ap()

    scr = {}

    def dscr(name, shape, dtype=BF16):
        scr[name] = nc.dram_tensor(name, list(shape), dtype,
                                   kind="Internal").ap()

    dscr("w1c", (E, H, D))
    dscr("w2c", (E, D, H))
    dscr("w1T", (E, D, H))
    dscr("Whc", (D, 2 * HID))
    dscr("Wqkc", (D, 128))
    dscr("Woutc", (HID, D))
    dscr("Wdqc", (D, 256))
    dscr("Wdkvc", (D, 256))
    dscr("Wuqc", (256, D))
    dscr("Wukc", (256, D))
    dscr("Wuvc", (256, D))
    dscr("Woc", (D, D))
    dscr("rowscr", (1, 4096), F32)
    dscr("bvec", (1, 511), F32)

    import contextlib
    with contextlib.ExitStack() as est:
        EE = est.enter_context
        sb = {}

        def sbuf(name, shape, dtype):
            sb[name] = EE(nc.sbuf_tensor(name, list(shape), dtype))

        sbuf("xres", (128, NCH * D), BF16)
        sbuf("wbig", (128, 32768), BF16)
        sbuf("actfm", (128, 16384), BF16)
        sbuf("wmed", (128, 10240), BF16)
        sbuf("hbuf", (128, 4096), BF16)
        sbuf("scr32", (128, 1024), F32)
        sbuf("stats", (128, 256), F32)
        sbuf("gates", (128, 128), F32)
        sbuf("biasT", (128, 512), BF16)
        sbuf("cmask", (128, 128), F32)
        sbuf("mask01", (128, 128), BF16)
        sbuf("ident", (128, 128), BF16)
        sbuf("b1col", (128, 256), F32)
        sbuf("cols", (128, 192), F32)
        sbuf("cols2", (128, 64), F32)
        sbuf("colsb", (128, 96), BF16)
        sbuf("relsb", (32, 1), BF16)
        sbuf("ohsb", (32, 512), BF16)
        sbuf("rows", (1, 3200), BF16)
        sbuf("rows32", (1, 640), F32)
        sbuf("onesb", (1, 128), BF16)
        sbuf("rwT", (128, 64), BF16)

        ps = {}
        for nm, shape, dt in (("pA", [128, 512], F32), ("pB", [128, 512], F32),
                              ("pC", [128, 512], F32),
                              ("pE2", [128, 1024], F32),
                              ("pF2", [128, 1024], F32),
                              ("pTP", [128, 1024], BF16)):
            ps[nm] = EE(nc.psum_tensor(nm, shape, dt))

        sems = {}
        for sname in ("sS", "sG", "sC", "sPE", "sV", "sA", "sP"):
            sems[sname] = EE(nc.semaphore(sname))

        block = EE(nc.Block())
        prog = Prog(nc, sems)

        def emit_all(engine, eng):
            prog.begin(engine, eng)
            program(prog, dram, scr, sb, ps, y_out, dbg if DEBUG else None)
            prog.end()

        @block.sync
        def _(e):
            emit_all("sync", e)

        @block.tensor
        def _(e):
            emit_all("tensor", e)

        @block.vector
        def _(e):
            emit_all("vector", e)

        @block.scalar
        def _(e):
            emit_all("scalar", e)

        @block.gpsimd
        def _(e):
            emit_all("gpsimd", e)

    return nc


def program(p, dram, scr, sb, ps, y_out, dbg=None):
    phase_load_and_prep(p, dram, scr, sb, ps)
    p.barrier(sb)
    if PHASES["attn_a"]:
        for seq in range(NSEQ_RUN):
            attn_a_seq(p, dram, scr, sb, ps, seq)
            p.barrier(sb)
    if dbg is not None and DEBUG_POINT == "attn_a":
        dump_debug(p, sb, dbg)
    prep_ffn_weights(p, dram, scr, sb, ps)
    p.barrier(sb)
    if PHASES["ffn1"]:
        for seq in range(NSEQ_RUN):
            ffn_seq(p, dram, scr, sb, ps, seq)
            p.barrier(sb)
    if dbg is not None and DEBUG_POINT == "ffn1":
        dump_debug(p, sb, dbg)
    if PHASES["attn_b"]:
        for seq in range(NSEQ_RUN):
            attn_b_seq(p, dram, scr, sb, ps, seq)
            p.barrier(sb)
    if PHASES["ffn2"]:
        for seq in range(NSEQ_RUN):
            ffn_seq(p, dram, scr, sb, ps, seq)
            p.barrier(sb)

    phase_output(p, sb, y_out)


def dump_debug(p, sb, dbg):
    pairs = [("d_stats", sb["stats"][:]), ("d_wmed", sb["wmed"][:]),
             ("d_actfm", sb["actfm"][:]), ("d_wbig", sb["wbig"][:, 0:4096]),
             ("d_rows", sb["rows"][:]), ("d_cols", sb["cols"][:]),
             ("d_cols2", sb["cols2"][:]), ("d_b1col", sb["b1col"][:]),
             ("d_biasT", sb["biasT"][:]), ("d_gates", sb["gates"][:]),
             ("d_hbuf", sb["hbuf"][:]), ("d_rows32", sb["rows32"][:]),
             ("d_rwT", sb["rwT"][:])]
    for nm, ap in pairs:
        p.dma("sync", dbg[nm], ap)
    p.dma("sync", dbg["d_bvec"], p.ns["bvec_scr"])
    p.dma("sync", dbg["d_w1T"], p.ns["w1T_scr"][0][0:128, :])
    p.wait("sync", "sS", p.tag("sS"))
    p.barrier(sb)



def ln_stats_seq(p, sb, seq, double, eps1, eps2):
    xres, stats, scr32, hbuf = sb["xres"], sb["stats"], sb["scr32"], sb["hbuf"]
    n = float(D)
    ssq_rd = {0: None, 1: None}
    for ci in range(CH_SEQ):
        sl = ci % 2
        xc = xres[:, (seq * CH_SEQ + ci) * D:(seq * CH_SEQ + ci + 1) * D]
        sq = hbuf[:, sl * D:(sl + 1) * D]
        p.wait("scalar", "sV", ssq_rd[sl])
        p.act(sq, xc, AF.Square, inc=True)
        sq_tag = p.tag("sA")
        p.vred(stats[:, 64 + ci:65 + ci], xc, ALU.add)
        p.wait("vector", "sA", sq_tag)
        p.vred(stats[:, 80 + ci:81 + ci], sq, ALU.add, inc=True)
        ssq_rd[sl] = p.tag("sV")
    p.wait("vector", "sV", p.tag("sV"))
    p.vts1(stats[:, 0:16], stats[:, 64:80], 1.0 / n, ALU.mult, inc=True)
    p.wait("vector", "sV", p.tag("sV"))
    p.vtt(stats[:, 128:144], stats[:, 0:16], stats[:, 0:16], ALU.mult,
          inc=True)
    p.wait("vector", "sV", p.tag("sV"))
    p.vstt(stats[:, 144:160], stats[:, 80:96], 1.0 / n, stats[:, 128:144],
           ALU.mult, ALU.subtract, inc=True)
    p.wait("scalar", "sV", p.tag("sV"))
    p.act(stats[:, 160:176], stats[:, 144:160], AF.Sqrt,
          bias=stats[:, 248:249] if eps1 == EPS_LN else stats[:, 249:250],
          scale=1.0, inc=True)
    p.wait("vector", "sA", p.tag("sA"))
    p.vrecip(stats[:, 16:32], stats[:, 160:176], inc=True)
    p.wait("vector", "sV", p.tag("sV"))
    if not double:
        return
    p.vtt(stats[:, 128:144], stats[:, 16:32], stats[:, 16:32], ALU.mult,
          inc=True)
    p.wait("vector", "sV", p.tag("sV"))
    p.vtt(stats[:, 144:160], stats[:, 144:160], stats[:, 128:144], ALU.mult,
          inc=True)
    p.wait("scalar", "sV", p.tag("sV"))
    p.wait("vector", "sV", p.tag("sV"))
    p.act(stats[:, 160:176], stats[:, 144:160], AF.Sqrt,
          bias=stats[:, 248:249] if eps2 == EPS_LN else stats[:, 249:250],
          scale=1.0, inc=True)
    p.wait("vector", "sA", p.tag("sA"))
    p.vrecip(stats[:, 32:48], stats[:, 160:176], inc=True)
    p.wait("vector", "sV", p.tag("sV"))
    p.vtt(stats[:, 48:64], stats[:, 16:32], stats[:, 32:48], ALU.mult,
          inc=True)
    p.wait("vector", "sV", p.tag("sV"))


def build_fm_chunk(p, sb, ps, seq, ci, dst_view, double, slot):
    xres, stats, scr32, hbuf = sb["xres"], sb["stats"], sb["scr32"], sb["hbuf"]
    ident, pTP = sb["ident"], ps["pTP"]
    st = p.ns.setdefault("fm", {"evac": None, "slot": {}})
    xc = xres[:, (seq * CH_SEQ + ci) * D:(seq * CH_SEQ + ci + 1) * D]
    stage = hbuf[:, slot * D:(slot + 1) * D]
    p.wait("vector", "sPE", st["slot"].get(slot))
    rcol = stats[:, 48 + ci:49 + ci] if double else stats[:, 16 + ci:17 + ci]
    p.vts(stage, xc, stats[:, ci:ci + 1], rcol, ALU.subtract, ALU.mult,
          inc=True)
    aff = p.tag("sV")
    p.wait("tensor", "sV", aff)
    p.wait("tensor", "sV", st["evac"])
    p.wait("tensor", "sA", st.get("evac_a"))
    for dc in range(DC):
        p.tp(pTP[:, dc * 128:(dc + 1) * 128], stage[:, dc * 128:(dc + 1) * 128],
             ident[:], inc=(dc == DC - 1))
    tptag = p.tag("sPE")
    st["slot"][slot] = tptag
    p.wait("vector", "sPE", tptag)
    ptv = pTP[:].rearrange("q (dc t) -> q dc t", t=128)
    p.vcopy(dst_view, ptv, inc=True)
    st["evac"] = p.tag("sV")
    return p.tag("sV")



def phase_load_and_prep(p, dram, scr, sb, ps):
    ns = p.ns
    xres, wbig, wmed = sb["xres"], sb["wbig"], sb["wmed"]
    cols, cols2, colsb = sb["cols"], sb["cols2"], sb["colsb"]

    xv = dram["x"].rearrange("(c p) d -> p c d", p=128)
    for i in range(4):
        p.dma("gpsimd", xres[:, i * 8 * D:(i + 1) * 8 * D],
              xv[:, i * 8:(i + 1) * 8, :])

    p.gmemset(sb["ident"][:], 0.0)
    p.gaffine(sb["ident"][:], sb["ident"][:], [[-1, 128]], ALU.not_equal, 1.0,
              0, 1)
    p.gmemset(sb["cmask"][:], 0.0)
    p.gaffine(sb["cmask"][:], sb["cmask"][:], [[-1, 128]], ALU.is_ge, NEGBIG,
              0, 1)
    p.gmemset(sb["mask01"][:], 1.0)
    p.gaffine(sb["mask01"][:], sb["mask01"][:], [[-1, 128]], ALU.is_ge, 0.0,
              0, 1, inc=True)
    p.vmemset(sb["onesb"][:], 1.0)
    p.vmemset(sb["colsb"][:, 80:81], 1.0)
    p.vmemset(sb["stats"][:, 248:249], EPS_LN)
    p.vmemset(sb["stats"][:, 249:250], EPS_FFN, inc=True)

    for nm_in, nm_out in (("W_h", "Whc"), ("W_qk", "Wqkc"), ("W_out", "Woutc"),
                          ("W_dq", "Wdqc"), ("W_dkv", "Wdkvc"),
                          ("W_uq", "Wuqc"), ("W_uk", "Wukc"),
                          ("W_uv", "Wuvc"), ("W_o", "Woc")):
        p.dma("gpsimd", scr[nm_out], dram[nm_in])
    ns["small_cast"] = p.tag("sG")
    for e in range(E):
        p.dma("gpsimd", scr["w1c"][e], dram["w1"][e])
        p.dma("gpsimd", scr["w2c"][e], dram["w2"][e])
    ns["big_cast"] = p.tag("sG")

    for dc in range(DC):
        p.dma("sync", cols[:, C_LNG + dc:C_LNG + dc + 1],
              dram["ln_g"][dc * 128:(dc + 1) * 128])
        p.dma("sync", cols[:, C_LNB + dc:C_LNB + dc + 1],
              dram["ln_b"][dc * 128:(dc + 1) * 128])
        p.dma("sync", cols[:, C_RLS + dc:C_RLS + dc + 1],
              dram["r_ls"][dc * 128:(dc + 1) * 128])
        p.dma("gpsimd", colsb[:, dc:dc + 1],
              dram["ln_b"][dc * 128:(dc + 1) * 128])
        p.dma("gpsimd", colsb[:, 8 + dc:9 + dc],
              dram["r_lb"][dc * 128:(dc + 1) * 128])
    for h in range(4):
        p.dma("sync", cols[:, C_GAM + h:C_GAM + h + 1], dram["os_gamma"][h, :])
        p.dma("sync", cols[:, C_BET + h:C_BET + h + 1], dram["os_beta"][h, :])
    for e in range(E):
        for dc in range(DC):
            p.dma("sync", cols[:, C_ELS + e * 8 + dc:C_ELS + e * 8 + dc + 1],
                  dram["e_ls"][e, dc * 128:(dc + 1) * 128])
            p.dma("gpsimd", colsb[:, 16 + e * 8 + dc:17 + e * 8 + dc],
                  dram["e_lb"][e, dc * 128:(dc + 1) * 128])
    for e in range(E):
        for hc in range(32):
            p.dma("sync", sb["b1col"][:, e * 32 + hc:e * 32 + hc + 1],
                  dram["b1"][e, hc * 128:(hc + 1) * 128])
    p.dma("gpsimd", sb["relsb"][:], dram["rel_emb"])
    p.dma("gpsimd", sb["ohsb"][:, 0:511], dram["OH"])
    p.dma("gpsimd", sb["rows"][0:1, 2048:3072], dram["b_out"][None, :])
    ns["cols_f32"] = p.tag("sS")
    ns["cols_bf"] = p.tag("sG")

    p.wait("vector", "sS", ns["cols_f32"])
    p.vts1(cols[:, C_GAM + 1:C_GAM + 2], cols[:, C_GAM + 1:C_GAM + 2],
           1.0 / S, ALU.mult)
    p.vts1(cols[:, C_BET + 1:C_BET + 2], cols[:, C_BET + 1:C_BET + 2],
           1.0 / S, ALU.mult, inc=True)

    p.wait("tensor", "sG", ns["cols_bf"])
    p.mm(ps["pA"][0:1, 0:511], sb["relsb"][:], sb["ohsb"][:, 0:511],
         start=True, stop=True, inc=True)
    t = p.tag("sPE")
    p.wait("vector", "sPE", t)
    p.vcopy(sb["rows32"][0:1, 0:511], ps["pA"][0:1, 0:511], inc=True)
    bvec_evac = p.tag("sV")
    t = p.tag("sV")
    p.wait("sync", "sV", t)
    p.dma("sync", scr["bvec"], sb["rows32"][0:1, 0:511])
    p.ns["bvec_scr"] = scr["bvec"]
    t = p.tag("sS")
    p.wait("gpsimd", "sS", t)
    for jc in range(2):
        for row in range(128):
            j = jc * 128 + row
            p.dma("gpsimd", sb["biasT"][row:row + 1, jc * 256:jc * 256 + 256],
                  scr["bvec"][0:1, 255 - j:255 - j + 256])

    p.wait("sync", "sG", ns["small_cast"])
    whv = scr["Whc"].rearrange("(dcb q) h -> q dcb h", q=128)
    p.dma("sync", wbig[:].rearrange("q (dcb h) -> q dcb h", h=4096), whv)
    wqv = scr["Wqkc"].rearrange("(dcb q) h -> q dcb h", q=128)
    p.dma("sync", wmed[:, 0:1024].rearrange("q (dcb h) -> q dcb h", h=128),
          wqv)
    ns["wh_raw"] = p.tag("sS")

    p.wait("tensor", "sS", ns["wh_raw"])
    p.wait("tensor", "sG", ns["cols_bf"])
    prev_use = {0: bvec_evac, 1: None}
    for nt in range(8):
        pt = ps["pA"] if nt % 2 == 0 else ps["pB"]
        sl = pt[0:1, 0:512]
        p.wait("tensor", "sV", prev_use[nt % 2])
        for dc in range(DC):
            p.mm(sl, colsb[:, dc:dc + 1],
                 wbig[:, dc * 4096 + nt * 512:dc * 4096 + nt * 512 + 512],
                 start=(dc == 0), stop=(dc == DC - 1), inc=(dc == DC - 1))
        mmtag = p.tag("sPE")
        p.wait("sync", "sV", p.ns.get("bh_scr_free"))
        p.dma("sync", sb["scr32"][0:1, 0:512],
              dram["b_h"][None, nt * 512:(nt + 1) * 512])
        ld = p.tag("sS")
        p.wait("vector", "sPE", mmtag)
        p.wait("vector", "sS", ld)
        p.vtt(sb["scr32"][0:1, 0:512], sl, sb["scr32"][0:1, 0:512], ALU.add,
              inc=True)
        prev_use[nt % 2] = p.tag("sV")
        p.ns["bh_scr_free"] = p.tag("sV")
        p.wait("vector", "sV", p.tag("sV"))
        if nt < 4:
            p.vcopy(sb["rows"][0:1, nt * 512:(nt + 1) * 512],
                    sb["scr32"][0:1, 0:512], inc=True)
            prev_use[nt % 2] = p.tag("sV")
            p.ns["bh_scr_free"] = p.tag("sV")
        else:
            t = p.tag("sV")
            p.wait("sync", "sV", t)
            p.dma("sync", scr["rowscr"][0:1, nt * 512:(nt + 1) * 512],
                  sb["scr32"][0:1, 0:512])
    t = p.tag("sS")
    p.wait("sync", "sS", t)
    for gc in range(16):
        a = 2048 + gc * 128
        p.dma("sync", cols2[:, gc:gc + 1], scr["rowscr"][0:1, a:a + 128])

    for dc in range(DC):
        p.mm(ps["pC"][:, 0:1], wmed[:, dc * 128:(dc + 1) * 128],
             colsb[:, dc:dc + 1], start=(dc == 0), stop=(dc == DC - 1),
             inc=(dc == DC - 1))
    t = p.tag("sPE")
    p.dma("sync", cols[:, C_BQK + 1:C_BQK + 2], dram["b_qk"][:])
    ld = p.tag("sS")
    p.wait("vector", "sPE", t)
    p.wait("vector", "sS", ld)
    p.vtt(cols[:, C_BQK:C_BQK + 1], ps["pC"][:, 0:1],
          cols[:, C_BQK + 1:C_BQK + 2], ALU.add, inc=True)
    bqk_read = p.tag("sV")

    p.wait("vector", "sS", ns["cols_f32"])
    p.wait("vector", "sPE", t)
    for dc in range(DC):
        p.vts1(wbig[:, dc * 4096:(dc + 1) * 4096],
               wbig[:, dc * 4096:(dc + 1) * 4096],
               cols[:, C_LNG + dc:C_LNG + dc + 1], ALU.mult)
        p.vts1(wmed[:, dc * 128:(dc + 1) * 128],
               wmed[:, dc * 128:(dc + 1) * 128],
               cols[:, C_LNG + dc:C_LNG + dc + 1], ALU.mult,
               inc=(dc == DC - 1))

    for dc in range(DC):
        for e in range(E):
            p.dma("gpsimd", sb["rwT"][:, dc * 8 + e:dc * 8 + e + 1],
                  dram["r_w"][e, dc * 128:(dc + 1) * 128])
    t = p.tag("sG")
    p.wait("tensor", "sG", t)
    p.wait("tensor", "sV", bqk_read)
    for dc in range(DC):
        p.mm(ps["pC"][0:8, 8:9], sb["rwT"][:, dc * 8:(dc + 1) * 8],
             colsb[:, 8 + dc:9 + dc], start=(dc == 0), stop=(dc == DC - 1),
             inc=(dc == DC - 1))
    t = p.tag("sPE")
    p.dma("sync", sb["rows32"][0:1, 520:528], dram["r_b"][None, :])
    ld = p.tag("sS")
    p.wait("vector", "sPE", t)
    p.wait("vector", "sS", ld)
    p.vcopy(sb["scr32"][0:8, 512:513], ps["pC"][0:8, 8:9], inc=True)
    t = p.tag("sV")
    p.wait("sync", "sV", t)
    p.dma("sync", scr["rowscr"][0:1, 3584:3592], sb["scr32"][0:8, 512:513])
    t = p.tag("sS")
    p.wait("sync", "sS", t)
    p.dma("sync", sb["rows32"][0:1, 512:520], scr["rowscr"][0:1, 3584:3592])
    t = p.tag("sS")
    p.wait("vector", "sS", t)
    p.vtt(sb["rows32"][0:1, 512:520], sb["rows32"][0:1, 512:520],
          sb["rows32"][0:1, 520:528], ALU.add, inc=True)
    p.wait("vector", "sV", p.tag("sV"))
    p.vcopy(sb["rows"][0:1, 3072:3080], sb["rows32"][0:1, 512:520], inc=True)
    for dc in range(DC):
        p.vts1(sb["rwT"][:, dc * 8:(dc + 1) * 8],
               sb["rwT"][:, dc * 8:(dc + 1) * 8],
               cols[:, C_RLS + dc:C_RLS + dc + 1], ALU.mult,
               inc=(dc == DC - 1))
    p.ns["pe_last"] = p.tag("sPE")

def prep_ffn_weights(p, dram, scr, sb, ps):
    p.ns["w1T_scr"] = scr["w1T"]
    """w1T scratch: transpose-DMA strips of w1c, fold e_ls rows, store.
    b1col += lb_e @ W1^T via [128,1]-column matmuls into pC[:,128:160].
    PSUM note: single start=True on the very first MM of the pC session."""
    ns = p.ns
    hbuf, b1col, cols, colsb = sb["hbuf"], sb["b1col"], sb["cols"], sb["colsb"]
    p.wait("sync", "sG", ns["big_cast"])
    store_tag = None
    scale_tag = None
    b1_use = None
    for e in range(E):
        mm_tag = None
        for dc in range(DC):
            p.wait("sync", "sS", store_tag)
            p.wait("sync", "sV", scale_tag)
            p.dma("sync", hbuf[:], scr["w1c"][e][:, dc * 128:(dc + 1) * 128],
                  transpose=True)
            ld = p.tag("sS")
            p.wait("tensor", "sS", ld)
            if dc == 0:
                p.wait("tensor", "sV", b1_use)
            for hc in range(32):
                p.mm(ps["pC"][:, 128 + hc:129 + hc],
                     hbuf[:, hc * 128:(hc + 1) * 128],
                     colsb[:, 16 + e * 8 + dc:17 + e * 8 + dc],
                     start=(dc == 0 and hc == 0),
                     stop=(dc == DC - 1 and hc == 31),
                     inc=(hc == 31))
            mm_tag = p.tag("sPE")
            p.wait("vector", "sPE", mm_tag)
            p.wait("vector", "sS", ld)
            p.vts1(hbuf[:], hbuf[:],
                   cols[:, C_ELS + e * 8 + dc:C_ELS + e * 8 + dc + 1],
                   ALU.mult, inc=True)
            scale_tag = p.tag("sV")
            p.wait("sync", "sV", scale_tag)
            p.dma("sync", scr["w1T"][e][dc * 128:(dc + 1) * 128, :], hbuf[:])
            store_tag = p.tag("sS")
        p.wait("vector", "sPE", mm_tag)
        p.vtt(b1col[:, e * 32:(e + 1) * 32], b1col[:, e * 32:(e + 1) * 32],
              ps["pC"][:, 128:160], ALU.add, inc=True)
        b1_use = p.tag("sV")
    p.ns["pe_last"] = p.tag("sPE")



def attn_a_seq(p, dram, scr, sb, ps, seq):
    ns = p.ns
    xres, wbig, wmed, actfm = sb["xres"], sb["wbig"], sb["wmed"], sb["actfm"]
    cols, cols2, stats, scr32 = (sb["cols"], sb["cols2"], sb["stats"],
                                 sb["scr32"])
    onesb, rows = sb["onesb"], sb["rows"]
    pA, pB, pC, pE2, pF2 = ps["pA"], ps["pB"], ps["pC"], ps["pE2"], ps["pF2"]

    ln_stats_seq(p, sb, seq, True, EPS_LN, EPS_LN)

    xa_view = actfm[:, 0:2048].rearrange("q (dc t) -> q dc t", t=256)

    def build_xa(g):
        for tc in range(2):
            t = build_fm_chunk(p, sb, ps, seq, g * 2 + tc,
                               xa_view[:, :, tc * 128:(tc + 1) * 128], True, tc)
        return t

    def v_tm_group(xa_tag, bankfree):
        for tc in range(2):
            for nt in range(4):
                bk = (tc * 4 + nt) % 2
                pt = pA if bk == 0 else pB
                p.wait("tensor", "sV", xa_tag)
                p.wait("tensor", "sA", bankfree[bk])
                for dc in range(DC):
                    p.mm(pt[:], xa_view[:, dc, tc * 128:(tc + 1) * 128],
                         wbig[:, dc * 4096 + nt * 512:
                              dc * 4096 + nt * 512 + 512],
                         start=(dc == 0), stop=False)
                p.mm(pt[:], onesb[:], rows[0:1, nt * 512:(nt + 1) * 512],
                     start=False, stop=True, inc=True)
                mmt = p.tag("sPE")
                p.wait("scalar", "sPE", mmt)
                p.act(actfm[:, AF_V + tc * 2048 + nt * 512:
                            AF_V + tc * 2048 + nt * 512 + 512],
                      pt[:], AF.Silu, inc=True)
                bankfree[bk] = p.tag("sA")
        return p.tag("sA")

    st = {"vbank": {0: None, 1: None}, "qkfree": None}
    for g in range(NG):
        xa_tag = build_xa(g)
        p.wait("tensor", "sV", xa_tag)
        p.wait("tensor", "sA", st["qkfree"])
        for dc in range(DC):
            p.mm(pC[:, 0:256], wmed[:, dc * 128:(dc + 1) * 128],
                 xa_view[:, dc, :], start=(dc == 0), stop=(dc == DC - 1),
                 inc=(dc == DC - 1))
        qmm = p.tag("sPE")
        p.wait("scalar", "sPE", qmm)
        p.act(wmed[:, WM_QKACT + g * 256:WM_QKACT + (g + 1) * 256],
              pC[:, 0:256], AF.Silu, bias=cols[:, C_BQK:C_BQK + 1], scale=1.0,
              inc=True)
        st["qkfree"] = p.tag("sA")
        qk_tag = st["qkfree"]
        vev = v_tm_group(xa_tag, st["vbank"])
        p.wait("vector", "sA", qk_tag)
        p.wait("vector", "sPE", ns.get("lkt_pe"))
        p.vts(sb["hbuf"][:, 2048:2304],
              wmed[:, WM_QKACT + g * 256:WM_QKACT + (g + 1) * 256],
              cols[:, C_GAM + 3:C_GAM + 4], cols[:, C_BET + 3:C_BET + 4],
              ALU.mult, ALU.add, inc=True)
        lk = p.tag("sV")
        fmst = p.ns["fm"]
        p.wait("tensor", "sV", lk)
        p.wait("tensor", "sV", fmst["evac"])
        for tc in range(2):
            p.tp(ps["pTP"][:, tc * 128:(tc + 1) * 128],
                 sb["hbuf"][:, 2048 + tc * 128:2048 + (tc + 1) * 128],
                 sb["ident"][:], inc=(tc == 1))
        ns["lkt_pe"] = p.tag("sPE")
        p.wait("vector", "sPE", ns["lkt_pe"])
        p.vcopy(wmed[:, WM_LKT:WM_LKT + 256], ps["pTP"][:, 0:256], inc=True)
        fmst["evac"] = p.tag("sV")
        lkt_tag = p.tag("sV")
        p.wait("tensor", "sV", lkt_tag)
        p.wait("tensor", "sA", vev)
        for tc in range(2):
            for nt in range(4):
                pt = pE2 if nt < 2 else pF2
                sl = pt[:, (nt % 2) * 512:(nt % 2) * 512 + 512]
                p.mm(sl, wmed[:, WM_LKT + tc * 128:WM_LKT + (tc + 1) * 128],
                     actfm[:, AF_V + tc * 2048 + nt * 512:
                           AF_V + tc * 2048 + nt * 512 + 512],
                     start=(g == 0 and tc == 0),
                     stop=(g == NG - 1 and tc == 1),
                     inc=(g == NG - 1 and tc == 1 and nt == 3))
    linkv_mm = p.tag("sPE")
    p.wait("vector", "sPE", linkv_mm)
    p.vcopy(wmed[:, WM_LINKV:WM_LINKV + 1024], pE2[:], inc=True)
    p.vcopy(wmed[:, WM_LINKV + 1024:WM_LINKV + 2048], pF2[:], inc=True)
    linkv_tag = p.tag("sV")

    st2 = {"vbank": {0: None, 1: None}, "gslot": {0: None, 1: None},
           "qlslot": {0: None, 1: None, 2: None},
           "wout": {0: None, 1: None, 2: None},
           "resfree": {0: linkv_tag, 1: linkv_tag}}
    woutv = scr["Woutc"].rearrange("(ecb q) d2 -> q ecb d2", q=128)
    for g in range(NG):
        xa_tag = build_xa(g)
        vev = v_tm_group(xa_tag, st2["vbank"])
        for gc in range(16):
            bk = gc % 2
            pt = pC if bk == 0 else pA
            sl = pt[:, 0:256]
            p.wait("tensor", "sA", vev)
            p.wait("tensor", "sA", st2["gslot"][bk])
            for dc in range(DC):
                p.mm(sl, wbig[:, dc * 4096 + 2048 + gc * 128:
                              dc * 4096 + 2048 + (gc + 1) * 128],
                     xa_view[:, dc, :], start=(dc == 0), stop=(dc == DC - 1),
                     inc=(dc == DC - 1))
            gmm = p.tag("sPE")
            p.wait("scalar", "sPE", gmm)
            p.act(actfm[:, AF_GATE + gc * 256:AF_GATE + (gc + 1) * 256], sl,
                  AF.Silu, bias=cols2[:, gc:gc + 1], scale=1.0, inc=True)
            st2["gslot"][bk] = p.tag("sA")
        gate_done = p.tag("sA")
        qk_g = wmed[:, WM_QKACT + g * 256:WM_QKACT + (g + 1) * 256]
        p.vts(wmed[:, WM_QQ:WM_QQ + 256], qk_g, cols[:, C_GAM:C_GAM + 1],
              cols[:, C_BET:C_BET + 1], ALU.mult, ALU.add)
        p.vts(wmed[:, WM_QKD:WM_QKD + 256], qk_g,
              cols[:, C_GAM + 2:C_GAM + 3], cols[:, C_BET + 2:C_BET + 3],
              ALU.mult, ALU.add)
        p.vts(wmed[:, WM_LQ:WM_LQ + 256], qk_g, cols[:, C_GAM + 1:C_GAM + 2],
              cols[:, C_BET + 1:C_BET + 2], ALU.mult, ALU.add, inc=True)
        heads_tag = p.tag("sV")
        p.wait("tensor", "sV", heads_tag)
        p.wait("tensor", "sA", gate_done)
        for jc in range(2):
            p.mm(pC[:, jc * 256:(jc + 1) * 256],
                 wmed[:, WM_QKD + jc * 128:WM_QKD + (jc + 1) * 128],
                 wmed[:, WM_QQ:WM_QQ + 256], start=True, stop=True,
                 inc=(jc == 1))
        sim_tag = p.tag("sPE")
        p.wait("vector", "sPE", sim_tag)
        for jc in range(2):
            p.vstt(scr32[:, jc * 256:(jc + 1) * 256],
                   pC[:, jc * 256:(jc + 1) * 256], 1.0 / G,
                   sb["biasT"][:, jc * 256:(jc + 1) * 256], ALU.mult, ALU.add)
            p.vstt(wmed[:, WM_ATT + jc * 256:WM_ATT + (jc + 1) * 256],
                   scr32[:, jc * 256:(jc + 1) * 256], 0.0,
                   scr32[:, jc * 256:(jc + 1) * 256], ALU.max, ALU.mult,
                   inc=(jc == 1))
        att_tag = p.tag("sV")
        for ec in range(16):
            bk = ec % 3
            pt = (pA, pB, pC)[bk]
            sl = pt[:, 0:256]
            p.wait("tensor", "sV", st2["qlslot"][bk])
            p.wait("tensor", "sV", att_tag)
            for tc in range(2):
                p.mm(sl, actfm[:, AF_V + tc * 2048 + ec * 128:
                               AF_V + tc * 2048 + (ec + 1) * 128],
                     wmed[:, WM_ATT + tc * 256:WM_ATT + (tc + 1) * 256],
                     start=(tc == 0), stop=False)
            p.mm(sl, wmed[:, WM_LINKV + ec * 128:WM_LINKV + (ec + 1) * 128],
                 wmed[:, WM_LQ:WM_LQ + 256], start=False, stop=True, inc=True)
            qlmm = p.tag("sPE")
            p.wait("vector", "sPE", qlmm)
            p.vtt(actfm[:, AF_GATED + ec * 256:AF_GATED + (ec + 1) * 256], sl,
                  actfm[:, AF_GATE + ec * 256:AF_GATE + (ec + 1) * 256],
                  ALU.mult, inc=True)
            st2["qlslot"][bk] = p.tag("sV")
        gated_tag = p.tag("sV")
        for ec in range(16):
            slot = ec % 3
            q = "sync" if ec % 2 == 0 else "scalar"
            p.wait(q, "sPE", st2["wout"][slot])
            p.dma(q, wmed[:, WM_WOUT + slot * 1024:
                          WM_WOUT + (slot + 1) * 1024], woutv[:, ec, :])
            dtag = p.tag(DMA_Q[q])
            wsl = wmed[:, WM_WOUT + slot * 1024:WM_WOUT + (slot + 1) * 1024]
            p.wait("tensor", DMA_Q[q], dtag)
            if ec == 0:
                p.wait("tensor", "sV", gated_tag)
                p.wait("tensor", "sV", st2["resfree"][0])
                p.wait("tensor", "sV", st2["resfree"][1])
            for tc in range(2):
                pt = pE2 if tc == 0 else pF2
                gsl = actfm[:, AF_GATED + ec * 256 + tc * 128:
                            AF_GATED + ec * 256 + (tc + 1) * 128]
                for half in range(2):
                    p.mm(pt[:, half * 512:(half + 1) * 512], gsl,
                         wsl[:, half * 512:(half + 1) * 512],
                         start=(ec == 0), stop=False,
                         inc=(tc == 1 and half == 1 and ec < 15))
            if ec == 15:
                for tc in range(2):
                    pt = pE2 if tc == 0 else pF2
                    for half in range(2):
                        p.mm(pt[:, half * 512:(half + 1) * 512], onesb[:],
                             rows[0:1, 2048 + half * 512:
                                  2048 + (half + 1) * 512],
                             start=False, stop=True,
                             inc=(tc == 1 and half == 1))
            st2["wout"][slot] = p.tag("sPE")
        wout_mm = p.tag("sPE")
        for tc in range(2):
            ci = g * 2 + tc
            c = seq * CH_SEQ + ci
            pt = pE2 if tc == 0 else pF2
            xc = xres[:, c * D:(c + 1) * D]
            p.wait("vector", "sPE", wout_mm)
            p.vts(scr32[:, 0:D], xc, stats[:, ci:ci + 1],
                  stats[:, 16 + ci:17 + ci], ALU.subtract, ALU.mult)
            p.vtt(scr32[:, 0:D], scr32[:, 0:D], pt[:], ALU.add)
            p.vtt(xc, scr32[:, 0:D], xc, ALU.add, inc=True)
            st2["resfree"][tc] = p.tag("sV")
    p.ns["pe_last"] = p.tag("sPE")



def ffn_seq(p, dram, scr, sb, ps, seq):
    ns = p.ns
    xres, wbig, wmed, actfm = sb["xres"], sb["wbig"], sb["wmed"], sb["actfm"]
    hbuf, stats, scr32, gates = (sb["hbuf"], sb["stats"], sb["scr32"],
                                 sb["gates"])
    gatesT = sb["wmed"][0:8, 0:2048]
    b2sb = sb["wmed"][0:8, 2048:3072]
    onesb, rows = sb["onesb"], sb["rows"]
    p.dma("gpsimd", b2sb, dram["b2"])
    b2_tag = p.tag("sG")
    pA, pB, pC, pE2, pF2, pTP = (ps["pA"], ps["pB"], ps["pC"], ps["pE2"],
                                 ps["pF2"], ps["pTP"])

    ln_stats_seq(p, sb, seq, True, EPS_LN, EPS_FFN)

    xh_view = actfm[:].rearrange("q (dc t) -> q dc t", t=2048)
    xhat_tag = None
    for ci in range(CH_SEQ):
        xhat_tag = build_fm_chunk(p, sb, ps, seq, ci,
                                  xh_view[:, :, ci * 128:(ci + 1) * 128],
                                  True, ci % 2)

    lfree = {0: None, 1: None}
    for ci in range(CH_SEQ):
        bk = ci % 2
        pt = pC if bk == 0 else pA
        sl = pt[:, 0:8]
        p.wait("tensor", "sV", xhat_tag)
        p.wait("tensor", "sV", lfree[bk])
        for dc in range(DC):
            p.mm(sl, xh_view[:, dc, ci * 128:(ci + 1) * 128],
                 sb["rwT"][:, dc * 8:(dc + 1) * 8], start=(dc == 0),
                 stop=False)
        p.mm(sl, onesb[:], rows[0:1, 3072:3080], start=False, stop=True,
             inc=True)
        lmm = p.tag("sPE")
        p.wait("vector", "sPE", lmm)
        p.vcopy(gates[:, ci * 8:(ci + 1) * 8], sl, inc=True)
        lfree[bk] = p.tag("sV")
    logits_tag = p.tag("sV")
    p.wait("scalar", "sV", logits_tag)
    p.act(gates[:], gates[:], AF.Exp, inc=True)
    t = p.tag("sA")
    p.wait("vector", "sA", t)
    gv = gates[:].rearrange("q (c e) -> q c e", e=8)
    m1 = stats[:, 176:192]
    m2 = stats[:, 192:208]
    dinv = stats[:, 208:224]
    w1t = stats[:, 224:240]
    p.vred(m1, gv, ALU.max, inc=True)
    p.wait("vector", "sV", p.tag("sV"))
    m1b = m1.unsqueeze(2).broadcast_to([128, 16, 8])
    is1 = scr32[:, 0:128].rearrange("q (c e) -> q c e", e=8)
    p.vtt(is1, gv, m1b, ALU.is_equal, inc=True)
    p.wait("vector", "sV", p.tag("sV"))
    masked = scr32[:, 128:256].rearrange("q (c e) -> q c e", e=8)
    p.vstt(masked, is1, NEGBIG, gv, ALU.mult, ALU.add, inc=True)
    p.wait("vector", "sV", p.tag("sV"))
    p.vred(m2, masked, ALU.max, inc=True)
    p.wait("vector", "sV", p.tag("sV"))
    m2b = m2.unsqueeze(2).broadcast_to([128, 16, 8])
    is2 = scr32[:, 256:384].rearrange("q (c e) -> q c e", e=8)
    p.vtt(is2, masked, m2b, ALU.is_equal, inc=True)
    p.vtt(dinv, m1, m2, ALU.add, inc=True)
    p.wait("vector", "sV", p.tag("sV"))
    p.vrecip(dinv, dinv, inc=True)
    p.wait("vector", "sV", p.tag("sV"))
    p.vtt(w1t, m1, dinv, ALU.mult)
    p.vtt(m2, m2, dinv, ALU.mult, inc=True)
    p.wait("vector", "sV", p.tag("sV"))
    w1b = w1t.unsqueeze(2).broadcast_to([128, 16, 8])
    w2b = m2.unsqueeze(2).broadcast_to([128, 16, 8])
    p.vtt(gv, is1, w1b, ALU.mult)
    p.vtt(is2, is2, w2b, ALU.mult, inc=True)
    p.wait("vector", "sV", p.tag("sV"))
    p.vtt(gv, gv, is2, ALU.add, inc=True)
    gates_tag = p.tag("sV")
    p.wait("vector", "sV", gates_tag)

    fmst = p.ns["fm"]
    p.wait("vector", "sPE", fmst["slot"].get(0))
    p.vcopy(hbuf[:, 0:128], gates[:], inc=True)
    gt0 = p.tag("sV")
    for rnd in range(2):
        p.wait("tensor", "sV", gt0)
        p.wait("tensor", "sV", fmst["evac"])
        for k in range(8):
            ci = rnd * 8 + k
            p.tp(pTP[0:8, k * 128:(k + 1) * 128],
                 hbuf[:, ci * 8:(ci + 1) * 8], sb["ident"][:], inc=(k == 7))
        tpt = p.tag("sPE")
        p.wait("vector", "sPE", tpt)
        p.vcopy(gatesT[0:8, rnd * 1024:(rnd + 1) * 1024], pTP[0:8, 0:1024],
                inc=True)
        fmst["evac"] = p.tag("sV")
    gatesT_tag = p.tag("sV")
    fmst["slot"][0] = p.tag("sPE")

    wload = {}
    slot_free = {0: None, 1: None}
    yfree = {0: gatesT_tag, 1: gatesT_tag}
    hfree = {i: None for i in range(4)}
    psfree = {i: None for i in range(3)}

    def issue_wload(e, q):
        slot = (e * NQ + q) % 2
        p.wait("sync", "sPE", slot_free[slot])
        w1v = scr["w1T"][e].rearrange("(dcb q2) h2 -> q2 dcb h2", q2=128)
        p.dma("sync",
              wbig[:, WB_W1Q[0] + slot * 8192:WB_W1Q[0] + slot * 8192 + 8192]
              .rearrange("q2 (dcb h2) -> q2 dcb h2", h2=1024),
              w1v[:, :, q * 1024:(q + 1) * 1024])
        for hc in range(8):
            p.dma("sync",
                  wbig[:, WB_W2Q[0] + slot * 8192 + hc * 1024:
                        WB_W2Q[0] + slot * 8192 + (hc + 1) * 1024],
                  scr["w2c"][e][:, q * 1024 + hc * 128:
                                q * 1024 + (hc + 1) * 128],
                  transpose=True)
        wload[(e, q)] = p.tag("sS")

    issue_wload(0, 0)
    for e in range(E):
        for q in range(NQ):
            slot = (e * NQ + q) % 2
            nxt = (e, q + 1) if q + 1 < NQ else ((e + 1, 0) if e + 1 < E
                                                 else None)
            if nxt is not None:
                issue_wload(*nxt)
            w1sl = wbig[:, WB_W1Q[0] + slot * 8192:
                        WB_W1Q[0] + slot * 8192 + 8192]
            w2sl = wbig[:, WB_W2Q[0] + slot * 8192:
                        WB_W2Q[0] + slot * 8192 + 8192]
            p.wait("tensor", "sS", wload[(e, q)])
            for tb in range(8):
                hact = {}

                def emit_h(hc):
                    hs = hc % 4
                    pb = hc % 3
                    pt = (pA, pB, pC)[pb]
                    sl = pt[:, 0:256]
                    p.wait("tensor", "sA", psfree[pb])
                    for dc in range(DC):
                        p.mm(sl, w1sl[:, dc * 1024 + hc * 128:
                                      dc * 1024 + (hc + 1) * 128],
                             xh_view[:, dc, tb * 256:(tb + 1) * 256],
                             start=(dc == 0), stop=(dc == DC - 1),
                             inc=(dc == DC - 1))
                    hmm = p.tag("sPE")
                    p.wait("scalar", "sPE", hmm)
                    p.wait("scalar", "sPE", hfree[hs])
                    p.act(hbuf[:, hs * 256:(hs + 1) * 256], sl, AF.Gelu,
                          bias=sb["b1col"][:, e * 32 + q * 8 + hc:
                                           e * 32 + q * 8 + hc + 1],
                          scale=1.0, inc=True)
                    psfree[pb] = p.tag("sA")
                    hact[hc] = p.tag("sA")

                def emit_y2(hc):
                    hs = hc % 4
                    p.wait("tensor", "sA", hact[hc])
                    if hc == 0:
                        p.wait("tensor", "sV", yfree[0])
                        p.wait("tensor", "sV", yfree[1])
                    for t128 in range(2):
                        pt2 = pE2 if t128 == 0 else pF2
                        for half in range(2):
                            p.mm(pt2[:, half * 512:(half + 1) * 512],
                                 hbuf[:, hs * 256 + t128 * 128:
                                      hs * 256 + (t128 + 1) * 128],
                                 w2sl[:, hc * 1024 + half * 512:
                                      hc * 1024 + (half + 1) * 512],
                                 start=(hc == 0), stop=(hc == 7),
                                 inc=(t128 == 1 and half == 1))
                    hfree[hs] = p.tag("sPE")

                emit_h(0)
                for hc in range(1, 8):
                    emit_h(hc)
                    emit_y2(hc - 1)
                emit_y2(7)
                y2mm = p.tag("sPE")
                for t128 in range(2):
                    ci = tb * 2 + t128
                    c = seq * CH_SEQ + ci
                    pt2 = pE2 if t128 == 0 else pF2
                    xc = xres[:, c * D:(c + 1) * D]
                    p.wait("vector", "sPE", y2mm)
                    p.vstt(xc, pt2[:], gates[:, ci * 8 + e:ci * 8 + e + 1],
                           xc, ALU.mult, ALU.add, inc=True)
                    yfree[t128] = p.tag("sV")
            slot_free[slot] = p.tag("sPE")
    bfree = {0: yfree[0], 1: yfree[1]}
    for ci in range(CH_SEQ):
        c = seq * CH_SEQ + ci
        bk = ci % 2
        pt2 = pE2 if bk == 0 else pF2
        p.wait("tensor", "sV", bfree[bk])
        for half in range(2):
            p.wait("tensor", "sG", b2_tag)
            p.mm(pt2[:, half * 512:(half + 1) * 512],
                 gatesT[0:8, ci * 128:(ci + 1) * 128],
                 b2sb[:, half * 512:(half + 1) * 512],
                 start=True, stop=True, inc=(half == 1))
        bmm = p.tag("sPE")
        xc = xres[:, c * D:(c + 1) * D]
        p.wait("vector", "sPE", bmm)
        p.vtt(xc, pt2[:], xc, ALU.add, inc=True)
        bfree[bk] = p.tag("sV")
    p.ns["pe_last"] = p.tag("sPE")



def attn_b_seq(p, dram, scr, sb, ps, seq):
    ns = p.ns
    xres, wbig, wmed, actfm = sb["xres"], sb["wbig"], sb["wmed"], sb["actfm"]
    hbuf, stats, scr32 = sb["hbuf"], sb["stats"], sb["scr32"]
    pA, pB, pC, pE2, pF2, pTP = (ps["pA"], ps["pB"], ps["pC"], ps["pE2"],
                                 ps["pF2"], ps["pTP"])

    ln_stats_seq(p, sb, seq, False, EPS_LN, EPS_LN)
    yv = actfm[:].rearrange("q (dc t) -> q dc t", t=2048)
    y_tag = None
    for ci in range(CH_SEQ):
        y_tag = build_fm_chunk(p, sb, ps, seq, ci,
                               yv[:, :, ci * 128:(ci + 1) * 128], False,
                               ci % 2)

    for nm, off in (("Wdqc", WM_DQ), ("Wdkvc", WM_DKV)):
        wv = scr[nm].rearrange("(dcb q2) c2 -> q2 dcb c2", q2=128)
        p.dma("sync", wmed[:, off:off + 2048]
              .rearrange("q2 (dcb c2) -> q2 dcb c2", c2=256), wv)
    dq_tag = p.tag("sS")

    bank = {0: ns["fm"]["evac"], 1: ns["fm"]["evac"]}
    for which, off_w, dst, dsto in ((0, WM_DQ, hbuf, 0),
                                    (1, WM_DKV, wmed, WM_CKV)):
        for cc in range(2):
            for nt in range(4):
                bk = (cc * 4 + nt) % 2
                pt = pA if bk == 0 else pB
                p.wait("tensor", "sS", dq_tag)
                p.wait("tensor", "sV", y_tag)
                p.wait("tensor", "sV", bank[bk])
                for dc in range(DC):
                    p.mm(pt[:], wmed[:, off_w + dc * 256 + cc * 128:
                                     off_w + dc * 256 + (cc + 1) * 128],
                         yv[:, dc, nt * 512:(nt + 1) * 512],
                         start=(dc == 0), stop=(dc == DC - 1),
                         inc=(dc == DC - 1))
                cmm = p.tag("sPE")
                p.wait("vector", "sPE", cmm)
                p.vcopy(dst[:, dsto + cc * 2048 + nt * 512:
                            dsto + cc * 2048 + (nt + 1) * 512], pt[:],
                        inc=True)
                bank[bk] = p.tag("sV")
    c_done_pe = p.tag("sPE")
    c_done = p.tag("sV")

    p.wait("sync", "sPE", c_done_pe)
    for nm, off in (("Wukc", WM_UK), ("Wuvc", WM_UV)):
        wv = scr[nm].rearrange("(ccb q2) d2 -> q2 ccb d2", q2=128)
        p.dma("sync", wmed[:, off:off + 2048]
              .rearrange("q2 (ccb d2) -> q2 ccb d2", d2=1024), wv)
    uk_tag = p.tag("sS")

    for qc in range(8):
        for nt in range(4):
            bk = (qc * 4 + nt) % 2
            pt = pA if bk == 0 else pB
            p.wait("tensor", "sS", uk_tag)
            p.wait("tensor", "sV", c_done)
            p.wait("tensor", "sV", bank[bk])
            for cc in range(2):
                p.mm(pt[:], wmed[:, WM_UK + cc * 1024 + qc * 128:
                                 WM_UK + cc * 1024 + (qc + 1) * 128],
                     wmed[:, WM_CKV + cc * 2048 + nt * 512:
                          WM_CKV + cc * 2048 + (nt + 1) * 512],
                     start=(cc == 0), stop=(cc == 1), inc=(cc == 1))
            kmm = p.tag("sPE")
            p.wait("vector", "sPE", kmm)
            p.vcopy(wbig[:, qc * 2048 + nt * 512:qc * 2048 + (nt + 1) * 512],
                    pt[:], inc=True)
            bank[bk] = p.tag("sV")
    vfree = {0: None, 1: None}
    for tc in range(16):
        bk = tc % 2
        pt2 = pE2 if bk == 0 else pF2
        p.wait("tensor", "sV", vfree[bk])
        for half in range(2):
            for cc in range(2):
                p.mm(pt2[:, half * 512:(half + 1) * 512],
                     wmed[:, WM_CKV + cc * 2048 + tc * 128:
                          WM_CKV + cc * 2048 + (tc + 1) * 128],
                     wmed[:, WM_UV + cc * 1024 + half * 512:
                          WM_UV + cc * 1024 + (half + 1) * 512],
                     start=(cc == 0), stop=(cc == 1),
                     inc=(cc == 1 and half == 1))
        vmm = p.tag("sPE")
        p.wait("vector", "sPE", vmm)
        p.vcopy(wbig[:, 16384 + tc * 1024:16384 + (tc + 1) * 1024], pt2[:],
                inc=True)
        vfree[bk] = p.tag("sV")
    kv_done_pe = p.tag("sPE")
    kv_done = p.tag("sV")

    p.wait("sync", "sPE", kv_done_pe)
    wv = scr["Wuqc"].rearrange("(ccb q2) d2 -> q2 ccb d2", q2=128)
    p.dma("sync", wmed[:, WM_UQ:WM_UQ + 2048]
          .rearrange("q2 (ccb d2) -> q2 ccb d2", d2=1024), wv)
    wv = scr["Woc"].rearrange("(vcb q2) d2 -> q2 vcb d2", q2=128)
    p.dma("sync", actfm[:, 0:8192]
          .rearrange("q2 (vcb d2) -> q2 vcb d2", d2=1024), wv)
    uq_tag = p.tag("sS")

    st = {"qt": {0: None, 1: None}, "exp": None,
          "avfree": {0: None, 1: None}, "linv": None,
          "res": kv_done, "tpdone": None, "pn": {}, "J": {}}
    fmst = p.ns["fm"]

    def emit_head_scores(i, h):
        J = (i + 1) * 128
        njt = (J + 511) // 512
        st["J"][h] = (i, J, njt)
        qc, base = h // 2, (h % 2) * 64
        p.wait("tensor", "sA", st["qt"][0])
        p.wait("tensor", "sA", st["qt"][1])
        p.wait("tensor", "sA", st["exp"])
        if h == 0:
            p.wait("tensor", "sV", st["res"])
        for jt in range(njt):
            N = min(512, J - jt * 512)
            pt2 = pE2 if jt < 2 else pF2
            sl = pt2[:, (jt % 2) * 512:(jt % 2) * 512 + N]
            p.mm(sl, wmed[base:base + 64, WM_QT + qc * 128:
                          WM_QT + (qc + 1) * 128],
                 wbig[base:base + 64, qc * 2048 + jt * 512:
                      qc * 2048 + jt * 512 + N],
                 start=True, stop=True, inc=(jt == njt - 1))
        smm = p.tag("sPE")
        p.wait("vector", "sPE", smm)
        for jt in range(njt):
            N = min(512, J - jt * 512)
            pt2 = pE2 if jt < 2 else pF2
            p.vred(stats[:, 176 + jt:177 + jt],
                   pt2[:, (jt % 2) * 512:(jt % 2) * 512 + N], ALU.max,
                   inc=(jt == njt - 1))
        p.wait("vector", "sV", p.tag("sV"))
        if njt > 1:
            p.vred(stats[:, 180:181], stats[:, 176:176 + njt], ALU.max,
                   inc=True)
            p.wait("vector", "sV", p.tag("sV"))
            p.vts1(stats[:, 181:182], stats[:, 180:181], -0.125, ALU.mult,
                   inc=True)
        else:
            p.vts1(stats[:, 181:182], stats[:, 176:177], -0.125, ALU.mult,
                   inc=True)
        mtag = p.tag("sV")
        p.wait("scalar", "sV", mtag)
        p.wait("scalar", "sPE", st["tpdone"])
        for jt in range(njt):
            N = min(512, J - jt * 512)
            pt2 = pE2 if jt < 2 else pF2
            p.act(wmed[:, WM_PSB + jt * 512:WM_PSB + jt * 512 + N],
                  pt2[:, (jt % 2) * 512:(jt % 2) * 512 + N], AF.Exp,
                  bias=stats[:, 181:182], scale=0.125,
                  inc=(jt == njt - 1))
        exp_tag = p.tag("sA")
        st["exp"] = exp_tag
        p.wait("vector", "sA", exp_tag)
        dj = i * 128
        p.vtt(wmed[:, WM_PSB + dj:WM_PSB + dj + 128],
              wmed[:, WM_PSB + dj:WM_PSB + dj + 128], sb["mask01"][:],
              ALU.mult, inc=True)
        st["pn"][h] = p.tag("sV")
        p.vred(stats[:, 182:183], wmed[:, WM_PSB:WM_PSB + J], ALU.add,
               inc=True)
        p.wait("vector", "sV", p.tag("sV"))
        p.vrecip(stats[:, 183:184], stats[:, 182:183], inc=True)
        st["linv"] = p.tag("sV")

    def emit_head_tpav(h):
        i, J, njt = st["J"][h]
        qc, base = h // 2, (h % 2) * 64
        pn_tag = st["pn"][h]
        for jc in range(i + 1):
            ksl = jc % 8
            p.wait("tensor", "sV", pn_tag)
            if ksl == 0:
                p.wait("tensor", "sV", fmst["evac"])
                p.wait("tensor", "sA", fmst.get("evac_a"))
            p.tp(pTP[:, ksl * 128:(ksl + 1) * 128],
                 wmed[:, WM_PSB + jc * 128:WM_PSB + (jc + 1) * 128],
                 sb["ident"][:], inc=(ksl == 7 or jc == i))
            if ksl == 7 or jc == i:
                tpt = p.tag("sPE")
                p.wait("scalar", "sPE", tpt)
                lo = (jc // 8) * 8
                p.acopy(wmed[:, WM_PT + lo * 128:WM_PT + (jc + 1) * 128],
                        pTP[:, 0:(jc + 1 - lo) * 128], inc=True)
                fmst["evac_a"] = p.tag("sA")
        st["tpdone"] = p.tag("sPE")
        pt_tag = p.tag("sA")
        p.wait("tensor", "sA", pt_tag)
        avp = pB if h % 2 == 0 else pA
        p.wait("tensor", "sA", st["avfree"].get(h % 2))
        for jc in range(i + 1):
            p.mm(avp[:, 256:320],
                 wmed[:, WM_PT + jc * 128:WM_PT + (jc + 1) * 128],
                 wbig[:, 16384 + jc * 1024 + h * 64:
                      16384 + jc * 1024 + (h + 1) * 64],
                 start=(jc == 0), stop=(jc == i), inc=(jc == i))
        avmm = p.tag("sPE")
        p.wait("scalar", "sPE", avmm)
        p.wait("scalar", "sV", st["linv"])
        p.act(wmed[:, WM_AO + h * 64:WM_AO + (h + 1) * 64], avp[:, 256:320],
              AF.Identity, scale=stats[:, 183:184], inc=True)
        st["avfree"][h % 2] = p.tag("sA")

    for i in range(CH_SEQ):
        for qc in range(8):
            bk = qc % 2
            pt = pA if bk == 0 else pC
            sl = pt[:, 0:128]
            p.wait("tensor", "sS", uq_tag)
            p.wait("tensor", "sA", st["qt"][bk])
            p.wait("tensor", "sA", st["avfree"][1])
            for cc in range(2):
                p.mm(sl, wmed[:, WM_UQ + cc * 1024 + qc * 128:
                              WM_UQ + cc * 1024 + (qc + 1) * 128],
                     hbuf[:, cc * 2048 + i * 128:cc * 2048 + (i + 1) * 128],
                     start=(cc == 0), stop=(cc == 1), inc=(cc == 1))
            qmm = p.tag("sPE")
            p.wait("scalar", "sPE", qmm)
            p.acopy(wmed[:, WM_QT + qc * 128:WM_QT + (qc + 1) * 128], sl,
                    inc=True)
            st["qt"][bk] = p.tag("sA")
        for h in range(NHEAD):
            if h > 0:
                emit_head_tpav(h - 1)
            emit_head_scores(i, h)
        emit_head_tpav(NHEAD - 1)
        p.wait("tensor", "sA", st["avfree"][0])
        p.wait("tensor", "sA", st["avfree"][1])
        p.wait("tensor", "sV", fmst["evac"])
        p.wait("tensor", "sA", fmst.get("evac_a"))
        for vc in range(8):
            p.tp(pTP[:, vc * 128:(vc + 1) * 128],
                 wmed[:, WM_AO + vc * 128:WM_AO + (vc + 1) * 128],
                 sb["ident"][:], inc=(vc == 7))
        aot = p.tag("sPE")
        st["tpdone"] = aot
        p.wait("vector", "sPE", aot)
        p.vcopy(wmed[:, 4096:5120], pTP[:], inc=True)
        fmst["evac"] = p.tag("sV")
        ao_tag = p.tag("sV")
        p.wait("tensor", "sV", ao_tag)
        p.wait("tensor", "sV", st["res"])
        for vc in range(8):
            for half in range(2):
                p.mm(pE2[:, half * 512:(half + 1) * 512],
                     wmed[:, 4096 + vc * 128:4096 + (vc + 1) * 128],
                     actfm[:, vc * 1024 + half * 512:
                           vc * 1024 + (half + 1) * 512],
                     start=(vc == 0), stop=(vc == 7),
                     inc=(vc == 7 and half == 1))
        wmm = p.tag("sPE")
        st["tpdone"] = wmm
        c = seq * CH_SEQ + i
        xc = xres[:, c * D:(c + 1) * D]
        p.wait("vector", "sPE", wmm)
        p.vtt(xc, pE2[:], xc, ALU.add, inc=True)
        st["res"] = p.tag("sV")
    p.ns["pe_last"] = p.tag("sPE")

def phase_output(p, sb, y_out):
    xres, s32 = sb["xres"], sb["scr32"]
    yv = y_out.rearrange("(c p) d -> p c d", p=128)
    store = {}
    for c in range(NCH):
        if c >= 1:
            p.wait("scalar", "sS", store[c - 1])
        p.act(s32[:, :], xres[:, c * D:(c + 1) * D], AF.Copy, scale=1.0,
              inc=True)
        t = p.tag("sA")
        p.wait("sync", "sA", t)
        store[c] = p.dma("sync", yv[:, c, :], s32[:, :])
    p.wait("sync", "sS", p.tag("sS"))



def _bucket1d():
    NUM_BUCKETS = 32
    MAX_DIST = 128
    nb = NUM_BUCKETS // 2
    max_exact = nb // 2
    out = np.zeros(511, dtype=np.int64)
    for idx, dd in enumerate(range(-255, 256)):
        n = dd
        ret = nb if n < 0 else 0
        n = abs(n)
        if n < max_exact:
            val = n
        else:
            val = max_exact + int(
                math.log(max(n, 1) / max_exact)
                / math.log(MAX_DIST / max_exact) * (nb - max_exact))
            val = min(val, nb - 1)
        out[idx] = ret + val
    return out


def _oh_const():
    b = _bucket1d()
    oh = np.zeros((32, 511), dtype=np.float32)
    oh[b, np.arange(511)] = 1.0
    return oh * math.sqrt(128.0)


_NC_CACHE = None


def kernel(L0_attn_a_x, L0_attn_a_ln_g, L0_attn_a_ln_b, L0_attn_a_W_h,
           L0_attn_a_b_h, L0_attn_a_W_qk, L0_attn_a_b_qk, L0_attn_a_os_gamma,
           L0_attn_a_os_beta, L0_attn_a_rel_emb, L0_attn_a_W_out,
           L0_attn_a_b_out, L0_attn_b_W_dq, L0_attn_b_W_uq, L0_attn_b_W_dkv,
           L0_attn_b_W_uk, L0_attn_b_W_uv, L0_attn_b_W_o,
           L0_ffn_router_ln_scale, L0_ffn_router_ln_bias, L0_ffn_router_w,
           L0_ffn_router_b, L0_ffn_expert_ln_scale, L0_ffn_expert_ln_bias,
           L0_ffn_w1, L0_ffn_b1, L0_ffn_w2, L0_ffn_b2):
    global _NC_CACHE
    x = np.ascontiguousarray(np.asarray(L0_attn_a_x)).astype(np.float32)

    common = {
        "ln_g": L0_attn_a_ln_g, "ln_b": L0_attn_a_ln_b,
        "W_h": L0_attn_a_W_h, "b_h": L0_attn_a_b_h,
        "W_qk": L0_attn_a_W_qk, "b_qk": L0_attn_a_b_qk,
        "os_gamma": L0_attn_a_os_gamma, "os_beta": L0_attn_a_os_beta,
        "rel_emb": L0_attn_a_rel_emb, "W_out": L0_attn_a_W_out,
        "b_out": L0_attn_a_b_out,
        "W_dq": L0_attn_b_W_dq, "W_uq": L0_attn_b_W_uq,
        "W_dkv": L0_attn_b_W_dkv, "W_uk": L0_attn_b_W_uk,
        "W_uv": L0_attn_b_W_uv, "W_o": L0_attn_b_W_o,
        "r_ls": L0_ffn_router_ln_scale, "r_lb": L0_ffn_router_ln_bias,
        "r_w": L0_ffn_router_w, "r_b": L0_ffn_router_b,
        "e_ls": L0_ffn_expert_ln_scale, "e_lb": L0_ffn_expert_ln_bias,
        "w1": L0_ffn_w1, "b1": L0_ffn_b1, "w2": L0_ffn_w2, "b2": L0_ffn_b2,
        "OH": _oh_const(),
    }
    common = {k: np.ascontiguousarray(np.asarray(v)).astype(np.float32)
              for k, v in common.items()}

    in_maps = []
    for c in range(8):
        m = dict(common)
        m["x"] = np.ascontiguousarray(x[2 * c:2 * c + 2].reshape(T, D))
        in_maps.append(m)

    if _NC_CACHE is None:
        _NC_CACHE = build_nc()
    res = run_bass_kernel_spmd(_NC_CACHE, in_maps, core_ids=list(range(8)))
    out = np.concatenate(
        [r["y"].reshape(NSEQ, S, D) for r in res.results], axis=0)
    return out.astype(np.float32)


# version 08
# speedup vs baseline: 1.1206x; 1.0728x over previous
import math
import numpy as np

import concourse.bass as bass
import concourse.mybir as mybir
from concourse.bass_utils import run_bass_kernel_spmd

F32 = mybir.dt.float32
BF16 = mybir.dt.bfloat16
AF = mybir.ActivationFunctionType
ALU = mybir.AluOpType
AX = mybir.AxisListType

D = 1024
DC = 8
S = 2048
NSEQ = 2
T = 4096
NCH = 32
CH_SEQ = 16
HID = 2048
G = 256
NG = 8
E = 8
H = 4096
NQ = 4
QH = 1024
NHEAD = 16
DK = 64
EPS_LN = 1e-5
EPS_FFN = 1e-6
NEGBIG = -1.0e30

DMA_Q = {"sync": "sS", "gpsimd": "sG", "scalar": "sC"}

PHASES = {"attn_a": True, "ffn1": True, "attn_b": True, "ffn2": True}
DEBUG = False
DEBUG_POINT = "attn_a"
NSEQ_RUN = NSEQ

WM_QK = 0
WM_QKACT = 1024
WM_LINKV = 3072
WM_LKT = 5120
WM_QQ = 5376
WM_QKD = 5632
WM_LQ = 5888
WM_ATT = 6144
WM_WOUT = 6656
WM_DQ = 0
WM_DKV = 2048
WM_CKV = 4096
WM_UK = 0
WM_UV = 2048
WM_UQ = 0
WM_PSB = 4096
WM_PT = 6144
WM_QT = 8192
WM_AO = 9216

AF_XA = 0
AF_V = 2048
AF_GATE = 6144
AF_GATED = 10240

WB_W1Q = (0, 8192)
WB_W2Q = (16384, 24576)

C_LNG = 0
C_LNB = 8
C_GAM = 16
C_BET = 20
C_BQK = 24
C_RLS = 32
C_ELS = 64


class Prog:
    def __init__(self, nc, sems):
        self.nc = nc
        self.sems = sems
        self.cur = None
        self.eng = None
        self.sim = {}
        self.final = {}
        self.ns = {}

    def begin(self, name, eng):
        self.cur = name
        self.eng = eng
        self.sim = {k: 0 for k in self.sems}
        self.ns = {}

    def end(self):
        if self.final:
            assert self.final == self.sim, (
                f"sim divergence on {self.cur}: {self.final} vs {self.sim}")
        else:
            self.final = dict(self.sim)

    def _emit(self, engine, make, inc):
        if inc is not None:
            self.sim[inc[0]] += inc[1]
        if engine == self.cur:
            instr = make(self.eng)
            if inc is not None:
                instr.then_inc(self.sems[inc[0]], inc[1])
        return self.sim[inc[0]] if inc is not None else None

    def wait(self, engine, sem, v):
        if v is None:
            return
        if engine == self.cur:
            self.eng.wait_ge(self.sems[sem], v)

    def tag(self, sem):
        return self.sim[sem]

    def dma(self, q, out, in_, **kw):
        return self._emit(q, lambda e: e.dma_start(out=out, in_=in_, **kw),
                          (DMA_Q[q], 16))

    def mm(self, out, lhsT, rhs, start, stop, inc=False, **kw):
        return self._emit(
            "tensor",
            lambda e: e.matmul(out, lhsT, rhs, start=start, stop=stop, **kw),
            ("sPE", 1) if inc else None)

    def tp(self, out, in_, ident, inc=False):
        return self._emit("tensor", lambda e: e.transpose(out, in_, ident),
                          ("sPE", 1) if inc else None)

    def vtt(self, out, in0, in1, op, inc=False):
        return self._emit("vector", lambda e: e.tensor_tensor(out, in0, in1, op),
                          ("sV", 1) if inc else None)

    def vts(self, out, in0, s1, s2, op0, op1, inc=False):
        return self._emit(
            "vector", lambda e: e.tensor_scalar(out, in0, s1, s2, op0, op1),
            ("sV", 1) if inc else None)

    def vts1(self, out, in0, s1, op0, inc=False):
        return self._emit(
            "vector", lambda e: e.tensor_scalar(out, in0, s1, None, op0),
            ("sV", 1) if inc else None)

    def vstt(self, out, in0, scalar, in1, op0, op1, inc=False):
        return self._emit(
            "vector",
            lambda e: e.scalar_tensor_tensor(out, in0, scalar, in1, op0, op1),
            ("sV", 1) if inc else None)

    def vred(self, out, in_, op, inc=False, **kw):
        return self._emit(
            "vector", lambda e: e.tensor_reduce(out, in_, AX.X, op, **kw),
            ("sV", 1) if inc else None)

    def vcopy(self, out, in_, inc=False):
        return self._emit("vector", lambda e: e.tensor_copy(out, in_),
                          ("sV", 1) if inc else None)

    def vrecip(self, out, in_, inc=False):
        return self._emit("vector", lambda e: e.reciprocal(out, in_),
                          ("sV", 1) if inc else None)

    def vmemset(self, ap, c, inc=False):
        return self._emit("vector", lambda e: e.memset(ap, c),
                          ("sV", 1) if inc else None)

    def act(self, out, in_, func, bias=None, scale=None, inc=False):
        kw = {}
        if bias is not None:
            kw["bias"] = bias
        if scale is not None:
            kw["scale"] = scale
        return self._emit("scalar",
                          lambda e: e.activation(out, in_, func, **kw),
                          ("sA", 1) if inc else None)

    def acopy(self, out, in_, inc=False):
        return self._emit("scalar", lambda e: e.copy(out, in_),
                          ("sA", 1) if inc else None)

    def gmemset(self, ap, c, inc=False):
        return self._emit("gpsimd", lambda e: e.memset(ap, c),
                          ("sP", 1) if inc else None)

    def gaffine(self, out, in_, pattern, compare_op, fill, base, chmul,
                inc=False):
        return self._emit(
            "gpsimd",
            lambda e: e.affine_select(out=out, in_=in_, pattern=pattern,
                                      compare_op=compare_op, fill=fill,
                                      base=base, channel_multiplier=chmul),
            ("sP", 1) if inc else None)

    def barrier(self, sb):
        stats = sb["stats"]
        pe = self.ns.get("pe_last")
        self.vmemset(stats[:, 252:253], 0.0, inc=True)
        self.act(stats[:, 253:254], stats[:, 253:254], AF.Copy, scale=1.0,
                 inc=True)
        self.gmemset(stats[:, 254:255], 0.0, inc=True)
        targets = [("sV", self.tag("sV")), ("sA", self.tag("sA")),
                   ("sP", self.tag("sP")), ("sS", self.tag("sS")),
                   ("sG", self.tag("sG")), ("sC", self.tag("sC"))]
        if pe is not None:
            targets.append(("sPE", pe))
        for engname in ("sync", "tensor", "vector", "scalar", "gpsimd"):
            for sem, v in targets:
                self.wait(engname, sem, v)



def build_nc():
    nc = bass.Bass("TRN2", target_bir_lowering=False)

    dram = {}

    def din(name, shape, dtype=F32):
        dram[name] = nc.dram_tensor(name, list(shape), dtype,
                                    kind="ExternalInput").ap()

    din("x", (T, D))
    din("ln_g", (D,))
    din("ln_b", (D,))
    din("W_h", (D, 2 * HID))
    din("b_h", (2 * HID,))
    din("W_qk", (D, 128))
    din("b_qk", (128,))
    din("os_gamma", (4, 128))
    din("os_beta", (4, 128))
    din("rel_emb", (32, 1))
    din("W_out", (HID, D))
    din("b_out", (D,))
    din("W_dq", (D, 256))
    din("W_uq", (256, D))
    din("W_dkv", (D, 256))
    din("W_uk", (256, D))
    din("W_uv", (256, D))
    din("W_o", (D, D))
    din("r_ls", (D,))
    din("r_lb", (D,))
    din("r_w", (E, D))
    din("r_b", (E,))
    din("e_ls", (E, D))
    din("e_lb", (E, D))
    din("w1", (E, H, D))
    din("b1", (E, H))
    din("w2", (E, D, H))
    din("b2", (E, D))
    din("OH", (32, 511))

    y_out = nc.dram_tensor("y", [T, D], F32, kind="ExternalOutput").ap()
    dbg = {}
    if DEBUG:
        for nm, shape, dt in (("d_stats", [128, 256], F32),
                              ("d_wmed", [128, 10240], BF16),
                              ("d_actfm", [128, 16384], BF16),
                              ("d_wbig", [128, 4096], BF16),
                              ("d_rows", [1, 3200], BF16),
                              ("d_cols", [128, 192], F32),
                              ("d_cols2", [128, 64], F32),
                              ("d_b1col", [128, 256], F32),
                              ("d_biasT", [128, 512], BF16),
                              ("d_gates", [128, 128], F32),
                              ("d_hbuf", [128, 4096], BF16),
                              ("d_rows32", [1, 640], F32),
                              ("d_bvec", [1, 511], F32),
                              ("d_rwT", [128, 64], BF16),
                              ("d_w1T", [128, 4096], BF16)):
            dbg[nm] = nc.dram_tensor(nm, shape, dt, kind="ExternalOutput").ap()

    scr = {}

    def dscr(name, shape, dtype=BF16):
        scr[name] = nc.dram_tensor(name, list(shape), dtype,
                                   kind="Internal").ap()

    dscr("w1c", (E, H, D))
    dscr("w2c", (E, D, H))
    dscr("w1T", (E, D, H))
    dscr("Whc", (D, 2 * HID))
    dscr("Wqkc", (D, 128))
    dscr("Woutc", (HID, D))
    dscr("Wdqc", (D, 256))
    dscr("Wdkvc", (D, 256))
    dscr("Wuqc", (256, D))
    dscr("Wukc", (256, D))
    dscr("Wuvc", (256, D))
    dscr("Woc", (D, D))
    dscr("rowscr", (1, 4096), F32)
    dscr("bvec", (1, 511), F32)

    import contextlib
    with contextlib.ExitStack() as est:
        EE = est.enter_context
        sb = {}

        def sbuf(name, shape, dtype):
            sb[name] = EE(nc.sbuf_tensor(name, list(shape), dtype))

        sbuf("xres", (128, NCH * D), BF16)
        sbuf("wbig", (128, 32768), BF16)
        sbuf("actfm", (128, 16384), BF16)
        sbuf("wmed", (128, 10240), BF16)
        sbuf("hbuf", (128, 4096), BF16)
        sbuf("scr32", (128, 1024), F32)
        sbuf("stats", (128, 256), F32)
        sbuf("gates", (128, 128), F32)
        sbuf("biasT", (128, 512), BF16)
        sbuf("cmask", (128, 128), F32)
        sbuf("mask01", (128, 128), BF16)
        sbuf("ident", (128, 128), BF16)
        sbuf("b1col", (128, 256), F32)
        sbuf("cols", (128, 192), F32)
        sbuf("cols2", (128, 64), F32)
        sbuf("colsb", (128, 96), BF16)
        sbuf("relsb", (32, 1), BF16)
        sbuf("ohsb", (32, 512), BF16)
        sbuf("rows", (1, 3200), BF16)
        sbuf("rows32", (1, 640), F32)
        sbuf("onesb", (1, 128), BF16)
        sbuf("rwT", (128, 64), BF16)

        ps = {}
        for nm, shape, dt in (("pA", [128, 512], F32), ("pB", [128, 512], F32),
                              ("pC", [128, 512], F32),
                              ("pE2", [128, 1024], F32),
                              ("pF2", [128, 1024], F32),
                              ("pTP", [128, 1024], BF16)):
            ps[nm] = EE(nc.psum_tensor(nm, shape, dt))

        sems = {}
        for sname in ("sS", "sG", "sC", "sPE", "sV", "sA", "sP"):
            sems[sname] = EE(nc.semaphore(sname))

        block = EE(nc.Block())
        prog = Prog(nc, sems)

        def emit_all(engine, eng):
            prog.begin(engine, eng)
            program(prog, dram, scr, sb, ps, y_out, dbg if DEBUG else None)
            prog.end()

        @block.sync
        def _(e):
            emit_all("sync", e)

        @block.tensor
        def _(e):
            emit_all("tensor", e)

        @block.vector
        def _(e):
            emit_all("vector", e)

        @block.scalar
        def _(e):
            emit_all("scalar", e)

        @block.gpsimd
        def _(e):
            emit_all("gpsimd", e)

    return nc


def program(p, dram, scr, sb, ps, y_out, dbg=None):
    phase_load_and_prep(p, dram, scr, sb, ps)
    p.barrier(sb)
    if PHASES["attn_a"]:
        for seq in range(NSEQ_RUN):
            attn_a_seq(p, dram, scr, sb, ps, seq)
            p.barrier(sb)
    if dbg is not None and DEBUG_POINT == "attn_a":
        dump_debug(p, sb, dbg)
    prep_ffn_weights(p, dram, scr, sb, ps)
    p.barrier(sb)
    if PHASES["ffn1"]:
        for seq in range(NSEQ_RUN):
            ffn_seq(p, dram, scr, sb, ps, seq)
            p.barrier(sb)
    if dbg is not None and DEBUG_POINT == "ffn1":
        dump_debug(p, sb, dbg)
    if PHASES["attn_b"]:
        for seq in range(NSEQ_RUN):
            attn_b_seq(p, dram, scr, sb, ps, seq)
            p.barrier(sb)
    if PHASES["ffn2"]:
        for seq in range(NSEQ_RUN):
            ffn_seq(p, dram, scr, sb, ps, seq)
            p.barrier(sb)

    phase_output(p, sb, y_out)


def dump_debug(p, sb, dbg):
    pairs = [("d_stats", sb["stats"][:]), ("d_wmed", sb["wmed"][:]),
             ("d_actfm", sb["actfm"][:]), ("d_wbig", sb["wbig"][:, 0:4096]),
             ("d_rows", sb["rows"][:]), ("d_cols", sb["cols"][:]),
             ("d_cols2", sb["cols2"][:]), ("d_b1col", sb["b1col"][:]),
             ("d_biasT", sb["biasT"][:]), ("d_gates", sb["gates"][:]),
             ("d_hbuf", sb["hbuf"][:]), ("d_rows32", sb["rows32"][:]),
             ("d_rwT", sb["rwT"][:])]
    for nm, ap in pairs:
        p.dma("sync", dbg[nm], ap)
    p.dma("sync", dbg["d_bvec"], p.ns["bvec_scr"])
    p.dma("sync", dbg["d_w1T"], p.ns["w1T_scr"][0][0:128, :])
    p.wait("sync", "sS", p.tag("sS"))
    p.barrier(sb)



def ln_stats_seq(p, sb, seq, double, eps1, eps2):
    xres, stats, scr32, hbuf = sb["xres"], sb["stats"], sb["scr32"], sb["hbuf"]
    n = float(D)
    ssq_rd = {0: None, 1: None}
    for ci in range(CH_SEQ):
        sl = ci % 2
        xc = xres[:, (seq * CH_SEQ + ci) * D:(seq * CH_SEQ + ci + 1) * D]
        sq = hbuf[:, sl * D:(sl + 1) * D]
        p.wait("scalar", "sV", ssq_rd[sl])
        p.act(sq, xc, AF.Square, inc=True)
        sq_tag = p.tag("sA")
        p.vred(stats[:, 64 + ci:65 + ci], xc, ALU.add)
        p.wait("vector", "sA", sq_tag)
        p.vred(stats[:, 80 + ci:81 + ci], sq, ALU.add, inc=True)
        ssq_rd[sl] = p.tag("sV")
    p.wait("vector", "sV", p.tag("sV"))
    p.vts1(stats[:, 0:16], stats[:, 64:80], 1.0 / n, ALU.mult, inc=True)
    p.wait("vector", "sV", p.tag("sV"))
    p.vtt(stats[:, 128:144], stats[:, 0:16], stats[:, 0:16], ALU.mult,
          inc=True)
    p.wait("vector", "sV", p.tag("sV"))
    p.vstt(stats[:, 144:160], stats[:, 80:96], 1.0 / n, stats[:, 128:144],
           ALU.mult, ALU.subtract, inc=True)
    p.wait("scalar", "sV", p.tag("sV"))
    p.act(stats[:, 160:176], stats[:, 144:160], AF.Sqrt,
          bias=stats[:, 248:249] if eps1 == EPS_LN else stats[:, 249:250],
          scale=1.0, inc=True)
    p.wait("vector", "sA", p.tag("sA"))
    p.vrecip(stats[:, 16:32], stats[:, 160:176], inc=True)
    p.wait("vector", "sV", p.tag("sV"))
    if not double:
        return
    p.vtt(stats[:, 128:144], stats[:, 16:32], stats[:, 16:32], ALU.mult,
          inc=True)
    p.wait("vector", "sV", p.tag("sV"))
    p.vtt(stats[:, 144:160], stats[:, 144:160], stats[:, 128:144], ALU.mult,
          inc=True)
    p.wait("scalar", "sV", p.tag("sV"))
    p.wait("vector", "sV", p.tag("sV"))
    p.act(stats[:, 160:176], stats[:, 144:160], AF.Sqrt,
          bias=stats[:, 248:249] if eps2 == EPS_LN else stats[:, 249:250],
          scale=1.0, inc=True)
    p.wait("vector", "sA", p.tag("sA"))
    p.vrecip(stats[:, 32:48], stats[:, 160:176], inc=True)
    p.wait("vector", "sV", p.tag("sV"))
    p.vtt(stats[:, 48:64], stats[:, 16:32], stats[:, 32:48], ALU.mult,
          inc=True)
    p.wait("vector", "sV", p.tag("sV"))


def build_fm_chunk(p, sb, ps, seq, ci, dst_view, double, slot):
    xres, stats, scr32, hbuf = sb["xres"], sb["stats"], sb["scr32"], sb["hbuf"]
    ident, pTP = sb["ident"], ps["pTP"]
    st = p.ns.setdefault("fm", {"evac": None, "slot": {}})
    xc = xres[:, (seq * CH_SEQ + ci) * D:(seq * CH_SEQ + ci + 1) * D]
    stage = hbuf[:, slot * D:(slot + 1) * D]
    p.wait("vector", "sPE", st["slot"].get(slot))
    rcol = stats[:, 48 + ci:49 + ci] if double else stats[:, 16 + ci:17 + ci]
    p.vts(stage, xc, stats[:, ci:ci + 1], rcol, ALU.subtract, ALU.mult,
          inc=True)
    aff = p.tag("sV")
    p.wait("tensor", "sV", aff)
    p.wait("tensor", "sV", st["evac"])
    p.wait("tensor", "sA", st.get("evac_a"))
    for dc in range(DC):
        p.tp(pTP[:, dc * 128:(dc + 1) * 128], stage[:, dc * 128:(dc + 1) * 128],
             ident[:], inc=(dc == DC - 1))
    tptag = p.tag("sPE")
    st["slot"][slot] = tptag
    p.wait("vector", "sPE", tptag)
    ptv = pTP[:].rearrange("q (dc t) -> q dc t", t=128)
    p.vcopy(dst_view, ptv, inc=True)
    st["evac"] = p.tag("sV")
    return p.tag("sV")



def phase_load_and_prep(p, dram, scr, sb, ps):
    ns = p.ns
    xres, wbig, wmed = sb["xres"], sb["wbig"], sb["wmed"]
    cols, cols2, colsb = sb["cols"], sb["cols2"], sb["colsb"]

    xv = dram["x"].rearrange("(c p) d -> p c d", p=128)
    for i in range(4):
        p.dma("gpsimd", xres[:, i * 8 * D:(i + 1) * 8 * D],
              xv[:, i * 8:(i + 1) * 8, :])

    p.gmemset(sb["ident"][:], 0.0)
    p.gaffine(sb["ident"][:], sb["ident"][:], [[-1, 128]], ALU.not_equal, 1.0,
              0, 1)
    p.gmemset(sb["cmask"][:], 0.0)
    p.gaffine(sb["cmask"][:], sb["cmask"][:], [[-1, 128]], ALU.is_ge, NEGBIG,
              0, 1)
    p.gmemset(sb["mask01"][:], 1.0)
    p.gaffine(sb["mask01"][:], sb["mask01"][:], [[-1, 128]], ALU.is_ge, 0.0,
              0, 1, inc=True)
    p.vmemset(sb["onesb"][:], 1.0)
    p.vmemset(sb["colsb"][:, 80:81], 1.0)
    p.vmemset(sb["stats"][:, 248:249], EPS_LN)
    p.vmemset(sb["stats"][:, 249:250], EPS_FFN, inc=True)

    for nm_in, nm_out in (("W_h", "Whc"), ("W_qk", "Wqkc"), ("W_out", "Woutc"),
                          ("W_dq", "Wdqc"), ("W_dkv", "Wdkvc"),
                          ("W_uq", "Wuqc"), ("W_uk", "Wukc"),
                          ("W_uv", "Wuvc"), ("W_o", "Woc")):
        p.dma("gpsimd", scr[nm_out], dram[nm_in])
    ns["small_cast"] = p.tag("sG")
    for e in range(E):
        p.dma("gpsimd", scr["w1c"][e], dram["w1"][e])
        p.dma("gpsimd", scr["w2c"][e], dram["w2"][e])
    ns["big_cast"] = p.tag("sG")

    for dc in range(DC):
        p.dma("sync", cols[:, C_LNG + dc:C_LNG + dc + 1],
              dram["ln_g"][dc * 128:(dc + 1) * 128])
        p.dma("sync", cols[:, C_LNB + dc:C_LNB + dc + 1],
              dram["ln_b"][dc * 128:(dc + 1) * 128])
        p.dma("sync", cols[:, C_RLS + dc:C_RLS + dc + 1],
              dram["r_ls"][dc * 128:(dc + 1) * 128])
        p.dma("gpsimd", colsb[:, dc:dc + 1],
              dram["ln_b"][dc * 128:(dc + 1) * 128])
        p.dma("gpsimd", colsb[:, 8 + dc:9 + dc],
              dram["r_lb"][dc * 128:(dc + 1) * 128])
    for h in range(4):
        p.dma("sync", cols[:, C_GAM + h:C_GAM + h + 1], dram["os_gamma"][h, :])
        p.dma("sync", cols[:, C_BET + h:C_BET + h + 1], dram["os_beta"][h, :])
    for e in range(E):
        for dc in range(DC):
            p.dma("sync", cols[:, C_ELS + e * 8 + dc:C_ELS + e * 8 + dc + 1],
                  dram["e_ls"][e, dc * 128:(dc + 1) * 128])
            p.dma("gpsimd", colsb[:, 16 + e * 8 + dc:17 + e * 8 + dc],
                  dram["e_lb"][e, dc * 128:(dc + 1) * 128])
    for e in range(E):
        for hc in range(32):
            p.dma("sync", sb["b1col"][:, e * 32 + hc:e * 32 + hc + 1],
                  dram["b1"][e, hc * 128:(hc + 1) * 128])
    p.dma("gpsimd", sb["relsb"][:], dram["rel_emb"])
    p.dma("gpsimd", sb["ohsb"][:, 0:511], dram["OH"])
    p.dma("gpsimd", sb["rows"][0:1, 2048:3072], dram["b_out"][None, :])
    ns["cols_f32"] = p.tag("sS")
    ns["cols_bf"] = p.tag("sG")

    p.wait("vector", "sS", ns["cols_f32"])
    p.vts1(cols[:, C_GAM + 1:C_GAM + 2], cols[:, C_GAM + 1:C_GAM + 2],
           1.0 / S, ALU.mult)
    p.vts1(cols[:, C_BET + 1:C_BET + 2], cols[:, C_BET + 1:C_BET + 2],
           1.0 / S, ALU.mult, inc=True)

    p.wait("tensor", "sG", ns["cols_bf"])
    p.mm(ps["pA"][0:1, 0:511], sb["relsb"][:], sb["ohsb"][:, 0:511],
         start=True, stop=True, inc=True)
    t = p.tag("sPE")
    p.wait("vector", "sPE", t)
    p.vcopy(sb["rows32"][0:1, 0:511], ps["pA"][0:1, 0:511], inc=True)
    bvec_evac = p.tag("sV")
    t = p.tag("sV")
    p.wait("sync", "sV", t)
    p.dma("sync", scr["bvec"], sb["rows32"][0:1, 0:511])
    p.ns["bvec_scr"] = scr["bvec"]
    t = p.tag("sS")
    p.wait("gpsimd", "sS", t)
    for jc in range(2):
        for row in range(128):
            j = jc * 128 + row
            p.dma("gpsimd", sb["biasT"][row:row + 1, jc * 256:jc * 256 + 256],
                  scr["bvec"][0:1, 255 - j:255 - j + 256])

    p.wait("sync", "sG", ns["small_cast"])
    whv = scr["Whc"].rearrange("(dcb q) h -> q dcb h", q=128)
    p.dma("sync", wbig[:].rearrange("q (dcb h) -> q dcb h", h=4096), whv)
    wqv = scr["Wqkc"].rearrange("(dcb q) h -> q dcb h", q=128)
    p.dma("sync", wmed[:, 0:1024].rearrange("q (dcb h) -> q dcb h", h=128),
          wqv)
    ns["wh_raw"] = p.tag("sS")

    p.wait("tensor", "sS", ns["wh_raw"])
    p.wait("tensor", "sG", ns["cols_bf"])
    prev_use = {0: bvec_evac, 1: None}
    for nt in range(8):
        pt = ps["pA"] if nt % 2 == 0 else ps["pB"]
        sl = pt[0:1, 0:512]
        p.wait("tensor", "sV", prev_use[nt % 2])
        for dc in range(DC):
            p.mm(sl, colsb[:, dc:dc + 1],
                 wbig[:, dc * 4096 + nt * 512:dc * 4096 + nt * 512 + 512],
                 start=(dc == 0), stop=(dc == DC - 1), inc=(dc == DC - 1))
        mmtag = p.tag("sPE")
        p.wait("sync", "sV", p.ns.get("bh_scr_free"))
        p.dma("sync", sb["scr32"][0:1, 0:512],
              dram["b_h"][None, nt * 512:(nt + 1) * 512])
        ld = p.tag("sS")
        p.wait("vector", "sPE", mmtag)
        p.wait("vector", "sS", ld)
        p.vtt(sb["scr32"][0:1, 0:512], sl, sb["scr32"][0:1, 0:512], ALU.add,
              inc=True)
        prev_use[nt % 2] = p.tag("sV")
        p.ns["bh_scr_free"] = p.tag("sV")
        p.wait("vector", "sV", p.tag("sV"))
        if nt < 4:
            p.vcopy(sb["rows"][0:1, nt * 512:(nt + 1) * 512],
                    sb["scr32"][0:1, 0:512], inc=True)
            prev_use[nt % 2] = p.tag("sV")
            p.ns["bh_scr_free"] = p.tag("sV")
        else:
            t = p.tag("sV")
            p.wait("sync", "sV", t)
            p.dma("sync", scr["rowscr"][0:1, nt * 512:(nt + 1) * 512],
                  sb["scr32"][0:1, 0:512])
    t = p.tag("sS")
    p.wait("sync", "sS", t)
    for gc in range(16):
        a = 2048 + gc * 128
        p.dma("sync", cols2[:, gc:gc + 1], scr["rowscr"][0:1, a:a + 128])

    for dc in range(DC):
        p.mm(ps["pC"][:, 0:1], wmed[:, dc * 128:(dc + 1) * 128],
             colsb[:, dc:dc + 1], start=(dc == 0), stop=(dc == DC - 1),
             inc=(dc == DC - 1))
    t = p.tag("sPE")
    p.dma("sync", cols[:, C_BQK + 1:C_BQK + 2], dram["b_qk"][:])
    ld = p.tag("sS")
    p.wait("vector", "sPE", t)
    p.wait("vector", "sS", ld)
    p.vtt(cols[:, C_BQK:C_BQK + 1], ps["pC"][:, 0:1],
          cols[:, C_BQK + 1:C_BQK + 2], ALU.add, inc=True)
    bqk_read = p.tag("sV")

    p.wait("vector", "sS", ns["cols_f32"])
    p.wait("vector", "sPE", t)
    for dc in range(DC):
        p.vts1(wbig[:, dc * 4096:(dc + 1) * 4096],
               wbig[:, dc * 4096:(dc + 1) * 4096],
               cols[:, C_LNG + dc:C_LNG + dc + 1], ALU.mult)
        p.vts1(wmed[:, dc * 128:(dc + 1) * 128],
               wmed[:, dc * 128:(dc + 1) * 128],
               cols[:, C_LNG + dc:C_LNG + dc + 1], ALU.mult,
               inc=(dc == DC - 1))

    for dc in range(DC):
        for e in range(E):
            p.dma("gpsimd", sb["rwT"][:, dc * 8 + e:dc * 8 + e + 1],
                  dram["r_w"][e, dc * 128:(dc + 1) * 128])
    t = p.tag("sG")
    p.wait("tensor", "sG", t)
    p.wait("tensor", "sV", bqk_read)
    for dc in range(DC):
        p.mm(ps["pC"][0:8, 8:9], sb["rwT"][:, dc * 8:(dc + 1) * 8],
             colsb[:, 8 + dc:9 + dc], start=(dc == 0), stop=(dc == DC - 1),
             inc=(dc == DC - 1))
    t = p.tag("sPE")
    p.dma("sync", sb["rows32"][0:1, 520:528], dram["r_b"][None, :])
    ld = p.tag("sS")
    p.wait("vector", "sPE", t)
    p.wait("vector", "sS", ld)
    p.vcopy(sb["scr32"][0:8, 512:513], ps["pC"][0:8, 8:9], inc=True)
    t = p.tag("sV")
    p.wait("sync", "sV", t)
    p.dma("sync", scr["rowscr"][0:1, 3584:3592], sb["scr32"][0:8, 512:513])
    t = p.tag("sS")
    p.wait("sync", "sS", t)
    p.dma("sync", sb["rows32"][0:1, 512:520], scr["rowscr"][0:1, 3584:3592])
    t = p.tag("sS")
    p.wait("vector", "sS", t)
    p.vtt(sb["rows32"][0:1, 512:520], sb["rows32"][0:1, 512:520],
          sb["rows32"][0:1, 520:528], ALU.add, inc=True)
    p.wait("vector", "sV", p.tag("sV"))
    p.vcopy(sb["rows"][0:1, 3072:3080], sb["rows32"][0:1, 512:520], inc=True)
    for dc in range(DC):
        p.vts1(sb["rwT"][:, dc * 8:(dc + 1) * 8],
               sb["rwT"][:, dc * 8:(dc + 1) * 8],
               cols[:, C_RLS + dc:C_RLS + dc + 1], ALU.mult,
               inc=(dc == DC - 1))
    p.ns["pe_last"] = p.tag("sPE")

def prep_ffn_weights(p, dram, scr, sb, ps):
    p.ns["w1T_scr"] = scr["w1T"]
    """w1T scratch: transpose-DMA strips of w1c, fold e_ls rows, store.
    b1col += lb_e @ W1^T via [128,1]-column matmuls into pC[:,128:160].
    PSUM note: single start=True on the very first MM of the pC session."""
    ns = p.ns
    hbuf, b1col, cols, colsb = sb["hbuf"], sb["b1col"], sb["cols"], sb["colsb"]
    p.wait("sync", "sG", ns["big_cast"])
    store_tag = None
    scale_tag = None
    b1_use = None
    for e in range(E):
        mm_tag = None
        for dc in range(DC):
            p.wait("sync", "sS", store_tag)
            p.wait("sync", "sV", scale_tag)
            p.dma("sync", hbuf[:], scr["w1c"][e][:, dc * 128:(dc + 1) * 128],
                  transpose=True)
            ld = p.tag("sS")
            p.wait("tensor", "sS", ld)
            if dc == 0:
                p.wait("tensor", "sV", b1_use)
            for hc in range(32):
                p.mm(ps["pC"][:, 128 + hc:129 + hc],
                     hbuf[:, hc * 128:(hc + 1) * 128],
                     colsb[:, 16 + e * 8 + dc:17 + e * 8 + dc],
                     start=(dc == 0 and hc == 0),
                     stop=(dc == DC - 1 and hc == 31),
                     inc=(hc == 31))
            mm_tag = p.tag("sPE")
            p.wait("vector", "sPE", mm_tag)
            p.wait("vector", "sS", ld)
            p.vts1(hbuf[:], hbuf[:],
                   cols[:, C_ELS + e * 8 + dc:C_ELS + e * 8 + dc + 1],
                   ALU.mult, inc=True)
            scale_tag = p.tag("sV")
            p.wait("sync", "sV", scale_tag)
            p.dma("sync", scr["w1T"][e][dc * 128:(dc + 1) * 128, :], hbuf[:])
            store_tag = p.tag("sS")
        p.wait("vector", "sPE", mm_tag)
        p.vtt(b1col[:, e * 32:(e + 1) * 32], b1col[:, e * 32:(e + 1) * 32],
              ps["pC"][:, 128:160], ALU.add, inc=True)
        b1_use = p.tag("sV")
    p.ns["pe_last"] = p.tag("sPE")



def attn_a_seq(p, dram, scr, sb, ps, seq):
    ns = p.ns
    xres, wbig, wmed, actfm = sb["xres"], sb["wbig"], sb["wmed"], sb["actfm"]
    cols, cols2, stats, scr32 = (sb["cols"], sb["cols2"], sb["stats"],
                                 sb["scr32"])
    onesb, rows = sb["onesb"], sb["rows"]
    pA, pB, pC, pE2, pF2 = ps["pA"], ps["pB"], ps["pC"], ps["pE2"], ps["pF2"]

    ln_stats_seq(p, sb, seq, True, EPS_LN, EPS_LN)

    xa_view = actfm[:, 0:2048].rearrange("q (dc t) -> q dc t", t=256)

    def build_xa(g):
        for tc in range(2):
            t = build_fm_chunk(p, sb, ps, seq, g * 2 + tc,
                               xa_view[:, :, tc * 128:(tc + 1) * 128], True, tc)
        return t

    def v_tm_group(xa_tag, bankfree):
        for tc in range(2):
            for nt in range(4):
                bk = (tc * 4 + nt) % 2
                pt = pA if bk == 0 else pB
                p.wait("tensor", "sV", xa_tag)
                p.wait("tensor", "sA", bankfree[bk])
                for dc in range(DC):
                    p.mm(pt[:], xa_view[:, dc, tc * 128:(tc + 1) * 128],
                         wbig[:, dc * 4096 + nt * 512:
                              dc * 4096 + nt * 512 + 512],
                         start=(dc == 0), stop=False)
                p.mm(pt[:], onesb[:], rows[0:1, nt * 512:(nt + 1) * 512],
                     start=False, stop=True, inc=True)
                mmt = p.tag("sPE")
                p.wait("scalar", "sPE", mmt)
                p.act(actfm[:, AF_V + tc * 2048 + nt * 512:
                            AF_V + tc * 2048 + nt * 512 + 512],
                      pt[:], AF.Silu, inc=True)
                bankfree[bk] = p.tag("sA")
        return p.tag("sA")

    st = {"vbank": {0: None, 1: None}, "qkfree": None}
    for g in range(NG):
        xa_tag = build_xa(g)
        p.wait("tensor", "sV", xa_tag)
        p.wait("tensor", "sA", st["qkfree"])
        for dc in range(DC):
            p.mm(pC[:, 0:256], wmed[:, dc * 128:(dc + 1) * 128],
                 xa_view[:, dc, :], start=(dc == 0), stop=(dc == DC - 1),
                 inc=(dc == DC - 1))
        qmm = p.tag("sPE")
        p.wait("scalar", "sPE", qmm)
        p.act(wmed[:, WM_QKACT + g * 256:WM_QKACT + (g + 1) * 256],
              pC[:, 0:256], AF.Silu, bias=cols[:, C_BQK:C_BQK + 1], scale=1.0,
              inc=True)
        st["qkfree"] = p.tag("sA")
        qk_tag = st["qkfree"]
        vev = v_tm_group(xa_tag, st["vbank"])
        p.wait("vector", "sA", qk_tag)
        p.wait("vector", "sPE", ns.get("lkt_pe"))
        p.vts(sb["hbuf"][:, 2048:2304],
              wmed[:, WM_QKACT + g * 256:WM_QKACT + (g + 1) * 256],
              cols[:, C_GAM + 3:C_GAM + 4], cols[:, C_BET + 3:C_BET + 4],
              ALU.mult, ALU.add, inc=True)
        lk = p.tag("sV")
        fmst = p.ns["fm"]
        p.wait("tensor", "sV", lk)
        p.wait("tensor", "sV", fmst["evac"])
        for tc in range(2):
            p.tp(ps["pTP"][:, tc * 128:(tc + 1) * 128],
                 sb["hbuf"][:, 2048 + tc * 128:2048 + (tc + 1) * 128],
                 sb["ident"][:], inc=(tc == 1))
        ns["lkt_pe"] = p.tag("sPE")
        p.wait("vector", "sPE", ns["lkt_pe"])
        p.vcopy(wmed[:, WM_LKT:WM_LKT + 256], ps["pTP"][:, 0:256], inc=True)
        fmst["evac"] = p.tag("sV")
        lkt_tag = p.tag("sV")
        p.wait("tensor", "sV", lkt_tag)
        p.wait("tensor", "sA", vev)
        for tc in range(2):
            for nt in range(4):
                pt = pE2 if nt < 2 else pF2
                sl = pt[:, (nt % 2) * 512:(nt % 2) * 512 + 512]
                p.mm(sl, wmed[:, WM_LKT + tc * 128:WM_LKT + (tc + 1) * 128],
                     actfm[:, AF_V + tc * 2048 + nt * 512:
                           AF_V + tc * 2048 + nt * 512 + 512],
                     start=(g == 0 and tc == 0),
                     stop=(g == NG - 1 and tc == 1),
                     inc=(g == NG - 1 and tc == 1 and nt == 3))
    linkv_mm = p.tag("sPE")
    p.wait("vector", "sPE", linkv_mm)
    p.vcopy(wmed[:, WM_LINKV:WM_LINKV + 1024], pE2[:], inc=True)
    p.vcopy(wmed[:, WM_LINKV + 1024:WM_LINKV + 2048], pF2[:], inc=True)
    linkv_tag = p.tag("sV")

    st2 = {"vbank": {0: None, 1: None}, "gslot": {0: None, 1: None},
           "qlslot": {0: None, 1: None, 2: None},
           "wout": {0: None, 1: None, 2: None},
           "resfree": {0: linkv_tag, 1: linkv_tag}}
    woutv = scr["Woutc"].rearrange("(ecb q) d2 -> q ecb d2", q=128)
    for g in range(NG):
        xa_tag = build_xa(g)
        vev = v_tm_group(xa_tag, st2["vbank"])
        for gc in range(16):
            bk = gc % 2
            pt = pC if bk == 0 else pA
            sl = pt[:, 0:256]
            p.wait("tensor", "sA", vev)
            p.wait("tensor", "sA", st2["gslot"][bk])
            for dc in range(DC):
                p.mm(sl, wbig[:, dc * 4096 + 2048 + gc * 128:
                              dc * 4096 + 2048 + (gc + 1) * 128],
                     xa_view[:, dc, :], start=(dc == 0), stop=(dc == DC - 1),
                     inc=(dc == DC - 1))
            gmm = p.tag("sPE")
            p.wait("scalar", "sPE", gmm)
            p.act(actfm[:, AF_GATE + gc * 256:AF_GATE + (gc + 1) * 256], sl,
                  AF.Silu, bias=cols2[:, gc:gc + 1], scale=1.0, inc=True)
            st2["gslot"][bk] = p.tag("sA")
        gate_done = p.tag("sA")
        qk_g = wmed[:, WM_QKACT + g * 256:WM_QKACT + (g + 1) * 256]
        p.vts(wmed[:, WM_QQ:WM_QQ + 256], qk_g, cols[:, C_GAM:C_GAM + 1],
              cols[:, C_BET:C_BET + 1], ALU.mult, ALU.add)
        p.vts(wmed[:, WM_QKD:WM_QKD + 256], qk_g,
              cols[:, C_GAM + 2:C_GAM + 3], cols[:, C_BET + 2:C_BET + 3],
              ALU.mult, ALU.add)
        p.vts(wmed[:, WM_LQ:WM_LQ + 256], qk_g, cols[:, C_GAM + 1:C_GAM + 2],
              cols[:, C_BET + 1:C_BET + 2], ALU.mult, ALU.add, inc=True)
        heads_tag = p.tag("sV")
        p.wait("tensor", "sV", heads_tag)
        p.wait("tensor", "sA", gate_done)
        for jc in range(2):
            p.mm(pC[:, jc * 256:(jc + 1) * 256],
                 wmed[:, WM_QKD + jc * 128:WM_QKD + (jc + 1) * 128],
                 wmed[:, WM_QQ:WM_QQ + 256], start=True, stop=True,
                 inc=(jc == 1))
        sim_tag = p.tag("sPE")
        p.wait("vector", "sPE", sim_tag)
        for jc in range(2):
            p.vstt(scr32[:, jc * 256:(jc + 1) * 256],
                   pC[:, jc * 256:(jc + 1) * 256], 1.0 / G,
                   sb["biasT"][:, jc * 256:(jc + 1) * 256], ALU.mult, ALU.add)
            p.vstt(wmed[:, WM_ATT + jc * 256:WM_ATT + (jc + 1) * 256],
                   scr32[:, jc * 256:(jc + 1) * 256], 0.0,
                   scr32[:, jc * 256:(jc + 1) * 256], ALU.max, ALU.mult,
                   inc=(jc == 1))
        att_tag = p.tag("sV")
        for ec in range(16):
            bk = ec % 3
            pt = (pA, pB, pC)[bk]
            sl = pt[:, 0:256]
            p.wait("tensor", "sV", st2["qlslot"][bk])
            p.wait("tensor", "sV", att_tag)
            for tc in range(2):
                p.mm(sl, actfm[:, AF_V + tc * 2048 + ec * 128:
                               AF_V + tc * 2048 + (ec + 1) * 128],
                     wmed[:, WM_ATT + tc * 256:WM_ATT + (tc + 1) * 256],
                     start=(tc == 0), stop=False)
            p.mm(sl, wmed[:, WM_LINKV + ec * 128:WM_LINKV + (ec + 1) * 128],
                 wmed[:, WM_LQ:WM_LQ + 256], start=False, stop=True, inc=True)
            qlmm = p.tag("sPE")
            p.wait("vector", "sPE", qlmm)
            p.vtt(actfm[:, AF_GATED + ec * 256:AF_GATED + (ec + 1) * 256], sl,
                  actfm[:, AF_GATE + ec * 256:AF_GATE + (ec + 1) * 256],
                  ALU.mult, inc=True)
            st2["qlslot"][bk] = p.tag("sV")
        gated_tag = p.tag("sV")
        for ec in range(16):
            slot = ec % 3
            q = "sync" if ec % 2 == 0 else "scalar"
            p.wait(q, "sPE", st2["wout"][slot])
            p.dma(q, wmed[:, WM_WOUT + slot * 1024:
                          WM_WOUT + (slot + 1) * 1024], woutv[:, ec, :])
            dtag = p.tag(DMA_Q[q])
            wsl = wmed[:, WM_WOUT + slot * 1024:WM_WOUT + (slot + 1) * 1024]
            p.wait("tensor", DMA_Q[q], dtag)
            if ec == 0:
                p.wait("tensor", "sV", gated_tag)
                p.wait("tensor", "sV", st2["resfree"][0])
                p.wait("tensor", "sV", st2["resfree"][1])
            for tc in range(2):
                pt = pE2 if tc == 0 else pF2
                gsl = actfm[:, AF_GATED + ec * 256 + tc * 128:
                            AF_GATED + ec * 256 + (tc + 1) * 128]
                for half in range(2):
                    p.mm(pt[:, half * 512:(half + 1) * 512], gsl,
                         wsl[:, half * 512:(half + 1) * 512],
                         start=(ec == 0), stop=False,
                         inc=(tc == 1 and half == 1 and ec < 15))
            if ec == 15:
                for tc in range(2):
                    pt = pE2 if tc == 0 else pF2
                    for half in range(2):
                        p.mm(pt[:, half * 512:(half + 1) * 512], onesb[:],
                             rows[0:1, 2048 + half * 512:
                                  2048 + (half + 1) * 512],
                             start=False, stop=True,
                             inc=(tc == 1 and half == 1))
            st2["wout"][slot] = p.tag("sPE")
        wout_mm = p.tag("sPE")
        for tc in range(2):
            ci = g * 2 + tc
            c = seq * CH_SEQ + ci
            pt = pE2 if tc == 0 else pF2
            xc = xres[:, c * D:(c + 1) * D]
            p.wait("vector", "sPE", wout_mm)
            p.vts(scr32[:, 0:D], xc, stats[:, ci:ci + 1],
                  stats[:, 16 + ci:17 + ci], ALU.subtract, ALU.mult)
            p.vtt(scr32[:, 0:D], scr32[:, 0:D], pt[:], ALU.add)
            p.vtt(xc, scr32[:, 0:D], xc, ALU.add, inc=True)
            st2["resfree"][tc] = p.tag("sV")
    p.ns["pe_last"] = p.tag("sPE")



def ffn_seq(p, dram, scr, sb, ps, seq):
    ns = p.ns
    xres, wbig, wmed, actfm = sb["xres"], sb["wbig"], sb["wmed"], sb["actfm"]
    hbuf, stats, scr32, gates = (sb["hbuf"], sb["stats"], sb["scr32"],
                                 sb["gates"])
    gatesT = sb["wmed"][0:8, 0:2048]
    b2sb = sb["wmed"][0:8, 2048:3072]
    onesb, rows = sb["onesb"], sb["rows"]
    p.dma("gpsimd", b2sb, dram["b2"])
    b2_tag = p.tag("sG")
    pA, pB, pC, pE2, pF2, pTP = (ps["pA"], ps["pB"], ps["pC"], ps["pE2"],
                                 ps["pF2"], ps["pTP"])

    ln_stats_seq(p, sb, seq, True, EPS_LN, EPS_FFN)

    xh_view = actfm[:].rearrange("q (dc t) -> q dc t", t=2048)
    xhat_tag = None
    for ci in range(CH_SEQ):
        xhat_tag = build_fm_chunk(p, sb, ps, seq, ci,
                                  xh_view[:, :, ci * 128:(ci + 1) * 128],
                                  True, ci % 2)

    lfree = {0: None, 1: None}
    for ci in range(CH_SEQ):
        bk = ci % 2
        pt = pC if bk == 0 else pA
        sl = pt[:, 0:8]
        p.wait("tensor", "sV", xhat_tag)
        p.wait("tensor", "sV", lfree[bk])
        for dc in range(DC):
            p.mm(sl, xh_view[:, dc, ci * 128:(ci + 1) * 128],
                 sb["rwT"][:, dc * 8:(dc + 1) * 8], start=(dc == 0),
                 stop=False)
        p.mm(sl, onesb[:], rows[0:1, 3072:3080], start=False, stop=True,
             inc=True)
        lmm = p.tag("sPE")
        p.wait("vector", "sPE", lmm)
        p.vcopy(gates[:, ci * 8:(ci + 1) * 8], sl, inc=True)
        lfree[bk] = p.tag("sV")
    logits_tag = p.tag("sV")
    p.wait("scalar", "sV", logits_tag)
    p.act(gates[:], gates[:], AF.Exp, inc=True)
    t = p.tag("sA")
    p.wait("vector", "sA", t)
    gv = gates[:].rearrange("q (c e) -> q c e", e=8)
    m1 = stats[:, 176:192]
    m2 = stats[:, 192:208]
    dinv = stats[:, 208:224]
    w1t = stats[:, 224:240]
    p.vred(m1, gv, ALU.max, inc=True)
    p.wait("vector", "sV", p.tag("sV"))
    m1b = m1.unsqueeze(2).broadcast_to([128, 16, 8])
    is1 = scr32[:, 0:128].rearrange("q (c e) -> q c e", e=8)
    p.vtt(is1, gv, m1b, ALU.is_equal, inc=True)
    p.wait("vector", "sV", p.tag("sV"))
    masked = scr32[:, 128:256].rearrange("q (c e) -> q c e", e=8)
    p.vstt(masked, is1, NEGBIG, gv, ALU.mult, ALU.add, inc=True)
    p.wait("vector", "sV", p.tag("sV"))
    p.vred(m2, masked, ALU.max, inc=True)
    p.wait("vector", "sV", p.tag("sV"))
    m2b = m2.unsqueeze(2).broadcast_to([128, 16, 8])
    is2 = scr32[:, 256:384].rearrange("q (c e) -> q c e", e=8)
    p.vtt(is2, masked, m2b, ALU.is_equal, inc=True)
    p.vtt(dinv, m1, m2, ALU.add, inc=True)
    p.wait("vector", "sV", p.tag("sV"))
    p.vrecip(dinv, dinv, inc=True)
    p.wait("vector", "sV", p.tag("sV"))
    p.vtt(w1t, m1, dinv, ALU.mult)
    p.vtt(m2, m2, dinv, ALU.mult, inc=True)
    p.wait("vector", "sV", p.tag("sV"))
    w1b = w1t.unsqueeze(2).broadcast_to([128, 16, 8])
    w2b = m2.unsqueeze(2).broadcast_to([128, 16, 8])
    p.vtt(gv, is1, w1b, ALU.mult)
    p.vtt(is2, is2, w2b, ALU.mult, inc=True)
    p.wait("vector", "sV", p.tag("sV"))
    p.vtt(gv, gv, is2, ALU.add, inc=True)
    gates_tag = p.tag("sV")
    p.wait("vector", "sV", gates_tag)

    fmst = p.ns["fm"]
    p.wait("vector", "sPE", fmst["slot"].get(0))
    p.vcopy(hbuf[:, 0:128], gates[:], inc=True)
    gt0 = p.tag("sV")
    for rnd in range(2):
        p.wait("tensor", "sV", gt0)
        p.wait("tensor", "sV", fmst["evac"])
        for k in range(8):
            ci = rnd * 8 + k
            p.tp(pTP[0:8, k * 128:(k + 1) * 128],
                 hbuf[:, ci * 8:(ci + 1) * 8], sb["ident"][:], inc=(k == 7))
        tpt = p.tag("sPE")
        p.wait("vector", "sPE", tpt)
        p.vcopy(gatesT[0:8, rnd * 1024:(rnd + 1) * 1024], pTP[0:8, 0:1024],
                inc=True)
        fmst["evac"] = p.tag("sV")
    gatesT_tag = p.tag("sV")
    fmst["slot"][0] = p.tag("sPE")

    wload = {}
    slot_free = {0: None, 1: None}
    yfree = {0: gatesT_tag, 1: gatesT_tag}
    hfree = {i: None for i in range(4)}
    psfree = {i: None for i in range(3)}

    def issue_wload(e, q):
        slot = (e * NQ + q) % 2
        p.wait("sync", "sPE", slot_free[slot])
        w1v = scr["w1T"][e].rearrange("(dcb q2) h2 -> q2 dcb h2", q2=128)
        p.dma("sync",
              wbig[:, WB_W1Q[0] + slot * 8192:WB_W1Q[0] + slot * 8192 + 8192]
              .rearrange("q2 (dcb h2) -> q2 dcb h2", h2=1024),
              w1v[:, :, q * 1024:(q + 1) * 1024])
        for hc in range(8):
            p.dma("sync",
                  wbig[:, WB_W2Q[0] + slot * 8192 + hc * 1024:
                        WB_W2Q[0] + slot * 8192 + (hc + 1) * 1024],
                  scr["w2c"][e][:, q * 1024 + hc * 128:
                                q * 1024 + (hc + 1) * 128],
                  transpose=True)
        wload[(e, q)] = p.tag("sS")

    issue_wload(0, 0)
    for e in range(E):
        for q in range(NQ):
            slot = (e * NQ + q) % 2
            nxt = (e, q + 1) if q + 1 < NQ else ((e + 1, 0) if e + 1 < E
                                                 else None)
            if nxt is not None:
                issue_wload(*nxt)
            w1sl = wbig[:, WB_W1Q[0] + slot * 8192:
                        WB_W1Q[0] + slot * 8192 + 8192]
            w2sl = wbig[:, WB_W2Q[0] + slot * 8192:
                        WB_W2Q[0] + slot * 8192 + 8192]
            p.wait("tensor", "sS", wload[(e, q)])
            for tb in range(8):
                hact = {}

                def emit_h(hc):
                    hs = hc % 4
                    pb = hc % 3
                    pt = (pA, pB, pC)[pb]
                    sl = pt[:, 0:256]
                    p.wait("tensor", "sA", psfree[pb])
                    for dc in range(DC):
                        p.mm(sl, w1sl[:, dc * 1024 + hc * 128:
                                      dc * 1024 + (hc + 1) * 128],
                             xh_view[:, dc, tb * 256:(tb + 1) * 256],
                             start=(dc == 0), stop=(dc == DC - 1),
                             inc=(dc == DC - 1))
                    hmm = p.tag("sPE")
                    p.wait("scalar", "sPE", hmm)
                    p.wait("scalar", "sPE", hfree[hs])
                    p.act(hbuf[:, hs * 256:(hs + 1) * 256], sl, AF.Gelu,
                          bias=sb["b1col"][:, e * 32 + q * 8 + hc:
                                           e * 32 + q * 8 + hc + 1],
                          scale=1.0, inc=True)
                    psfree[pb] = p.tag("sA")
                    hact[hc] = p.tag("sA")

                def emit_y2(hc):
                    hs = hc % 4
                    p.wait("tensor", "sA", hact[hc])
                    if hc == 0:
                        p.wait("tensor", "sV", yfree[0])
                        p.wait("tensor", "sV", yfree[1])
                    for t128 in range(2):
                        pt2 = pE2 if t128 == 0 else pF2
                        for half in range(2):
                            p.mm(pt2[:, half * 512:(half + 1) * 512],
                                 hbuf[:, hs * 256 + t128 * 128:
                                      hs * 256 + (t128 + 1) * 128],
                                 w2sl[:, hc * 1024 + half * 512:
                                      hc * 1024 + (half + 1) * 512],
                                 start=(hc == 0), stop=(hc == 7),
                                 inc=(t128 == 1 and half == 1))
                    hfree[hs] = p.tag("sPE")

                emit_h(0)
                emit_h(1)
                for hc in range(2, 8):
                    emit_h(hc)
                    emit_y2(hc - 2)
                emit_y2(6)
                emit_y2(7)
                y2mm = p.tag("sPE")
                for t128 in range(2):
                    ci = tb * 2 + t128
                    c = seq * CH_SEQ + ci
                    pt2 = pE2 if t128 == 0 else pF2
                    xc = xres[:, c * D:(c + 1) * D]
                    p.wait("vector", "sPE", y2mm)
                    p.vstt(xc, pt2[:], gates[:, ci * 8 + e:ci * 8 + e + 1],
                           xc, ALU.mult, ALU.add, inc=True)
                    yfree[t128] = p.tag("sV")
            slot_free[slot] = p.tag("sPE")
    bfree = {0: yfree[0], 1: yfree[1]}
    for ci in range(CH_SEQ):
        c = seq * CH_SEQ + ci
        bk = ci % 2
        pt2 = pE2 if bk == 0 else pF2
        p.wait("tensor", "sV", bfree[bk])
        for half in range(2):
            p.wait("tensor", "sG", b2_tag)
            p.mm(pt2[:, half * 512:(half + 1) * 512],
                 gatesT[0:8, ci * 128:(ci + 1) * 128],
                 b2sb[:, half * 512:(half + 1) * 512],
                 start=True, stop=True, inc=(half == 1))
        bmm = p.tag("sPE")
        xc = xres[:, c * D:(c + 1) * D]
        p.wait("vector", "sPE", bmm)
        p.vtt(xc, pt2[:], xc, ALU.add, inc=True)
        bfree[bk] = p.tag("sV")
    p.ns["pe_last"] = p.tag("sPE")



def attn_b_seq(p, dram, scr, sb, ps, seq):
    ns = p.ns
    xres, wbig, wmed, actfm = sb["xres"], sb["wbig"], sb["wmed"], sb["actfm"]
    hbuf, stats, scr32 = sb["hbuf"], sb["stats"], sb["scr32"]
    pA, pB, pC, pE2, pF2, pTP = (ps["pA"], ps["pB"], ps["pC"], ps["pE2"],
                                 ps["pF2"], ps["pTP"])

    ln_stats_seq(p, sb, seq, False, EPS_LN, EPS_LN)
    yv = actfm[:].rearrange("q (dc t) -> q dc t", t=2048)
    y_tag = None
    for ci in range(CH_SEQ):
        y_tag = build_fm_chunk(p, sb, ps, seq, ci,
                               yv[:, :, ci * 128:(ci + 1) * 128], False,
                               ci % 2)

    for nm, off in (("Wdqc", WM_DQ), ("Wdkvc", WM_DKV)):
        wv = scr[nm].rearrange("(dcb q2) c2 -> q2 dcb c2", q2=128)
        p.dma("sync", wmed[:, off:off + 2048]
              .rearrange("q2 (dcb c2) -> q2 dcb c2", c2=256), wv)
    dq_tag = p.tag("sS")

    bank = {0: ns["fm"]["evac"], 1: ns["fm"]["evac"]}
    for which, off_w, dst, dsto in ((0, WM_DQ, hbuf, 0),
                                    (1, WM_DKV, wmed, WM_CKV)):
        for cc in range(2):
            for nt in range(4):
                bk = (cc * 4 + nt) % 2
                pt = pA if bk == 0 else pB
                p.wait("tensor", "sS", dq_tag)
                p.wait("tensor", "sV", y_tag)
                p.wait("tensor", "sV", bank[bk])
                for dc in range(DC):
                    p.mm(pt[:], wmed[:, off_w + dc * 256 + cc * 128:
                                     off_w + dc * 256 + (cc + 1) * 128],
                         yv[:, dc, nt * 512:(nt + 1) * 512],
                         start=(dc == 0), stop=(dc == DC - 1),
                         inc=(dc == DC - 1))
                cmm = p.tag("sPE")
                p.wait("vector", "sPE", cmm)
                p.vcopy(dst[:, dsto + cc * 2048 + nt * 512:
                            dsto + cc * 2048 + (nt + 1) * 512], pt[:],
                        inc=True)
                bank[bk] = p.tag("sV")
    c_done_pe = p.tag("sPE")
    c_done = p.tag("sV")

    p.wait("sync", "sPE", c_done_pe)
    for nm, off in (("Wukc", WM_UK), ("Wuvc", WM_UV)):
        wv = scr[nm].rearrange("(ccb q2) d2 -> q2 ccb d2", q2=128)
        p.dma("sync", wmed[:, off:off + 2048]
              .rearrange("q2 (ccb d2) -> q2 ccb d2", d2=1024), wv)
    uk_tag = p.tag("sS")

    for qc in range(8):
        for nt in range(4):
            bk = (qc * 4 + nt) % 2
            pt = pA if bk == 0 else pB
            p.wait("tensor", "sS", uk_tag)
            p.wait("tensor", "sV", c_done)
            p.wait("tensor", "sV", bank[bk])
            for cc in range(2):
                p.mm(pt[:], wmed[:, WM_UK + cc * 1024 + qc * 128:
                                 WM_UK + cc * 1024 + (qc + 1) * 128],
                     wmed[:, WM_CKV + cc * 2048 + nt * 512:
                          WM_CKV + cc * 2048 + (nt + 1) * 512],
                     start=(cc == 0), stop=(cc == 1), inc=(cc == 1))
            kmm = p.tag("sPE")
            p.wait("vector", "sPE", kmm)
            p.vcopy(wbig[:, qc * 2048 + nt * 512:qc * 2048 + (nt + 1) * 512],
                    pt[:], inc=True)
            bank[bk] = p.tag("sV")
    vfree = {0: None, 1: None}
    for tc in range(16):
        bk = tc % 2
        pt2 = pE2 if bk == 0 else pF2
        p.wait("tensor", "sV", vfree[bk])
        for half in range(2):
            for cc in range(2):
                p.mm(pt2[:, half * 512:(half + 1) * 512],
                     wmed[:, WM_CKV + cc * 2048 + tc * 128:
                          WM_CKV + cc * 2048 + (tc + 1) * 128],
                     wmed[:, WM_UV + cc * 1024 + half * 512:
                          WM_UV + cc * 1024 + (half + 1) * 512],
                     start=(cc == 0), stop=(cc == 1),
                     inc=(cc == 1 and half == 1))
        vmm = p.tag("sPE")
        p.wait("vector", "sPE", vmm)
        p.vcopy(wbig[:, 16384 + tc * 1024:16384 + (tc + 1) * 1024], pt2[:],
                inc=True)
        vfree[bk] = p.tag("sV")
    kv_done_pe = p.tag("sPE")
    kv_done = p.tag("sV")

    p.wait("sync", "sPE", kv_done_pe)
    wv = scr["Wuqc"].rearrange("(ccb q2) d2 -> q2 ccb d2", q2=128)
    p.dma("sync", wmed[:, WM_UQ:WM_UQ + 2048]
          .rearrange("q2 (ccb d2) -> q2 ccb d2", d2=1024), wv)
    wv = scr["Woc"].rearrange("(vcb q2) d2 -> q2 vcb d2", q2=128)
    p.dma("sync", actfm[:, 0:8192]
          .rearrange("q2 (vcb d2) -> q2 vcb d2", d2=1024), wv)
    uq_tag = p.tag("sS")

    st = {"qt": {0: None, 1: None}, "exp": None,
          "avfree": {0: None, 1: None}, "linv": None,
          "res": kv_done, "tpdone": None, "pn": {}, "J": {}}
    fmst = p.ns["fm"]

    def emit_head_scores(i, h):
        J = (i + 1) * 128
        njt = (J + 511) // 512
        st["J"][h] = (i, J, njt)
        qc, base = h // 2, (h % 2) * 64
        p.wait("tensor", "sA", st["qt"][0])
        p.wait("tensor", "sA", st["qt"][1])
        p.wait("tensor", "sA", st["exp"])
        if h == 0:
            p.wait("tensor", "sV", st["res"])
        for jt in range(njt):
            N = min(512, J - jt * 512)
            pt2 = pE2 if jt < 2 else pF2
            sl = pt2[:, (jt % 2) * 512:(jt % 2) * 512 + N]
            p.mm(sl, wmed[base:base + 64, WM_QT + qc * 128:
                          WM_QT + (qc + 1) * 128],
                 wbig[base:base + 64, qc * 2048 + jt * 512:
                      qc * 2048 + jt * 512 + N],
                 start=True, stop=True, inc=(jt == njt - 1))
        smm = p.tag("sPE")
        p.wait("vector", "sPE", smm)
        for jt in range(njt):
            N = min(512, J - jt * 512)
            pt2 = pE2 if jt < 2 else pF2
            p.vred(stats[:, 176 + jt:177 + jt],
                   pt2[:, (jt % 2) * 512:(jt % 2) * 512 + N], ALU.max,
                   inc=(jt == njt - 1))
        p.wait("vector", "sV", p.tag("sV"))
        if njt > 1:
            p.vred(stats[:, 180:181], stats[:, 176:176 + njt], ALU.max,
                   inc=True)
            p.wait("vector", "sV", p.tag("sV"))
            p.vts1(stats[:, 181:182], stats[:, 180:181], -0.125, ALU.mult,
                   inc=True)
        else:
            p.vts1(stats[:, 181:182], stats[:, 176:177], -0.125, ALU.mult,
                   inc=True)
        mtag = p.tag("sV")
        p.wait("scalar", "sV", mtag)
        p.wait("scalar", "sPE", st["tpdone"])
        for jt in range(njt):
            N = min(512, J - jt * 512)
            pt2 = pE2 if jt < 2 else pF2
            p.act(wmed[:, WM_PSB + jt * 512:WM_PSB + jt * 512 + N],
                  pt2[:, (jt % 2) * 512:(jt % 2) * 512 + N], AF.Exp,
                  bias=stats[:, 181:182], scale=0.125,
                  inc=(jt == njt - 1))
        exp_tag = p.tag("sA")
        st["exp"] = exp_tag
        p.wait("vector", "sA", exp_tag)
        dj = i * 128
        p.vtt(wmed[:, WM_PSB + dj:WM_PSB + dj + 128],
              wmed[:, WM_PSB + dj:WM_PSB + dj + 128], sb["mask01"][:],
              ALU.mult, inc=True)
        st["pn"][h] = p.tag("sV")
        p.vred(stats[:, 182:183], wmed[:, WM_PSB:WM_PSB + J], ALU.add,
               inc=True)
        p.wait("vector", "sV", p.tag("sV"))
        p.vrecip(stats[:, 183:184], stats[:, 182:183], inc=True)
        st["linv"] = p.tag("sV")

    def emit_head_tpav(h):
        i, J, njt = st["J"][h]
        qc, base = h // 2, (h % 2) * 64
        pn_tag = st["pn"][h]
        for jc in range(i + 1):
            ksl = jc % 8
            p.wait("tensor", "sV", pn_tag)
            if ksl == 0:
                p.wait("tensor", "sV", fmst["evac"])
                p.wait("tensor", "sA", fmst.get("evac_a"))
            p.tp(pTP[:, ksl * 128:(ksl + 1) * 128],
                 wmed[:, WM_PSB + jc * 128:WM_PSB + (jc + 1) * 128],
                 sb["ident"][:], inc=(ksl == 7 or jc == i))
            if ksl == 7 or jc == i:
                tpt = p.tag("sPE")
                p.wait("scalar", "sPE", tpt)
                lo = (jc // 8) * 8
                p.acopy(wmed[:, WM_PT + lo * 128:WM_PT + (jc + 1) * 128],
                        pTP[:, 0:(jc + 1 - lo) * 128], inc=True)
                fmst["evac_a"] = p.tag("sA")
        st["tpdone"] = p.tag("sPE")
        pt_tag = p.tag("sA")
        p.wait("tensor", "sA", pt_tag)
        avp = pB if h % 2 == 0 else pA
        p.wait("tensor", "sA", st["avfree"].get(h % 2))
        for jc in range(i + 1):
            p.mm(avp[:, 256:320],
                 wmed[:, WM_PT + jc * 128:WM_PT + (jc + 1) * 128],
                 wbig[:, 16384 + jc * 1024 + h * 64:
                      16384 + jc * 1024 + (h + 1) * 64],
                 start=(jc == 0), stop=(jc == i), inc=(jc == i))
        avmm = p.tag("sPE")
        p.wait("scalar", "sPE", avmm)
        p.wait("scalar", "sV", st["linv"])
        p.act(wmed[:, WM_AO + h * 64:WM_AO + (h + 1) * 64], avp[:, 256:320],
              AF.Identity, scale=stats[:, 183:184], inc=True)
        st["avfree"][h % 2] = p.tag("sA")

    for i in range(CH_SEQ):
        for qc in range(8):
            bk = qc % 2
            pt = pA if bk == 0 else pC
            sl = pt[:, 0:128]
            p.wait("tensor", "sS", uq_tag)
            p.wait("tensor", "sA", st["qt"][bk])
            p.wait("tensor", "sA", st["avfree"][1])
            for cc in range(2):
                p.mm(sl, wmed[:, WM_UQ + cc * 1024 + qc * 128:
                              WM_UQ + cc * 1024 + (qc + 1) * 128],
                     hbuf[:, cc * 2048 + i * 128:cc * 2048 + (i + 1) * 128],
                     start=(cc == 0), stop=(cc == 1), inc=(cc == 1))
            qmm = p.tag("sPE")
            p.wait("scalar", "sPE", qmm)
            p.acopy(wmed[:, WM_QT + qc * 128:WM_QT + (qc + 1) * 128], sl,
                    inc=True)
            st["qt"][bk] = p.tag("sA")
        for h in range(NHEAD):
            if h > 0:
                emit_head_tpav(h - 1)
            emit_head_scores(i, h)
        emit_head_tpav(NHEAD - 1)
        p.wait("tensor", "sA", st["avfree"][0])
        p.wait("tensor", "sA", st["avfree"][1])
        p.wait("tensor", "sV", fmst["evac"])
        p.wait("tensor", "sA", fmst.get("evac_a"))
        for vc in range(8):
            p.tp(pTP[:, vc * 128:(vc + 1) * 128],
                 wmed[:, WM_AO + vc * 128:WM_AO + (vc + 1) * 128],
                 sb["ident"][:], inc=(vc == 7))
        aot = p.tag("sPE")
        st["tpdone"] = aot
        p.wait("vector", "sPE", aot)
        p.vcopy(wmed[:, 4096:5120], pTP[:], inc=True)
        fmst["evac"] = p.tag("sV")
        ao_tag = p.tag("sV")
        p.wait("tensor", "sV", ao_tag)
        p.wait("tensor", "sV", st["res"])
        for vc in range(8):
            for half in range(2):
                p.mm(pE2[:, half * 512:(half + 1) * 512],
                     wmed[:, 4096 + vc * 128:4096 + (vc + 1) * 128],
                     actfm[:, vc * 1024 + half * 512:
                           vc * 1024 + (half + 1) * 512],
                     start=(vc == 0), stop=(vc == 7),
                     inc=(vc == 7 and half == 1))
        wmm = p.tag("sPE")
        st["tpdone"] = wmm
        c = seq * CH_SEQ + i
        xc = xres[:, c * D:(c + 1) * D]
        p.wait("vector", "sPE", wmm)
        p.vtt(xc, pE2[:], xc, ALU.add, inc=True)
        st["res"] = p.tag("sV")
    p.ns["pe_last"] = p.tag("sPE")

def phase_output(p, sb, y_out):
    xres = sb["xres"]
    yv = y_out.rearrange("(c p) d -> p c d", p=128)
    for i in range(4):
        p.dma("gpsimd", yv[:, i * 8:(i + 1) * 8, :],
              xres[:, i * 8 * D:(i + 1) * 8 * D]
              .rearrange("q (c d) -> q c d", d=D))
    p.wait("gpsimd", "sG", p.tag("sG"))
    p.wait("sync", "sG", p.tag("sG"))



def _bucket1d():
    NUM_BUCKETS = 32
    MAX_DIST = 128
    nb = NUM_BUCKETS // 2
    max_exact = nb // 2
    out = np.zeros(511, dtype=np.int64)
    for idx, dd in enumerate(range(-255, 256)):
        n = dd
        ret = nb if n < 0 else 0
        n = abs(n)
        if n < max_exact:
            val = n
        else:
            val = max_exact + int(
                math.log(max(n, 1) / max_exact)
                / math.log(MAX_DIST / max_exact) * (nb - max_exact))
            val = min(val, nb - 1)
        out[idx] = ret + val
    return out


def _oh_const():
    b = _bucket1d()
    oh = np.zeros((32, 511), dtype=np.float32)
    oh[b, np.arange(511)] = 1.0
    return oh * math.sqrt(128.0)


_NC_CACHE = None


def kernel(L0_attn_a_x, L0_attn_a_ln_g, L0_attn_a_ln_b, L0_attn_a_W_h,
           L0_attn_a_b_h, L0_attn_a_W_qk, L0_attn_a_b_qk, L0_attn_a_os_gamma,
           L0_attn_a_os_beta, L0_attn_a_rel_emb, L0_attn_a_W_out,
           L0_attn_a_b_out, L0_attn_b_W_dq, L0_attn_b_W_uq, L0_attn_b_W_dkv,
           L0_attn_b_W_uk, L0_attn_b_W_uv, L0_attn_b_W_o,
           L0_ffn_router_ln_scale, L0_ffn_router_ln_bias, L0_ffn_router_w,
           L0_ffn_router_b, L0_ffn_expert_ln_scale, L0_ffn_expert_ln_bias,
           L0_ffn_w1, L0_ffn_b1, L0_ffn_w2, L0_ffn_b2):
    global _NC_CACHE
    x = np.ascontiguousarray(np.asarray(L0_attn_a_x)).astype(np.float32)

    common = {
        "ln_g": L0_attn_a_ln_g, "ln_b": L0_attn_a_ln_b,
        "W_h": L0_attn_a_W_h, "b_h": L0_attn_a_b_h,
        "W_qk": L0_attn_a_W_qk, "b_qk": L0_attn_a_b_qk,
        "os_gamma": L0_attn_a_os_gamma, "os_beta": L0_attn_a_os_beta,
        "rel_emb": L0_attn_a_rel_emb, "W_out": L0_attn_a_W_out,
        "b_out": L0_attn_a_b_out,
        "W_dq": L0_attn_b_W_dq, "W_uq": L0_attn_b_W_uq,
        "W_dkv": L0_attn_b_W_dkv, "W_uk": L0_attn_b_W_uk,
        "W_uv": L0_attn_b_W_uv, "W_o": L0_attn_b_W_o,
        "r_ls": L0_ffn_router_ln_scale, "r_lb": L0_ffn_router_ln_bias,
        "r_w": L0_ffn_router_w, "r_b": L0_ffn_router_b,
        "e_ls": L0_ffn_expert_ln_scale, "e_lb": L0_ffn_expert_ln_bias,
        "w1": L0_ffn_w1, "b1": L0_ffn_b1, "w2": L0_ffn_w2, "b2": L0_ffn_b2,
        "OH": _oh_const(),
    }
    common = {k: np.ascontiguousarray(np.asarray(v)).astype(np.float32)
              for k, v in common.items()}

    in_maps = []
    for c in range(8):
        m = dict(common)
        m["x"] = np.ascontiguousarray(x[2 * c:2 * c + 2].reshape(T, D))
        in_maps.append(m)

    if _NC_CACHE is None:
        _NC_CACHE = build_nc()
    res = run_bass_kernel_spmd(_NC_CACHE, in_maps, core_ids=list(range(8)))
    out = np.concatenate(
        [r["y"].reshape(NSEQ, S, D) for r in res.results], axis=0)
    return out.astype(np.float32)


# version 09
# speedup vs baseline: 1.1358x; 1.0135x over previous
import math
import numpy as np

import concourse.bass as bass
import concourse.mybir as mybir
from concourse.bass_utils import run_bass_kernel_spmd

F32 = mybir.dt.float32
BF16 = mybir.dt.bfloat16
AF = mybir.ActivationFunctionType
ALU = mybir.AluOpType
AX = mybir.AxisListType

D = 1024
DC = 8
S = 2048
NSEQ = 2
T = 4096
NCH = 32
CH_SEQ = 16
HID = 2048
G = 256
NG = 8
E = 8
H = 4096
NQ = 4
QH = 1024
NHEAD = 16
DK = 64
EPS_LN = 1e-5
EPS_FFN = 1e-6
NEGBIG = -1.0e30

DMA_Q = {"sync": "sS", "gpsimd": "sG", "scalar": "sC"}

PHASES = {"attn_a": True, "ffn1": True, "attn_b": True, "ffn2": True}
DEBUG = False
DEBUG_POINT = "attn_a"
NSEQ_RUN = NSEQ

WM_QK = 0
WM_QKACT = 1024
WM_LINKV = 3072
WM_LKT = 5120
WM_QQ = 5376
WM_QKD = 5632
WM_LQ = 5888
WM_ATT = 6144
WM_WOUT = 6656
WM_DQ = 0
WM_DKV = 2048
WM_CKV = 4096
WM_UK = 0
WM_UV = 2048
WM_UQ = 0
WM_PSB = 4096
WM_PT = 6144
WM_QT = 8192
WM_AO = 9216

AF_XA = 0
AF_V = 2048
AF_GATE = 6144
AF_GATED = 10240

WB_W1Q = (0, 8192)
WB_W2Q = (16384, 24576)

C_LNG = 0
C_LNB = 8
C_GAM = 16
C_BET = 20
C_BQK = 24
C_RLS = 32
C_ELS = 64


class Prog:
    def __init__(self, nc, sems):
        self.nc = nc
        self.sems = sems
        self.cur = None
        self.eng = None
        self.sim = {}
        self.final = {}
        self.ns = {}

    def begin(self, name, eng):
        self.cur = name
        self.eng = eng
        self.sim = {k: 0 for k in self.sems}
        self.ns = {}

    def end(self):
        if self.final:
            assert self.final == self.sim, (
                f"sim divergence on {self.cur}: {self.final} vs {self.sim}")
        else:
            self.final = dict(self.sim)

    def _emit(self, engine, make, inc):
        if inc is not None:
            self.sim[inc[0]] += inc[1]
        if engine == self.cur:
            instr = make(self.eng)
            if inc is not None:
                instr.then_inc(self.sems[inc[0]], inc[1])
        return self.sim[inc[0]] if inc is not None else None

    def wait(self, engine, sem, v):
        if v is None:
            return
        if engine == self.cur:
            self.eng.wait_ge(self.sems[sem], v)

    def tag(self, sem):
        return self.sim[sem]

    def dma(self, q, out, in_, **kw):
        return self._emit(q, lambda e: e.dma_start(out=out, in_=in_, **kw),
                          (DMA_Q[q], 16))

    def mm(self, out, lhsT, rhs, start, stop, inc=False, **kw):
        return self._emit(
            "tensor",
            lambda e: e.matmul(out, lhsT, rhs, start=start, stop=stop, **kw),
            ("sPE", 1) if inc else None)

    def tp(self, out, in_, ident, inc=False):
        return self._emit("tensor", lambda e: e.transpose(out, in_, ident),
                          ("sPE", 1) if inc else None)

    def vtt(self, out, in0, in1, op, inc=False):
        return self._emit("vector", lambda e: e.tensor_tensor(out, in0, in1, op),
                          ("sV", 1) if inc else None)

    def vts(self, out, in0, s1, s2, op0, op1, inc=False):
        return self._emit(
            "vector", lambda e: e.tensor_scalar(out, in0, s1, s2, op0, op1),
            ("sV", 1) if inc else None)

    def vts1(self, out, in0, s1, op0, inc=False):
        return self._emit(
            "vector", lambda e: e.tensor_scalar(out, in0, s1, None, op0),
            ("sV", 1) if inc else None)

    def vstt(self, out, in0, scalar, in1, op0, op1, inc=False):
        return self._emit(
            "vector",
            lambda e: e.scalar_tensor_tensor(out, in0, scalar, in1, op0, op1),
            ("sV", 1) if inc else None)

    def vred(self, out, in_, op, inc=False, **kw):
        return self._emit(
            "vector", lambda e: e.tensor_reduce(out, in_, AX.X, op, **kw),
            ("sV", 1) if inc else None)

    def vcopy(self, out, in_, inc=False):
        return self._emit("vector", lambda e: e.tensor_copy(out, in_),
                          ("sV", 1) if inc else None)

    def vrecip(self, out, in_, inc=False):
        return self._emit("vector", lambda e: e.reciprocal(out, in_),
                          ("sV", 1) if inc else None)

    def vmemset(self, ap, c, inc=False):
        return self._emit("vector", lambda e: e.memset(ap, c),
                          ("sV", 1) if inc else None)

    def act(self, out, in_, func, bias=None, scale=None, inc=False):
        kw = {}
        if bias is not None:
            kw["bias"] = bias
        if scale is not None:
            kw["scale"] = scale
        return self._emit("scalar",
                          lambda e: e.activation(out, in_, func, **kw),
                          ("sA", 1) if inc else None)

    def acopy(self, out, in_, inc=False):
        return self._emit("scalar", lambda e: e.copy(out, in_),
                          ("sA", 1) if inc else None)

    def gmemset(self, ap, c, inc=False):
        return self._emit("gpsimd", lambda e: e.memset(ap, c),
                          ("sP", 1) if inc else None)

    def gaffine(self, out, in_, pattern, compare_op, fill, base, chmul,
                inc=False):
        return self._emit(
            "gpsimd",
            lambda e: e.affine_select(out=out, in_=in_, pattern=pattern,
                                      compare_op=compare_op, fill=fill,
                                      base=base, channel_multiplier=chmul),
            ("sP", 1) if inc else None)

    def barrier(self, sb):
        stats = sb["stats"]
        pe = self.ns.get("pe_last")
        self.vmemset(stats[:, 252:253], 0.0, inc=True)
        self.act(stats[:, 253:254], stats[:, 253:254], AF.Copy, scale=1.0,
                 inc=True)
        self.gmemset(stats[:, 254:255], 0.0, inc=True)
        targets = [("sV", self.tag("sV")), ("sA", self.tag("sA")),
                   ("sP", self.tag("sP")), ("sS", self.tag("sS")),
                   ("sG", self.tag("sG")), ("sC", self.tag("sC"))]
        if pe is not None:
            targets.append(("sPE", pe))
        for engname in ("sync", "tensor", "vector", "scalar", "gpsimd"):
            for sem, v in targets:
                self.wait(engname, sem, v)



def build_nc():
    nc = bass.Bass("TRN2", target_bir_lowering=False)

    dram = {}

    def din(name, shape, dtype=F32):
        dram[name] = nc.dram_tensor(name, list(shape), dtype,
                                    kind="ExternalInput").ap()

    din("x", (T, D))
    din("ln_g", (D,))
    din("ln_b", (D,))
    din("W_h", (D, 2 * HID))
    din("b_h", (2 * HID,))
    din("W_qk", (D, 128))
    din("b_qk", (128,))
    din("os_gamma", (4, 128))
    din("os_beta", (4, 128))
    din("rel_emb", (32, 1))
    din("W_out", (HID, D))
    din("b_out", (D,))
    din("W_dq", (D, 256))
    din("W_uq", (256, D))
    din("W_dkv", (D, 256))
    din("W_uk", (256, D))
    din("W_uv", (256, D))
    din("W_o", (D, D))
    din("r_ls", (D,))
    din("r_lb", (D,))
    din("r_w", (E, D))
    din("r_b", (E,))
    din("e_ls", (E, D))
    din("e_lb", (E, D))
    din("w1", (E, H, D))
    din("b1", (E, H))
    din("w2", (E, D, H))
    din("b2", (E, D))
    din("OH", (32, 511))

    y_out = nc.dram_tensor("y", [T, D], F32, kind="ExternalOutput").ap()
    dbg = {}
    if DEBUG:
        for nm, shape, dt in (("d_stats", [128, 256], F32),
                              ("d_wmed", [128, 10240], BF16),
                              ("d_actfm", [128, 16384], BF16),
                              ("d_wbig", [128, 4096], BF16),
                              ("d_rows", [1, 3200], BF16),
                              ("d_cols", [128, 192], F32),
                              ("d_cols2", [128, 64], F32),
                              ("d_b1col", [128, 256], F32),
                              ("d_biasT", [128, 512], BF16),
                              ("d_gates", [128, 128], F32),
                              ("d_hbuf", [128, 4096], BF16),
                              ("d_rows32", [1, 640], F32),
                              ("d_bvec", [1, 511], F32),
                              ("d_rwT", [128, 64], BF16),
                              ("d_w1T", [128, 4096], BF16)):
            dbg[nm] = nc.dram_tensor(nm, shape, dt, kind="ExternalOutput").ap()

    scr = {}

    def dscr(name, shape, dtype=BF16):
        scr[name] = nc.dram_tensor(name, list(shape), dtype,
                                   kind="Internal").ap()

    dscr("w1c", (E, H, D))
    dscr("w2c", (E, D, H))
    dscr("w1T", (E, D, H))
    dscr("Whc", (D, 2 * HID))
    dscr("Wqkc", (D, 128))
    dscr("Woutc", (HID, D))
    dscr("Wdqc", (D, 256))
    dscr("Wdkvc", (D, 256))
    dscr("Wuqc", (256, D))
    dscr("Wukc", (256, D))
    dscr("Wuvc", (256, D))
    dscr("Woc", (D, D))
    dscr("rowscr", (1, 4096), F32)
    dscr("bvec", (1, 511), F32)

    import contextlib
    with contextlib.ExitStack() as est:
        EE = est.enter_context
        sb = {}

        def sbuf(name, shape, dtype):
            sb[name] = EE(nc.sbuf_tensor(name, list(shape), dtype))

        sbuf("xres", (128, NCH * D), BF16)
        sbuf("wbig", (128, 32768), BF16)
        sbuf("actfm", (128, 16384), BF16)
        sbuf("wmed", (128, 10240), BF16)
        sbuf("hbuf", (128, 4096), BF16)
        sbuf("scr32", (128, 1024), F32)
        sbuf("stats", (128, 256), F32)
        sbuf("gates", (128, 128), F32)
        sbuf("biasT", (128, 512), BF16)
        sbuf("cmask", (128, 128), F32)
        sbuf("mask01", (128, 128), BF16)
        sbuf("ident", (128, 128), BF16)
        sbuf("b1col", (128, 256), F32)
        sbuf("cols", (128, 192), F32)
        sbuf("cols2", (128, 64), F32)
        sbuf("colsb", (128, 96), BF16)
        sbuf("relsb", (32, 1), BF16)
        sbuf("ohsb", (32, 512), BF16)
        sbuf("rows", (1, 3200), BF16)
        sbuf("rows32", (1, 640), F32)
        sbuf("onesb", (1, 128), BF16)
        sbuf("rwT", (128, 64), BF16)

        ps = {}
        for nm, shape, dt in (("pA", [128, 512], F32), ("pB", [128, 512], F32),
                              ("pC", [128, 512], F32),
                              ("pE2", [128, 1024], F32),
                              ("pF2", [128, 1024], F32),
                              ("pTP", [128, 1024], BF16)):
            ps[nm] = EE(nc.psum_tensor(nm, shape, dt))

        sems = {}
        for sname in ("sS", "sG", "sC", "sPE", "sV", "sA", "sP"):
            sems[sname] = EE(nc.semaphore(sname))

        block = EE(nc.Block())
        prog = Prog(nc, sems)

        def emit_all(engine, eng):
            prog.begin(engine, eng)
            program(prog, dram, scr, sb, ps, y_out, dbg if DEBUG else None)
            prog.end()

        @block.sync
        def _(e):
            emit_all("sync", e)

        @block.tensor
        def _(e):
            emit_all("tensor", e)

        @block.vector
        def _(e):
            emit_all("vector", e)

        @block.scalar
        def _(e):
            emit_all("scalar", e)

        @block.gpsimd
        def _(e):
            emit_all("gpsimd", e)

    return nc


def program(p, dram, scr, sb, ps, y_out, dbg=None):
    phase_load_and_prep(p, dram, scr, sb, ps)
    p.barrier(sb)
    if PHASES["attn_a"]:
        for seq in range(NSEQ_RUN):
            attn_a_seq(p, dram, scr, sb, ps, seq)
            p.barrier(sb)
    if dbg is not None and DEBUG_POINT == "attn_a":
        dump_debug(p, sb, dbg)
    prep_ffn_weights(p, dram, scr, sb, ps)
    p.barrier(sb)
    if PHASES["ffn1"]:
        for seq in range(NSEQ_RUN):
            ffn_seq(p, dram, scr, sb, ps, seq)
            p.barrier(sb)
    if dbg is not None and DEBUG_POINT == "ffn1":
        dump_debug(p, sb, dbg)
    if PHASES["attn_b"]:
        for seq in range(NSEQ_RUN):
            attn_b_seq(p, dram, scr, sb, ps, seq)
            p.barrier(sb)
    if PHASES["ffn2"]:
        for seq in range(NSEQ_RUN):
            ffn_seq(p, dram, scr, sb, ps, seq)
            p.barrier(sb)

    phase_output(p, sb, y_out)


def dump_debug(p, sb, dbg):
    pairs = [("d_stats", sb["stats"][:]), ("d_wmed", sb["wmed"][:]),
             ("d_actfm", sb["actfm"][:]), ("d_wbig", sb["wbig"][:, 0:4096]),
             ("d_rows", sb["rows"][:]), ("d_cols", sb["cols"][:]),
             ("d_cols2", sb["cols2"][:]), ("d_b1col", sb["b1col"][:]),
             ("d_biasT", sb["biasT"][:]), ("d_gates", sb["gates"][:]),
             ("d_hbuf", sb["hbuf"][:]), ("d_rows32", sb["rows32"][:]),
             ("d_rwT", sb["rwT"][:])]
    for nm, ap in pairs:
        p.dma("sync", dbg[nm], ap)
    p.dma("sync", dbg["d_bvec"], p.ns["bvec_scr"])
    p.dma("sync", dbg["d_w1T"], p.ns["w1T_scr"][0][0:128, :])
    p.wait("sync", "sS", p.tag("sS"))
    p.barrier(sb)



def ln_stats_seq(p, sb, seq, double, eps1, eps2):
    xres, stats, scr32, hbuf = sb["xres"], sb["stats"], sb["scr32"], sb["hbuf"]
    n = float(D)
    ssq_rd = {0: None, 1: None}
    for ci in range(CH_SEQ):
        sl = ci % 2
        xc = xres[:, (seq * CH_SEQ + ci) * D:(seq * CH_SEQ + ci + 1) * D]
        sq = hbuf[:, sl * D:(sl + 1) * D]
        p.wait("scalar", "sV", ssq_rd[sl])
        p.act(sq, xc, AF.Square, inc=True)
        sq_tag = p.tag("sA")
        p.vred(stats[:, 64 + ci:65 + ci], xc, ALU.add)
        p.wait("vector", "sA", sq_tag)
        p.vred(stats[:, 80 + ci:81 + ci], sq, ALU.add, inc=True)
        ssq_rd[sl] = p.tag("sV")
    p.wait("vector", "sV", p.tag("sV"))
    p.vts1(stats[:, 0:16], stats[:, 64:80], 1.0 / n, ALU.mult, inc=True)
    p.wait("vector", "sV", p.tag("sV"))
    p.vtt(stats[:, 128:144], stats[:, 0:16], stats[:, 0:16], ALU.mult,
          inc=True)
    p.wait("vector", "sV", p.tag("sV"))
    p.vstt(stats[:, 144:160], stats[:, 80:96], 1.0 / n, stats[:, 128:144],
           ALU.mult, ALU.subtract, inc=True)
    p.wait("scalar", "sV", p.tag("sV"))
    p.act(stats[:, 160:176], stats[:, 144:160], AF.Sqrt,
          bias=stats[:, 248:249] if eps1 == EPS_LN else stats[:, 249:250],
          scale=1.0, inc=True)
    p.wait("vector", "sA", p.tag("sA"))
    p.vrecip(stats[:, 16:32], stats[:, 160:176], inc=True)
    p.wait("vector", "sV", p.tag("sV"))
    if not double:
        return
    p.vtt(stats[:, 128:144], stats[:, 16:32], stats[:, 16:32], ALU.mult,
          inc=True)
    p.wait("vector", "sV", p.tag("sV"))
    p.vtt(stats[:, 144:160], stats[:, 144:160], stats[:, 128:144], ALU.mult,
          inc=True)
    p.wait("scalar", "sV", p.tag("sV"))
    p.wait("vector", "sV", p.tag("sV"))
    p.act(stats[:, 160:176], stats[:, 144:160], AF.Sqrt,
          bias=stats[:, 248:249] if eps2 == EPS_LN else stats[:, 249:250],
          scale=1.0, inc=True)
    p.wait("vector", "sA", p.tag("sA"))
    p.vrecip(stats[:, 32:48], stats[:, 160:176], inc=True)
    p.wait("vector", "sV", p.tag("sV"))
    p.vtt(stats[:, 48:64], stats[:, 16:32], stats[:, 32:48], ALU.mult,
          inc=True)
    p.wait("vector", "sV", p.tag("sV"))


def build_fm_chunk(p, sb, ps, seq, ci, dst_view, double, slot):
    xres, stats, scr32, hbuf = sb["xres"], sb["stats"], sb["scr32"], sb["hbuf"]
    ident, pTP = sb["ident"], ps["pTP"]
    st = p.ns.setdefault("fm", {"evac": None, "slot": {}})
    xc = xres[:, (seq * CH_SEQ + ci) * D:(seq * CH_SEQ + ci + 1) * D]
    stage = hbuf[:, slot * D:(slot + 1) * D]
    p.wait("vector", "sPE", st["slot"].get(slot))
    rcol = stats[:, 48 + ci:49 + ci] if double else stats[:, 16 + ci:17 + ci]
    p.vts(stage, xc, stats[:, ci:ci + 1], rcol, ALU.subtract, ALU.mult,
          inc=True)
    aff = p.tag("sV")
    p.wait("tensor", "sV", aff)
    p.wait("tensor", "sV", st["evac"])
    p.wait("tensor", "sA", st.get("evac_a"))
    for dc in range(DC):
        p.tp(pTP[:, dc * 128:(dc + 1) * 128], stage[:, dc * 128:(dc + 1) * 128],
             ident[:], inc=(dc == DC - 1))
    tptag = p.tag("sPE")
    st["slot"][slot] = tptag
    p.wait("vector", "sPE", tptag)
    ptv = pTP[:].rearrange("q (dc t) -> q dc t", t=128)
    p.vcopy(dst_view, ptv, inc=True)
    st["evac"] = p.tag("sV")
    return p.tag("sV")



def phase_load_and_prep(p, dram, scr, sb, ps):
    ns = p.ns
    xres, wbig, wmed = sb["xres"], sb["wbig"], sb["wmed"]
    cols, cols2, colsb = sb["cols"], sb["cols2"], sb["colsb"]

    xv = dram["x"].rearrange("(c p) d -> p c d", p=128)
    for i in range(4):
        p.dma("gpsimd", xres[:, i * 8 * D:(i + 1) * 8 * D],
              xv[:, i * 8:(i + 1) * 8, :])

    p.gmemset(sb["ident"][:], 0.0)
    p.gaffine(sb["ident"][:], sb["ident"][:], [[-1, 128]], ALU.not_equal, 1.0,
              0, 1)
    p.gmemset(sb["cmask"][:], 0.0)
    p.gaffine(sb["cmask"][:], sb["cmask"][:], [[-1, 128]], ALU.is_ge, NEGBIG,
              0, 1)
    p.gmemset(sb["mask01"][:], 1.0)
    p.gaffine(sb["mask01"][:], sb["mask01"][:], [[-1, 128]], ALU.is_ge, 0.0,
              0, 1, inc=True)
    p.vmemset(sb["onesb"][:], 1.0)
    p.vmemset(sb["colsb"][:, 80:81], 1.0)
    p.vmemset(sb["stats"][:, 248:249], EPS_LN)
    p.vmemset(sb["stats"][:, 249:250], EPS_FFN, inc=True)

    for nm_in, nm_out in (("W_h", "Whc"), ("W_qk", "Wqkc"), ("W_out", "Woutc"),
                          ("W_dq", "Wdqc"), ("W_dkv", "Wdkvc"),
                          ("W_uq", "Wuqc"), ("W_uk", "Wukc"),
                          ("W_uv", "Wuvc"), ("W_o", "Woc")):
        p.dma("gpsimd", scr[nm_out], dram[nm_in])
    ns["small_cast"] = p.tag("sG")
    for e in range(E):
        p.dma("gpsimd", scr["w1c"][e], dram["w1"][e])
        p.dma("gpsimd", scr["w2c"][e], dram["w2"][e])
    ns["big_cast"] = p.tag("sG")

    for dc in range(DC):
        p.dma("sync", cols[:, C_LNG + dc:C_LNG + dc + 1],
              dram["ln_g"][dc * 128:(dc + 1) * 128])
        p.dma("sync", cols[:, C_LNB + dc:C_LNB + dc + 1],
              dram["ln_b"][dc * 128:(dc + 1) * 128])
        p.dma("sync", cols[:, C_RLS + dc:C_RLS + dc + 1],
              dram["r_ls"][dc * 128:(dc + 1) * 128])
        p.dma("gpsimd", colsb[:, dc:dc + 1],
              dram["ln_b"][dc * 128:(dc + 1) * 128])
        p.dma("gpsimd", colsb[:, 8 + dc:9 + dc],
              dram["r_lb"][dc * 128:(dc + 1) * 128])
    for h in range(4):
        p.dma("sync", cols[:, C_GAM + h:C_GAM + h + 1], dram["os_gamma"][h, :])
        p.dma("sync", cols[:, C_BET + h:C_BET + h + 1], dram["os_beta"][h, :])
    for e in range(E):
        for dc in range(DC):
            p.dma("sync", cols[:, C_ELS + e * 8 + dc:C_ELS + e * 8 + dc + 1],
                  dram["e_ls"][e, dc * 128:(dc + 1) * 128])
            p.dma("gpsimd", colsb[:, 16 + e * 8 + dc:17 + e * 8 + dc],
                  dram["e_lb"][e, dc * 128:(dc + 1) * 128])
    for e in range(E):
        for hc in range(32):
            p.dma("sync", sb["b1col"][:, e * 32 + hc:e * 32 + hc + 1],
                  dram["b1"][e, hc * 128:(hc + 1) * 128])
    p.dma("gpsimd", sb["relsb"][:], dram["rel_emb"])
    p.dma("gpsimd", sb["ohsb"][:, 0:511], dram["OH"])
    p.dma("gpsimd", sb["rows"][0:1, 2048:3072], dram["b_out"][None, :])
    ns["cols_f32"] = p.tag("sS")
    ns["cols_bf"] = p.tag("sG")

    p.wait("vector", "sS", ns["cols_f32"])
    p.vts1(cols[:, C_GAM + 1:C_GAM + 2], cols[:, C_GAM + 1:C_GAM + 2],
           1.0 / S, ALU.mult)
    p.vts1(cols[:, C_BET + 1:C_BET + 2], cols[:, C_BET + 1:C_BET + 2],
           1.0 / S, ALU.mult, inc=True)

    p.wait("tensor", "sG", ns["cols_bf"])
    p.mm(ps["pA"][0:1, 0:511], sb["relsb"][:], sb["ohsb"][:, 0:511],
         start=True, stop=True, inc=True)
    t = p.tag("sPE")
    p.wait("vector", "sPE", t)
    p.vcopy(sb["rows32"][0:1, 0:511], ps["pA"][0:1, 0:511], inc=True)
    bvec_evac = p.tag("sV")
    t = p.tag("sV")
    p.wait("sync", "sV", t)
    p.dma("sync", scr["bvec"], sb["rows32"][0:1, 0:511])
    p.ns["bvec_scr"] = scr["bvec"]
    t = p.tag("sS")
    p.wait("gpsimd", "sS", t)
    for jc in range(2):
        for row in range(128):
            j = jc * 128 + row
            p.dma("gpsimd", sb["biasT"][row:row + 1, jc * 256:jc * 256 + 256],
                  scr["bvec"][0:1, 255 - j:255 - j + 256])

    p.wait("sync", "sG", ns["small_cast"])
    whv = scr["Whc"].rearrange("(dcb q) h -> q dcb h", q=128)
    p.dma("sync", wbig[:].rearrange("q (dcb h) -> q dcb h", h=4096), whv)
    wqv = scr["Wqkc"].rearrange("(dcb q) h -> q dcb h", q=128)
    p.dma("sync", wmed[:, 0:1024].rearrange("q (dcb h) -> q dcb h", h=128),
          wqv)
    ns["wh_raw"] = p.tag("sS")

    p.wait("tensor", "sS", ns["wh_raw"])
    p.wait("tensor", "sG", ns["cols_bf"])
    prev_use = {0: bvec_evac, 1: None}
    for nt in range(8):
        pt = ps["pA"] if nt % 2 == 0 else ps["pB"]
        sl = pt[0:1, 0:512]
        p.wait("tensor", "sV", prev_use[nt % 2])
        for dc in range(DC):
            p.mm(sl, colsb[:, dc:dc + 1],
                 wbig[:, dc * 4096 + nt * 512:dc * 4096 + nt * 512 + 512],
                 start=(dc == 0), stop=(dc == DC - 1), inc=(dc == DC - 1))
        mmtag = p.tag("sPE")
        p.wait("sync", "sV", p.ns.get("bh_scr_free"))
        p.dma("sync", sb["scr32"][0:1, 0:512],
              dram["b_h"][None, nt * 512:(nt + 1) * 512])
        ld = p.tag("sS")
        p.wait("vector", "sPE", mmtag)
        p.wait("vector", "sS", ld)
        p.vtt(sb["scr32"][0:1, 0:512], sl, sb["scr32"][0:1, 0:512], ALU.add,
              inc=True)
        prev_use[nt % 2] = p.tag("sV")
        p.ns["bh_scr_free"] = p.tag("sV")
        p.wait("vector", "sV", p.tag("sV"))
        if nt < 4:
            p.vcopy(sb["rows"][0:1, nt * 512:(nt + 1) * 512],
                    sb["scr32"][0:1, 0:512], inc=True)
            prev_use[nt % 2] = p.tag("sV")
            p.ns["bh_scr_free"] = p.tag("sV")
        else:
            t = p.tag("sV")
            p.wait("sync", "sV", t)
            p.dma("sync", scr["rowscr"][0:1, nt * 512:(nt + 1) * 512],
                  sb["scr32"][0:1, 0:512])
    t = p.tag("sS")
    p.wait("sync", "sS", t)
    for gc in range(16):
        a = 2048 + gc * 128
        p.dma("sync", cols2[:, gc:gc + 1], scr["rowscr"][0:1, a:a + 128])

    for dc in range(DC):
        p.mm(ps["pC"][:, 0:1], wmed[:, dc * 128:(dc + 1) * 128],
             colsb[:, dc:dc + 1], start=(dc == 0), stop=(dc == DC - 1),
             inc=(dc == DC - 1))
    t = p.tag("sPE")
    p.dma("sync", cols[:, C_BQK + 1:C_BQK + 2], dram["b_qk"][:])
    ld = p.tag("sS")
    p.wait("vector", "sPE", t)
    p.wait("vector", "sS", ld)
    p.vtt(cols[:, C_BQK:C_BQK + 1], ps["pC"][:, 0:1],
          cols[:, C_BQK + 1:C_BQK + 2], ALU.add, inc=True)
    bqk_read = p.tag("sV")

    p.wait("vector", "sS", ns["cols_f32"])
    p.wait("vector", "sPE", t)
    for dc in range(DC):
        p.vts1(wbig[:, dc * 4096:(dc + 1) * 4096],
               wbig[:, dc * 4096:(dc + 1) * 4096],
               cols[:, C_LNG + dc:C_LNG + dc + 1], ALU.mult)
        p.vts1(wmed[:, dc * 128:(dc + 1) * 128],
               wmed[:, dc * 128:(dc + 1) * 128],
               cols[:, C_LNG + dc:C_LNG + dc + 1], ALU.mult,
               inc=(dc == DC - 1))

    for dc in range(DC):
        for e in range(E):
            p.dma("gpsimd", sb["rwT"][:, dc * 8 + e:dc * 8 + e + 1],
                  dram["r_w"][e, dc * 128:(dc + 1) * 128])
    t = p.tag("sG")
    p.wait("tensor", "sG", t)
    p.wait("tensor", "sV", bqk_read)
    for dc in range(DC):
        p.mm(ps["pC"][0:8, 8:9], sb["rwT"][:, dc * 8:(dc + 1) * 8],
             colsb[:, 8 + dc:9 + dc], start=(dc == 0), stop=(dc == DC - 1),
             inc=(dc == DC - 1))
    t = p.tag("sPE")
    p.dma("sync", sb["rows32"][0:1, 520:528], dram["r_b"][None, :])
    ld = p.tag("sS")
    p.wait("vector", "sPE", t)
    p.wait("vector", "sS", ld)
    p.vcopy(sb["scr32"][0:8, 512:513], ps["pC"][0:8, 8:9], inc=True)
    t = p.tag("sV")
    p.wait("sync", "sV", t)
    p.dma("sync", scr["rowscr"][0:1, 3584:3592], sb["scr32"][0:8, 512:513])
    t = p.tag("sS")
    p.wait("sync", "sS", t)
    p.dma("sync", sb["rows32"][0:1, 512:520], scr["rowscr"][0:1, 3584:3592])
    t = p.tag("sS")
    p.wait("vector", "sS", t)
    p.vtt(sb["rows32"][0:1, 512:520], sb["rows32"][0:1, 512:520],
          sb["rows32"][0:1, 520:528], ALU.add, inc=True)
    p.wait("vector", "sV", p.tag("sV"))
    p.vcopy(sb["rows"][0:1, 3072:3080], sb["rows32"][0:1, 512:520], inc=True)
    for dc in range(DC):
        p.vts1(sb["rwT"][:, dc * 8:(dc + 1) * 8],
               sb["rwT"][:, dc * 8:(dc + 1) * 8],
               cols[:, C_RLS + dc:C_RLS + dc + 1], ALU.mult,
               inc=(dc == DC - 1))
    p.ns["pe_last"] = p.tag("sPE")

def prep_ffn_weights(p, dram, scr, sb, ps):
    p.ns["w1T_scr"] = scr["w1T"]
    """w1T scratch: transpose-DMA strips of w1c, fold e_ls rows, store.
    b1col += lb_e @ W1^T via [128,1]-column matmuls into pC[:,128:160].
    PSUM note: single start=True on the very first MM of the pC session."""
    ns = p.ns
    hbuf, b1col, cols, colsb = sb["hbuf"], sb["b1col"], sb["cols"], sb["colsb"]
    p.wait("sync", "sG", ns["big_cast"])
    mm_done = {0: None, 1: None}
    sc_done = {0: None, 1: None}
    st_done = {0: None, 1: None}
    b1_use = None
    for e in range(E):
        mm_tag = None
        for dc in range(DC):
            for k in range(2):
                hsl = hbuf[:, k * 2048:(k + 1) * 2048]
                q = "sync" if k == 0 else "scalar"
                p.wait(q, DMA_Q["sync" if k == 0 else "scalar"],
                       st_done[k])
                p.wait(q, "sV", sc_done[k])
                p.dma(q, hsl,
                      scr["w1c"][e][k * 2048:(k + 1) * 2048,
                                    dc * 128:(dc + 1) * 128],
                      transpose=True)
                ld = p.tag(DMA_Q[q])
                p.wait("tensor", DMA_Q[q], ld)
                if dc == 0 and k == 0:
                    p.wait("tensor", "sV", b1_use)
                for hc in range(16):
                    hh = k * 16 + hc
                    p.mm(ps["pC"][:, 128 + hh:129 + hh],
                         hsl[:, hc * 128:(hc + 1) * 128],
                         colsb[:, 16 + e * 8 + dc:17 + e * 8 + dc],
                         start=(dc == 0 and k == 0 and hc == 0),
                         stop=(dc == DC - 1 and k == 1 and hc == 15),
                         inc=(hc == 15))
                mm_done[k] = p.tag("sPE")
                mm_tag = mm_done[k]
                p.wait("vector", "sPE", mm_done[k])
                p.wait("vector", DMA_Q[q], ld)
                p.vts1(hsl, hsl,
                       cols[:, C_ELS + e * 8 + dc:C_ELS + e * 8 + dc + 1],
                       ALU.mult, inc=True)
                sc_done[k] = p.tag("sV")
                p.wait(q, "sV", sc_done[k])
                p.dma(q, scr["w1T"][e][dc * 128:(dc + 1) * 128,
                                       k * 2048:(k + 1) * 2048], hsl)
                st_done[k] = p.tag(DMA_Q[q])
        p.wait("vector", "sPE", mm_tag)
        p.vtt(b1col[:, e * 32:(e + 1) * 32], b1col[:, e * 32:(e + 1) * 32],
              ps["pC"][:, 128:160], ALU.add, inc=True)
        b1_use = p.tag("sV")
    p.ns["pe_last"] = p.tag("sPE")



def attn_a_seq(p, dram, scr, sb, ps, seq):
    ns = p.ns
    xres, wbig, wmed, actfm = sb["xres"], sb["wbig"], sb["wmed"], sb["actfm"]
    cols, cols2, stats, scr32 = (sb["cols"], sb["cols2"], sb["stats"],
                                 sb["scr32"])
    onesb, rows = sb["onesb"], sb["rows"]
    pA, pB, pC, pE2, pF2 = ps["pA"], ps["pB"], ps["pC"], ps["pE2"], ps["pF2"]

    ln_stats_seq(p, sb, seq, True, EPS_LN, EPS_LN)

    xa_view = actfm[:, 0:2048].rearrange("q (dc t) -> q dc t", t=256)

    def build_xa(g):
        for tc in range(2):
            t = build_fm_chunk(p, sb, ps, seq, g * 2 + tc,
                               xa_view[:, :, tc * 128:(tc + 1) * 128], True, tc)
        return t

    def v_tm_group(xa_tag, bankfree):
        for tc in range(2):
            for nt in range(4):
                bk = (tc * 4 + nt) % 2
                pt = pA if bk == 0 else pB
                p.wait("tensor", "sV", xa_tag)
                p.wait("tensor", "sA", bankfree[bk])
                for dc in range(DC):
                    p.mm(pt[:], xa_view[:, dc, tc * 128:(tc + 1) * 128],
                         wbig[:, dc * 4096 + nt * 512:
                              dc * 4096 + nt * 512 + 512],
                         start=(dc == 0), stop=False)
                p.mm(pt[:], onesb[:], rows[0:1, nt * 512:(nt + 1) * 512],
                     start=False, stop=True, inc=True)
                mmt = p.tag("sPE")
                p.wait("scalar", "sPE", mmt)
                p.act(actfm[:, AF_V + tc * 2048 + nt * 512:
                            AF_V + tc * 2048 + nt * 512 + 512],
                      pt[:], AF.Silu, inc=True)
                bankfree[bk] = p.tag("sA")
        return p.tag("sA")

    st = {"vbank": {0: None, 1: None}, "qkfree": None}
    for g in range(NG):
        xa_tag = build_xa(g)
        p.wait("tensor", "sV", xa_tag)
        p.wait("tensor", "sA", st["qkfree"])
        for dc in range(DC):
            p.mm(pC[:, 0:256], wmed[:, dc * 128:(dc + 1) * 128],
                 xa_view[:, dc, :], start=(dc == 0), stop=(dc == DC - 1),
                 inc=(dc == DC - 1))
        qmm = p.tag("sPE")
        p.wait("scalar", "sPE", qmm)
        p.act(wmed[:, WM_QKACT + g * 256:WM_QKACT + (g + 1) * 256],
              pC[:, 0:256], AF.Silu, bias=cols[:, C_BQK:C_BQK + 1], scale=1.0,
              inc=True)
        st["qkfree"] = p.tag("sA")
        qk_tag = st["qkfree"]
        vev = v_tm_group(xa_tag, st["vbank"])
        p.wait("vector", "sA", qk_tag)
        p.wait("vector", "sPE", ns.get("lkt_pe"))
        p.vts(sb["hbuf"][:, 2048:2304],
              wmed[:, WM_QKACT + g * 256:WM_QKACT + (g + 1) * 256],
              cols[:, C_GAM + 3:C_GAM + 4], cols[:, C_BET + 3:C_BET + 4],
              ALU.mult, ALU.add, inc=True)
        lk = p.tag("sV")
        fmst = p.ns["fm"]
        p.wait("tensor", "sV", lk)
        p.wait("tensor", "sV", fmst["evac"])
        for tc in range(2):
            p.tp(ps["pTP"][:, tc * 128:(tc + 1) * 128],
                 sb["hbuf"][:, 2048 + tc * 128:2048 + (tc + 1) * 128],
                 sb["ident"][:], inc=(tc == 1))
        ns["lkt_pe"] = p.tag("sPE")
        p.wait("vector", "sPE", ns["lkt_pe"])
        p.vcopy(wmed[:, WM_LKT:WM_LKT + 256], ps["pTP"][:, 0:256], inc=True)
        fmst["evac"] = p.tag("sV")
        lkt_tag = p.tag("sV")
        p.wait("tensor", "sV", lkt_tag)
        p.wait("tensor", "sA", vev)
        for tc in range(2):
            for nt in range(4):
                pt = pE2 if nt < 2 else pF2
                sl = pt[:, (nt % 2) * 512:(nt % 2) * 512 + 512]
                p.mm(sl, wmed[:, WM_LKT + tc * 128:WM_LKT + (tc + 1) * 128],
                     actfm[:, AF_V + tc * 2048 + nt * 512:
                           AF_V + tc * 2048 + nt * 512 + 512],
                     start=(g == 0 and tc == 0),
                     stop=(g == NG - 1 and tc == 1),
                     inc=(g == NG - 1 and tc == 1 and nt == 3))
    linkv_mm = p.tag("sPE")
    p.wait("vector", "sPE", linkv_mm)
    p.vcopy(wmed[:, WM_LINKV:WM_LINKV + 1024], pE2[:], inc=True)
    p.vcopy(wmed[:, WM_LINKV + 1024:WM_LINKV + 2048], pF2[:], inc=True)
    linkv_tag = p.tag("sV")

    st2 = {"vbank": {0: None, 1: None}, "gslot": {0: None, 1: None},
           "qlslot": {0: None, 1: None, 2: None},
           "wout": {0: None, 1: None, 2: None},
           "resfree": {0: linkv_tag, 1: linkv_tag}}
    woutv = scr["Woutc"].rearrange("(ecb q) d2 -> q ecb d2", q=128)
    for g in range(NG):
        xa_tag = build_xa(g)
        vev = v_tm_group(xa_tag, st2["vbank"])
        for gc in range(16):
            bk = gc % 2
            pt = pC if bk == 0 else pA
            sl = pt[:, 0:256]
            p.wait("tensor", "sA", vev)
            p.wait("tensor", "sA", st2["gslot"][bk])
            for dc in range(DC):
                p.mm(sl, wbig[:, dc * 4096 + 2048 + gc * 128:
                              dc * 4096 + 2048 + (gc + 1) * 128],
                     xa_view[:, dc, :], start=(dc == 0), stop=(dc == DC - 1),
                     inc=(dc == DC - 1))
            gmm = p.tag("sPE")
            p.wait("scalar", "sPE", gmm)
            p.act(actfm[:, AF_GATE + gc * 256:AF_GATE + (gc + 1) * 256], sl,
                  AF.Silu, bias=cols2[:, gc:gc + 1], scale=1.0, inc=True)
            st2["gslot"][bk] = p.tag("sA")
        gate_done = p.tag("sA")
        qk_g = wmed[:, WM_QKACT + g * 256:WM_QKACT + (g + 1) * 256]
        p.vts(wmed[:, WM_QQ:WM_QQ + 256], qk_g, cols[:, C_GAM:C_GAM + 1],
              cols[:, C_BET:C_BET + 1], ALU.mult, ALU.add)
        p.vts(wmed[:, WM_QKD:WM_QKD + 256], qk_g,
              cols[:, C_GAM + 2:C_GAM + 3], cols[:, C_BET + 2:C_BET + 3],
              ALU.mult, ALU.add)
        p.vts(wmed[:, WM_LQ:WM_LQ + 256], qk_g, cols[:, C_GAM + 1:C_GAM + 2],
              cols[:, C_BET + 1:C_BET + 2], ALU.mult, ALU.add, inc=True)
        heads_tag = p.tag("sV")
        p.wait("tensor", "sV", heads_tag)
        p.wait("tensor", "sA", gate_done)
        for jc in range(2):
            p.mm(pC[:, jc * 256:(jc + 1) * 256],
                 wmed[:, WM_QKD + jc * 128:WM_QKD + (jc + 1) * 128],
                 wmed[:, WM_QQ:WM_QQ + 256], start=True, stop=True,
                 inc=(jc == 1))
        sim_tag = p.tag("sPE")
        p.wait("vector", "sPE", sim_tag)
        for jc in range(2):
            p.vstt(scr32[:, jc * 256:(jc + 1) * 256],
                   pC[:, jc * 256:(jc + 1) * 256], 1.0 / G,
                   sb["biasT"][:, jc * 256:(jc + 1) * 256], ALU.mult, ALU.add)
            p.vstt(wmed[:, WM_ATT + jc * 256:WM_ATT + (jc + 1) * 256],
                   scr32[:, jc * 256:(jc + 1) * 256], 0.0,
                   scr32[:, jc * 256:(jc + 1) * 256], ALU.max, ALU.mult,
                   inc=(jc == 1))
        att_tag = p.tag("sV")
        for ec in range(16):
            bk = ec % 3
            pt = (pA, pB, pC)[bk]
            sl = pt[:, 0:256]
            p.wait("tensor", "sV", st2["qlslot"][bk])
            p.wait("tensor", "sV", att_tag)
            for tc in range(2):
                p.mm(sl, actfm[:, AF_V + tc * 2048 + ec * 128:
                               AF_V + tc * 2048 + (ec + 1) * 128],
                     wmed[:, WM_ATT + tc * 256:WM_ATT + (tc + 1) * 256],
                     start=(tc == 0), stop=False)
            p.mm(sl, wmed[:, WM_LINKV + ec * 128:WM_LINKV + (ec + 1) * 128],
                 wmed[:, WM_LQ:WM_LQ + 256], start=False, stop=True, inc=True)
            qlmm = p.tag("sPE")
            p.wait("vector", "sPE", qlmm)
            p.vtt(actfm[:, AF_GATED + ec * 256:AF_GATED + (ec + 1) * 256], sl,
                  actfm[:, AF_GATE + ec * 256:AF_GATE + (ec + 1) * 256],
                  ALU.mult, inc=True)
            st2["qlslot"][bk] = p.tag("sV")
        gated_tag = p.tag("sV")
        for ec in range(16):
            slot = ec % 3
            q = "sync" if ec % 2 == 0 else "scalar"
            p.wait(q, "sPE", st2["wout"][slot])
            p.dma(q, wmed[:, WM_WOUT + slot * 1024:
                          WM_WOUT + (slot + 1) * 1024], woutv[:, ec, :])
            dtag = p.tag(DMA_Q[q])
            wsl = wmed[:, WM_WOUT + slot * 1024:WM_WOUT + (slot + 1) * 1024]
            p.wait("tensor", DMA_Q[q], dtag)
            if ec == 0:
                p.wait("tensor", "sV", gated_tag)
                p.wait("tensor", "sV", st2["resfree"][0])
                p.wait("tensor", "sV", st2["resfree"][1])
            for tc in range(2):
                pt = pE2 if tc == 0 else pF2
                gsl = actfm[:, AF_GATED + ec * 256 + tc * 128:
                            AF_GATED + ec * 256 + (tc + 1) * 128]
                for half in range(2):
                    p.mm(pt[:, half * 512:(half + 1) * 512], gsl,
                         wsl[:, half * 512:(half + 1) * 512],
                         start=(ec == 0), stop=False,
                         inc=(tc == 1 and half == 1 and ec < 15))
            if ec == 15:
                for tc in range(2):
                    pt = pE2 if tc == 0 else pF2
                    for half in range(2):
                        p.mm(pt[:, half * 512:(half + 1) * 512], onesb[:],
                             rows[0:1, 2048 + half * 512:
                                  2048 + (half + 1) * 512],
                             start=False, stop=True,
                             inc=(tc == 1 and half == 1))
            st2["wout"][slot] = p.tag("sPE")
        wout_mm = p.tag("sPE")
        for tc in range(2):
            ci = g * 2 + tc
            c = seq * CH_SEQ + ci
            pt = pE2 if tc == 0 else pF2
            xc = xres[:, c * D:(c + 1) * D]
            p.wait("vector", "sPE", wout_mm)
            p.vts(scr32[:, 0:D], xc, stats[:, ci:ci + 1],
                  stats[:, 16 + ci:17 + ci], ALU.subtract, ALU.mult)
            p.vtt(scr32[:, 0:D], scr32[:, 0:D], pt[:], ALU.add)
            p.vtt(xc, scr32[:, 0:D], xc, ALU.add, inc=True)
            st2["resfree"][tc] = p.tag("sV")
    p.ns["pe_last"] = p.tag("sPE")



def ffn_seq(p, dram, scr, sb, ps, seq):
    ns = p.ns
    xres, wbig, wmed, actfm = sb["xres"], sb["wbig"], sb["wmed"], sb["actfm"]
    hbuf, stats, scr32, gates = (sb["hbuf"], sb["stats"], sb["scr32"],
                                 sb["gates"])
    gatesT = sb["wmed"][0:8, 0:2048]
    b2sb = sb["wmed"][0:8, 2048:3072]
    onesb, rows = sb["onesb"], sb["rows"]
    p.dma("gpsimd", b2sb, dram["b2"])
    b2_tag = p.tag("sG")
    pA, pB, pC, pE2, pF2, pTP = (ps["pA"], ps["pB"], ps["pC"], ps["pE2"],
                                 ps["pF2"], ps["pTP"])

    ln_stats_seq(p, sb, seq, True, EPS_LN, EPS_FFN)

    xh_view = actfm[:].rearrange("q (dc t) -> q dc t", t=2048)
    xhat_tag = None
    for ci in range(CH_SEQ):
        xhat_tag = build_fm_chunk(p, sb, ps, seq, ci,
                                  xh_view[:, :, ci * 128:(ci + 1) * 128],
                                  True, ci % 2)

    lfree = {0: None, 1: None}
    for ci in range(CH_SEQ):
        bk = ci % 2
        pt = pC if bk == 0 else pA
        sl = pt[:, 0:8]
        p.wait("tensor", "sV", xhat_tag)
        p.wait("tensor", "sV", lfree[bk])
        for dc in range(DC):
            p.mm(sl, xh_view[:, dc, ci * 128:(ci + 1) * 128],
                 sb["rwT"][:, dc * 8:(dc + 1) * 8], start=(dc == 0),
                 stop=False)
        p.mm(sl, onesb[:], rows[0:1, 3072:3080], start=False, stop=True,
             inc=True)
        lmm = p.tag("sPE")
        p.wait("vector", "sPE", lmm)
        p.vcopy(gates[:, ci * 8:(ci + 1) * 8], sl, inc=True)
        lfree[bk] = p.tag("sV")
    logits_tag = p.tag("sV")
    p.wait("scalar", "sV", logits_tag)
    p.act(gates[:], gates[:], AF.Exp, inc=True)
    t = p.tag("sA")
    p.wait("vector", "sA", t)
    gv = gates[:].rearrange("q (c e) -> q c e", e=8)
    m1 = stats[:, 176:192]
    m2 = stats[:, 192:208]
    dinv = stats[:, 208:224]
    w1t = stats[:, 224:240]
    p.vred(m1, gv, ALU.max, inc=True)
    p.wait("vector", "sV", p.tag("sV"))
    m1b = m1.unsqueeze(2).broadcast_to([128, 16, 8])
    is1 = scr32[:, 0:128].rearrange("q (c e) -> q c e", e=8)
    p.vtt(is1, gv, m1b, ALU.is_equal, inc=True)
    p.wait("vector", "sV", p.tag("sV"))
    masked = scr32[:, 128:256].rearrange("q (c e) -> q c e", e=8)
    p.vstt(masked, is1, NEGBIG, gv, ALU.mult, ALU.add, inc=True)
    p.wait("vector", "sV", p.tag("sV"))
    p.vred(m2, masked, ALU.max, inc=True)
    p.wait("vector", "sV", p.tag("sV"))
    m2b = m2.unsqueeze(2).broadcast_to([128, 16, 8])
    is2 = scr32[:, 256:384].rearrange("q (c e) -> q c e", e=8)
    p.vtt(is2, masked, m2b, ALU.is_equal, inc=True)
    p.vtt(dinv, m1, m2, ALU.add, inc=True)
    p.wait("vector", "sV", p.tag("sV"))
    p.vrecip(dinv, dinv, inc=True)
    p.wait("vector", "sV", p.tag("sV"))
    p.vtt(w1t, m1, dinv, ALU.mult)
    p.vtt(m2, m2, dinv, ALU.mult, inc=True)
    p.wait("vector", "sV", p.tag("sV"))
    w1b = w1t.unsqueeze(2).broadcast_to([128, 16, 8])
    w2b = m2.unsqueeze(2).broadcast_to([128, 16, 8])
    p.vtt(gv, is1, w1b, ALU.mult)
    p.vtt(is2, is2, w2b, ALU.mult, inc=True)
    p.wait("vector", "sV", p.tag("sV"))
    p.vtt(gv, gv, is2, ALU.add, inc=True)
    gates_tag = p.tag("sV")
    p.wait("vector", "sV", gates_tag)

    fmst = p.ns["fm"]
    p.wait("vector", "sPE", fmst["slot"].get(0))
    p.vcopy(hbuf[:, 0:128], gates[:], inc=True)
    gt0 = p.tag("sV")
    for rnd in range(2):
        p.wait("tensor", "sV", gt0)
        p.wait("tensor", "sV", fmst["evac"])
        for k in range(8):
            ci = rnd * 8 + k
            p.tp(pTP[0:8, k * 128:(k + 1) * 128],
                 hbuf[:, ci * 8:(ci + 1) * 8], sb["ident"][:], inc=(k == 7))
        tpt = p.tag("sPE")
        p.wait("vector", "sPE", tpt)
        p.vcopy(gatesT[0:8, rnd * 1024:(rnd + 1) * 1024], pTP[0:8, 0:1024],
                inc=True)
        fmst["evac"] = p.tag("sV")
    gatesT_tag = p.tag("sV")
    fmst["slot"][0] = p.tag("sPE")

    wload = {}
    slot_free = {0: None, 1: None}
    yfree = {0: gatesT_tag, 1: gatesT_tag}
    hfree = {i: None for i in range(4)}
    psfree = {i: None for i in range(3)}

    def issue_wload(e, q):
        slot = (e * NQ + q) % 2
        p.wait("sync", "sPE", slot_free[slot])
        w1v = scr["w1T"][e].rearrange("(dcb q2) h2 -> q2 dcb h2", q2=128)
        p.dma("sync",
              wbig[:, WB_W1Q[0] + slot * 8192:WB_W1Q[0] + slot * 8192 + 8192]
              .rearrange("q2 (dcb h2) -> q2 dcb h2", h2=1024),
              w1v[:, :, q * 1024:(q + 1) * 1024])
        for hc in range(8):
            p.dma("sync",
                  wbig[:, WB_W2Q[0] + slot * 8192 + hc * 1024:
                        WB_W2Q[0] + slot * 8192 + (hc + 1) * 1024],
                  scr["w2c"][e][:, q * 1024 + hc * 128:
                                q * 1024 + (hc + 1) * 128],
                  transpose=True)
        wload[(e, q)] = p.tag("sS")

    issue_wload(0, 0)
    for e in range(E):
        for q in range(NQ):
            slot = (e * NQ + q) % 2
            nxt = (e, q + 1) if q + 1 < NQ else ((e + 1, 0) if e + 1 < E
                                                 else None)
            if nxt is not None:
                issue_wload(*nxt)
            w1sl = wbig[:, WB_W1Q[0] + slot * 8192:
                        WB_W1Q[0] + slot * 8192 + 8192]
            w2sl = wbig[:, WB_W2Q[0] + slot * 8192:
                        WB_W2Q[0] + slot * 8192 + 8192]
            p.wait("tensor", "sS", wload[(e, q)])
            for tb in range(8):
                hact = {}

                def emit_h(hc):
                    hs = hc % 4
                    pb = hc % 3
                    pt = (pA, pB, pC)[pb]
                    sl = pt[:, 0:256]
                    p.wait("tensor", "sA", psfree[pb])
                    for dc in range(DC):
                        p.mm(sl, w1sl[:, dc * 1024 + hc * 128:
                                      dc * 1024 + (hc + 1) * 128],
                             xh_view[:, dc, tb * 256:(tb + 1) * 256],
                             start=(dc == 0), stop=(dc == DC - 1),
                             inc=(dc == DC - 1))
                    hmm = p.tag("sPE")
                    p.wait("scalar", "sPE", hmm)
                    p.wait("scalar", "sPE", hfree[hs])
                    p.act(hbuf[:, hs * 256:(hs + 1) * 256], sl, AF.Gelu,
                          bias=sb["b1col"][:, e * 32 + q * 8 + hc:
                                           e * 32 + q * 8 + hc + 1],
                          scale=1.0, inc=True)
                    psfree[pb] = p.tag("sA")
                    hact[hc] = p.tag("sA")

                def emit_y2(hc):
                    hs = hc % 4
                    p.wait("tensor", "sA", hact[hc])
                    if hc == 0:
                        p.wait("tensor", "sV", yfree[0])
                        p.wait("tensor", "sV", yfree[1])
                    for t128 in range(2):
                        pt2 = pE2 if t128 == 0 else pF2
                        for half in range(2):
                            p.mm(pt2[:, half * 512:(half + 1) * 512],
                                 hbuf[:, hs * 256 + t128 * 128:
                                      hs * 256 + (t128 + 1) * 128],
                                 w2sl[:, hc * 1024 + half * 512:
                                      hc * 1024 + (half + 1) * 512],
                                 start=(hc == 0), stop=(hc == 7),
                                 inc=(t128 == 1 and half == 1))
                    hfree[hs] = p.tag("sPE")

                emit_h(0)
                emit_h(1)
                for hc in range(2, 8):
                    emit_h(hc)
                    emit_y2(hc - 2)
                emit_y2(6)
                emit_y2(7)
                y2mm = p.tag("sPE")
                for t128 in range(2):
                    ci = tb * 2 + t128
                    c = seq * CH_SEQ + ci
                    pt2 = pE2 if t128 == 0 else pF2
                    xc = xres[:, c * D:(c + 1) * D]
                    p.wait("vector", "sPE", y2mm)
                    p.vstt(xc, pt2[:], gates[:, ci * 8 + e:ci * 8 + e + 1],
                           xc, ALU.mult, ALU.add, inc=True)
                    yfree[t128] = p.tag("sV")
            slot_free[slot] = p.tag("sPE")
    bfree = {0: yfree[0], 1: yfree[1]}
    for ci in range(CH_SEQ):
        c = seq * CH_SEQ + ci
        bk = ci % 2
        pt2 = pE2 if bk == 0 else pF2
        p.wait("tensor", "sV", bfree[bk])
        for half in range(2):
            p.wait("tensor", "sG", b2_tag)
            p.mm(pt2[:, half * 512:(half + 1) * 512],
                 gatesT[0:8, ci * 128:(ci + 1) * 128],
                 b2sb[:, half * 512:(half + 1) * 512],
                 start=True, stop=True, inc=(half == 1))
        bmm = p.tag("sPE")
        xc = xres[:, c * D:(c + 1) * D]
        p.wait("vector", "sPE", bmm)
        p.vtt(xc, pt2[:], xc, ALU.add, inc=True)
        bfree[bk] = p.tag("sV")
    p.ns["pe_last"] = p.tag("sPE")



def attn_b_seq(p, dram, scr, sb, ps, seq):
    ns = p.ns
    xres, wbig, wmed, actfm = sb["xres"], sb["wbig"], sb["wmed"], sb["actfm"]
    hbuf, stats, scr32 = sb["hbuf"], sb["stats"], sb["scr32"]
    pA, pB, pC, pE2, pF2, pTP = (ps["pA"], ps["pB"], ps["pC"], ps["pE2"],
                                 ps["pF2"], ps["pTP"])

    ln_stats_seq(p, sb, seq, False, EPS_LN, EPS_LN)
    yv = actfm[:].rearrange("q (dc t) -> q dc t", t=2048)
    y_tag = None
    for ci in range(CH_SEQ):
        y_tag = build_fm_chunk(p, sb, ps, seq, ci,
                               yv[:, :, ci * 128:(ci + 1) * 128], False,
                               ci % 2)

    for nm, off in (("Wdqc", WM_DQ), ("Wdkvc", WM_DKV)):
        wv = scr[nm].rearrange("(dcb q2) c2 -> q2 dcb c2", q2=128)
        p.dma("sync", wmed[:, off:off + 2048]
              .rearrange("q2 (dcb c2) -> q2 dcb c2", c2=256), wv)
    dq_tag = p.tag("sS")

    bank = {0: ns["fm"]["evac"], 1: ns["fm"]["evac"]}
    for which, off_w, dst, dsto in ((0, WM_DQ, hbuf, 0),
                                    (1, WM_DKV, wmed, WM_CKV)):
        for cc in range(2):
            for nt in range(4):
                bk = (cc * 4 + nt) % 2
                pt = pA if bk == 0 else pB
                p.wait("tensor", "sS", dq_tag)
                p.wait("tensor", "sV", y_tag)
                p.wait("tensor", "sV", bank[bk])
                for dc in range(DC):
                    p.mm(pt[:], wmed[:, off_w + dc * 256 + cc * 128:
                                     off_w + dc * 256 + (cc + 1) * 128],
                         yv[:, dc, nt * 512:(nt + 1) * 512],
                         start=(dc == 0), stop=(dc == DC - 1),
                         inc=(dc == DC - 1))
                cmm = p.tag("sPE")
                p.wait("vector", "sPE", cmm)
                p.vcopy(dst[:, dsto + cc * 2048 + nt * 512:
                            dsto + cc * 2048 + (nt + 1) * 512], pt[:],
                        inc=True)
                bank[bk] = p.tag("sV")
    c_done_pe = p.tag("sPE")
    c_done = p.tag("sV")

    p.wait("sync", "sPE", c_done_pe)
    for nm, off in (("Wukc", WM_UK), ("Wuvc", WM_UV)):
        wv = scr[nm].rearrange("(ccb q2) d2 -> q2 ccb d2", q2=128)
        p.dma("sync", wmed[:, off:off + 2048]
              .rearrange("q2 (ccb d2) -> q2 ccb d2", d2=1024), wv)
    uk_tag = p.tag("sS")

    for qc in range(8):
        for nt in range(4):
            bk = (qc * 4 + nt) % 2
            pt = pA if bk == 0 else pB
            p.wait("tensor", "sS", uk_tag)
            p.wait("tensor", "sV", c_done)
            p.wait("tensor", "sV", bank[bk])
            for cc in range(2):
                p.mm(pt[:], wmed[:, WM_UK + cc * 1024 + qc * 128:
                                 WM_UK + cc * 1024 + (qc + 1) * 128],
                     wmed[:, WM_CKV + cc * 2048 + nt * 512:
                          WM_CKV + cc * 2048 + (nt + 1) * 512],
                     start=(cc == 0), stop=(cc == 1), inc=(cc == 1))
            kmm = p.tag("sPE")
            p.wait("vector", "sPE", kmm)
            p.vcopy(wbig[:, qc * 2048 + nt * 512:qc * 2048 + (nt + 1) * 512],
                    pt[:], inc=True)
            bank[bk] = p.tag("sV")
    vfree = {0: None, 1: None}
    for tc in range(16):
        bk = tc % 2
        pt2 = pE2 if bk == 0 else pF2
        p.wait("tensor", "sV", vfree[bk])
        for half in range(2):
            for cc in range(2):
                p.mm(pt2[:, half * 512:(half + 1) * 512],
                     wmed[:, WM_CKV + cc * 2048 + tc * 128:
                          WM_CKV + cc * 2048 + (tc + 1) * 128],
                     wmed[:, WM_UV + cc * 1024 + half * 512:
                          WM_UV + cc * 1024 + (half + 1) * 512],
                     start=(cc == 0), stop=(cc == 1),
                     inc=(cc == 1 and half == 1))
        vmm = p.tag("sPE")
        p.wait("vector", "sPE", vmm)
        p.vcopy(wbig[:, 16384 + tc * 1024:16384 + (tc + 1) * 1024], pt2[:],
                inc=True)
        vfree[bk] = p.tag("sV")
    kv_done_pe = p.tag("sPE")
    kv_done = p.tag("sV")

    p.wait("sync", "sPE", kv_done_pe)
    wv = scr["Wuqc"].rearrange("(ccb q2) d2 -> q2 ccb d2", q2=128)
    p.dma("sync", wmed[:, WM_UQ:WM_UQ + 2048]
          .rearrange("q2 (ccb d2) -> q2 ccb d2", d2=1024), wv)
    wv = scr["Woc"].rearrange("(vcb q2) d2 -> q2 vcb d2", q2=128)
    p.dma("sync", actfm[:, 0:8192]
          .rearrange("q2 (vcb d2) -> q2 vcb d2", d2=1024), wv)
    uq_tag = p.tag("sS")

    st = {"qt": {0: None, 1: None}, "exp": None,
          "avfree": {0: None, 1: None}, "linv": None,
          "res": kv_done, "tpdone": None, "pn": {}, "J": {}}
    fmst = p.ns["fm"]

    def emit_head_scores(i, h):
        J = (i + 1) * 128
        njt = (J + 511) // 512
        st["J"][h] = (i, J, njt)
        qc, base = h // 2, (h % 2) * 64
        p.wait("tensor", "sA", st["qt"][0])
        p.wait("tensor", "sA", st["qt"][1])
        p.wait("tensor", "sA", st["exp"])
        if h == 0:
            p.wait("tensor", "sV", st["res"])
        for jt in range(njt):
            N = min(512, J - jt * 512)
            pt2 = pE2 if jt < 2 else pF2
            sl = pt2[:, (jt % 2) * 512:(jt % 2) * 512 + N]
            p.mm(sl, wmed[base:base + 64, WM_QT + qc * 128:
                          WM_QT + (qc + 1) * 128],
                 wbig[base:base + 64, qc * 2048 + jt * 512:
                      qc * 2048 + jt * 512 + N],
                 start=True, stop=True, inc=(jt == njt - 1))
        smm = p.tag("sPE")
        p.wait("vector", "sPE", smm)
        for jt in range(njt):
            N = min(512, J - jt * 512)
            pt2 = pE2 if jt < 2 else pF2
            p.vred(stats[:, 176 + jt:177 + jt],
                   pt2[:, (jt % 2) * 512:(jt % 2) * 512 + N], ALU.max,
                   inc=(jt == njt - 1))
        p.wait("vector", "sV", p.tag("sV"))
        if njt > 1:
            p.vred(stats[:, 180:181], stats[:, 176:176 + njt], ALU.max,
                   inc=True)
            p.wait("vector", "sV", p.tag("sV"))
            p.vts1(stats[:, 181:182], stats[:, 180:181], -0.125, ALU.mult,
                   inc=True)
        else:
            p.vts1(stats[:, 181:182], stats[:, 176:177], -0.125, ALU.mult,
                   inc=True)
        mtag = p.tag("sV")
        p.wait("scalar", "sV", mtag)
        p.wait("scalar", "sPE", st["tpdone"])
        for jt in range(njt):
            N = min(512, J - jt * 512)
            pt2 = pE2 if jt < 2 else pF2
            p.act(wmed[:, WM_PSB + jt * 512:WM_PSB + jt * 512 + N],
                  pt2[:, (jt % 2) * 512:(jt % 2) * 512 + N], AF.Exp,
                  bias=stats[:, 181:182], scale=0.125,
                  inc=(jt == njt - 1))
        exp_tag = p.tag("sA")
        st["exp"] = exp_tag
        p.wait("vector", "sA", exp_tag)
        dj = i * 128
        p.vtt(wmed[:, WM_PSB + dj:WM_PSB + dj + 128],
              wmed[:, WM_PSB + dj:WM_PSB + dj + 128], sb["mask01"][:],
              ALU.mult, inc=True)
        st["pn"][h] = p.tag("sV")
        p.vred(stats[:, 182:183], wmed[:, WM_PSB:WM_PSB + J], ALU.add,
               inc=True)
        p.wait("vector", "sV", p.tag("sV"))
        p.vrecip(stats[:, 183:184], stats[:, 182:183], inc=True)
        st["linv"] = p.tag("sV")

    def emit_head_tpav(h):
        i, J, njt = st["J"][h]
        qc, base = h // 2, (h % 2) * 64
        pn_tag = st["pn"][h]
        for jc in range(i + 1):
            ksl = jc % 8
            p.wait("tensor", "sV", pn_tag)
            if ksl == 0:
                p.wait("tensor", "sV", fmst["evac"])
                p.wait("tensor", "sA", fmst.get("evac_a"))
            p.tp(pTP[:, ksl * 128:(ksl + 1) * 128],
                 wmed[:, WM_PSB + jc * 128:WM_PSB + (jc + 1) * 128],
                 sb["ident"][:], inc=(ksl == 7 or jc == i))
            if ksl == 7 or jc == i:
                tpt = p.tag("sPE")
                p.wait("scalar", "sPE", tpt)
                lo = (jc // 8) * 8
                p.acopy(wmed[:, WM_PT + lo * 128:WM_PT + (jc + 1) * 128],
                        pTP[:, 0:(jc + 1 - lo) * 128], inc=True)
                fmst["evac_a"] = p.tag("sA")
        st["tpdone"] = p.tag("sPE")
        pt_tag = p.tag("sA")
        p.wait("tensor", "sA", pt_tag)
        avp = pB if h % 2 == 0 else pA
        p.wait("tensor", "sA", st["avfree"].get(h % 2))
        for jc in range(i + 1):
            p.mm(avp[:, 256:320],
                 wmed[:, WM_PT + jc * 128:WM_PT + (jc + 1) * 128],
                 wbig[:, 16384 + jc * 1024 + h * 64:
                      16384 + jc * 1024 + (h + 1) * 64],
                 start=(jc == 0), stop=(jc == i), inc=(jc == i))
        avmm = p.tag("sPE")
        p.wait("scalar", "sPE", avmm)
        p.wait("scalar", "sV", st["linv"])
        p.act(wmed[:, WM_AO + h * 64:WM_AO + (h + 1) * 64], avp[:, 256:320],
              AF.Identity, scale=stats[:, 183:184], inc=True)
        st["avfree"][h % 2] = p.tag("sA")

    for i in range(CH_SEQ):
        for qc in range(8):
            bk = qc % 2
            pt = pA if bk == 0 else pC
            sl = pt[:, 0:128]
            p.wait("tensor", "sS", uq_tag)
            p.wait("tensor", "sA", st["qt"][bk])
            p.wait("tensor", "sA", st["avfree"][1])
            for cc in range(2):
                p.mm(sl, wmed[:, WM_UQ + cc * 1024 + qc * 128:
                              WM_UQ + cc * 1024 + (qc + 1) * 128],
                     hbuf[:, cc * 2048 + i * 128:cc * 2048 + (i + 1) * 128],
                     start=(cc == 0), stop=(cc == 1), inc=(cc == 1))
            qmm = p.tag("sPE")
            p.wait("scalar", "sPE", qmm)
            p.acopy(wmed[:, WM_QT + qc * 128:WM_QT + (qc + 1) * 128], sl,
                    inc=True)
            st["qt"][bk] = p.tag("sA")
        for h in range(NHEAD):
            if h > 0:
                emit_head_tpav(h - 1)
            emit_head_scores(i, h)
        emit_head_tpav(NHEAD - 1)
        p.wait("tensor", "sA", st["avfree"][0])
        p.wait("tensor", "sA", st["avfree"][1])
        p.wait("tensor", "sV", fmst["evac"])
        p.wait("tensor", "sA", fmst.get("evac_a"))
        for vc in range(8):
            p.tp(pTP[:, vc * 128:(vc + 1) * 128],
                 wmed[:, WM_AO + vc * 128:WM_AO + (vc + 1) * 128],
                 sb["ident"][:], inc=(vc == 7))
        aot = p.tag("sPE")
        st["tpdone"] = aot
        p.wait("vector", "sPE", aot)
        p.vcopy(wmed[:, 4096:5120], pTP[:], inc=True)
        fmst["evac"] = p.tag("sV")
        ao_tag = p.tag("sV")
        p.wait("tensor", "sV", ao_tag)
        p.wait("tensor", "sV", st["res"])
        for vc in range(8):
            for half in range(2):
                p.mm(pE2[:, half * 512:(half + 1) * 512],
                     wmed[:, 4096 + vc * 128:4096 + (vc + 1) * 128],
                     actfm[:, vc * 1024 + half * 512:
                           vc * 1024 + (half + 1) * 512],
                     start=(vc == 0), stop=(vc == 7),
                     inc=(vc == 7 and half == 1))
        wmm = p.tag("sPE")
        st["tpdone"] = wmm
        c = seq * CH_SEQ + i
        xc = xres[:, c * D:(c + 1) * D]
        p.wait("vector", "sPE", wmm)
        p.vtt(xc, pE2[:], xc, ALU.add, inc=True)
        st["res"] = p.tag("sV")
    p.ns["pe_last"] = p.tag("sPE")

def phase_output(p, sb, y_out):
    xres = sb["xres"]
    yv = y_out.rearrange("(c p) d -> p c d", p=128)
    for i in range(4):
        p.dma("gpsimd", yv[:, i * 8:(i + 1) * 8, :],
              xres[:, i * 8 * D:(i + 1) * 8 * D]
              .rearrange("q (c d) -> q c d", d=D))
    p.wait("gpsimd", "sG", p.tag("sG"))
    p.wait("sync", "sG", p.tag("sG"))



def _bucket1d():
    NUM_BUCKETS = 32
    MAX_DIST = 128
    nb = NUM_BUCKETS // 2
    max_exact = nb // 2
    out = np.zeros(511, dtype=np.int64)
    for idx, dd in enumerate(range(-255, 256)):
        n = dd
        ret = nb if n < 0 else 0
        n = abs(n)
        if n < max_exact:
            val = n
        else:
            val = max_exact + int(
                math.log(max(n, 1) / max_exact)
                / math.log(MAX_DIST / max_exact) * (nb - max_exact))
            val = min(val, nb - 1)
        out[idx] = ret + val
    return out


def _oh_const():
    b = _bucket1d()
    oh = np.zeros((32, 511), dtype=np.float32)
    oh[b, np.arange(511)] = 1.0
    return oh * math.sqrt(128.0)


_NC_CACHE = None


def kernel(L0_attn_a_x, L0_attn_a_ln_g, L0_attn_a_ln_b, L0_attn_a_W_h,
           L0_attn_a_b_h, L0_attn_a_W_qk, L0_attn_a_b_qk, L0_attn_a_os_gamma,
           L0_attn_a_os_beta, L0_attn_a_rel_emb, L0_attn_a_W_out,
           L0_attn_a_b_out, L0_attn_b_W_dq, L0_attn_b_W_uq, L0_attn_b_W_dkv,
           L0_attn_b_W_uk, L0_attn_b_W_uv, L0_attn_b_W_o,
           L0_ffn_router_ln_scale, L0_ffn_router_ln_bias, L0_ffn_router_w,
           L0_ffn_router_b, L0_ffn_expert_ln_scale, L0_ffn_expert_ln_bias,
           L0_ffn_w1, L0_ffn_b1, L0_ffn_w2, L0_ffn_b2):
    global _NC_CACHE
    x = np.ascontiguousarray(np.asarray(L0_attn_a_x)).astype(np.float32)

    common = {
        "ln_g": L0_attn_a_ln_g, "ln_b": L0_attn_a_ln_b,
        "W_h": L0_attn_a_W_h, "b_h": L0_attn_a_b_h,
        "W_qk": L0_attn_a_W_qk, "b_qk": L0_attn_a_b_qk,
        "os_gamma": L0_attn_a_os_gamma, "os_beta": L0_attn_a_os_beta,
        "rel_emb": L0_attn_a_rel_emb, "W_out": L0_attn_a_W_out,
        "b_out": L0_attn_a_b_out,
        "W_dq": L0_attn_b_W_dq, "W_uq": L0_attn_b_W_uq,
        "W_dkv": L0_attn_b_W_dkv, "W_uk": L0_attn_b_W_uk,
        "W_uv": L0_attn_b_W_uv, "W_o": L0_attn_b_W_o,
        "r_ls": L0_ffn_router_ln_scale, "r_lb": L0_ffn_router_ln_bias,
        "r_w": L0_ffn_router_w, "r_b": L0_ffn_router_b,
        "e_ls": L0_ffn_expert_ln_scale, "e_lb": L0_ffn_expert_ln_bias,
        "w1": L0_ffn_w1, "b1": L0_ffn_b1, "w2": L0_ffn_w2, "b2": L0_ffn_b2,
        "OH": _oh_const(),
    }
    common = {k: np.ascontiguousarray(np.asarray(v)).astype(np.float32)
              for k, v in common.items()}

    in_maps = []
    for c in range(8):
        m = dict(common)
        m["x"] = np.ascontiguousarray(x[2 * c:2 * c + 2].reshape(T, D))
        in_maps.append(m)

    if _NC_CACHE is None:
        _NC_CACHE = build_nc()
    res = run_bass_kernel_spmd(_NC_CACHE, in_maps, core_ids=list(range(8)))
    out = np.concatenate(
        [r["y"].reshape(NSEQ, S, D) for r in res.results], axis=0)
    return out.astype(np.float32)
